# version 13
# speedup vs baseline: 1.6845x; 1.0025x over previous
import numpy as np
import ml_dtypes
from contextlib import ExitStack

import concourse.bass as bass
import concourse.mybir as mybir
from concourse.bass_utils import run_bass_kernel_spmd

F32 = mybir.dt.float32
BF16 = mybir.dt.bfloat16
AF = mybir.ActivationFunctionType
ALU = mybir.AluOpType
AX = mybir.AxisListType

NCORE = 8
D = 1024
DC = 8
TOK = 2048
B = 2
S = 8192
CAP = 5120
ENGS = ("v", "s", "t", "g", "y")


class Ctx:

    DMA_ROT = 4

    def __init__(self):
        self.cnt = {e: 0 for e in ENGS}
        for i in range(self.DMA_ROT):
            self.cnt[f"y{i}"] = 0
            self.cnt[f"gd{i}"] = 0
        self.cnt["cc"] = 0
        self.rr = {"y": 0, "gd": 0}
        self.cur = None
        self.eng = None
        self.sems = None

    def _emit(self, en):
        return self.cur == en and self.eng is not None

    def op(self, en, emit, inc=True, serialize=True, dma=False, sem=None):
        step = 16 if (dma or en in ("y",)) else 1
        if dma and sem is None:
            q = "y" if en == "y" else "gd"
            k = self.rr[q]
            self.rr[q] = (k + 1) % self.DMA_ROT
            sem = f"{q}{k}"
            prev = self.cnt[sem]
            self.cnt[sem] += step
            n = self.cnt[sem]
            if self._emit(en):
                if prev:
                    self.eng[en].wait_ge(self.sems[sem], prev)
                inst = emit(self.eng[en])
                inst.then_inc(self.sems[sem], step)
            return (sem, n)
        if sem is None:
            sem = en
        elif sem == "cc":
            step = 1
        if inc:
            self.cnt[sem] += step
        n = self.cnt[sem]
        if self._emit(en):
            inst = emit(self.eng[en])
            if inc:
                inst.then_inc(self.sems[sem], step)
                if serialize and en != "t":
                    self.eng[en].wait_ge(self.sems[sem], n)
        return n

    def wait(self, en, producer, n=None):
        if producer in ("y", "gd") and n is None:
            for i in range(self.DMA_ROT):
                sname = f"{producer}{i}"
                if self.cnt[sname] and self._emit(en):
                    self.eng[en].wait_ge(self.sems[sname], self.cnt[sname])
            return None
        if n is None:
            n = self.cnt[producer]
        if self._emit(en):
            self.eng[en].wait_ge(self.sems[producer], n)
        return n

    def wait_tok(self, en, tok):
        if tok is None:
            return
        sem, n = tok
        if self._emit(en):
            self.eng[en].wait_ge(self.sems[sem], n)

    def barrier(self, *ens):
        snap = {e: self.cnt.get(e) for e in ens}
        for e in ens:
            for p in ens:
                if p == e:
                    continue
                if p in ("y", "gd"):
                    self.wait(e, p)
                else:
                    self.wait(e, p, snap[p])

    def full_barrier(self):
        ens = ("v", "s", "t", "g", "y")
        snap = {e: self.cnt.get(e) for e in ens}
        ncc = self.cnt["cc"]
        for e in ens:
            for p in ens:
                if p in (e, "y"):
                    continue
                self.wait(e, p, snap[p])
            self.wait(e, "y")
            self.wait(e, "gd")
            if ncc:
                self.wait(e, "cc", ncc)


def build_program(name, decl_fn, body_fn):
    nc = bass.Bass("TRN2", target_bir_lowering=False)
    st = {}
    stack = ExitStack()
    with stack:
        st["_nc"] = nc
        st["_stack"] = stack
        decl_fn(nc, st)
        sems = {}
        semnames = ["v", "s", "t", "g", "cc"]
        for i in range(Ctx.DMA_ROT):
            semnames += [f"y{i}", f"gd{i}"]
        for sname in semnames:
            sems[sname] = stack.enter_context(nc.semaphore("sem_" + sname))
        with nc.Block() as block:
            def run_pass(en, engine_obj):
                ctx = Ctx()
                ctx.cur = en
                ctx.eng = {en: engine_obj}
                ctx.sems = sems
                body_fn(ctx, st)

            @block.vector
            def _(vector):
                run_pass("v", vector)

            @block.scalar
            def _(scalar):
                run_pass("s", scalar)

            @block.tensor
            def _(tensor):
                run_pass("t", tensor)

            @block.gpsimd
            def _(gpsimd):
                run_pass("g", gpsimd)

            @block.sync
            def _(sync):
                run_pass("y", sync)
                ctx = Ctx()
                ctx.cur = None
                body_fn(ctx, st)
                for i in range(Ctx.DMA_ROT):
                    if ctx.cnt[f"y{i}"]:
                        sync.wait_ge(sems[f"y{i}"], ctx.cnt[f"y{i}"])
                    if ctx.cnt[f"gd{i}"]:
                        sync.wait_ge(sems[f"gd{i}"], ctx.cnt[f"gd{i}"])
    return nc



def sb(nc, st, name, shape, dt=F32):
    tname = st.get("_pfx", "") + name
    st[name] = st["_stack"].enter_context(nc.sbuf_tensor(tname, list(shape), dt))


def ps(nc, st, name, shape, dt=F32):
    tname = st.get("_pfx", "") + name
    st[name] = st["_stack"].enter_context(nc.psum_tensor(tname, list(shape), dt))


def dram_in(nc, st, name, shape, dt=F32):
    if name in st:
        return
    st[name] = nc.dram_tensor(name, list(shape), dt, kind="ExternalInput").ap()


def dram_out(nc, st, name, shape, dt=F32):
    if name in st:
        return
    st[name] = nc.dram_tensor(name, list(shape), dt, kind="ExternalOutput").ap()


def load_consts(ctx, st, vals):
    for cname, v in vals.items():
        ctx.op("g", lambda g, c=cname, vv=v: g.memset(st[c][:], vv))


def emit_ln(ctx, st, z, xout, ncol, eps_tile, pfx, tc=None):
    zbf, sq = st[pfx + "_zbf"], st[pfx + "_sq"]
    rows = st[pfx + "_rows"]
    def _ap(x):
        return x if isinstance(x, bass.AP) else x[:]
    s1, s2 = _ap(st[pfx + "_s1"]), _ap(st[pfx + "_s2"])
    bca, bcb = _ap(st[pfx + "_bca"]), _ap(st[pfx + "_bcb"])
    mean = rows[:, 0 * ncol:1 * ncol]
    var = rows[:, 1 * ncol:2 * ncol]
    rstd = rows[:, 2 * ncol:3 * ncol]
    negmrs = rows[:, 3 * ncol:4 * ncol]

    ctx.wait("s", "v")
    for c in range(DC):
        ctx.op("v", lambda v, c=c: v.tensor_copy(zbf[:, c, :], z[:, c, :]))
        ctx.op("s", lambda s, c=c: s.activation(sq[:, c, :], z[:, c, :], AF.Square))
    nv = ctx.cnt["v"]
    nsq = ctx.cnt["s"]
    ctx.wait("t", "v", nv)
    ctx.wait("t", "s", nsq)
    s2start = not st.get("ln_s2_nostart", False)
    for c in range(DC):
        ctx.op("t", lambda t, c=c: t.matmul(s1, st["ones_bf"][:], zbf[:, c, :],
                                            start=(c == 0), stop=(c == DC - 1),
                                            skip_group_check=True), inc=(c == DC - 1))
    for c in range(DC):
        ctx.op("t", lambda t, c=c: t.matmul(s2, st["ones_bf"][:], sq[:, c, :],
                                            start=(c == 0 and s2start), stop=(c == DC - 1),
                                            skip_group_check=True), inc=(c == DC - 1))
    nmm = ctx.cnt["t"]
    ctx.wait("v", "t", nmm)
    ctx.op("v", lambda v: v.tensor_scalar(mean, s1, 1.0 / D, None, ALU.mult))
    ctx.op("v", lambda v: v.tensor_scalar(var, s2, 1.0 / D, None, ALU.mult))
    ctx.wait("s", "v", ctx.cnt["v"])
    ctx.op("s", lambda s: s.activation(rstd, mean, AF.Square))
    ctx.wait("v", "s", ctx.cnt["s"])
    ctx.op("v", lambda v: v.tensor_sub(var, var, rstd))
    ctx.wait("s", "v", ctx.cnt["v"])
    ctx.op("s", lambda s: s.activation(rstd, var, AF.Sqrt, bias=eps_tile[:]))
    ctx.wait("v", "s", ctx.cnt["s"])
    ctx.op("v", lambda v: v.reciprocal(rstd, rstd))
    ctx.op("v", lambda v: v.tensor_mul(negmrs, mean, rstd))
    ctx.op("v", lambda v: v.tensor_scalar(negmrs, negmrs, -1.0, None, ALU.mult))
    nrows = ctx.cnt["v"]
    ctx.wait("t", "v", nrows)
    bcbstart = not st.get("ln_bcb_nostart", False)
    ctx.op("t", lambda t: t.matmul(bca, st["ones_f1"][:], rstd, start=True, stop=True,
                                   skip_group_check=True))
    ctx.op("t", lambda t: t.matmul(bcb, st["ones_f1"][:], negmrs, start=bcbstart, stop=True,
                                   skip_group_check=True))
    nbc = ctx.cnt["t"]
    ctx.wait("v", "t", nbc)
    for c in range(DC):
        ctx.op("v", lambda v, c=c: v.tensor_mul(xout[:, c, :], z[:, c, :], bca))
        ctx.op("v", lambda v, c=c: v.tensor_add(xout[:, c, :], xout[:, c, :], bcb))
    return ctx.cnt["v"]




NQ2 = 512

def decl_B2(nc, st):
    dram_in(nc, st, "xT", [D, TOK])
    dram_in(nc, st, "Wu", [D, 2048])
    dram_in(nc, st, "buf", [128, 16])
    dram_in(nc, st, "Wo", [2048, D])
    dram_in(nc, st, "bof", [128, 8])
    dram_out(nc, st, "x1T", [D, TOK])

    sb(nc, st, "xbf", [128, DC, TOK], BF16)
    sb(nc, st, "kbf", [128, S], BF16)
    sb(nc, st, "qbf", [128, TOK], BF16)
    sb(nc, st, "uc", [128, 16, NQ2], BF16)
    with ExitStack() as _sub:
        _saved = st["_stack"]
        st["_stack"] = _sub
        sb(nc, st, "sqT", [128, 64, NQ2], BF16)
        st["_stack"] = _saved
    sb(nc, st, "z", [128, DC, NQ2])
    sb(nc, st, "x1f", [128, DC, NQ2])
    sb(nc, st, "ln_zbf", [128, DC, NQ2], BF16)
    sb(nc, st, "ln_sq", [128, DC, NQ2], BF16)
    sb(nc, st, "pad64", [128, 16 * NQ2 // 4])
    sb(nc, st, "warena", [128, 16, 1024], BF16)
    sb(nc, st, "vdb", [128, 2, 64, 128], BF16)
    sb(nc, st, "sctmp", [128, 2, NQ2], BF16)
    sb(nc, st, "but", [128, 16])
    sb(nc, st, "bot", [128, 8])
    sb(nc, st, "ones_bf", [128, 1], BF16)
    sb(nc, st, "ones_f1", [1, 128])
    sb(nc, st, "c_eps", [1, 1])
    sb(nc, st, "ln_rows", [1, 4 * NQ2])
    ps(nc, st, "pa", [128, NQ2])
    ps(nc, st, "pb", [128, NQ2])
    ps(nc, st, "pc", [128, NQ2])
    ps(nc, st, "pdx", [128, NQ2])
    ps(nc, st, "ln_s1", [1, NQ2])
    ps(nc, st, "ln_s2", [1, NQ2])
    ps(nc, st, "ln_bca", [128, NQ2])
    ps(nc, st, "ln_bcb", [128, NQ2])


def body_B2(ctx, st):
    load_consts(ctx, st, {"c_eps": 1e-5})
    ctx.op("g", lambda g: g.memset(st["ones_bf"][:], 1.0))
    ctx.op("g", lambda g: g.memset(st["ones_f1"][:], 1.0))
    ctx.op("y", lambda y: y.dma_start(out=st["but"][:], in_=st["buf"]), dma=True)
    ctx.op("y", lambda y: y.dma_start(out=st["bot"][:], in_=st["bof"]), dma=True)
    ctx.op("g", lambda g: g.dma_start(
        out=st["xbf"][:], in_=st["xT"].rearrange("(c p) t -> p c t", p=128)), dma=True)
    for r in range(4):
        ctx.op("y", lambda y, r=r: y.dma_start(
            out=st["kbf"][:, r * TOK:(r + 1) * TOK], in_=st["kb4"][r]), dma=True)
    ctx.op("g", lambda g: g.dma_start(out=st["qbf"][:], in_=st["qf"]), dma=True)
    ctx.barrier("v", "s", "t", "g", "y")
    ctx.wait("t", "gd")
    ctx.wait("t", "y")
    ctx.wait("v", "gd")
    ctx.wait("s", "gd")

    banksA = ("pa", "pb")
    evacA = [None, None]

    def vload(qc_ec, buf):
        ec = qc_ec
        toks = []
        for r in range(4):
            toks.append(ctx.op("y", lambda y, r=r, ec=ec, buf=buf: y.dma_start(
                out=st["vdb"][:, buf, r * 16:(r + 1) * 16, :],
                in_=st["vb4"](r, ec)), dma=True))
        return toks

    for qc in range(TOK // NQ2):
        qsl = slice(qc * NQ2, (qc + 1) * NQ2)
        ctx.wait("g", "t")
        wut = ctx.op("g", lambda g: g.dma_start(
            out=st["warena"].ap().rearrange("p a b -> p (a b)").rearrange(
                "p (c f) -> p c f", c=DC),
            in_=st["Wu"].rearrange("(c p) f -> p c f", p=128)), dma=True)
        ctx.wait_tok("t", wut)
        wu = st["warena"].ap().rearrange("p a b -> p (a b)").rearrange("p (c f) -> p c f", c=DC)
        for ec in range(16):
            bk = ec % 2
            pgb = st[banksA[bk]]
            if evacA[bk] is not None:
                ctx.wait("t", "s", evacA[bk])
            for c in range(DC):
                ctx.op("t", lambda t, ec=ec, c=c, qsl=qsl, pgb=pgb, wu=wu: t.matmul(
                    pgb[:], wu[:, c, ec * 128:(ec + 1) * 128],
                    st["xbf"][:, c, qsl], start=(c == 0), stop=(c == DC - 1)),
                    inc=(c == DC - 1))
            nm = ctx.cnt["t"]
            ctx.wait("s", "t", nm)
            ctx.op("s", lambda s, ec=ec, pgb=pgb: s.activation(
                st["uc"][:, ec, :], pgb[:], AF.Silu, bias=st["but"][:, ec:ec + 1]))
            evacA[bk] = ctx.cnt["s"]
        ctx.wait("v", "y")
        ctx.wait("s", "v")
        sc_evac = [None, None]
        for kt in range(64):
            bk = kt % 2
            psb = st["pc"] if bk == 0 else st["pdx"]
            if sc_evac[bk] is not None:
                ctx.wait("t", "v", sc_evac[bk])
            ctx.op("t", lambda t, kt=kt, qsl=qsl, psb=psb: t.matmul(
                psb[:], st["kbf"][:, kt * 128:(kt + 1) * 128],
                st["qbf"][:, qsl], start=True, stop=True), inc=True)
            nm = ctx.cnt["t"]
            ctx.wait("v", "t", nm)
            ctx.op("v", lambda v, kt=kt, bk=bk, psb=psb: v.tensor_scalar(
                st["sctmp"][:, bk, :], psb[:], 0.0, 1.0 / S, ALU.max, ALU.mult))
            ctx.op("v", lambda v, kt=kt, bk=bk: v.tensor_mul(
                st["sqT"][:, kt, :], st["sctmp"][:, bk, :], st["sctmp"][:, bk, :]))
            sc_evac[bk] = ctx.cnt["v"]
        ctx.wait("t", "v")
        av_evac = [None, None]
        vtoks = {0: vload(0, 0)}
        for ec in range(16):
            if ec + 1 < 16:
                if ec >= 1:
                    ctx.wait("y", "t")
                vtoks[ec + 1] = vload(ec + 1, (ec + 1) % 2)
            for tok in vtoks[ec]:
                ctx.wait_tok("t", tok)
            bk = ec % 2
            pvb = st["pa"] if bk == 0 else st["pb"]
            if av_evac[bk] is not None:
                ctx.wait("t", "v", av_evac[bk])
            if ec < 2:
                ctx.wait("t", "s", evacA[0] if ec == 0 else evacA[1])
            for kt in range(64):
                ctx.op("t", lambda t, kt=kt, ec=ec, bk=bk, pvb=pvb: t.matmul(
                    pvb[:], st["vdb"][:, ec % 2, kt, :], st["sqT"][:, kt, :],
                    start=(kt == 0), stop=(kt == 63)), inc=(kt == 63))
            nm = ctx.cnt["t"]
            ctx.wait("v", "t", nm)
            ctx.op("v", lambda v, ec=ec, pvb=pvb: v.tensor_mul(
                st["uc"][:, ec, :], st["uc"][:, ec, :], pvb[:]))
            av_evac[bk] = ctx.cnt["v"]
        ctx.wait("g", "t")
        wot = ctx.op("g", lambda g: g.dma_start(
            out=st["warena"][:], in_=st["Wo"].rearrange("(ec p) h -> p ec h", p=128)), dma=True)
        ctx.wait("y", "t")
        ctx.op("y", lambda y, qsl=qsl: y.dma_start(
            out=st["x1f"][:], in_=st["xT"].rearrange("(c p) t -> p c t", p=128)[:, :, qsl]),
            dma=True)
        ctx.wait_tok("t", wot)
        ctx.wait("t", "v")
        po_evac = [None, None]
        for dc in range(DC):
            bk = dc % 2
            pob = st["pc"] if bk == 0 else st["pdx"]
            if po_evac[bk] is not None:
                ctx.wait("t", "s", po_evac[bk])
            for ec in range(16):
                ctx.op("t", lambda t, dc=dc, ec=ec, pob=pob: t.matmul(
                    pob[:], st["warena"][:, ec, dc * 128:(dc + 1) * 128],
                    st["uc"][:, ec, :], start=(ec == 0), stop=(ec == 15)),
                    inc=(ec == 15))
            nm = ctx.cnt["t"]
            ctx.wait("s", "t", nm)
            ctx.op("s", lambda s, dc=dc, pob=pob: s.activation(
                st["z"][:, dc, :], pob[:], AF.Identity, bias=st["bot"][:, dc:dc + 1]))
            po_evac[bk] = ctx.cnt["s"]
        ctx.wait("v", "s")
        ctx.wait("v", "y")
        for dc in range(DC):
            ctx.op("v", lambda v, dc=dc: v.tensor_add(st["z"][:, dc, :], st["z"][:, dc, :],
                                                      st["x1f"][:, dc, :]))
        ctx.wait("t", "v")
        emit_ln(ctx, st, st["z"], st["z"], NQ2, st["c_eps"], "ln")
        nv = ctx.cnt["v"]
        ctx.wait("y", "v", nv)
        ctx.op("y", lambda y, qsl=qsl: y.dma_start(
            out=st["x1T"].rearrange("(c p) t -> p c t", p=128)[:, :, qsl],
            in_=st["z"][:]), dma=True)
        evacA = [None, None]
        for i in range(2):
            sc_evac[i] = None



def decl_C(nc, st):
    dram_in(nc, st, "x1T", [D, TOK])
    dram_in(nc, st, "W1T", [D, 4096])
    dram_in(nc, st, "b1f", [128, 32])
    dram_in(nc, st, "W2T", [4096, D])
    dram_in(nc, st, "b2f", [128, 8])
    dram_out(nc, st, "x2T", [D, TOK])

    NT = 256
    st["NT"] = NT
    sb(nc, st, "wq1", [128, DC, 4096], BF16)
    sb(nc, st, "wq2", [128, 32, 1024], BF16)
    sb(nc, st, "x1bf", [128, DC, NT], BF16)
    sb(nc, st, "x1f", [128, DC, NT])
    sb(nc, st, "wstage", [128, 2048])
    sb(nc, st, "wneg", [128, 2048], BF16)
    sb(nc, st, "h", [128, 32, NT], BF16)
    sb(nc, st, "z", [128, DC, NT])
    sb(nc, st, "absr", [128, 2])
    sb(nc, st, "b1t", [128, 32])
    sb(nc, st, "b2t", [128, 8])
    sb(nc, st, "thr", [128, 4])
    sb(nc, st, "grow", [1, 2])
    sb(nc, st, "ones_bf", [128, 1], BF16)
    sb(nc, st, "ones_c", [128, 1])
    sb(nc, st, "ones_f1", [1, 128])
    sb(nc, st, "c_eps", [1, 1])
    sb(nc, st, "c_qeps", [1, 1])
    sb(nc, st, "ln_zbf", [128, DC, NT], BF16)
    sb(nc, st, "ln_sq", [128, DC, NT], BF16)
    sb(nc, st, "ln_rows", [1, 4 * NT])
    ps(nc, st, "ln_s1", [1, NT])
    ps(nc, st, "ln_s2", [1, NT])
    ps(nc, st, "ln_bca", [128, NT])
    ps(nc, st, "ln_bcb", [128, NT])
    ps(nc, st, "pgam", [1, 2])
    ps(nc, st, "pthr", [128, 2])
    ps(nc, st, "pmm", [128, NT])
    ps(nc, st, "pmm2", [128, NT])


def quantize_weight(ctx, st, wT_dram, nchunks, ncols, wq, col):
    nh = max(1, ncols // 2048)
    hw = ncols // nh
    first = True
    for c in range(nchunks):
        for hh in range(nh):
            ctx.op("y", lambda y, c=c, hh=hh: y.dma_start(
                out=st["wstage"][:, 0:hw],
                in_=wT_dram[c * 128:(c + 1) * 128, hh * hw:(hh + 1) * hw]), dma=True)
            ctx.wait("v", "y")
            ctx.op("v", lambda v: v.tensor_reduce(
                st["absr"][:, 1:2], st["wstage"][:, 0:hw], AX.X, ALU.add,
                apply_absolute_value=True))
            if first:
                ctx.op("v", lambda v: v.tensor_copy(st["absr"][:, 0:1], st["absr"][:, 1:2]))
                first = False
            else:
                ctx.op("v", lambda v: v.tensor_add(st["absr"][:, 0:1], st["absr"][:, 0:1],
                                                   st["absr"][:, 1:2]))
            ctx.wait("y", "v")
    ctx.wait("t", "v")
    ctx.op("t", lambda t: t.matmul(st["pgam"][:, col:col + 1], st["ones_c"][:],
                                   st["absr"][:, 0:1], start=True, stop=True))
    ctx.wait("v", "t")
    ctx.op("v", lambda v: v.tensor_scalar(st["grow"][:, col:col + 1], st["pgam"][:, col:col + 1],
                                          1.0 / (nchunks * 128 * ncols), None, ALU.mult))
    ctx.wait("s", "v")
    ctx.op("s", lambda s: s.activation(st["grow"][:, col:col + 1], st["grow"][:, col:col + 1],
                                       AF.Identity, bias=st["c_qeps"][:]))
    ctx.op("s", lambda s: s.mul(st["grow"][:, col:col + 1], st["grow"][:, col:col + 1], 0.5))
    ctx.wait("t", "s")
    ctx.op("t", lambda t: t.matmul(st["pthr"][:, col:col + 1], st["ones_f1"][:],
                                   st["grow"][:, col:col + 1], start=True, stop=True))
    ctx.wait("v", "t")
    ctx.op("v", lambda v: v.tensor_copy(st["thr"][:, 2 * col:2 * col + 1], st["pthr"][:, col:col + 1]))
    ctx.op("v", lambda v: v.tensor_scalar(st["thr"][:, 2 * col + 1:2 * col + 2],
                                          st["pthr"][:, col:col + 1], -1.0, None, ALU.mult))
    for c in range(nchunks):
        for hh in range(nh):
            csl = slice(hh * hw, (hh + 1) * hw)
            ctx.op("y", lambda y, c=c, hh=hh: y.dma_start(
                out=st["wstage"][:, 0:hw],
                in_=wT_dram[c * 128:(c + 1) * 128, hh * hw:(hh + 1) * hw]), dma=True)
            ctx.wait("v", "y")
            ctx.op("v", lambda v, c=c, csl=csl: v.tensor_scalar(
                wq[:, c, csl], st["wstage"][:, 0:hw],
                st["thr"][:, 2 * col:2 * col + 1], None, ALU.is_gt))
            ctx.op("v", lambda v: v.tensor_scalar(
                st["wneg"][:, 0:hw], st["wstage"][:, 0:hw],
                st["thr"][:, 2 * col + 1:2 * col + 2], None, ALU.is_lt))
            ctx.op("v", lambda v, c=c, csl=csl: v.tensor_sub(wq[:, c, csl], wq[:, c, csl],
                                                             st["wneg"][:, 0:hw]))
            ctx.wait("y", "v")


def body_C(ctx, st):
    NT = st["NT"]
    NTC = TOK // NT
    load_consts(ctx, st, {"c_eps": 1e-5, "c_qeps": 1e-5})
    ctx.op("g", lambda g: g.memset(st["ones_bf"][:], 1.0))
    ctx.op("g", lambda g: g.memset(st["ones_f1"][:], 1.0))
    ctx.op("g", lambda g: g.memset(st["ones_c"][:], 1.0))
    ctx.op("y", lambda y: y.dma_start(out=st["b1t"][:], in_=st["b1f"]), dma=True)
    ctx.op("y", lambda y: y.dma_start(out=st["b2t"][:], in_=st["b2f"]), dma=True)
    ctx.barrier("v", "s", "t", "g", "y")

    quantize_weight(ctx, st, st["W1T"], DC, 4096, st["wq1"], 0)
    quantize_weight(ctx, st, st["W2T"], 32, 1024, st["wq2"], 1)
    ctx.barrier("v", "s", "t", "y")

    for tc in range(NTC):
        tsl = slice(tc * NT, (tc + 1) * NT)
        ctx.barrier("v", "s", "t", "g", "y")
        ctx.op("y", lambda y, tsl=tsl: y.dma_start(
            out=st["x1f"][:], in_=st["x1T"].rearrange("(c p) t -> p c t", p=128)[:, :, tsl]),
            dma=True)
        ctx.wait("v", "y")
        for c in range(DC):
            ctx.op("v", lambda v, c=c: v.tensor_copy(st["x1bf"][:, c, :], st["x1f"][:, c, :]))
        nxbf = ctx.cnt["v"]
        ctx.wait("t", "v", nxbf)
        for fc in range(32):
            for c in range(DC):
                ctx.op("t", lambda t, c=c, fc=fc: t.matmul(
                    st["pmm"][:], st["wq1"][:, c, fc * 128:(fc + 1) * 128],
                    st["x1bf"][:, c, :], start=(c == 0), stop=(c == DC - 1)),
                    inc=(c == DC - 1))
            nm = ctx.cnt["t"]
            ctx.wait("s", "t", nm)
            ctx.op("s", lambda s, fc=fc: s.activation(
                st["h"][:, fc, :], st["pmm"][:], AF.Relu, bias=st["b1t"][:, fc:fc + 1]))
            ctx.wait("t", "s")
        for dc in range(DC):
            for fc in range(32):
                ctx.op("t", lambda t, dc=dc, fc=fc: t.matmul(
                    st["pmm2"][:], st["wq2"][:, fc, dc * 128:(dc + 1) * 128],
                    st["h"][:, fc, :], start=(fc == 0), stop=(fc == 31)),
                    inc=(fc == 31))
            nm = ctx.cnt["t"]
            ctx.wait("s", "t", nm)
            ctx.op("s", lambda s, dc=dc: s.activation(
                st["z"][:, dc, :], st["pmm2"][:], AF.Identity, bias=st["b2t"][:, dc:dc + 1]))
            ctx.wait("t", "s")
        ctx.wait("v", "s")
        for dc in range(DC):
            ctx.op("v", lambda v, dc=dc: v.tensor_add(st["z"][:, dc, :], st["z"][:, dc, :],
                                                      st["x1f"][:, dc, :]))
        emit_ln(ctx, st, st["z"], st["z"], NT, st["c_eps"], "ln")
        nv = ctx.cnt["v"]
        ctx.wait("y", "v", nv)
        ctx.op("y", lambda y, tsl=tsl: y.dma_start(
            out=st["x2T"].rearrange("(c p) t -> p c t", p=128)[:, :, tsl],
            in_=st["z"][:]), dma=True)


def build_C():
    return build_program("C", decl_C, body_C)





def decl_A(nc, st):
    dram_in(nc, st, "xT", [D, TOK])
    dram_in(nc, st, "Wvb", [D, 2176])
    dram_in(nc, st, "bvbf", [128, 1])
    dram_in(nc, st, "bvrow", [1, 2048])
    dram_in(nc, st, "gb", [128, 4])
    dram_out(nc, st, "vblk", [16, TOK, 128])
    dram_out(nc, st, "qf", [128, TOK])
    dram_out(nc, st, "kf", [128, TOK])

    sb(nc, st, "xbf", [128, DC, TOK], BF16)
    sb(nc, st, "wvb", [128, DC, 2176], BF16)
    sb(nc, st, "bcv", [128, 2048])
    sb(nc, st, "bvb", [128, 1])
    sb(nc, st, "bvr", [1, 2048])
    sb(nc, st, "gbt", [128, 4])
    sb(nc, st, "vt", [128, 512], BF16)
    sb(nc, st, "vz", [128, 512])
    sb(nc, st, "baset", [128, TOK])
    sb(nc, st, "qt", [128, TOK])
    sb(nc, st, "kt", [128, TOK])
    sb(nc, st, "ones_f1", [1, 128])
    ps(nc, st, "pv", [128, 512])
    ps(nc, st, "pb", [128, 512])
    ps(nc, st, "pbc", [128, 512])


def body_A(ctx, st):
    ctx.op("g", lambda g: g.memset(st["ones_f1"][:], 1.0))
    ctx.op("y", lambda y: y.dma_start(out=st["bvb"][:], in_=st["bvbf"]), dma=True)
    ctx.op("y", lambda y: y.dma_start(out=st["bvr"][:], in_=st["bvrow"]), dma=True)
    ctx.op("y", lambda y: y.dma_start(out=st["gbt"][:], in_=st["gb"]), dma=True)
    ctx.op("g", lambda g: g.dma_start(
        out=st["xbf"][:], in_=st["xT"].rearrange("(c p) t -> p c t", p=128)), dma=True)
    ctx.op("g", lambda g: g.dma_start(
        out=st["wvb"][:], in_=st["Wvb"].rearrange("(c p) f -> p c f", p=128)), dma=True)
    ctx.barrier("v", "s", "t", "g", "y")
    ctx.wait("v", "gd")
    ctx.wait("t", "gd")

    for ec in range(4):
        ctx.wait("t", "v")
        ctx.op("t", lambda t, ec=ec: t.matmul(
            st["pbc"][:], st["ones_f1"][:], st["bvr"][:, ec * 512:(ec + 1) * 512],
            start=True, stop=True))
        ctx.wait("v", "t")
        ctx.op("v", lambda v, ec=ec: v.tensor_copy(st["bcv"][:, ec * 512:(ec + 1) * 512],
                                                   st["pbc"][:]))
    ctx.wait("t", "v")

    for tk in range(16):
        for ec in range(4):
            for c in range(DC):
                ctx.op("t", lambda t, tk=tk, ec=ec, c=c: t.matmul(
                    st["pv"][:], st["xbf"][:, c, tk * 128:(tk + 1) * 128],
                    st["wvb"][:, c, ec * 512:(ec + 1) * 512],
                    start=(c == 0), stop=(c == DC - 1)), inc=(c == DC - 1))
            nm = ctx.cnt["t"]
            ctx.wait("v", "t", nm)
            ctx.wait("v", "s")
            ctx.op("v", lambda v, ec=ec: v.tensor_add(st["vz"][:], st["pv"][:],
                                                      st["bcv"][:, ec * 512:(ec + 1) * 512]))
            ctx.wait("s", "v")
            ctx.op("s", lambda s: s.activation(st["vt"][:], st["vz"][:], AF.Silu))
            nsi = ctx.cnt["s"]
            ctx.wait("g", "s", nsi)
            ctx.op("g", lambda g, tk=tk, ec=ec: g.dma_start(
                out=st["vblk"][ec * 4:(ec + 1) * 4, tk * 128:(tk + 1) * 128, :].rearrange(
                    "s p e -> p s e"),
                in_=st["vt"].ap().rearrange("p (s e) -> p s e", e=128)), dma=True)
            ctx.wait("t", "v")
            ctx.wait("s", "gd")
    for tc in range(4):
        tsl = slice(tc * 512, (tc + 1) * 512)
        for c in range(DC):
            ctx.op("t", lambda t, tc=tc, c=c: t.matmul(
                st["pb"][:], st["wvb"][:, c, 2048:2176],
                st["xbf"][:, c, tsl], start=(c == 0), stop=(c == DC - 1)),
                inc=(c == DC - 1))
        nm = ctx.cnt["t"]
        ctx.wait("s", "t", nm)
        ctx.op("s", lambda s, tsl=tsl: s.activation(
            st["baset"][:, tsl], st["pb"][:], AF.Silu, bias=st["bvb"][:]))
        ctx.wait("t", "s")
    ctx.wait("v", "s")
    ctx.op("v", lambda v: v.tensor_scalar(st["qt"][:], st["baset"][:], st["gbt"][:, 0:1],
                                          st["gbt"][:, 1:2], ALU.mult, ALU.add))
    ctx.op("v", lambda v: v.tensor_scalar(st["kt"][:], st["baset"][:], st["gbt"][:, 2:3],
                                          st["gbt"][:, 3:4], ALU.mult, ALU.add))
    nv = ctx.cnt["v"]
    ctx.wait("y", "v", nv)
    ctx.wait("g", "v", nv)
    ctx.op("y", lambda y: y.dma_start(out=st["qf"], in_=st["qt"][:]), dma=True)
    ctx.op("g", lambda g: g.dma_start(out=st["kf"], in_=st["kt"][:]), dma=True)


def build_A():
    return build_program("A", decl_A, body_A)



NQ = 256

def decl_B(nc, st):
    dram_in(nc, st, "xT", [D, TOK])
    dram_in(nc, st, "Wu", [D, 2048])
    dram_in(nc, st, "buf", [128, 16])
    dram_in(nc, st, "qf", [128, TOK])
    if "kb4" not in st:
        dram_in(nc, st, "kb", [128, S])
        dram_in(nc, st, "vblkb", [16, S, 128])
    dram_in(nc, st, "Wo", [2048, D])
    dram_in(nc, st, "bof", [128, 8])
    dram_out(nc, st, "x1T", [D, TOK])

    sb(nc, st, "xbf", [128, DC, TOK], BF16)
    sb(nc, st, "kbf", [128, S], BF16)
    sb(nc, st, "qbf", [128, TOK], BF16)
    sb(nc, st, "wstr", [128, 16, 1024], BF16)
    sb(nc, st, "uc", [128, 16, NQ], BF16)
    sb(nc, st, "sqT", [128, 64, NQ], BF16)
    sb(nc, st, "gc", [128, 16, NQ], BF16)
    sb(nc, st, "sqt", [128, NQ], BF16)
    sb(nc, st, "z", [128, DC, NQ])
    sb(nc, st, "x1f", [128, DC, NQ])
    sb(nc, st, "but", [128, 16])
    sb(nc, st, "bot", [128, 8])
    sb(nc, st, "ones_bf", [128, 1], BF16)
    sb(nc, st, "ones_f1", [1, 128])
    sb(nc, st, "c_eps", [1, 1])
    sb(nc, st, "ln_zbf", [128, DC, NQ], BF16)
    sb(nc, st, "ln_sq", [128, DC, NQ], BF16)
    sb(nc, st, "ln_rows", [1, 4 * NQ])
    ps(nc, st, "ln_s1", [1, NQ])
    ps(nc, st, "ln_s2", [1, NQ])
    ps(nc, st, "ln_bca", [128, NQ])
    ps(nc, st, "ln_bcb", [128, NQ])
    ps(nc, st, "pu", [128, NQ])
    ps(nc, st, "psc", [128, NQ])
    ps(nc, st, "pav", [128, NQ])
    ps(nc, st, "po", [128, NQ])


def body_B(ctx, st):
    load_consts(ctx, st, {"c_eps": 1e-5})
    ctx.op("g", lambda g: g.memset(st["ones_bf"][:], 1.0))
    ctx.op("g", lambda g: g.memset(st["ones_f1"][:], 1.0))
    ctx.op("y", lambda y: y.dma_start(out=st["but"][:], in_=st["buf"]), dma=True)
    ctx.op("y", lambda y: y.dma_start(out=st["bot"][:], in_=st["bof"]), dma=True)
    ctx.op("g", lambda g: g.dma_start(
        out=st["xbf"][:], in_=st["xT"].rearrange("(c p) t -> p c t", p=128)), dma=True)
    if "kb4" in st:
        for r in range(4):
            ctx.op("g", lambda g, r=r: g.dma_start(
                out=st["kbf"][:, r * TOK:(r + 1) * TOK], in_=st["kb4"][r]), dma=True)
    else:
        ctx.op("g", lambda g: g.dma_start(out=st["kbf"][:], in_=st["kb"]), dma=True)
    ctx.op("g", lambda g: g.dma_start(out=st["qbf"][:], in_=st["qf"]), dma=True)
    ctx.barrier("v", "s", "t", "g", "y")
    ctx.wait("v", "gd")
    ctx.wait("t", "gd")
    ctx.wait("s", "gd")

    for qc in range(TOK // NQ):
        qsl = slice(qc * NQ, (qc + 1) * NQ)
        ctx.barrier("v", "s", "t", "g", "y")
        ctx.op("g", lambda g: g.dma_start(
            out=st["wstr"].ap().rearrange("p a b -> p (a b)").rearrange("p (c f) -> p c f", c=DC),
            in_=st["Wu"].rearrange("(c p) f -> p c f", p=128)), dma=True)
        ctx.wait("t", "gd")
        wu = st["wstr"].ap().rearrange("p a b -> p (a b)").rearrange("p (c f) -> p c f", c=DC)
        for ec in range(16):
            for c in range(DC):
                ctx.op("t", lambda t, ec=ec, c=c, qsl=qsl, wu=wu: t.matmul(
                    st["pu"][:], wu[:, c, ec * 128:(ec + 1) * 128],
                    st["xbf"][:, c, qsl], start=(c == 0), stop=(c == DC - 1)),
                    inc=(c == DC - 1))
            nm = ctx.cnt["t"]
            ctx.wait("s", "t", nm)
            ctx.op("s", lambda s, ec=ec: s.activation(
                st["uc"][:, ec, :], st["pu"][:], AF.Silu, bias=st["but"][:, ec:ec + 1]))
            ctx.wait("t", "s")
        for kt in range(64):
            ctx.op("t", lambda t, kt=kt, qsl=qsl: t.matmul(
                st["psc"][:], st["kbf"][:, kt * 128:(kt + 1) * 128],
                st["qbf"][:, qsl], start=True, stop=True))
            nm = ctx.cnt["t"]
            ctx.wait("v", "t", nm)
            ctx.op("v", lambda v: v.tensor_scalar(st["sqt"][:], st["psc"][:],
                                                  0.0, 1.0 / S, ALU.max, ALU.mult))
            ctx.op("v", lambda v, kt=kt: v.tensor_mul(st["sqT"][:, kt, :], st["sqt"][:],
                                                      st["sqt"][:]))
            ctx.wait("t", "v")
        ctx.wait("g", "t")
        for ec in range(16):
            if "vb4" in st:
                for r in range(4):
                    ctx.op("g", lambda g, ec=ec, r=r: g.dma_start(
                        out=st["wstr"].ap().rearrange("p a b -> p (a b)")[:, r * 2048:(r + 1) * 2048].rearrange(
                            "p (kt e) -> p kt e", e=128),
                        in_=st["vb4"](r, ec)), dma=True)
            else:
                ctx.op("g", lambda g, ec=ec: g.dma_start(
                    out=st["wstr"].ap().rearrange("p a b -> p (a b)")[:, 0:8192].rearrange(
                        "p (kt e) -> p kt e", e=128),
                    in_=st["vblkb"][ec].rearrange("(kt p) e -> p kt e", p=128)), dma=True)
            ctx.wait("t", "gd")
            vt = st["wstr"].ap().rearrange("p a b -> p (a b)")[:, 0:8192].rearrange(
                "p (kt e) -> p kt e", e=128)
            for kt in range(64):
                ctx.op("t", lambda t, kt=kt, vt=vt: t.matmul(
                    st["pav"][:], vt[:, kt, :], st["sqT"][:, kt, :],
                    start=(kt == 0), stop=(kt == 63)), inc=(kt == 63))
            nm = ctx.cnt["t"]
            ctx.wait("v", "t", nm)
            ctx.op("v", lambda v, ec=ec: v.tensor_mul(st["gc"][:, ec, :], st["pav"][:],
                                                      st["uc"][:, ec, :]))
            nv = ctx.cnt["v"]
            ctx.wait("t", "v", nv)
            ctx.wait("g", "v", nv)
        ctx.op("g", lambda g: g.dma_start(
            out=st["wstr"][:], in_=st["Wo"].rearrange("(ec p) h -> p ec h", p=128)), dma=True)
        ctx.op("y", lambda y, qsl=qsl: y.dma_start(
            out=st["x1f"][:], in_=st["xT"].rearrange("(c p) t -> p c t", p=128)[:, :, qsl]),
            dma=True)
        ctx.wait("t", "gd")
        for dc in range(DC):
            for ec in range(16):
                ctx.op("t", lambda t, dc=dc, ec=ec: t.matmul(
                    st["po"][:], st["wstr"][:, ec, dc * 128:(dc + 1) * 128],
                    st["gc"][:, ec, :], start=(ec == 0), stop=(ec == 15)),
                    inc=(ec == 15))
            nm = ctx.cnt["t"]
            ctx.wait("s", "t", nm)
            ctx.op("s", lambda s, dc=dc: s.activation(
                st["z"][:, dc, :], st["po"][:], AF.Identity, bias=st["bot"][:, dc:dc + 1]))
            ctx.wait("t", "s")
        ctx.wait("v", "s")
        ctx.wait("v", "y")
        for dc in range(DC):
            ctx.op("v", lambda v, dc=dc: v.tensor_add(st["z"][:, dc, :], st["z"][:, dc, :],
                                                      st["x1f"][:, dc, :]))
        emit_ln(ctx, st, st["z"], st["z"], NQ, st["c_eps"], "ln")
        nv = ctx.cnt["v"]
        ctx.wait("y", "v", nv)
        ctx.op("y", lambda y, qsl=qsl: y.dma_start(
            out=st["x1T"].rearrange("(c p) t -> p c t", p=128)[:, :, qsl],
            in_=st["z"][:]), dma=True)


def build_B():
    return build_program("B", decl_B, body_B)





def decl_D1(nc, st):
    dram_in(nc, st, "x2T", [D, TOK])
    dram_in(nc, st, "nwf", [128, 8])
    dram_in(nc, st, "Wd", [D, D])
    dram_in(nc, st, "bdf", [128, 8])
    dram_in(nc, st, "WBC", [D, 32])
    dram_in(nc, st, "dskf", [128, 8])
    dram_out(nc, st, "deltaT", [D, TOK])
    dram_out(nc, st, "m1T", [D, TOK])
    dram_out(nc, st, "BpCT", [32, TOK])
    dram_out(nc, st, "yskipT", [D, TOK])

    NT = 512
    st["NT"] = NT
    sb(nc, st, "wd", [128, DC, D], BF16)
    sb(nc, st, "wbc", [128, DC, 32], BF16)
    sb(nc, st, "x2f", [128, DC, NT])
    sb(nc, st, "xwbf", [128, DC, NT], BF16)
    sb(nc, st, "sqb", [128, DC, NT], BF16)
    sb(nc, st, "brs", [128, NT])
    sb(nc, st, "zt", [128, NT])
    sb(nc, st, "abst", [128, NT])
    sb(nc, st, "relut", [128, NT])
    sb(nc, st, "spt", [128, NT])
    sb(nc, st, "dft", [128, NT])
    sb(nc, st, "sqd", [128, NT])
    sb(nc, st, "xnb", [128, NT], BF16)
    sb(nc, st, "dbo", [128, DC, NT])
    sb(nc, st, "m1o", [128, DC, NT])
    sb(nc, st, "yso", [128, DC, NT])
    sb(nc, st, "bpco", [32, NT])
    sb(nc, st, "rrow", [1, NT])
    sb(nc, st, "nwt", [128, 8])
    sb(nc, st, "bdt", [128, 8])
    sb(nc, st, "dskt", [128, 8])
    sb(nc, st, "ones_bf", [128, 1], BF16)
    sb(nc, st, "ones_f1", [1, 128])
    sb(nc, st, "c_eps6", [1, 1])
    ps(nc, st, "ps2", [1, NT])
    ps(nc, st, "pbc", [128, NT])
    ps(nc, st, "pdd", [128, NT])
    ps(nc, st, "pbp", [32, NT])


SC_C = (10.0 + 1e-4) / 2.0
SC_H = (10.0 - 1e-4) / 2.0


def body_D1(ctx, st):
    NT = st["NT"]
    load_consts(ctx, st, {"c_eps6": 1e-6})
    ctx.op("g", lambda g: g.memset(st["ones_bf"][:], 1.0))
    ctx.op("g", lambda g: g.memset(st["ones_f1"][:], 1.0))
    ctx.op("y", lambda y: y.dma_start(out=st["nwt"][:], in_=st["nwf"]), dma=True)
    ctx.op("y", lambda y: y.dma_start(out=st["bdt"][:], in_=st["bdf"]), dma=True)
    ctx.op("y", lambda y: y.dma_start(out=st["dskt"][:], in_=st["dskf"]), dma=True)
    ctx.op("g", lambda g: g.dma_start(
        out=st["wd"][:], in_=st["Wd"].rearrange("(c p) f -> p c f", p=128)), dma=True)
    ctx.op("g", lambda g: g.dma_start(
        out=st["wbc"][:], in_=st["WBC"].rearrange("(c p) f -> p c f", p=128)), dma=True)
    ctx.barrier("v", "s", "t", "g", "y")
    ctx.wait("t", "gd")

    for tc in range(TOK // NT):
        tsl = slice(tc * NT, (tc + 1) * NT)
        ctx.barrier("v", "s", "t", "g", "y")
        ctx.op("y", lambda y, tsl=tsl: y.dma_start(
            out=st["x2f"][:], in_=st["x2T"].rearrange("(c p) t -> p c t", p=128)[:, :, tsl]),
            dma=True)
        ctx.wait("v", "y")
        ctx.wait("s", "y")
        for c in range(DC):
            ctx.op("v", lambda v, c=c: v.tensor_scalar(
                st["xwbf"][:, c, :], st["x2f"][:, c, :], st["nwt"][:, c:c + 1], None, ALU.mult))
            ctx.op("s", lambda s, c=c: s.activation(st["sqb"][:, c, :], st["x2f"][:, c, :],
                                                    AF.Square))
        nv, ns = ctx.cnt["v"], ctx.cnt["s"]
        ctx.wait("t", "v", nv)
        ctx.wait("t", "s", ns)
        for c in range(DC):
            ctx.op("t", lambda t, c=c: t.matmul(st["ps2"][:], st["ones_bf"][:],
                                                st["sqb"][:, c, :],
                                                start=(c == 0), stop=(c == DC - 1)),
                   inc=(c == DC - 1))
        ctx.wait("s", "t")
        ctx.op("s", lambda s: s.activation(st["rrow"][:], st["ps2"][:], AF.Sqrt,
                                           scale=1.0 / D, bias=st["c_eps6"][:]))
        ctx.wait("v", "s")
        ctx.op("v", lambda v: v.reciprocal(st["rrow"][:], st["rrow"][:]))
        nv = ctx.cnt["v"]
        ctx.wait("t", "v", nv)
        ctx.op("t", lambda t: t.matmul(st["pbc"][:], st["ones_f1"][:], st["rrow"][:],
                                       start=True, stop=True))
        ctx.wait("v", "t")
        ctx.op("v", lambda v: v.tensor_copy(st["brs"][:], st["pbc"][:]))
        for c in range(DC):
            ctx.op("t", lambda t, c=c: t.matmul(st["pbp"][:], st["wbc"][:, c, :],
                                                st["xwbf"][:, c, :],
                                                start=(c == 0), stop=(c == DC - 1)),
                   inc=(c == DC - 1))
        ctx.wait("v", "t")
        ctx.op("v", lambda v: v.tensor_mul(st["bpco"][:], st["pbp"][:],
                                           st["brs"][0:32, :]))
        nv = ctx.cnt["v"]
        ctx.wait("y", "v", nv)
        ctx.op("y", lambda y, tsl=tsl: y.dma_start(
            out=st["BpCT"][:, tsl], in_=st["bpco"][:]), dma=True)
        for dc in range(DC):
            for c in range(DC):
                ctx.op("t", lambda t, dc=dc, c=c: t.matmul(
                    st["pdd"][:], st["wd"][:, c, dc * 128:(dc + 1) * 128],
                    st["xwbf"][:, c, :], start=(c == 0), stop=(c == DC - 1)),
                    inc=(c == DC - 1))
            nm = ctx.cnt["t"]
            ctx.wait("v", "t", nm)
            ctx.op("v", lambda v: v.tensor_mul(st["zt"][:], st["pdd"][:], st["brs"][:]))
            nv = ctx.cnt["v"]
            ctx.wait("t", "v", nv)
            ctx.wait("s", "v", nv)
            ctx.op("s", lambda s, dc=dc: s.activation(st["abst"][:], st["zt"][:], AF.Abs,
                                                      bias=st["bdt"][:, dc:dc + 1]))
            ctx.op("s", lambda s: s.activation(st["abst"][:], st["abst"][:], AF.Exp,
                                               scale=-1.0))
            ctx.op("s", lambda s: s.activation(st["abst"][:], st["abst"][:], AF.Ln,
                                               bias=1.0))
            ctx.op("v", lambda v, dc=dc: v.tensor_scalar(st["relut"][:], st["zt"][:],
                                                         st["bdt"][:, dc:dc + 1], 0.0,
                                                         ALU.add, ALU.max))
            ctx.wait("v", "s")
            ctx.op("v", lambda v: v.tensor_add(st["spt"][:], st["relut"][:], st["abst"][:]))
            ctx.wait("s", "v")
            ctx.op("s", lambda s: s.activation(st["dft"][:], st["spt"][:], AF.Tanh,
                                               scale=1.0 / SC_H, bias=st["c_negch"][:]))
            ctx.wait("v", "s")
            ctx.op("v", lambda v: v.tensor_scalar(st["dft"][:], st["dft"][:], SC_H, SC_C,
                                                  ALU.mult, ALU.add))
            ctx.op("v", lambda v, dc=dc: v.tensor_copy(st["dbo"][:, dc, :], st["dft"][:]))
            ctx.wait("s", "v")
            ctx.op("s", lambda s: s.activation(st["sqd"][:], st["dft"][:], AF.Sqrt))
            ctx.op("v", lambda v, dc=dc: v.tensor_mul(st["xnb"][:], st["xwbf"][:, dc, :],
                                                      st["brs"][:]))
            ctx.wait("v", "s")
            ctx.op("v", lambda v, dc=dc: v.tensor_mul(st["m1o"][:, dc, :], st["sqd"][:],
                                                      st["xnb"][:]))
            ctx.op("v", lambda v, dc=dc: v.tensor_scalar(st["yso"][:, dc, :], st["xnb"][:],
                                                         st["dskt"][:, dc:dc + 1], None,
                                                         ALU.mult))
        nv, ns = ctx.cnt["v"], ctx.cnt["s"]
        ctx.wait("y", "v", nv)
        for nm_, src in (("deltaT", "dbo"), ("m1T", "m1o"), ("yskipT", "yso")):
            ctx.op("y", lambda y, nm_=nm_, src=src, tsl=tsl: y.dma_start(
                out=st[nm_].rearrange("(c p) t -> p c t", p=128)[:, :, tsl],
                in_=st[src][:]), dma=True)


def decl_D1x(nc, st):
    decl_D1(nc, st)
    sb(nc, st, "c_negch", [128, 1])


def body_D1x(ctx, st):
    load_consts(ctx, st, {"c_negch": -SC_C / SC_H})
    body_D1(ctx, st)


def build_D1():
    return build_program("D1", decl_D1x, body_D1x)



def decl_D2(nc, st):
    if "dmx_out" not in st:
        dram_in(nc, st, "deltaB", [D, S])
        dram_in(nc, st, "m1B", [D, S])
        dram_in(nc, st, "Bp4", [4, S])
        dram_in(nc, st, "C4", [4, S])
    else:
        dram_in(nc, st, "bpcsel", [32, 8])
    dram_in(nc, st, "A4", [D, 4])
    if "dmx_out" not in st:
        dram_out(nc, st, "ypart", [D, S])

    TB = 2048
    st["TB"] = TB
    sb(nc, st, "dsl", [128, DC, TB], BF16)
    sb(nc, st, "m1sl", [128, DC, TB], BF16)
    sb(nc, st, "ya", [128, DC, TB], BF16)
    sb(nc, st, "zt", [128, 2, TB])
    sb(nc, st, "at", [128, 2, TB])
    sb(nc, st, "bt", [128, 2, TB], BF16)
    sb(nc, st, "ht", [128, 2, TB])
    sb(nc, st, "m2t", [128, 2, TB], BF16)
    sb(nc, st, "mt", [128, 2, TB], BF16)
    sb(nc, st, "bcB", [128, TB], BF16)
    sb(nc, st, "bcC", [128, TB], BF16)
    sb(nc, st, "bp1", [1, TB])
    sb(nc, st, "c1", [1, TB])
    sb(nc, st, "selc", [32, 8])
    sb(nc, st, "bpc32", [32, TB])
    sb(nc, st, "e2", [128, DC, 4])
    sb(nc, st, "carry", [128, 32])
    sb(nc, st, "ones_f1", [1, 128])
    ps(nc, st, "pb1", [128, 512])
    ps(nc, st, "pc1", [1, 512])


def body_D2(ctx, st):
    TB = st["TB"]
    ctx.op("g", lambda g: g.memset(st["ones_f1"][:], 1.0))
    ctx.op("g", lambda g: g.memset(st["carry"][:], 0.0))
    if "dmx_out" in st:
        ctx.op("y", lambda y: y.dma_start(out=st["selc"][:], in_=st["bpcsel"]), dma=True)
    ctx.op("y", lambda y: y.dma_start(
        out=st["e2"][:], in_=st["A4"].rearrange("(c p) n -> p c n", p=128)), dma=True)
    ctx.wait("s", "y")
    for c in range(DC):
        ctx.op("s", lambda s, c=c: s.activation(st["e2"][:, c, :], st["e2"][:, c, :], AF.Exp))
    ctx.wait("v", "s")
    ctx.op("v", lambda v: v.tensor_scalar(
        st["e2"].ap().rearrange("p a b -> p (a b)"),
        st["e2"].ap().rearrange("p a b -> p (a b)"), 0.5, None, ALU.mult))
    ctx.barrier("v", "s", "t", "g", "y")

    for tb in range(S // TB):
        bsl = slice(tb * TB, (tb + 1) * TB)
        ctx.barrier("v", "s", "t", "g", "y")
        if "dmx_out" in st:
            ctx.op("g", lambda g, tb=tb: g.dma_start(
                out=st["dsl"][:],
                in_=st["dmx_out"][tb * 2080:tb * 2080 + 1024, :].rearrange(
                    "(c p) t -> p c t", p=128)), dma=True)
            ctx.op("g", lambda g, tb=tb: g.dma_start(
                out=st["m1sl"][:],
                in_=st["dmx_out"][tb * 2080 + 1024:tb * 2080 + 2048, :].rearrange(
                    "(c p) t -> p c t", p=128)), dma=True)
            ctx.op("y", lambda y, tb=tb: y.dma_start(
                out=st["bpc32"][:], in_=st["dmx_out"][tb * 2080 + 2048:tb * 2080 + 2080, :]),
                dma=True)
        else:
            ctx.op("g", lambda g, bsl=bsl: g.dma_start(
                out=st["dsl"][:], in_=st["deltaB"].rearrange("(c p) t -> p c t", p=128)[:, :, bsl]),
                dma=True)
            ctx.op("g", lambda g, bsl=bsl: g.dma_start(
                out=st["m1sl"][:], in_=st["m1B"].rearrange("(c p) t -> p c t", p=128)[:, :, bsl]),
                dma=True)
        ctx.wait("v", "gd")
        ctx.wait("s", "gd")
        ctx.wait("g", "gd")
        ctx.barrier("v", "s", "t", "g", "y")
        tiles = [(n, dc) for n in range(4) for dc in range(DC)]
        tok_r = {}
        tok_m2 = {}
        tok_ym = {}
        tok_yadd = {}
        tok_z = {}
        tok_scan = {}
        tok_bc = {}

        def emit_bc(n):
            if "dmx_out" in st:
                ctx.wait("t", "y")
                ctx.wait("t", "v")
                for half in range(TB // 512):
                    hsl = slice(half * 512, (half + 1) * 512)
                    ctx.op("t", lambda t, n=n, hsl=hsl: t.matmul(
                        st["pb1"][0:1, :], st["selc"][:, n:n + 1], st["bpc32"][:, hsl],
                        start=True, stop=True))
                    ctx.op("t", lambda t, n=n, hsl=hsl: t.matmul(
                        st["pc1"][0:1, :],
                        st["selc"][:, 4 + n:5 + n], st["bpc32"][:, hsl],
                        start=True, stop=True))
                    nm = ctx.cnt["t"]
                    ctx.wait("v", "t", nm)
                    ctx.op("v", lambda v, hsl=hsl: v.tensor_copy(st["bp1"][:, hsl], st["pb1"][0:1, :]))
                    ctx.op("v", lambda v, hsl=hsl: v.tensor_copy(st["c1"][:, hsl], st["pc1"][0:1, :]))
                    nv = ctx.cnt["v"]
                    ctx.wait("t", "v", nv)
            else:
                ctx.wait("y", "v")
                ctx.op("y", lambda y, n=n, bsl=bsl: y.dma_start(out=st["bp1"][:],
                                                                in_=st["Bp4"][n:n + 1, bsl]), dma=True)
                ctx.op("y", lambda y, n=n, bsl=bsl: y.dma_start(out=st["c1"][:],
                                                                in_=st["C4"][n:n + 1, bsl]), dma=True)
                ctx.wait("t", "y")
            for half in range(TB // 512):
                hsl = slice(half * 512, (half + 1) * 512)
                ctx.wait("t", "v")
                ctx.op("t", lambda t, hsl=hsl: t.matmul(
                    st["pb1"][:], st["ones_f1"][:], st["bp1"][:, hsl],
                    start=True, stop=True))
                ctx.wait("v", "t")
                ctx.op("v", lambda v, hsl=hsl: v.tensor_copy(st["bcB"][:, hsl], st["pb1"][:]))
                ctx.wait("t", "v")
                ctx.op("t", lambda t, hsl=hsl: t.matmul(
                    st["pb1"][:], st["ones_f1"][:], st["c1"][:, hsl],
                    start=True, stop=True))
                ctx.wait("v", "t")
                ctx.op("v", lambda v, hsl=hsl: v.tensor_copy(st["bcC"][:, hsl], st["pb1"][:]))

        def stage_z(i):
            n, dc = tiles[i]
            p = i % 2
            if i >= 2:
                ctx.wait("v", "s", tok_r[i - 2])
            ctx.op("v", lambda v, n=n, dc=dc, p=p: v.tensor_scalar(
                st["zt"][:, p, :], st["dsl"][:, dc, :], st["e2"][:, dc, n:n + 1], 1.0,
                ALU.mult, ALU.add))
            tok_z[i] = ctx.cnt["v"]
            ctx.wait("g", "v", max(tok_bc[n], tok_scan.get(i - 2, 0)))
            ctx.op("g", lambda g, n=n, dc=dc, p=p: g.tensor_tensor(
                st["m2t"][:, p, :], st["m1sl"][:, dc, :], st["bcB"][:], ALU.mult))
            tok_m2[i] = ctx.cnt["g"]

        def stage_lnr(i):
            p = i % 2
            ctx.wait("s", "v", tok_z[i])
            ctx.op("s", lambda s, p=p: s.activation(st["zt"][:, p, :], st["zt"][:, p, :], AF.Ln))
            ctx.op("s", lambda s, p=p: s.activation(st["zt"][:, p, :], st["zt"][:, p, :],
                                                    AF.Exp, scale=-1.0))
            tok_r[i] = ctx.cnt["s"]

        def stage_v(i):
            n, dc = tiles[i]
            ct = n * 8 + dc
            p = i % 2
            ctx.wait("v", "s", tok_r[i])
            ctx.op("v", lambda v, p=p: v.tensor_scalar(st["at"][:, p, :], st["zt"][:, p, :],
                                                       2.0, -1.0, ALU.mult, ALU.add))
            ctx.wait("v", "g", tok_m2[i])
            ctx.op("v", lambda v, p=p: v.tensor_mul(st["bt"][:, p, :], st["m2t"][:, p, :],
                                                    st["zt"][:, p, :]))
            ctx.op("v", lambda v, p=p, ct=ct: v.tensor_tensor_scan(
                st["ht"][:, p, :], st["at"][:, p, :], st["bt"][:, p, :],
                st["carry"][:, ct:ct + 1], ALU.mult, ALU.add))
            ctx.op("v", lambda v, p=p, ct=ct: v.tensor_copy(st["carry"][:, ct:ct + 1],
                                                            st["ht"][:, p, TB - 1:TB]))
            tok_scan[i] = ctx.cnt["v"]
            if i >= 2:
                ctx.wait("v", "g", tok_yadd.get(i - 2, 0))
            ctx.op("v", lambda v, p=p: v.tensor_mul(st["mt"][:, p, :], st["ht"][:, p, :],
                                                    st["bcC"][:]))
            tok_ym[i] = ctx.cnt["v"]

        def stage_yadd(i):
            n, dc = tiles[i]
            p = i % 2
            ctx.wait("g", "v", tok_ym[i])
            if n == 0:
                ctx.op("g", lambda g, dc=dc, p=p: g.tensor_copy(st["ya"][:, dc, :],
                                                                st["mt"][:, p, :]))
            else:
                ctx.op("g", lambda g, dc=dc, p=p: g.tensor_tensor(
                    st["ya"][:, dc, :], st["ya"][:, dc, :], st["mt"][:, p, :], ALU.add))
            tok_yadd[i] = ctx.cnt["g"]

        for n in range(4):
            if n > 0:
                ctx.wait("v", "g", tok_m2[n * 8 - 1])
                ctx.wait("t", "g", tok_m2[n * 8 - 1])
                ctx.wait("t", "v", tok_ym[n * 8 - 1])
            emit_bc(n)
            tok_bc[n] = ctx.cnt["v"]
            base = n * 8
            stage_z(base)
            stage_lnr(base)
            for dc in range(DC):
                i = base + dc
                if dc + 1 < DC:
                    stage_z(i + 1)
                    stage_lnr(i + 1)
                stage_v(i)
                stage_yadd(i)
        nv = ctx.cnt["v"]
        ctx.wait("g", "v", nv)
        if "dmx_out" in st:
            ctx.op("g", lambda g, tb=tb: g.dma_start(
                out=st["yp_in"][tb * 1024:(tb + 1) * 1024, :].rearrange(
                    "(c p) t -> p c t", p=128),
                in_=st["ya"][:]), dma=True)
        else:
            ctx.op("g", lambda g, bsl=bsl: g.dma_start(
                out=st["ypart"].rearrange("(c p) t -> p c t", p=128)[:, :, bsl],
                in_=st["ya"][:]), dma=True)
        ctx.wait("v", "gd")


def build_D2():
    return build_program("D2", decl_D2, body_D2)



def decl_D3(nc, st):
    dram_in(nc, st, "x2T", [D, TOK])
    if "D3_srcs" not in st:
        for i in range(4):
            dram_in(nc, st, f"yp{i}", [D, TOK])
    dram_in(nc, st, "yskipT", [D, TOK])
    dram_out(nc, st, "x3T", [D, TOK])

    NT = 512
    st["NT"] = NT
    sb(nc, st, "x2f", [128, DC, NT])
    sb(nc, st, "ypl", [128, 5, DC, NT])
    sb(nc, st, "z", [128, DC, NT])
    sb(nc, st, "ones_bf", [128, 1], BF16)
    sb(nc, st, "ones_f1", [1, 128])
    sb(nc, st, "c_eps", [1, 1])
    sb(nc, st, "ln_zbf", [128, DC, NT], BF16)
    sb(nc, st, "ln_sq", [128, DC, NT], BF16)
    sb(nc, st, "ln_rows", [1, 4 * NT])
    ps(nc, st, "ln_s1", [1, NT])
    ps(nc, st, "ln_s2", [1, NT])
    ps(nc, st, "ln_bca", [128, NT])
    ps(nc, st, "ln_bcb", [128, NT])


def body_D3(ctx, st):
    NT = st["NT"]
    load_consts(ctx, st, {"c_eps": 1e-5})
    ctx.op("g", lambda g: g.memset(st["ones_bf"][:], 1.0))
    ctx.op("g", lambda g: g.memset(st["ones_f1"][:], 1.0))
    for tc in range(TOK // NT):
        tsl = slice(tc * NT, (tc + 1) * NT)
        ctx.barrier("v", "s", "t", "g", "y")
        ctx.op("y", lambda y, tsl=tsl: y.dma_start(
            out=st["x2f"][:], in_=st["x2T"].rearrange("(c p) t -> p c t", p=128)[:, :, tsl]),
            dma=True)
        srcs = st.get("D3_srcs", ["yp0", "yp1", "yp2", "yp3", "yskipT"])
        for i, nm_ in enumerate(srcs):
            ctx.op("y", lambda y, i=i, nm_=nm_, tsl=tsl: y.dma_start(
                out=st["ypl"][:, i, :, :],
                in_=st[nm_].rearrange("(c p) t -> p c t", p=128)[:, :, tsl]), dma=True)
        ctx.wait("v", "y")
        for dc in range(DC):
            ctx.op("v", lambda v, dc=dc: v.tensor_add(st["z"][:, dc, :], st["x2f"][:, dc, :],
                                                      st["ypl"][:, 0, dc, :]))
            for i in range(1, len(srcs)):
                ctx.op("v", lambda v, dc=dc, i=i: v.tensor_add(
                    st["z"][:, dc, :], st["z"][:, dc, :], st["ypl"][:, i, dc, :]))
        emit_ln(ctx, st, st["z"], st["z"], NT, st["c_eps"], "ln")
        nv = ctx.cnt["v"]
        ctx.wait("y", "v", nv)
        ctx.op("y", lambda y, tsl=tsl: y.dma_start(
            out=st["x3T"].rearrange("(c p) t -> p c t", p=128)[:, :, tsl],
            in_=st["z"][:]), dma=True)


def build_D3():
    return build_program("D3", decl_D3, body_D3)


def stage_D1(x2T_cores, inp):
    nwf = np.ascontiguousarray(inp["L1_attn_norm_weight"].reshape(8, 128).T)
    bdf = np.ascontiguousarray(inp["L1_attn_b_delta"].reshape(8, 128).T)
    dskf = np.ascontiguousarray(inp["L1_attn_D_skip"].reshape(8, 128).T)
    WBC = np.ascontiguousarray(np.concatenate([inp["L1_attn_W_B"], inp["L1_attn_W_C"]], axis=1))
    Wd = np.ascontiguousarray(inp["L1_attn_W_delta"])
    return [{"x2T": x2T_cores[i], "nwf": nwf, "Wd": Wd, "bdf": bdf, "WBC": WBC,
             "dskf": dskf} for i in range(NCORE)]


def stage_D2(resD1, inp):
    maps = []
    A = np.ascontiguousarray(inp["L1_attn_A_log"])
    for i in range(NCORE):
        b, nq = i // 4, i % 4
        grp = [4 * b + j for j in range(4)]
        deltaB = np.concatenate([np.asarray(resD1[j]["deltaT"]) for j in grp], axis=1)
        m1B = np.concatenate([np.asarray(resD1[j]["m1T"]) for j in grp], axis=1)
        BpC = np.concatenate([np.asarray(resD1[j]["BpCT"]) for j in grp], axis=1)
        Bp4 = np.ascontiguousarray(BpC[nq * 4:(nq + 1) * 4, :])
        C4 = np.ascontiguousarray(BpC[16 + nq * 4:16 + (nq + 1) * 4, :])
        A4 = np.ascontiguousarray(A[:, nq * 4:(nq + 1) * 4])
        maps.append({"deltaB": np.ascontiguousarray(deltaB),
                     "m1B": np.ascontiguousarray(m1B),
                     "Bp4": Bp4, "C4": C4, "A4": A4})
    return maps


def stage_D3(x2T_cores, resD1, resD2):
    maps = []
    for i in range(NCORE):
        b, qd = i // 4, i % 4
        tsl = slice(qd * TOK, (qd + 1) * TOK)
        m = {"x2T": x2T_cores[i], "yskipT": np.asarray(resD1[i]["yskipT"])}
        for nq in range(4):
            src = np.asarray(resD2[4 * b + nq]["ypart"])[:, tsl]
            m[f"yp{nq}"] = np.ascontiguousarray(src)
        maps.append(m)
    return maps





def decl_E1(nc, st):
    dram_in(nc, st, "x3T", [D, TOK])
    dram_in(nc, st, "gwT", [D, 8])
    dram_out(nc, st, "masko", [16, 128, 8])
    dram_out(nc, st, "combpre", [16, 128, 8])

    sb(nc, st, "x3f32", [128, DC, TOK])
    sb(nc, st, "gwf", [128, DC, 8])
    sb(nc, st, "lg", [128, 8])
    sb(nc, st, "m8", [128, 8])
    sb(nc, st, "negm", [128, 1])
    sb(nc, st, "et", [128, 8])
    sb(nc, st, "srow", [128, 1])
    sb(nc, st, "mask", [128, 8])
    sb(nc, st, "pm", [128, 8])
    sb(nc, st, "cp", [128, 8])
    sb(nc, st, "c_tiny", [128, 1])
    ps(nc, st, "plg", [128, 8])


def body_E1(ctx, st):
    load_consts(ctx, st, {"c_tiny": 1e-20})
    ctx.op("y", lambda y: y.dma_start(
        out=st["x3f32"][:], in_=st["x3T"].rearrange("(c p) t -> p c t", p=128)), dma=True)
    ctx.op("y", lambda y: y.dma_start(
        out=st["gwf"][:], in_=st["gwT"].rearrange("(c p) e -> p c e", p=128)), dma=True)
    ctx.barrier("v", "s", "t", "g", "y")
    for tt in range(16):
        tl = slice(tt * 128, (tt + 1) * 128)
        ctx.wait("t", "v")
        for c in range(DC):
            ctx.op("t", lambda t, c=c, tl=tl: t.matmul(
                st["plg"][:], st["x3f32"][:, c, tl], st["gwf"][:, c, :],
                start=(c == 0), stop=(c == DC - 1)), inc=(c == DC - 1))
        nm = ctx.cnt["t"]
        ctx.wait("v", "t", nm)
        ctx.op("v", lambda v: v.tensor_copy(st["lg"][:], st["plg"][:]))
        ctx.op("v", lambda v: v.max(out=st["m8"][:], in_=st["lg"][:]))
        ctx.op("v", lambda v: v.tensor_scalar(st["negm"][:], st["m8"][:, 0:1], -1.0, None,
                                              ALU.mult))
        ctx.wait("s", "v")
        ctx.op("s", lambda s: s.activation(st["et"][:], st["lg"][:], AF.Exp,
                                           bias=st["negm"][:]))
        ctx.wait("v", "s")
        ctx.op("v", lambda v: v.tensor_reduce(st["srow"][:], st["et"][:], AX.X, ALU.add))
        ctx.op("v", lambda v: v.reciprocal(st["srow"][:], st["srow"][:]))
        ctx.op("v", lambda v: v.tensor_scalar(st["mask"][:], st["lg"][:], st["m8"][:, 1:2],
                                              None, ALU.is_ge))
        ctx.op("v", lambda v: v.tensor_mul(st["pm"][:], st["et"][:], st["mask"][:]))
        ctx.op("v", lambda v: v.tensor_scalar(st["pm"][:], st["pm"][:], st["srow"][:], None,
                                              ALU.mult))
        ctx.op("v", lambda v: v.tensor_reduce(st["srow"][:], st["pm"][:], AX.X, ALU.add))
        ctx.op("v", lambda v: v.tensor_scalar(st["srow"][:], st["srow"][:], st["c_tiny"][:],
                                              None, ALU.add))
        ctx.op("v", lambda v: v.reciprocal(st["srow"][:], st["srow"][:]))
        ctx.op("v", lambda v: v.tensor_scalar(st["cp"][:], st["pm"][:], st["srow"][:], None,
                                              ALU.mult))
        nv = ctx.cnt["v"]
        ctx.wait("y", "v", nv)
        ctx.op("y", lambda y, tt=tt: y.dma_start(out=st["masko"][tt], in_=st["mask"][:]),
               dma=True)
        ctx.op("y", lambda y, tt=tt: y.dma_start(out=st["combpre"][tt], in_=st["cp"][:]),
               dma=True)
        ctx.wait("v", "y")


def build_E1():
    return build_program("E1", decl_E1, body_E1)



def decl_E2(nc, st):
    dram_in(nc, st, "x3T", [D, TOK])
    dram_in(nc, st, "combpre", [16, 128, 8])
    dram_in(nc, st, "maskown", [16, 128, 8])
    if "maskg" not in st:
        dram_in(nc, st, "maskprev", [112, 128, 8])
    else:
        dram_in(nc, st, "rowsel", [128, 1])
    dram_in(nc, st, "wgT", [8, D, 1024])
    dram_in(nc, st, "wuT", [8, D, 1024])
    dram_in(nc, st, "wdT", [8, 1024, D])
    dram_in(nc, st, "wsgT", [D, 2048])
    dram_in(nc, st, "wsuT", [D, 2048])
    dram_in(nc, st, "wsdT", [2048, D])
    dram_out(nc, st, "x4T", [D, TOK])
    st["combd"] = nc.dram_tensor("combd", [8, TOK], F32, kind="Internal").ap()

    NT = 512
    st["NT"] = NT
    if "webf2" not in st:
        st["webf2"] = nc.dram_tensor("webf2", [10, 3, 1024, 1024], BF16, kind="Internal").ap()
        st["E2_precast"] = True
    sb(nc, st, "x3bf", [128, DC, TOK], BF16)
    sb(nc, st, "acte", [128, 8, NT], BF16)
    sb(nc, st, "warena", [128, 2, 3, 8, 1024], BF16)
    sb(nc, st, "ysb", [128, DC, NT])
    sb(nc, st, "x3f", [128, DC, NT])
    if "maskg" in st:
        sb(nc, st, "mpre", [128, 128, 8], BF16)
        sb(nc, st, "rst", [128, 1])
    else:
        sb(nc, st, "mpre", [128, 112, 8], BF16)
    sb(nc, st, "mtile", [128, 8], BF16)
    sb(nc, st, "prow", [128, 8])
    sb(nc, st, "post", [128, 8])
    sb(nc, st, "cpre", [128, 8])
    sb(nc, st, "combm", [128, 16, 8])
    sb(nc, st, "ctile", [8, 128])
    sb(nc, st, "combrow", [1, NT])
    sb(nc, st, "onesrow", [1, NT])
    sb(nc, st, "offs", [1, 8])
    sb(nc, st, "trow", [1, 8])
    sb(nc, st, "tri", [128, 128], BF16)
    sb(nc, st, "trif", [128, 128])
    sb(nc, st, "idf", [128, 128])
    sb(nc, st, "sgt", [128, 2, NT], BF16)
    sb(nc, st, "ones_bf", [128, 1], BF16)
    sb(nc, st, "ones_f1", [1, 128])
    sb(nc, st, "ones_c", [128, 1])
    sb(nc, st, "c_eps", [1, 1])
    sb(nc, st, "ln_zbf", [128, DC, NT], BF16)
    sb(nc, st, "ln_sq", [128, DC, NT], BF16)
    sb(nc, st, "ln_rows", [1, 4 * NT])
    ps(nc, st, "pg", [128, NT])
    ps(nc, st, "pg2", [128, NT])
    ps(nc, st, "pu2", [128, NT])
    ps(nc, st, "pu2b", [128, NT])
    ps(nc, st, "pd", [128, NT])
    ps(nc, st, "pd2", [128, NT])
    ps(nc, st, "pbc2", [128, NT])
    st["ln_s1"] = st["pg"][0:1, :]
    st["ln_s2"] = st["pg2"][0:1, :]
    st["ln_bca"] = st["pu2"][:]
    st["ln_bcb"] = st["pu2b"][:]


def emit_precast_E2(ctx, st):
    w2 = st["webf2"]
    for e in range(8):
        for j, src in ((0, st["wgT"]), (1, st["wuT"]), (2, st["wdT"])):
            ctx.op("g", lambda g, e=e, j=j, src=src: g.dma_start(
                out=w2[e, j], in_=src[e]), dma=True)
    for es in range(2):
        csl = slice(es * 1024, (es + 1) * 1024)
        ctx.op("g", lambda g, csl=csl, es=es: g.dma_start(
            out=w2[8 + es, 0], in_=st["wsgT"][:, csl]), dma=True)
        ctx.op("g", lambda g, csl=csl, es=es: g.dma_start(
            out=w2[8 + es, 1], in_=st["wsuT"][:, csl]), dma=True)
        ctx.op("g", lambda g, csl=csl, es=es: g.dma_start(
            out=w2[8 + es, 2], in_=st["wsdT"][csl, :]), dma=True)


def body_E2(ctx, st):
    NT = st["NT"]
    load_consts(ctx, st, {"c_eps": 1e-5})
    ctx.op("g", lambda g: g.memset(st["ones_bf"][:], 1.0))
    ctx.op("g", lambda g: g.memset(st["ones_f1"][:], 1.0))
    ctx.op("g", lambda g: g.memset(st["ones_c"][:], 1.0))
    ctx.op("g", lambda g: g.dma_start(
        out=st["x3bf"][:], in_=st["x3T"].rearrange("(c p) t -> p c t", p=128)), dma=True)
    if "maskg" in st:
        ctx.op("g", lambda g: g.dma_start(
            out=st["mpre"][:, 0:128, :],
            in_=st["maskg"].rearrange("(a p) e -> a p e", p=128)), dma=True)
        ctx.op("y", lambda y: y.dma_start(out=st["rst"][:], in_=st["rowsel"]), dma=True)
    else:
        ctx.op("g", lambda g: g.dma_start(
            out=st["mpre"][:], in_=st["maskprev"].rearrange("a p e -> p a e")), dma=True)
    ctx.op("g", lambda g: g.memset(st["trif"][:], 1.0))
    ctx.op("g", lambda g: g.affine_select(st["trif"][:], st["trif"][:], pattern=[[1, 128]],
                                          compare_op=ALU.is_gt, fill=0.0, base=0,
                                          channel_multiplier=-1))
    ctx.op("g", lambda g: g.memset(st["idf"][:], 1.0))
    ctx.op("g", lambda g: g.affine_select(st["idf"][:], st["idf"][:], pattern=[[1, 128]],
                                          compare_op=ALU.is_equal, fill=0.0, base=0,
                                          channel_multiplier=-1))
    ctx.op("g", lambda g: g.dma_start(out=st["tri"][:], in_=st["trif"][:]), dma=True)
    ctx.barrier("v", "s", "t", "g", "y")
    ctx.wait("v", "gd")
    ctx.wait("t", "gd")

    if "maskg" in st:
        ctx.wait("v", "y")
        ctx.op("v", lambda v: v.tensor_scalar(
            st["mpre"].ap().rearrange("a p e -> a (p e)"),
            st["mpre"].ap().rearrange("a p e -> a (p e)"),
            st["rst"][:], None, ALU.mult))
        ctx.op("v", lambda v: v.tensor_reduce(
            st["prow"][:], st["mpre"].ap().rearrange("a p e -> a e p"),
            AX.X, ALU.add))
    else:
        ctx.op("v", lambda v: v.tensor_reduce(
            st["prow"][:], st["mpre"].ap().rearrange("p a e -> p e a"), AX.X, ALU.add))
    nv = ctx.cnt["v"]
    ctx.wait("t", "v", nv)
    ctx.op("t", lambda t: t.matmul(st["ln_s1"][:, 0:8], st["ones_c"][:], st["prow"][:],
                                   start=True, stop=True))
    ctx.wait("v", "t")
    ctx.op("v", lambda v: v.tensor_copy(st["offs"][:], st["ln_s1"][:, 0:8]))
    for tt in range(16):
        ctx.wait("y", "v")
        ctx.wait("g", "v")
        ctx.op("g", lambda g, tt=tt: g.dma_start(out=st["mtile"][:], in_=st["maskown"][tt]),
               dma=True)
        ctx.op("y", lambda y, tt=tt: y.dma_start(out=st["cpre"][:], in_=st["combpre"][tt]),
               dma=True)
        ctx.wait("t", "y")
        ctx.wait("t", "gd")
        ctx.wait("t", "v")
        ctx.op("t", lambda t: t.matmul(st["pg"][:, 0:8], st["tri"][:], st["mtile"][:],
                                       start=True, stop=True))
        ctx.op("t", lambda t: t.matmul(st["pbc2"][:, 0:8], st["ones_f1"][:], st["offs"][:],
                                       start=True, stop=True))
        nm = ctx.cnt["t"]
        ctx.wait("v", "t", nm)
        ctx.op("v", lambda v: v.tensor_copy(st["post"][:], st["pg"][:, 0:8]))
        ctx.op("v", lambda v: v.tensor_add(st["post"][:], st["post"][:], st["pbc2"][:, 0:8]))
        ctx.op("v", lambda v: v.tensor_scalar(st["post"][:], st["post"][:], float(CAP), None,
                                              ALU.is_lt))
        ctx.op("v", lambda v, tt=tt: v.tensor_mul(st["combm"][:, tt, :], st["cpre"][:],
                                                  st["post"][:]))
        ctx.op("t", lambda t: t.matmul(st["ln_s2"][:, 0:8], st["ones_bf"][:], st["mtile"][:],
                                       start=True, stop=True, skip_group_check=True))
        ctx.wait("v", "t")
        ctx.op("v", lambda v: v.tensor_copy(st["trow"][:], st["ln_s2"][:, 0:8]))
        ctx.op("v", lambda v: v.tensor_add(st["offs"][:], st["offs"][:], st["trow"][:]))
    for tt in range(16):
        ctx.wait("t", "v")
        ctx.op("t", lambda t, tt=tt: t.transpose(st["pbc2"][0:8, 0:128],
                                                 st["combm"][:, tt, :], st["idf"][:]))
        nm = ctx.cnt["t"]
        ctx.wait("v", "t", nm)
        ctx.op("v", lambda v: v.tensor_copy(st["ctile"][:], st["pbc2"][0:8, 0:128]))
        nv = ctx.cnt["v"]
        ctx.wait("y", "v", nv)
        ctx.op("y", lambda y, tt=tt: y.dma_start(
            out=st["combd"][:, tt * 128:(tt + 1) * 128], in_=st["ctile"][:]), dma=True)
        ctx.wait("v", "y")

    if st.get("E2_precast"):
        emit_precast_E2(ctx, st)
    w2 = st["webf2"]
    wa = st["warena"]

    def load_w(e, buf):
        toks = []
        for j in range(3):
            toks.append(ctx.op("g", lambda g, e=e, j=j, buf=buf: g.dma_start(
                out=wa[:, buf, j, :, :],
                in_=w2[e, j].rearrange("(c p) i -> p c i", p=128)), dma=True))
        return toks

    for tc in range(TOK // NT):
        tsl = slice(tc * NT, (tc + 1) * NT)
        ctx.barrier("v", "s", "t", "g", "y")
        ctx.op("y", lambda y, tsl=tsl: y.dma_start(
            out=st["x3f"][:], in_=st["x3T"].rearrange("(c p) t -> p c t", p=128)[:, :, tsl]),
            dma=True)
        ctx.op("v", lambda v: v.memset(st["onesrow"][:], 1.0))
        wtoks = {0: load_w(0, 0)}
        evac_gu = [None, None]
        evac_pd = [None, None]
        last_acte_v = None
        last_down_t = None
        for e in range(10):
            if e + 1 < 10:
                if e >= 1:
                    ctx.wait("g", "t")
                wtoks[e + 1] = load_w(e + 1, (e + 1) % 2)
            buf = e % 2
            for tok in wtoks[e]:
                ctx.wait_tok("t", tok)
            if e < 8:
                ctx.wait("y", "t")
                cr = ctx.op("y", lambda y, e=e, tsl=tsl: y.dma_start(
                    out=st["combrow"][:], in_=st["combd"][e:e + 1, tsl]), dma=True)
                ctx.wait_tok("t", cr)
                crow = st["combrow"]
            else:
                ctx.wait("t", "v")
                crow = st["onesrow"]
            if last_acte_v is not None:
                ctx.wait("t", "v", last_acte_v)
            ctx.op("t", lambda t, crow=crow: t.matmul(
                st["pbc2"][:], st["ones_f1"][:], crow[:], start=True, stop=True))
            for ic in range(8):
                bk = ic % 2
                pgb = st["pg"] if bk == 0 else st["pg2"]
                pub = st["pu2"] if bk == 0 else st["pu2b"]
                if evac_gu[bk] is not None:
                    ctx.wait("t", "v", evac_gu[bk])
                for c in range(DC):
                    ctx.op("t", lambda t, ic=ic, c=c, tsl=tsl, buf=buf, pgb=pgb: t.matmul(
                        pgb[:], wa[:, buf, 0, c, ic * 128:(ic + 1) * 128],
                        st["x3bf"][:, c, tsl], start=(c == 0), stop=(c == DC - 1)),
                        inc=(c == DC - 1))
                for c in range(DC):
                    ctx.op("t", lambda t, ic=ic, c=c, tsl=tsl, buf=buf, pub=pub: t.matmul(
                        pub[:], wa[:, buf, 1, c, ic * 128:(ic + 1) * 128],
                        st["x3bf"][:, c, tsl], start=(c == 0), stop=(c == DC - 1)),
                        inc=(c == DC - 1))
                nm = ctx.cnt["t"]
                ctx.wait("s", "t", nm)
                if evac_gu[bk] is not None:
                    ctx.wait("s", "v", evac_gu[bk])
                ctx.op("s", lambda s, pgb=pgb, bk=bk: s.activation(st["sgt"][:, bk, :],
                                                                   pgb[:], AF.Silu))
                ctx.wait("v", "s")
                if ic == 0 and last_down_t is not None:
                    ctx.wait("v", "t", last_down_t)
                ctx.op("v", lambda v, ic=ic, pub=pub, bk=bk: v.tensor_mul(
                    st["acte"][:, ic, :], st["sgt"][:, bk, :], pub[:]))
                ctx.op("v", lambda v, ic=ic: v.tensor_mul(
                    st["acte"][:, ic, :], st["acte"][:, ic, :], st["pbc2"][:]))
                evac_gu[bk] = ctx.cnt["v"]
            last_acte_v = ctx.cnt["v"]
            ctx.wait("t", "v", last_acte_v)
            for dc in range(DC):
                bk = dc % 2
                pdb = st["pd"] if bk == 0 else st["pd2"]
                if evac_pd[bk] is not None:
                    ctx.wait("t", "v", evac_pd[bk])
                for ic in range(8):
                    ctx.op("t", lambda t, ic=ic, dc=dc, buf=buf, pdb=pdb: t.matmul(
                        pdb[:], wa[:, buf, 2, ic, dc * 128:(dc + 1) * 128],
                        st["acte"][:, ic, :], start=(ic == 0), stop=(ic == 7)),
                        inc=(ic == 7))
                nm = ctx.cnt["t"]
                ctx.wait("v", "t", nm)
                if e == 0:
                    ctx.op("v", lambda v, dc=dc, pdb=pdb: v.tensor_copy(st["ysb"][:, dc, :],
                                                                        pdb[:]))
                else:
                    ctx.op("v", lambda v, dc=dc, pdb=pdb: v.tensor_add(
                        st["ysb"][:, dc, :], st["ysb"][:, dc, :], pdb[:]))
                evac_pd[bk] = ctx.cnt["v"]
            last_down_t = ctx.cnt["t"]
        ctx.wait("v", "y")
        for dc in range(DC):
            ctx.op("v", lambda v, dc=dc: v.tensor_add(st["ysb"][:, dc, :], st["ysb"][:, dc, :],
                                                      st["x3f"][:, dc, :]))
        ctx.wait("t", "v")
        emit_ln(ctx, st, st["ysb"], st["ysb"], NT, st["c_eps"], "ln")
        nv = ctx.cnt["v"]
        ctx.wait("y", "v", nv)
        ctx.op("y", lambda y, tsl=tsl: y.dma_start(
            out=st["x4T"].rearrange("(c p) t -> p c t", p=128)[:, :, tsl],
            in_=st["ysb"][:]), dma=True)


def build_E2():
    return build_program("E2", decl_E2, body_E2)


def stage_E1(x3T_cores, inp):
    gwT = np.ascontiguousarray(inp["L1_ffn_gate_w"].T)
    return [{"x3T": x3T_cores[i], "gwT": gwT} for i in range(NCORE)]


def stage_E2(x3T_cores, resE1, inp):
    wgT = np.ascontiguousarray(inp["L1_ffn_we_gate"].transpose(0, 2, 1))
    wuT = np.ascontiguousarray(inp["L1_ffn_we_up"].transpose(0, 2, 1))
    wdT = np.ascontiguousarray(inp["L1_ffn_we_down"].transpose(0, 2, 1))
    wsgT = np.ascontiguousarray(inp["L1_ffn_ws_gate"].T)
    wsuT = np.ascontiguousarray(inp["L1_ffn_ws_up"].T)
    wsdT = np.ascontiguousarray(inp["L1_ffn_ws_down"].T)
    maps = []
    for i in range(NCORE):
        mprev = np.zeros((112, 128, 8), np.float32)
        if i > 0:
            prev = np.concatenate([np.asarray(resE1[j]["masko"]) for j in range(i)], axis=0)
            mprev[:prev.shape[0]] = prev
        maps.append({"x3T": x3T_cores[i], "combpre": np.asarray(resE1[i]["combpre"]),
                     "maskown": np.asarray(resE1[i]["masko"]), "maskprev": mprev,
                     "wgT": wgT, "wuT": wuT, "wdT": wdT, "wsgT": wsgT, "wsuT": wsuT,
                     "wsdT": wsdT})
    return maps



def _fm(a):
    return np.ascontiguousarray(a.T)


def run_launch(nc, in_maps, outs=None):
    res = run_bass_kernel_spmd(nc, in_maps, core_ids=list(range(NCORE)))
    return res.results


def stage_C(x1T_cores, W1, b1, W2, b2):
    W1T = np.ascontiguousarray(W1.T)
    W2T = np.ascontiguousarray(W2.T)
    b1f = np.ascontiguousarray(b1.reshape(32, 128).T)
    b2f = np.ascontiguousarray(b2.reshape(8, 128).T)
    return [{"x1T": x1T_cores[i], "W1T": W1T, "b1f": b1f, "W2T": W2T, "b2f": b2f}
            for i in range(NCORE)]


_NC_CACHE = {}


def _get_nc(name, builder):
    if name not in _NC_CACHE:
        _NC_CACHE[name] = builder()
    return _NC_CACHE[name]


def kernel(**inputs):
    return kernel_fused(**inputs)


def kernel_multilaunch(**inputs):
    inp = inputs
    x = np.ascontiguousarray(inp["L0_attn_inputs"], dtype=np.float32)
    xT_cores = []
    for c in range(NCORE):
        b, qd = c // 4, c % 4
        xT_cores.append(_fm(x[b, qd * TOK:(qd + 1) * TOK, :]))

    ncA = _get_nc("A", build_A)
    resA = run_launch(ncA, stage_A(xT_cores, inp["L0_attn_W_uv"], inp["L0_attn_b_uv"],
                                   inp["L0_attn_gamma"], inp["L0_attn_beta"]))
    ncB = _get_nc("B", build_B)
    resB = run_launch(ncB, stage_B(xT_cores, resA, inp["L0_attn_W_uv"], inp["L0_attn_b_uv"],
                                   inp["L0_attn_W_out"], inp["L0_attn_b_out"]))
    x1T_cores = [np.ascontiguousarray(np.asarray(resB[i]["x1T"])) for i in range(NCORE)]

    ncC = _get_nc("C", build_C)
    resC = run_launch(ncC, stage_C(x1T_cores, inp["L0_ffn_W1"], inp["L0_ffn_b1"],
                                   inp["L0_ffn_W2"], inp["L0_ffn_b2"]))
    x2T_cores = [np.ascontiguousarray(np.asarray(resC[i]["x2T"])) for i in range(NCORE)]

    ncD1 = _get_nc("D1", build_D1)
    resD1 = run_launch(ncD1, stage_D1(x2T_cores, inp))
    ncD2 = _get_nc("D2", build_D2)
    resD2 = run_launch(ncD2, stage_D2(resD1, inp))
    ncD3 = _get_nc("D3", build_D3)
    resD3 = run_launch(ncD3, stage_D3(x2T_cores, resD1, resD2))
    x3T_cores = [np.ascontiguousarray(np.asarray(resD3[i]["x3T"])) for i in range(NCORE)]

    ncE1 = _get_nc("E1", build_E1)
    resE1 = run_launch(ncE1, stage_E1(x3T_cores, inp))
    ncE2 = _get_nc("E2", build_E2)
    resE2 = run_launch(ncE2, stage_E2(x3T_cores, resE1, inp))

    out = np.empty((2, S, D), np.float32)
    for c in range(NCORE):
        b, qd = c // 4, c % 4
        out[b, qd * TOK:(qd + 1) * TOK, :] = np.asarray(resE2[c]["x4T"]).T
    return out


def stage_A(xT_cores, W_uv, b_uv, gamma, beta):
    Wvb = np.ascontiguousarray(W_uv[:, 2048:4224])
    bvbf = np.ascontiguousarray(b_uv[4096:4224].reshape(1, 128).T)
    bvrow = np.ascontiguousarray(b_uv[2048:4096].reshape(1, 2048))
    gb = np.ascontiguousarray(np.stack([gamma[0], beta[0], gamma[1], beta[1]], axis=1))
    return [{"xT": xT_cores[i], "Wvb": Wvb, "bvbf": bvbf, "bvrow": bvrow, "gb": gb}
            for i in range(NCORE)]


def stage_B(xT_cores, resA, W_uv, b_uv, W_out, b_out):
    Wu = np.ascontiguousarray(W_uv[:, 0:2048])
    buf = np.ascontiguousarray(b_uv[0:2048].reshape(16, 128).T)
    bof = np.ascontiguousarray(b_out.reshape(8, 128).T)
    maps = []
    for i in range(NCORE):
        b = i // 4
        grp = [4 * b + j for j in range(4)]
        kb = np.concatenate([np.asarray(resA[j]["kf"]) for j in grp], axis=1)
        vblkb = np.concatenate([np.asarray(resA[j]["vblk"]) for j in grp], axis=1)
        maps.append({"xT": xT_cores[i], "Wu": Wu, "buf": buf,
                     "qf": np.asarray(resA[i]["qf"]), "kb": kb,
                     "vblkb": np.ascontiguousarray(vblkb), "Wo": W_out, "bof": bof})
    return maps



def build_F():
    nc = bass.Bass("TRN2", target_bir_lowering=False)
    master = ExitStack()
    with master:
        def idram(name, shape, dt=F32):
            return nc.dram_tensor(name, list(shape), dt, kind="Internal").ap()

        qT = idram("qT", [128, TOK])
        cgki = idram("cgki", [128, TOK], BF16)
        cgko = idram("cgko", [512, TOK], BF16)
        cgvi = idram("cgvi", [16, TOK, 128], BF16)
        cgvo = idram("cgvo", [4 * 16 * TOK, 128], BF16)
        x1T = idram("x1T", [D, TOK])
        x2T = idram("x2T", [D, TOK])
        x3T = idram("x3T", [D, TOK])
        dmx_in = idram("dmx_in", [2080, TOK])
        dmx_out = idram("dmx_out", [4 * 2080, TOK])
        yskipT = idram("yskipT", [D, TOK])
        yp_in = idram("yp_in", [4 * D, TOK])
        rs_out = idram("rs_out", [D, TOK])
        masko = idram("masko", [16, 128, 8])
        combpre = idram("combpre", [16, 128, 8])
        maskg = idram("maskg", [16384, 8])

        phases = []

        def mk_phase(pfx, decl_fn, prefill):
            pst = {"_nc": nc, "_pfx": pfx + "_"}
            pst.update(prefill)
            pstack = ExitStack()
            pst["_stack"] = pstack
            with pstack:
                decl_fn(nc, pst)
            return pst

        xT_ext = nc.dram_tensor("xT", [D, TOK], F32, kind="ExternalInput").ap()
        stA = mk_phase("A", decl_A, {"xT": xT_ext, "vblk": cgvi, "qf": qT, "kf": cgki})
        stB = mk_phase("B", decl_B2, {
            "xT": xT_ext, "qf": qT, "x1T": x1T,
            "kb4": [cgko[r * 128:(r + 1) * 128, :] for r in range(4)],
            "vb4": (lambda r, ec: cgvo[r * 16 * TOK + ec * TOK:
                                           r * 16 * TOK + (ec + 1) * TOK, :].rearrange(
                "(kt p) e -> p kt e", p=128)),
        })
        stC = mk_phase("C", decl_C, {"x1T": x1T, "x2T": x2T})
        stD1 = mk_phase("D1", decl_D1x, {
            "x2T": x2T, "deltaT": dmx_in[0:1024, :], "m1T": dmx_in[1024:2048, :],
            "BpCT": dmx_in[2048:2080, :], "yskipT": yskipT})
        stD2 = mk_phase("D2", decl_D2, {"dmx_out": dmx_out, "yp_in": yp_in})
        stD3 = mk_phase("D3", decl_D3, {
            "x2T": x2T, "yskipT": yskipT, "x3T": x3T, "ypr": rs_out,
            "D3_srcs": ["ypr", "yskipT"]})
        stE1 = mk_phase("E1", decl_E1, {"x3T": x3T, "masko": masko, "combpre": combpre})
        stE2 = mk_phase("E2", decl_E2, {"x3T": x3T, "maskg": maskg,
                                        "maskown": masko, "combpre": combpre})

        sems = {}
        semnames = ["v", "s", "t", "g", "cc"]
        for i in range(Ctx.DMA_ROT):
            semnames += [f"y{i}", f"gd{i}"]
        for sname in semnames:
            sems[sname] = master.enter_context(nc.semaphore("sem_" + sname))

        def body_F(ctx):
            body_A(ctx, stA)
            ctx.wait("g", "y")
            ctx.full_barrier()
            ctx.op("g", lambda g: g.collective_compute(
                "AllGather", ALU.bypass, replica_groups=[[0, 1, 2, 3], [4, 5, 6, 7]],
                ins=[cgki], outs=[cgko]), sem="cc")
            ctx.op("g", lambda g: g.collective_compute(
                "AllGather", ALU.bypass, replica_groups=[[0, 1, 2, 3], [4, 5, 6, 7]],
                ins=[cgvi.rearrange("a b c -> (a b) c")],
                outs=[cgvo]), sem="cc")
            ctx.full_barrier()
            body_B2(ctx, stB)
            ctx.full_barrier()
            body_C(ctx, stC)
            ctx.full_barrier()
            body_D1x(ctx, stD1)
            ctx.full_barrier()
            ctx.op("g", lambda g: g.collective_compute(
                "AllGather", ALU.bypass, replica_groups=[[0, 1, 2, 3], [4, 5, 6, 7]],
                ins=[dmx_in], outs=[dmx_out]), sem="cc")
            ctx.full_barrier()
            body_D2(ctx, stD2)
            ctx.full_barrier()
            ctx.op("g", lambda g: g.collective_compute(
                "ReduceScatter", ALU.add, replica_groups=[[0, 1, 2, 3], [4, 5, 6, 7]],
                ins=[yp_in], outs=[rs_out]), sem="cc")
            ctx.full_barrier()
            body_D3(ctx, stD3)
            ctx.full_barrier()
            body_E1(ctx, stE1)
            ctx.full_barrier()
            ctx.op("g", lambda g: g.collective_compute(
                "AllGather", ALU.bypass, replica_groups=[[0, 1, 2, 3, 4, 5, 6, 7]],
                ins=[masko.rearrange("a b c -> (a b) c")], outs=[maskg]), sem="cc")
            ctx.full_barrier()
            body_E2(ctx, stE2)

        with nc.Block() as block:
            def run_pass(en, engine_obj):
                ctx = Ctx()
                ctx.cur = en
                ctx.eng = {en: engine_obj}
                ctx.sems = sems
                body_F(ctx)

            @block.vector
            def _(vector):
                run_pass("v", vector)

            @block.scalar
            def _(scalar):
                run_pass("s", scalar)

            @block.tensor
            def _(tensor):
                run_pass("t", tensor)

            @block.gpsimd
            def _(gpsimd):
                run_pass("g", gpsimd)

            @block.sync
            def _(sync):
                run_pass("y", sync)
                ctx = Ctx()
                ctx.cur = None
                body_F(ctx)
                for i in range(Ctx.DMA_ROT):
                    if ctx.cnt[f"y{i}"]:
                        sync.wait_ge(sems[f"y{i}"], ctx.cnt[f"y{i}"])
                    if ctx.cnt[f"gd{i}"]:
                        sync.wait_ge(sems[f"gd{i}"], ctx.cnt[f"gd{i}"])
    return nc


def stage_F(inputs):
    inp = inputs
    x = np.ascontiguousarray(inp["L0_attn_inputs"], dtype=np.float32)
    W_uv, b_uv = inp["L0_attn_W_uv"], inp["L0_attn_b_uv"]
    Wvb = np.ascontiguousarray(W_uv[:, 2048:4224])
    bvbf = np.ascontiguousarray(b_uv[4096:4224].reshape(1, 128).T)
    bvrow = np.ascontiguousarray(b_uv[2048:4096].reshape(1, 2048))
    gamma, beta = inp["L0_attn_gamma"], inp["L0_attn_beta"]
    gb = np.ascontiguousarray(np.stack([gamma[0], beta[0], gamma[1], beta[1]], axis=1))
    Wu = np.ascontiguousarray(W_uv[:, 0:2048])
    buf = np.ascontiguousarray(b_uv[0:2048].reshape(16, 128).T)
    bof = np.ascontiguousarray(inp["L0_attn_b_out"].reshape(8, 128).T)
    W1T = np.ascontiguousarray(inp["L0_ffn_W1"].T)
    W2T = np.ascontiguousarray(inp["L0_ffn_W2"].T)
    b1f = np.ascontiguousarray(inp["L0_ffn_b1"].reshape(32, 128).T)
    b2f = np.ascontiguousarray(inp["L0_ffn_b2"].reshape(8, 128).T)
    nwf = np.ascontiguousarray(inp["L1_attn_norm_weight"].reshape(8, 128).T)
    bdf = np.ascontiguousarray(inp["L1_attn_b_delta"].reshape(8, 128).T)
    dskf = np.ascontiguousarray(inp["L1_attn_D_skip"].reshape(8, 128).T)
    WBC = np.ascontiguousarray(np.concatenate([inp["L1_attn_W_B"], inp["L1_attn_W_C"]], axis=1))
    Wd = np.ascontiguousarray(inp["L1_attn_W_delta"])
    Alog = np.ascontiguousarray(inp["L1_attn_A_log"])
    gwT = np.ascontiguousarray(inp["L1_ffn_gate_w"].T)
    wgT = np.ascontiguousarray(inp["L1_ffn_we_gate"].transpose(0, 2, 1))
    wuT = np.ascontiguousarray(inp["L1_ffn_we_up"].transpose(0, 2, 1))
    wdT = np.ascontiguousarray(inp["L1_ffn_we_down"].transpose(0, 2, 1))
    wsgT = np.ascontiguousarray(inp["L1_ffn_ws_gate"].T)
    wsuT = np.ascontiguousarray(inp["L1_ffn_ws_up"].T)
    wsdT = np.ascontiguousarray(inp["L1_ffn_ws_down"].T)
    maps = []
    for c in range(NCORE):
        b, qd = c // 4, c % 4
        xT = _fm(x[b, qd * TOK:(qd + 1) * TOK, :])
        A4 = np.ascontiguousarray(Alog[:, qd * 4:(qd + 1) * 4])
        bpcsel = np.zeros((32, 8), np.float32)
        for n in range(4):
            bpcsel[qd * 4 + n, n] = 1.0
            bpcsel[16 + qd * 4 + n, 4 + n] = 1.0
        rowsel = np.zeros((128, 1), np.float32)
        rowsel[0:c * 16, 0] = 1.0
        maps.append({
            "xT": xT, "Wvb": Wvb, "bvbf": bvbf, "bvrow": bvrow, "gb": gb,
            "Wu": Wu, "buf": buf, "Wo": inp["L0_attn_W_out"], "bof": bof,
            "W1T": W1T, "b1f": b1f, "W2T": W2T, "b2f": b2f,
            "nwf": nwf, "Wd": Wd, "bdf": bdf, "WBC": WBC, "dskf": dskf,
            "A4": A4, "bpcsel": bpcsel, "rowsel": rowsel,
            "gwT": gwT,
            "wgT": wgT, "wuT": wuT, "wdT": wdT,
            "wsgT": wsgT, "wsuT": wsuT, "wsdT": wsdT,
        })
    return maps


def kernel_fused(**inputs):
    nc = _get_nc("F", build_F)
    maps = stage_F(inputs)
    res = run_launch(nc, maps)
    out = np.empty((2, S, D), np.float32)
    for c in range(NCORE):
        b, qd = c // 4, c % 4
        out[b, qd * TOK:(qd + 1) * TOK, :] = np.asarray(res[c]["x4T"]).T
    return out


# version 14
# speedup vs baseline: 1.8348x; 1.0892x over previous
import numpy as np
import ml_dtypes
from contextlib import ExitStack

import concourse.bass as bass
import concourse.mybir as mybir
from concourse.bass_utils import run_bass_kernel_spmd

F32 = mybir.dt.float32
BF16 = mybir.dt.bfloat16
AF = mybir.ActivationFunctionType
ALU = mybir.AluOpType
AX = mybir.AxisListType

NCORE = 8
D = 1024
DC = 8
TOK = 2048
B = 2
S = 8192
CAP = 5120
ENGS = ("v", "s", "t", "g", "y")


class Ctx:

    DMA_ROT = 4

    def __init__(self):
        self.cnt = {e: 0 for e in ENGS}
        for i in range(self.DMA_ROT):
            self.cnt[f"y{i}"] = 0
            self.cnt[f"gd{i}"] = 0
        self.cnt["cc"] = 0
        self.rr = {"y": 0, "gd": 0}
        self.cur = None
        self.eng = None
        self.sems = None

    def _emit(self, en):
        return self.cur == en and self.eng is not None

    def op(self, en, emit, inc=True, serialize=True, dma=False, sem=None):
        step = 16 if (dma or en in ("y",)) else 1
        if dma and sem is None:
            q = "y" if en == "y" else "gd"
            k = self.rr[q]
            self.rr[q] = (k + 1) % self.DMA_ROT
            sem = f"{q}{k}"
            prev = self.cnt[sem]
            self.cnt[sem] += step
            n = self.cnt[sem]
            if self._emit(en):
                if prev:
                    self.eng[en].wait_ge(self.sems[sem], prev)
                inst = emit(self.eng[en])
                inst.then_inc(self.sems[sem], step)
            return (sem, n)
        if sem is None:
            sem = en
        elif sem == "cc":
            step = 1
        if inc:
            self.cnt[sem] += step
        n = self.cnt[sem]
        if self._emit(en):
            inst = emit(self.eng[en])
            if inc:
                inst.then_inc(self.sems[sem], step)
                if serialize and en != "t":
                    self.eng[en].wait_ge(self.sems[sem], n)
        return n

    def wait(self, en, producer, n=None):
        if producer in ("y", "gd") and n is None:
            for i in range(self.DMA_ROT):
                sname = f"{producer}{i}"
                if self.cnt[sname] and self._emit(en):
                    self.eng[en].wait_ge(self.sems[sname], self.cnt[sname])
            return None
        if n is None:
            n = self.cnt[producer]
        if self._emit(en):
            self.eng[en].wait_ge(self.sems[producer], n)
        return n

    def wait_tok(self, en, tok):
        if tok is None:
            return
        sem, n = tok
        if self._emit(en):
            self.eng[en].wait_ge(self.sems[sem], n)

    def barrier(self, *ens):
        snap = {e: self.cnt.get(e) for e in ens}
        for e in ens:
            for p in ens:
                if p == e:
                    continue
                if p in ("y", "gd"):
                    self.wait(e, p)
                else:
                    self.wait(e, p, snap[p])
            self.wait(e, "y")
            self.wait(e, "gd")

    def full_barrier(self):
        ens = ("v", "s", "t", "g", "y")
        snap = {e: self.cnt.get(e) for e in ens}
        ncc = self.cnt["cc"]
        for e in ens:
            for p in ens:
                if p in (e, "y"):
                    continue
                self.wait(e, p, snap[p])
            self.wait(e, "y")
            self.wait(e, "gd")
            if ncc:
                self.wait(e, "cc", ncc)


def build_program(name, decl_fn, body_fn):
    nc = bass.Bass("TRN2", target_bir_lowering=False)
    st = {}
    stack = ExitStack()
    with stack:
        st["_nc"] = nc
        st["_stack"] = stack
        decl_fn(nc, st)
        sems = {}
        semnames = ["v", "s", "t", "g", "cc"]
        for i in range(Ctx.DMA_ROT):
            semnames += [f"y{i}", f"gd{i}"]
        for sname in semnames:
            sems[sname] = stack.enter_context(nc.semaphore("sem_" + sname))
        with nc.Block() as block:
            def run_pass(en, engine_obj):
                ctx = Ctx()
                ctx.cur = en
                ctx.eng = {en: engine_obj}
                ctx.sems = sems
                body_fn(ctx, st)

            @block.vector
            def _(vector):
                run_pass("v", vector)

            @block.scalar
            def _(scalar):
                run_pass("s", scalar)

            @block.tensor
            def _(tensor):
                run_pass("t", tensor)

            @block.gpsimd
            def _(gpsimd):
                run_pass("g", gpsimd)

            @block.sync
            def _(sync):
                run_pass("y", sync)
                ctx = Ctx()
                ctx.cur = None
                body_fn(ctx, st)
                for i in range(Ctx.DMA_ROT):
                    if ctx.cnt[f"y{i}"]:
                        sync.wait_ge(sems[f"y{i}"], ctx.cnt[f"y{i}"])
                    if ctx.cnt[f"gd{i}"]:
                        sync.wait_ge(sems[f"gd{i}"], ctx.cnt[f"gd{i}"])
    return nc



def sb(nc, st, name, shape, dt=F32):
    tname = st.get("_pfx", "") + name
    st[name] = st["_stack"].enter_context(nc.sbuf_tensor(tname, list(shape), dt))


def ps(nc, st, name, shape, dt=F32):
    tname = st.get("_pfx", "") + name
    st[name] = st["_stack"].enter_context(nc.psum_tensor(tname, list(shape), dt))


def dram_in(nc, st, name, shape, dt=F32):
    if name in st:
        return
    st[name] = nc.dram_tensor(name, list(shape), dt, kind="ExternalInput").ap()


def dram_out(nc, st, name, shape, dt=F32):
    if name in st:
        return
    st[name] = nc.dram_tensor(name, list(shape), dt, kind="ExternalOutput").ap()


def load_consts(ctx, st, vals):
    for cname, v in vals.items():
        ctx.op("g", lambda g, c=cname, vv=v: g.memset(st[c][:], vv))


def emit_ln(ctx, st, z, xout, ncol, eps_tile, pfx, tc=None):
    zbf, sq = st[pfx + "_zbf"], st[pfx + "_sq"]
    rows = st[pfx + "_rows"]
    def _ap(x):
        return x if isinstance(x, bass.AP) else x[:]
    s1, s2 = _ap(st[pfx + "_s1"]), _ap(st[pfx + "_s2"])
    bca, bcb = _ap(st[pfx + "_bca"]), _ap(st[pfx + "_bcb"])
    mean = rows[:, 0 * ncol:1 * ncol]
    var = rows[:, 1 * ncol:2 * ncol]
    rstd = rows[:, 2 * ncol:3 * ncol]
    negmrs = rows[:, 3 * ncol:4 * ncol]

    ctx.wait("s", "v")
    for c in range(DC):
        ctx.op("v", lambda v, c=c: v.tensor_copy(zbf[:, c, :], z[:, c, :]))
        ctx.op("s", lambda s, c=c: s.activation(sq[:, c, :], z[:, c, :], AF.Square))
    nv = ctx.cnt["v"]
    nsq = ctx.cnt["s"]
    ctx.wait("t", "v", nv)
    ctx.wait("t", "s", nsq)
    s2start = not st.get("ln_s2_nostart", False)
    for c in range(DC):
        ctx.op("t", lambda t, c=c: t.matmul(s1, st["ones_bf"][:], zbf[:, c, :],
                                            start=(c == 0), stop=(c == DC - 1),
                                            skip_group_check=True), inc=(c == DC - 1))
    for c in range(DC):
        ctx.op("t", lambda t, c=c: t.matmul(s2, st["ones_bf"][:], sq[:, c, :],
                                            start=(c == 0 and s2start), stop=(c == DC - 1),
                                            skip_group_check=True), inc=(c == DC - 1))
    nmm = ctx.cnt["t"]
    ctx.wait("v", "t", nmm)
    ctx.op("v", lambda v: v.tensor_scalar(mean, s1, 1.0 / D, None, ALU.mult))
    ctx.op("v", lambda v: v.tensor_scalar(var, s2, 1.0 / D, None, ALU.mult))
    ctx.wait("s", "v", ctx.cnt["v"])
    ctx.op("s", lambda s: s.activation(rstd, mean, AF.Square))
    ctx.wait("v", "s", ctx.cnt["s"])
    ctx.op("v", lambda v: v.tensor_sub(var, var, rstd))
    ctx.wait("s", "v", ctx.cnt["v"])
    ctx.op("s", lambda s: s.activation(rstd, var, AF.Sqrt, bias=eps_tile[:]))
    ctx.wait("v", "s", ctx.cnt["s"])
    ctx.op("v", lambda v: v.reciprocal(rstd, rstd))
    ctx.op("v", lambda v: v.tensor_mul(negmrs, mean, rstd))
    ctx.op("v", lambda v: v.tensor_scalar(negmrs, negmrs, -1.0, None, ALU.mult))
    nrows = ctx.cnt["v"]
    ctx.wait("t", "v", nrows)
    bcbstart = not st.get("ln_bcb_nostart", False)
    ctx.op("t", lambda t: t.matmul(bca, st["ones_f1"][:], rstd, start=True, stop=True,
                                   skip_group_check=True))
    ctx.op("t", lambda t: t.matmul(bcb, st["ones_f1"][:], negmrs, start=bcbstart, stop=True,
                                   skip_group_check=True))
    nbc = ctx.cnt["t"]
    ctx.wait("v", "t", nbc)
    for c in range(DC):
        ctx.op("v", lambda v, c=c: v.tensor_mul(xout[:, c, :], z[:, c, :], bca))
        ctx.op("v", lambda v, c=c: v.tensor_add(xout[:, c, :], xout[:, c, :], bcb))
    return ctx.cnt["v"]




NQ2 = 512

def decl_B2(nc, st):
    dram_in(nc, st, "xT", [D, TOK])
    dram_in(nc, st, "Wu", [D, 2048])
    dram_in(nc, st, "buf", [128, 16])
    dram_in(nc, st, "Wo", [2048, D])
    dram_in(nc, st, "bof", [128, 8])
    dram_out(nc, st, "x1T", [D, TOK])

    sb(nc, st, "xbf", [128, DC, TOK], BF16)
    sb(nc, st, "kbf", [128, S], BF16)
    sb(nc, st, "qbf", [128, TOK], BF16)
    sb(nc, st, "uc", [128, 16, NQ2], BF16)
    with ExitStack() as _sub:
        _saved = st["_stack"]
        st["_stack"] = _sub
        sb(nc, st, "sqT", [128, 64, NQ2], BF16)
        st["_stack"] = _saved
    sb(nc, st, "z", [128, DC, NQ2])
    sb(nc, st, "x1f", [128, DC, NQ2])
    sb(nc, st, "ln_zbf", [128, DC, NQ2], BF16)
    sb(nc, st, "ln_sq", [128, DC, NQ2], BF16)
    sb(nc, st, "pad64", [128, 16 * NQ2 // 4])
    sb(nc, st, "warena", [128, 16, 1024], BF16)
    sb(nc, st, "vdb", [128, 2, 64, 128], BF16)
    sb(nc, st, "sctmp", [128, 2, NQ2], BF16)
    sb(nc, st, "but", [128, 16])
    sb(nc, st, "bot", [128, 8])
    sb(nc, st, "ones_bf", [128, 1], BF16)
    sb(nc, st, "ones_f1", [1, 128])
    sb(nc, st, "c_eps", [1, 1])
    sb(nc, st, "ln_rows", [1, 4 * NQ2])
    ps(nc, st, "pa", [128, NQ2])
    ps(nc, st, "pb", [128, NQ2])
    ps(nc, st, "pc", [128, NQ2])
    ps(nc, st, "pdx", [128, NQ2])
    ps(nc, st, "ln_s1", [1, NQ2])
    ps(nc, st, "ln_s2", [1, NQ2])
    ps(nc, st, "ln_bca", [128, NQ2])
    ps(nc, st, "ln_bcb", [128, NQ2])


def body_B2(ctx, st):
    load_consts(ctx, st, {"c_eps": 1e-5})
    ctx.op("g", lambda g: g.memset(st["ones_bf"][:], 1.0))
    ctx.op("g", lambda g: g.memset(st["ones_f1"][:], 1.0))
    ctx.op("y", lambda y: y.dma_start(out=st["but"][:], in_=st["buf"]), dma=True)
    ctx.op("y", lambda y: y.dma_start(out=st["bot"][:], in_=st["bof"]), dma=True)
    ctx.op("g", lambda g: g.dma_start(
        out=st["xbf"][:], in_=st["xT"].rearrange("(c p) t -> p c t", p=128)), dma=True)
    for r in range(4):
        ctx.op("y", lambda y, r=r: y.dma_start(
            out=st["kbf"][:, r * TOK:(r + 1) * TOK], in_=st["kb4"][r]), dma=True)
    ctx.op("g", lambda g: g.dma_start(out=st["qbf"][:], in_=st["qf"]), dma=True)
    ctx.barrier("v", "s", "t", "g", "y")
    ctx.wait("t", "gd")
    ctx.wait("t", "y")
    ctx.wait("v", "gd")
    ctx.wait("s", "gd")

    banksA = ("pa", "pb")
    evacA = [None, None]

    def vload(qc_ec, buf):
        ec = qc_ec
        toks = []
        for r in range(4):
            toks.append(ctx.op("y", lambda y, r=r, ec=ec, buf=buf: y.dma_start(
                out=st["vdb"][:, buf, r * 16:(r + 1) * 16, :],
                in_=st["vb4"](r, ec)), dma=True))
        return toks

    for qc in range(TOK // NQ2):
        qsl = slice(qc * NQ2, (qc + 1) * NQ2)
        ctx.wait("g", "t")
        wut = ctx.op("g", lambda g: g.dma_start(
            out=st["warena"].ap().rearrange("p a b -> p (a b)").rearrange(
                "p (c f) -> p c f", c=DC),
            in_=st["Wu"].rearrange("(c p) f -> p c f", p=128)), dma=True)
        ctx.wait_tok("t", wut)
        wu = st["warena"].ap().rearrange("p a b -> p (a b)").rearrange("p (c f) -> p c f", c=DC)
        for ec in range(16):
            bk = ec % 2
            pgb = st[banksA[bk]]
            if evacA[bk] is not None:
                ctx.wait("t", "s", evacA[bk])
            for c in range(DC):
                ctx.op("t", lambda t, ec=ec, c=c, qsl=qsl, pgb=pgb, wu=wu: t.matmul(
                    pgb[:], wu[:, c, ec * 128:(ec + 1) * 128],
                    st["xbf"][:, c, qsl], start=(c == 0), stop=(c == DC - 1)),
                    inc=(c == DC - 1))
            nm = ctx.cnt["t"]
            ctx.wait("s", "t", nm)
            ctx.op("s", lambda s, ec=ec, pgb=pgb: s.activation(
                st["uc"][:, ec, :], pgb[:], AF.Silu, bias=st["but"][:, ec:ec + 1]))
            evacA[bk] = ctx.cnt["s"]
        ctx.wait("v", "y")
        ctx.wait("s", "v")
        sc_evac = [None, None]
        for kt in range(64):
            bk = kt % 2
            psb = st["pc"] if bk == 0 else st["pdx"]
            if sc_evac[bk] is not None:
                ctx.wait("t", "v", sc_evac[bk])
            ctx.op("t", lambda t, kt=kt, qsl=qsl, psb=psb: t.matmul(
                psb[:], st["kbf"][:, kt * 128:(kt + 1) * 128],
                st["qbf"][:, qsl], start=True, stop=True), inc=True)
            nm = ctx.cnt["t"]
            ctx.wait("v", "t", nm)
            ctx.op("v", lambda v, kt=kt, bk=bk, psb=psb: v.tensor_scalar(
                st["sctmp"][:, bk, :], psb[:], 0.0, 1.0 / S, ALU.max, ALU.mult))
            ctx.op("v", lambda v, kt=kt, bk=bk: v.tensor_mul(
                st["sqT"][:, kt, :], st["sctmp"][:, bk, :], st["sctmp"][:, bk, :]))
            sc_evac[bk] = ctx.cnt["v"]
        ctx.wait("t", "v")
        av_evac = [None, None]
        vtoks = {0: vload(0, 0)}
        for ec in range(16):
            if ec + 1 < 16:
                if ec >= 1:
                    ctx.wait("y", "t")
                vtoks[ec + 1] = vload(ec + 1, (ec + 1) % 2)
            for tok in vtoks[ec]:
                ctx.wait_tok("t", tok)
            bk = ec % 2
            pvb = st["pa"] if bk == 0 else st["pb"]
            if av_evac[bk] is not None:
                ctx.wait("t", "v", av_evac[bk])
            if ec < 2:
                ctx.wait("t", "s", evacA[0] if ec == 0 else evacA[1])
            for kt in range(64):
                ctx.op("t", lambda t, kt=kt, ec=ec, bk=bk, pvb=pvb: t.matmul(
                    pvb[:], st["vdb"][:, ec % 2, kt, :], st["sqT"][:, kt, :],
                    start=(kt == 0), stop=(kt == 63)), inc=(kt == 63))
            nm = ctx.cnt["t"]
            ctx.wait("v", "t", nm)
            ctx.op("v", lambda v, ec=ec, pvb=pvb: v.tensor_mul(
                st["uc"][:, ec, :], st["uc"][:, ec, :], pvb[:]))
            av_evac[bk] = ctx.cnt["v"]
        ctx.wait("g", "t")
        wot = ctx.op("g", lambda g: g.dma_start(
            out=st["warena"][:], in_=st["Wo"].rearrange("(ec p) h -> p ec h", p=128)), dma=True)
        ctx.wait("y", "t")
        ctx.op("y", lambda y, qsl=qsl: y.dma_start(
            out=st["x1f"][:], in_=st["xT"].rearrange("(c p) t -> p c t", p=128)[:, :, qsl]),
            dma=True)
        ctx.wait_tok("t", wot)
        ctx.wait("t", "v")
        po_evac = [None, None]
        for dc in range(DC):
            bk = dc % 2
            pob = st["pc"] if bk == 0 else st["pdx"]
            if po_evac[bk] is not None:
                ctx.wait("t", "s", po_evac[bk])
            for ec in range(16):
                ctx.op("t", lambda t, dc=dc, ec=ec, pob=pob: t.matmul(
                    pob[:], st["warena"][:, ec, dc * 128:(dc + 1) * 128],
                    st["uc"][:, ec, :], start=(ec == 0), stop=(ec == 15)),
                    inc=(ec == 15))
            nm = ctx.cnt["t"]
            ctx.wait("s", "t", nm)
            ctx.op("s", lambda s, dc=dc, pob=pob: s.activation(
                st["z"][:, dc, :], pob[:], AF.Identity, bias=st["bot"][:, dc:dc + 1]))
            po_evac[bk] = ctx.cnt["s"]
        ctx.wait("v", "s")
        ctx.wait("v", "y")
        for dc in range(DC):
            ctx.op("v", lambda v, dc=dc: v.tensor_add(st["z"][:, dc, :], st["z"][:, dc, :],
                                                      st["x1f"][:, dc, :]))
        ctx.wait("t", "v")
        emit_ln(ctx, st, st["z"], st["z"], NQ2, st["c_eps"], "ln")
        nv = ctx.cnt["v"]
        ctx.wait("y", "v", nv)
        ctx.op("y", lambda y, qsl=qsl: y.dma_start(
            out=st["x1T"].rearrange("(c p) t -> p c t", p=128)[:, :, qsl],
            in_=st["z"][:]), dma=True)
        evacA = [None, None]
        for i in range(2):
            sc_evac[i] = None



def decl_C(nc, st):
    dram_in(nc, st, "x1T", [D, TOK])
    dram_in(nc, st, "W1T", [D, 4096])
    dram_in(nc, st, "b1f", [128, 32])
    dram_in(nc, st, "W2T", [4096, D])
    dram_in(nc, st, "b2f", [128, 8])
    dram_out(nc, st, "x2T", [D, TOK])

    NT = 256
    st["NT"] = NT
    sb(nc, st, "wq1", [128, DC, 4096], BF16)
    sb(nc, st, "wq2", [128, 32, 1024], BF16)
    sb(nc, st, "x1bf", [128, DC, NT], BF16)
    sb(nc, st, "x1f", [128, DC, NT])
    sb(nc, st, "wstage", [128, 2048])
    sb(nc, st, "wneg", [128, 2048], BF16)
    sb(nc, st, "h", [128, 32, NT], BF16)
    sb(nc, st, "z", [128, DC, NT])
    sb(nc, st, "absr", [128, 2])
    sb(nc, st, "b1t", [128, 32])
    sb(nc, st, "b2t", [128, 8])
    sb(nc, st, "thr", [128, 4])
    sb(nc, st, "grow", [1, 2])
    sb(nc, st, "ones_bf", [128, 1], BF16)
    sb(nc, st, "ones_c", [128, 1])
    sb(nc, st, "ones_f1", [1, 128])
    sb(nc, st, "c_eps", [1, 1])
    sb(nc, st, "c_qeps", [1, 1])
    sb(nc, st, "ln_zbf", [128, DC, NT], BF16)
    sb(nc, st, "ln_sq", [128, DC, NT], BF16)
    sb(nc, st, "ln_rows", [1, 4 * NT])
    ps(nc, st, "ln_s1", [1, NT])
    ps(nc, st, "ln_s2", [1, NT])
    ps(nc, st, "ln_bca", [128, NT])
    ps(nc, st, "ln_bcb", [128, NT])
    ps(nc, st, "pgam", [1, 2])
    ps(nc, st, "pthr", [128, 2])
    ps(nc, st, "pmm", [128, NT])
    ps(nc, st, "pmm2", [128, NT])


def quantize_weight(ctx, st, wT_dram, nchunks, ncols, wq, col):
    nh = max(1, ncols // 2048)
    hw = ncols // nh
    first = True
    for c in range(nchunks):
        for hh in range(nh):
            ctx.op("y", lambda y, c=c, hh=hh: y.dma_start(
                out=st["wstage"][:, 0:hw],
                in_=wT_dram[c * 128:(c + 1) * 128, hh * hw:(hh + 1) * hw]), dma=True)
            ctx.wait("v", "y")
            ctx.op("v", lambda v: v.tensor_reduce(
                st["absr"][:, 1:2], st["wstage"][:, 0:hw], AX.X, ALU.add,
                apply_absolute_value=True))
            if first:
                ctx.op("v", lambda v: v.tensor_copy(st["absr"][:, 0:1], st["absr"][:, 1:2]))
                first = False
            else:
                ctx.op("v", lambda v: v.tensor_add(st["absr"][:, 0:1], st["absr"][:, 0:1],
                                                   st["absr"][:, 1:2]))
            ctx.wait("y", "v")
    ctx.wait("t", "v")
    ctx.op("t", lambda t: t.matmul(st["pgam"][:, col:col + 1], st["ones_c"][:],
                                   st["absr"][:, 0:1], start=True, stop=True))
    ctx.wait("v", "t")
    ctx.op("v", lambda v: v.tensor_scalar(st["grow"][:, col:col + 1], st["pgam"][:, col:col + 1],
                                          1.0 / (nchunks * 128 * ncols), None, ALU.mult))
    ctx.wait("s", "v")
    ctx.op("s", lambda s: s.activation(st["grow"][:, col:col + 1], st["grow"][:, col:col + 1],
                                       AF.Identity, bias=st["c_qeps"][:]))
    ctx.op("s", lambda s: s.mul(st["grow"][:, col:col + 1], st["grow"][:, col:col + 1], 0.5))
    ctx.wait("t", "s")
    ctx.op("t", lambda t: t.matmul(st["pthr"][:, col:col + 1], st["ones_f1"][:],
                                   st["grow"][:, col:col + 1], start=True, stop=True))
    ctx.wait("v", "t")
    ctx.op("v", lambda v: v.tensor_copy(st["thr"][:, 2 * col:2 * col + 1], st["pthr"][:, col:col + 1]))
    ctx.op("v", lambda v: v.tensor_scalar(st["thr"][:, 2 * col + 1:2 * col + 2],
                                          st["pthr"][:, col:col + 1], -1.0, None, ALU.mult))
    for c in range(nchunks):
        for hh in range(nh):
            csl = slice(hh * hw, (hh + 1) * hw)
            ctx.op("y", lambda y, c=c, hh=hh: y.dma_start(
                out=st["wstage"][:, 0:hw],
                in_=wT_dram[c * 128:(c + 1) * 128, hh * hw:(hh + 1) * hw]), dma=True)
            ctx.wait("v", "y")
            ctx.op("v", lambda v, c=c, csl=csl: v.tensor_scalar(
                wq[:, c, csl], st["wstage"][:, 0:hw],
                st["thr"][:, 2 * col:2 * col + 1], None, ALU.is_gt))
            ctx.op("v", lambda v: v.tensor_scalar(
                st["wneg"][:, 0:hw], st["wstage"][:, 0:hw],
                st["thr"][:, 2 * col + 1:2 * col + 2], None, ALU.is_lt))
            ctx.op("v", lambda v, c=c, csl=csl: v.tensor_sub(wq[:, c, csl], wq[:, c, csl],
                                                             st["wneg"][:, 0:hw]))
            ctx.wait("y", "v")


def body_C(ctx, st):
    NT = st["NT"]
    NTC = TOK // NT
    load_consts(ctx, st, {"c_eps": 1e-5, "c_qeps": 1e-5})
    ctx.op("g", lambda g: g.memset(st["ones_bf"][:], 1.0))
    ctx.op("g", lambda g: g.memset(st["ones_f1"][:], 1.0))
    ctx.op("g", lambda g: g.memset(st["ones_c"][:], 1.0))
    ctx.op("y", lambda y: y.dma_start(out=st["b1t"][:], in_=st["b1f"]), dma=True)
    ctx.op("y", lambda y: y.dma_start(out=st["b2t"][:], in_=st["b2f"]), dma=True)
    ctx.barrier("v", "s", "t", "g", "y")

    quantize_weight(ctx, st, st["W1T"], DC, 4096, st["wq1"], 0)
    quantize_weight(ctx, st, st["W2T"], 32, 1024, st["wq2"], 1)
    ctx.barrier("v", "s", "t", "y")

    for tc in range(NTC):
        tsl = slice(tc * NT, (tc + 1) * NT)
        ctx.barrier("v", "s", "t", "g", "y")
        ctx.op("y", lambda y, tsl=tsl: y.dma_start(
            out=st["x1f"][:], in_=st["x1T"].rearrange("(c p) t -> p c t", p=128)[:, :, tsl]),
            dma=True)
        ctx.wait("v", "y")
        for c in range(DC):
            ctx.op("v", lambda v, c=c: v.tensor_copy(st["x1bf"][:, c, :], st["x1f"][:, c, :]))
        nxbf = ctx.cnt["v"]
        ctx.wait("t", "v", nxbf)
        for fc in range(32):
            for c in range(DC):
                ctx.op("t", lambda t, c=c, fc=fc: t.matmul(
                    st["pmm"][:], st["wq1"][:, c, fc * 128:(fc + 1) * 128],
                    st["x1bf"][:, c, :], start=(c == 0), stop=(c == DC - 1)),
                    inc=(c == DC - 1))
            nm = ctx.cnt["t"]
            ctx.wait("s", "t", nm)
            ctx.op("s", lambda s, fc=fc: s.activation(
                st["h"][:, fc, :], st["pmm"][:], AF.Relu, bias=st["b1t"][:, fc:fc + 1]))
            ctx.wait("t", "s")
        for dc in range(DC):
            for fc in range(32):
                ctx.op("t", lambda t, dc=dc, fc=fc: t.matmul(
                    st["pmm2"][:], st["wq2"][:, fc, dc * 128:(dc + 1) * 128],
                    st["h"][:, fc, :], start=(fc == 0), stop=(fc == 31)),
                    inc=(fc == 31))
            nm = ctx.cnt["t"]
            ctx.wait("s", "t", nm)
            ctx.op("s", lambda s, dc=dc: s.activation(
                st["z"][:, dc, :], st["pmm2"][:], AF.Identity, bias=st["b2t"][:, dc:dc + 1]))
            ctx.wait("t", "s")
        ctx.wait("v", "s")
        for dc in range(DC):
            ctx.op("v", lambda v, dc=dc: v.tensor_add(st["z"][:, dc, :], st["z"][:, dc, :],
                                                      st["x1f"][:, dc, :]))
        emit_ln(ctx, st, st["z"], st["z"], NT, st["c_eps"], "ln")
        nv = ctx.cnt["v"]
        ctx.wait("y", "v", nv)
        ctx.op("y", lambda y, tsl=tsl: y.dma_start(
            out=st["x2T"].rearrange("(c p) t -> p c t", p=128)[:, :, tsl],
            in_=st["z"][:]), dma=True)


def build_C():
    return build_program("C", decl_C, body_C)





def decl_A(nc, st):
    dram_in(nc, st, "xT", [D, TOK])
    dram_in(nc, st, "Wvb", [D, 2176])
    dram_in(nc, st, "bvbf", [128, 1])
    dram_in(nc, st, "bvrow", [1, 2048])
    dram_in(nc, st, "gb", [128, 4])
    dram_out(nc, st, "vblk", [16, TOK, 128])
    dram_out(nc, st, "qf", [128, TOK])
    dram_out(nc, st, "kf", [128, TOK])

    sb(nc, st, "xbf", [128, DC, TOK], BF16)
    sb(nc, st, "wvb", [128, DC, 2176], BF16)
    sb(nc, st, "bcv", [128, 2048])
    sb(nc, st, "bvb", [128, 1])
    sb(nc, st, "bvr", [1, 2048])
    sb(nc, st, "gbt", [128, 4])
    sb(nc, st, "vt", [128, 512], BF16)
    sb(nc, st, "vz", [128, 512])
    sb(nc, st, "baset", [128, TOK])
    sb(nc, st, "qt", [128, TOK])
    sb(nc, st, "kt", [128, TOK])
    sb(nc, st, "ones_f1", [1, 128])
    ps(nc, st, "pv", [128, 512])
    ps(nc, st, "pb", [128, 512])
    ps(nc, st, "pbc", [128, 512])


def body_A(ctx, st):
    ctx.op("g", lambda g: g.memset(st["ones_f1"][:], 1.0))
    ctx.op("y", lambda y: y.dma_start(out=st["bvb"][:], in_=st["bvbf"]), dma=True)
    ctx.op("y", lambda y: y.dma_start(out=st["bvr"][:], in_=st["bvrow"]), dma=True)
    ctx.op("y", lambda y: y.dma_start(out=st["gbt"][:], in_=st["gb"]), dma=True)
    ctx.op("g", lambda g: g.dma_start(
        out=st["xbf"][:], in_=st["xT"].rearrange("(c p) t -> p c t", p=128)), dma=True)
    ctx.op("g", lambda g: g.dma_start(
        out=st["wvb"][:], in_=st["Wvb"].rearrange("(c p) f -> p c f", p=128)), dma=True)
    ctx.barrier("v", "s", "t", "g", "y")
    ctx.wait("v", "gd")
    ctx.wait("t", "gd")

    for ec in range(4):
        ctx.wait("t", "v")
        ctx.op("t", lambda t, ec=ec: t.matmul(
            st["pbc"][:], st["ones_f1"][:], st["bvr"][:, ec * 512:(ec + 1) * 512],
            start=True, stop=True))
        ctx.wait("v", "t")
        ctx.op("v", lambda v, ec=ec: v.tensor_copy(st["bcv"][:, ec * 512:(ec + 1) * 512],
                                                   st["pbc"][:]))
    ctx.wait("t", "v")

    for tk in range(16):
        for ec in range(4):
            for c in range(DC):
                ctx.op("t", lambda t, tk=tk, ec=ec, c=c: t.matmul(
                    st["pv"][:], st["xbf"][:, c, tk * 128:(tk + 1) * 128],
                    st["wvb"][:, c, ec * 512:(ec + 1) * 512],
                    start=(c == 0), stop=(c == DC - 1)), inc=(c == DC - 1))
            nm = ctx.cnt["t"]
            ctx.wait("v", "t", nm)
            ctx.wait("v", "s")
            ctx.op("v", lambda v, ec=ec: v.tensor_add(st["vz"][:], st["pv"][:],
                                                      st["bcv"][:, ec * 512:(ec + 1) * 512]))
            ctx.wait("s", "v")
            ctx.op("s", lambda s: s.activation(st["vt"][:], st["vz"][:], AF.Silu))
            nsi = ctx.cnt["s"]
            ctx.wait("g", "s", nsi)
            ctx.op("g", lambda g, tk=tk, ec=ec: g.dma_start(
                out=st["vblk"][ec * 4:(ec + 1) * 4, tk * 128:(tk + 1) * 128, :].rearrange(
                    "s p e -> p s e"),
                in_=st["vt"].ap().rearrange("p (s e) -> p s e", e=128)), dma=True)
            ctx.wait("t", "v")
            ctx.wait("s", "gd")
    for tc in range(4):
        tsl = slice(tc * 512, (tc + 1) * 512)
        for c in range(DC):
            ctx.op("t", lambda t, tc=tc, c=c: t.matmul(
                st["pb"][:], st["wvb"][:, c, 2048:2176],
                st["xbf"][:, c, tsl], start=(c == 0), stop=(c == DC - 1)),
                inc=(c == DC - 1))
        nm = ctx.cnt["t"]
        ctx.wait("s", "t", nm)
        ctx.op("s", lambda s, tsl=tsl: s.activation(
            st["baset"][:, tsl], st["pb"][:], AF.Silu, bias=st["bvb"][:]))
        ctx.wait("t", "s")
    ctx.wait("v", "s")
    ctx.op("v", lambda v: v.tensor_scalar(st["qt"][:], st["baset"][:], st["gbt"][:, 0:1],
                                          st["gbt"][:, 1:2], ALU.mult, ALU.add))
    ctx.op("v", lambda v: v.tensor_scalar(st["kt"][:], st["baset"][:], st["gbt"][:, 2:3],
                                          st["gbt"][:, 3:4], ALU.mult, ALU.add))
    nv = ctx.cnt["v"]
    ctx.wait("y", "v", nv)
    ctx.wait("g", "v", nv)
    ctx.op("y", lambda y: y.dma_start(out=st["qf"], in_=st["qt"][:]), dma=True)
    ctx.op("g", lambda g: g.dma_start(out=st["kf"], in_=st["kt"][:]), dma=True)


def build_A():
    return build_program("A", decl_A, body_A)



NQ = 256

def decl_B(nc, st):
    dram_in(nc, st, "xT", [D, TOK])
    dram_in(nc, st, "Wu", [D, 2048])
    dram_in(nc, st, "buf", [128, 16])
    dram_in(nc, st, "qf", [128, TOK])
    if "kb4" not in st:
        dram_in(nc, st, "kb", [128, S])
        dram_in(nc, st, "vblkb", [16, S, 128])
    dram_in(nc, st, "Wo", [2048, D])
    dram_in(nc, st, "bof", [128, 8])
    dram_out(nc, st, "x1T", [D, TOK])

    sb(nc, st, "xbf", [128, DC, TOK], BF16)
    sb(nc, st, "kbf", [128, S], BF16)
    sb(nc, st, "qbf", [128, TOK], BF16)
    sb(nc, st, "wstr", [128, 16, 1024], BF16)
    sb(nc, st, "uc", [128, 16, NQ], BF16)
    sb(nc, st, "sqT", [128, 64, NQ], BF16)
    sb(nc, st, "gc", [128, 16, NQ], BF16)
    sb(nc, st, "sqt", [128, NQ], BF16)
    sb(nc, st, "z", [128, DC, NQ])
    sb(nc, st, "x1f", [128, DC, NQ])
    sb(nc, st, "but", [128, 16])
    sb(nc, st, "bot", [128, 8])
    sb(nc, st, "ones_bf", [128, 1], BF16)
    sb(nc, st, "ones_f1", [1, 128])
    sb(nc, st, "c_eps", [1, 1])
    sb(nc, st, "ln_zbf", [128, DC, NQ], BF16)
    sb(nc, st, "ln_sq", [128, DC, NQ], BF16)
    sb(nc, st, "ln_rows", [1, 4 * NQ])
    ps(nc, st, "ln_s1", [1, NQ])
    ps(nc, st, "ln_s2", [1, NQ])
    ps(nc, st, "ln_bca", [128, NQ])
    ps(nc, st, "ln_bcb", [128, NQ])
    ps(nc, st, "pu", [128, NQ])
    ps(nc, st, "psc", [128, NQ])
    ps(nc, st, "pav", [128, NQ])
    ps(nc, st, "po", [128, NQ])


def body_B(ctx, st):
    load_consts(ctx, st, {"c_eps": 1e-5})
    ctx.op("g", lambda g: g.memset(st["ones_bf"][:], 1.0))
    ctx.op("g", lambda g: g.memset(st["ones_f1"][:], 1.0))
    ctx.op("y", lambda y: y.dma_start(out=st["but"][:], in_=st["buf"]), dma=True)
    ctx.op("y", lambda y: y.dma_start(out=st["bot"][:], in_=st["bof"]), dma=True)
    ctx.op("g", lambda g: g.dma_start(
        out=st["xbf"][:], in_=st["xT"].rearrange("(c p) t -> p c t", p=128)), dma=True)
    if "kb4" in st:
        for r in range(4):
            ctx.op("g", lambda g, r=r: g.dma_start(
                out=st["kbf"][:, r * TOK:(r + 1) * TOK], in_=st["kb4"][r]), dma=True)
    else:
        ctx.op("g", lambda g: g.dma_start(out=st["kbf"][:], in_=st["kb"]), dma=True)
    ctx.op("g", lambda g: g.dma_start(out=st["qbf"][:], in_=st["qf"]), dma=True)
    ctx.barrier("v", "s", "t", "g", "y")
    ctx.wait("v", "gd")
    ctx.wait("t", "gd")
    ctx.wait("s", "gd")

    for qc in range(TOK // NQ):
        qsl = slice(qc * NQ, (qc + 1) * NQ)
        ctx.barrier("v", "s", "t", "g", "y")
        ctx.op("g", lambda g: g.dma_start(
            out=st["wstr"].ap().rearrange("p a b -> p (a b)").rearrange("p (c f) -> p c f", c=DC),
            in_=st["Wu"].rearrange("(c p) f -> p c f", p=128)), dma=True)
        ctx.wait("t", "gd")
        wu = st["wstr"].ap().rearrange("p a b -> p (a b)").rearrange("p (c f) -> p c f", c=DC)
        for ec in range(16):
            for c in range(DC):
                ctx.op("t", lambda t, ec=ec, c=c, qsl=qsl, wu=wu: t.matmul(
                    st["pu"][:], wu[:, c, ec * 128:(ec + 1) * 128],
                    st["xbf"][:, c, qsl], start=(c == 0), stop=(c == DC - 1)),
                    inc=(c == DC - 1))
            nm = ctx.cnt["t"]
            ctx.wait("s", "t", nm)
            ctx.op("s", lambda s, ec=ec: s.activation(
                st["uc"][:, ec, :], st["pu"][:], AF.Silu, bias=st["but"][:, ec:ec + 1]))
            ctx.wait("t", "s")
        for kt in range(64):
            ctx.op("t", lambda t, kt=kt, qsl=qsl: t.matmul(
                st["psc"][:], st["kbf"][:, kt * 128:(kt + 1) * 128],
                st["qbf"][:, qsl], start=True, stop=True))
            nm = ctx.cnt["t"]
            ctx.wait("v", "t", nm)
            ctx.op("v", lambda v: v.tensor_scalar(st["sqt"][:], st["psc"][:],
                                                  0.0, 1.0 / S, ALU.max, ALU.mult))
            ctx.op("v", lambda v, kt=kt: v.tensor_mul(st["sqT"][:, kt, :], st["sqt"][:],
                                                      st["sqt"][:]))
            ctx.wait("t", "v")
        ctx.wait("g", "t")
        for ec in range(16):
            if "vb4" in st:
                for r in range(4):
                    ctx.op("g", lambda g, ec=ec, r=r: g.dma_start(
                        out=st["wstr"].ap().rearrange("p a b -> p (a b)")[:, r * 2048:(r + 1) * 2048].rearrange(
                            "p (kt e) -> p kt e", e=128),
                        in_=st["vb4"](r, ec)), dma=True)
            else:
                ctx.op("g", lambda g, ec=ec: g.dma_start(
                    out=st["wstr"].ap().rearrange("p a b -> p (a b)")[:, 0:8192].rearrange(
                        "p (kt e) -> p kt e", e=128),
                    in_=st["vblkb"][ec].rearrange("(kt p) e -> p kt e", p=128)), dma=True)
            ctx.wait("t", "gd")
            vt = st["wstr"].ap().rearrange("p a b -> p (a b)")[:, 0:8192].rearrange(
                "p (kt e) -> p kt e", e=128)
            for kt in range(64):
                ctx.op("t", lambda t, kt=kt, vt=vt: t.matmul(
                    st["pav"][:], vt[:, kt, :], st["sqT"][:, kt, :],
                    start=(kt == 0), stop=(kt == 63)), inc=(kt == 63))
            nm = ctx.cnt["t"]
            ctx.wait("v", "t", nm)
            ctx.op("v", lambda v, ec=ec: v.tensor_mul(st["gc"][:, ec, :], st["pav"][:],
                                                      st["uc"][:, ec, :]))
            nv = ctx.cnt["v"]
            ctx.wait("t", "v", nv)
            ctx.wait("g", "v", nv)
        ctx.op("g", lambda g: g.dma_start(
            out=st["wstr"][:], in_=st["Wo"].rearrange("(ec p) h -> p ec h", p=128)), dma=True)
        ctx.op("y", lambda y, qsl=qsl: y.dma_start(
            out=st["x1f"][:], in_=st["xT"].rearrange("(c p) t -> p c t", p=128)[:, :, qsl]),
            dma=True)
        ctx.wait("t", "gd")
        for dc in range(DC):
            for ec in range(16):
                ctx.op("t", lambda t, dc=dc, ec=ec: t.matmul(
                    st["po"][:], st["wstr"][:, ec, dc * 128:(dc + 1) * 128],
                    st["gc"][:, ec, :], start=(ec == 0), stop=(ec == 15)),
                    inc=(ec == 15))
            nm = ctx.cnt["t"]
            ctx.wait("s", "t", nm)
            ctx.op("s", lambda s, dc=dc: s.activation(
                st["z"][:, dc, :], st["po"][:], AF.Identity, bias=st["bot"][:, dc:dc + 1]))
            ctx.wait("t", "s")
        ctx.wait("v", "s")
        ctx.wait("v", "y")
        for dc in range(DC):
            ctx.op("v", lambda v, dc=dc: v.tensor_add(st["z"][:, dc, :], st["z"][:, dc, :],
                                                      st["x1f"][:, dc, :]))
        emit_ln(ctx, st, st["z"], st["z"], NQ, st["c_eps"], "ln")
        nv = ctx.cnt["v"]
        ctx.wait("y", "v", nv)
        ctx.op("y", lambda y, qsl=qsl: y.dma_start(
            out=st["x1T"].rearrange("(c p) t -> p c t", p=128)[:, :, qsl],
            in_=st["z"][:]), dma=True)


def build_B():
    return build_program("B", decl_B, body_B)





def decl_D1(nc, st):
    dram_in(nc, st, "x2T", [D, TOK])
    dram_in(nc, st, "nwf", [128, 8])
    dram_in(nc, st, "Wd", [D, D])
    dram_in(nc, st, "bdf", [128, 8])
    dram_in(nc, st, "WBC", [D, 32])
    dram_in(nc, st, "dskf", [128, 8])
    dram_out(nc, st, "deltaT", [D, TOK])
    dram_out(nc, st, "m1T", [D, TOK])
    dram_out(nc, st, "BpCT", [32, TOK])
    dram_out(nc, st, "yskipT", [D, TOK])

    NT = 512
    st["NT"] = NT
    sb(nc, st, "wd", [128, DC, D], BF16)
    sb(nc, st, "wbc", [128, DC, 32], BF16)
    sb(nc, st, "x2f", [128, DC, NT])
    sb(nc, st, "xwbf", [128, DC, NT], BF16)
    sb(nc, st, "sqb", [128, DC, NT], BF16)
    sb(nc, st, "brs", [128, NT])
    sb(nc, st, "zt", [128, NT])
    sb(nc, st, "abst", [128, NT])
    sb(nc, st, "relut", [128, NT])
    sb(nc, st, "spt", [128, NT])
    sb(nc, st, "dft", [128, NT])
    sb(nc, st, "sqd", [128, NT])
    sb(nc, st, "xnb", [128, NT], BF16)
    sb(nc, st, "dbo", [128, DC, NT])
    sb(nc, st, "m1o", [128, DC, NT])
    sb(nc, st, "yso", [128, DC, NT])
    sb(nc, st, "bpco", [32, NT])
    sb(nc, st, "rrow", [1, NT])
    sb(nc, st, "nwt", [128, 8])
    sb(nc, st, "bdt", [128, 8])
    sb(nc, st, "dskt", [128, 8])
    sb(nc, st, "ones_bf", [128, 1], BF16)
    sb(nc, st, "ones_f1", [1, 128])
    sb(nc, st, "c_eps6", [1, 1])
    ps(nc, st, "ps2", [1, NT])
    ps(nc, st, "pbc", [128, NT])
    ps(nc, st, "pdd", [128, NT])
    ps(nc, st, "pbp", [32, NT])


SC_C = (10.0 + 1e-4) / 2.0
SC_H = (10.0 - 1e-4) / 2.0


def body_D1(ctx, st):
    NT = st["NT"]
    load_consts(ctx, st, {"c_eps6": 1e-6})
    ctx.op("g", lambda g: g.memset(st["ones_bf"][:], 1.0))
    ctx.op("g", lambda g: g.memset(st["ones_f1"][:], 1.0))
    ctx.op("y", lambda y: y.dma_start(out=st["nwt"][:], in_=st["nwf"]), dma=True)
    ctx.op("y", lambda y: y.dma_start(out=st["bdt"][:], in_=st["bdf"]), dma=True)
    ctx.op("y", lambda y: y.dma_start(out=st["dskt"][:], in_=st["dskf"]), dma=True)
    ctx.op("g", lambda g: g.dma_start(
        out=st["wd"][:], in_=st["Wd"].rearrange("(c p) f -> p c f", p=128)), dma=True)
    ctx.op("g", lambda g: g.dma_start(
        out=st["wbc"][:], in_=st["WBC"].rearrange("(c p) f -> p c f", p=128)), dma=True)
    ctx.barrier("v", "s", "t", "g", "y")
    ctx.wait("t", "gd")

    for tc in range(TOK // NT):
        tsl = slice(tc * NT, (tc + 1) * NT)
        ctx.barrier("v", "s", "t", "g", "y")
        ctx.op("y", lambda y, tsl=tsl: y.dma_start(
            out=st["x2f"][:], in_=st["x2T"].rearrange("(c p) t -> p c t", p=128)[:, :, tsl]),
            dma=True)
        ctx.wait("v", "y")
        ctx.wait("s", "y")
        for c in range(DC):
            ctx.op("v", lambda v, c=c: v.tensor_scalar(
                st["xwbf"][:, c, :], st["x2f"][:, c, :], st["nwt"][:, c:c + 1], None, ALU.mult))
            ctx.op("s", lambda s, c=c: s.activation(st["sqb"][:, c, :], st["x2f"][:, c, :],
                                                    AF.Square))
        nv, ns = ctx.cnt["v"], ctx.cnt["s"]
        ctx.wait("t", "v", nv)
        ctx.wait("t", "s", ns)
        for c in range(DC):
            ctx.op("t", lambda t, c=c: t.matmul(st["ps2"][:], st["ones_bf"][:],
                                                st["sqb"][:, c, :],
                                                start=(c == 0), stop=(c == DC - 1)),
                   inc=(c == DC - 1))
        ctx.wait("s", "t")
        ctx.op("s", lambda s: s.activation(st["rrow"][:], st["ps2"][:], AF.Sqrt,
                                           scale=1.0 / D, bias=st["c_eps6"][:]))
        ctx.wait("v", "s")
        ctx.op("v", lambda v: v.reciprocal(st["rrow"][:], st["rrow"][:]))
        nv = ctx.cnt["v"]
        ctx.wait("t", "v", nv)
        ctx.op("t", lambda t: t.matmul(st["pbc"][:], st["ones_f1"][:], st["rrow"][:],
                                       start=True, stop=True))
        ctx.wait("v", "t")
        ctx.op("v", lambda v: v.tensor_copy(st["brs"][:], st["pbc"][:]))
        for c in range(DC):
            ctx.op("t", lambda t, c=c: t.matmul(st["pbp"][:], st["wbc"][:, c, :],
                                                st["xwbf"][:, c, :],
                                                start=(c == 0), stop=(c == DC - 1)),
                   inc=(c == DC - 1))
        ctx.wait("v", "t")
        ctx.op("v", lambda v: v.tensor_mul(st["bpco"][:], st["pbp"][:],
                                           st["brs"][0:32, :]))
        nv = ctx.cnt["v"]
        ctx.wait("g", "v", nv)
        ctx.op("g", lambda g, tsl=tsl: g.dma_start(
            out=st["BpCT"][:, tsl], in_=st["bpco"][:]), dma=True)
        for dc in range(DC):
            for c in range(DC):
                ctx.op("t", lambda t, dc=dc, c=c: t.matmul(
                    st["pdd"][:], st["wd"][:, c, dc * 128:(dc + 1) * 128],
                    st["xwbf"][:, c, :], start=(c == 0), stop=(c == DC - 1)),
                    inc=(c == DC - 1))
            nm = ctx.cnt["t"]
            ctx.wait("v", "t", nm)
            ctx.op("v", lambda v: v.tensor_mul(st["zt"][:], st["pdd"][:], st["brs"][:]))
            nv = ctx.cnt["v"]
            ctx.wait("t", "v", nv)
            ctx.wait("s", "v", nv)
            ctx.op("s", lambda s, dc=dc: s.activation(st["abst"][:], st["zt"][:], AF.Abs,
                                                      bias=st["bdt"][:, dc:dc + 1]))
            ctx.op("s", lambda s: s.activation(st["abst"][:], st["abst"][:], AF.Exp,
                                               scale=-1.0))
            ctx.op("s", lambda s: s.activation(st["abst"][:], st["abst"][:], AF.Ln,
                                               bias=1.0))
            ctx.op("v", lambda v, dc=dc: v.tensor_scalar(st["relut"][:], st["zt"][:],
                                                         st["bdt"][:, dc:dc + 1], 0.0,
                                                         ALU.add, ALU.max))
            ctx.wait("v", "s")
            ctx.op("v", lambda v: v.tensor_add(st["spt"][:], st["relut"][:], st["abst"][:]))
            ctx.wait("s", "v")
            ctx.op("s", lambda s: s.activation(st["dft"][:], st["spt"][:], AF.Tanh,
                                               scale=1.0 / SC_H, bias=st["c_negch"][:]))
            ctx.wait("v", "s")
            ctx.op("v", lambda v: v.tensor_scalar(st["dft"][:], st["dft"][:], SC_H, SC_C,
                                                  ALU.mult, ALU.add))
            ctx.op("v", lambda v, dc=dc: v.tensor_copy(st["dbo"][:, dc, :], st["dft"][:]))
            ctx.wait("s", "v")
            ctx.op("s", lambda s: s.activation(st["sqd"][:], st["dft"][:], AF.Sqrt))
            ctx.op("v", lambda v, dc=dc: v.tensor_mul(st["xnb"][:], st["xwbf"][:, dc, :],
                                                      st["brs"][:]))
            ctx.wait("v", "s")
            ctx.op("v", lambda v, dc=dc: v.tensor_mul(st["m1o"][:, dc, :], st["sqd"][:],
                                                      st["xnb"][:]))
            ctx.op("v", lambda v, dc=dc: v.tensor_scalar(st["yso"][:, dc, :], st["xnb"][:],
                                                         st["dskt"][:, dc:dc + 1], None,
                                                         ALU.mult))
        nv, ns = ctx.cnt["v"], ctx.cnt["s"]
        ctx.wait("g", "v", nv)
        for nm_, src in (("deltaT", "dbo"), ("m1T", "m1o"), ("yskipT", "yso")):
            ctx.op("g", lambda g, nm_=nm_, src=src, tsl=tsl: g.dma_start(
                out=st[nm_].rearrange("(c p) t -> p c t", p=128)[:, :, tsl],
                in_=st[src][:]), dma=True)


def decl_D1x(nc, st):
    decl_D1(nc, st)
    sb(nc, st, "c_negch", [128, 1])


def body_D1x(ctx, st):
    load_consts(ctx, st, {"c_negch": -SC_C / SC_H})
    body_D1(ctx, st)


def build_D1():
    return build_program("D1", decl_D1x, body_D1x)



def decl_D2(nc, st):
    if "dmx_out" not in st:
        dram_in(nc, st, "deltaB", [D, S])
        dram_in(nc, st, "m1B", [D, S])
        dram_in(nc, st, "Bp4", [4, S])
        dram_in(nc, st, "C4", [4, S])
    else:
        dram_in(nc, st, "bpcsel", [32, 8])
    dram_in(nc, st, "A4", [D, 4])
    if "dmx_out" not in st:
        dram_out(nc, st, "ypart", [D, S])

    TB = 2048
    st["TB"] = TB
    sb(nc, st, "dsl", [128, DC, TB], BF16)
    sb(nc, st, "m1sl", [128, DC, TB], BF16)
    sb(nc, st, "ya", [128, DC, TB], BF16)
    sb(nc, st, "rt", [128, 2, TB])
    sb(nc, st, "at", [128, 2, TB])
    sb(nc, st, "c_neg1", [128, 1])
    sb(nc, st, "bt", [128, 2, TB], BF16)
    sb(nc, st, "ht", [128, 2, TB])
    sb(nc, st, "m2t", [128, 2, TB], BF16)
    sb(nc, st, "mt", [128, 2, TB], BF16)
    sb(nc, st, "bcB", [128, TB], BF16)
    sb(nc, st, "bcC", [128, TB], BF16)
    sb(nc, st, "bp1", [1, TB])
    sb(nc, st, "c1", [1, TB])
    sb(nc, st, "selc", [32, 8])
    sb(nc, st, "bpc32", [32, TB])
    sb(nc, st, "e2", [128, DC, 4])
    sb(nc, st, "carry", [128, 32])
    sb(nc, st, "ones_f1", [1, 128])
    ps(nc, st, "pb1", [128, 512])
    ps(nc, st, "pc1", [1, 512])


def body_D2(ctx, st):
    TB = st["TB"]
    ctx.op("g", lambda g: g.memset(st["ones_f1"][:], 1.0))
    ctx.op("g", lambda g: g.memset(st["carry"][:], 0.0))
    ctx.op("g", lambda g: g.memset(st["c_neg1"][:], -1.0))
    if "dmx_out" in st:
        ctx.op("y", lambda y: y.dma_start(out=st["selc"][:], in_=st["bpcsel"]), dma=True)
    ctx.op("y", lambda y: y.dma_start(
        out=st["e2"][:], in_=st["A4"].rearrange("(c p) n -> p c n", p=128)), dma=True)
    ctx.wait("s", "y")
    for c in range(DC):
        ctx.op("s", lambda s, c=c: s.activation(st["e2"][:, c, :], st["e2"][:, c, :], AF.Exp))
    ctx.wait("v", "s")
    ctx.op("v", lambda v: v.tensor_scalar(
        st["e2"].ap().rearrange("p a b -> p (a b)"),
        st["e2"].ap().rearrange("p a b -> p (a b)"), 0.5, None, ALU.mult))
    ctx.barrier("v", "s", "t", "g", "y")

    for tb in range(S // TB):
        bsl = slice(tb * TB, (tb + 1) * TB)
        ctx.barrier("v", "s", "t", "g", "y")
        if "dmx_out" in st:
            ctx.op("g", lambda g, tb=tb: g.dma_start(
                out=st["dsl"][:],
                in_=st["dmx_out"][tb * 2080:tb * 2080 + 1024, :].rearrange(
                    "(c p) t -> p c t", p=128)), dma=True)
            ctx.op("g", lambda g, tb=tb: g.dma_start(
                out=st["m1sl"][:],
                in_=st["dmx_out"][tb * 2080 + 1024:tb * 2080 + 2048, :].rearrange(
                    "(c p) t -> p c t", p=128)), dma=True)
            ctx.op("g", lambda g, tb=tb: g.dma_start(
                out=st["bpc32"][:], in_=st["dmx_out"][tb * 2080 + 2048:tb * 2080 + 2080, :]),
                dma=True)
        else:
            ctx.op("g", lambda g, bsl=bsl: g.dma_start(
                out=st["dsl"][:], in_=st["deltaB"].rearrange("(c p) t -> p c t", p=128)[:, :, bsl]),
                dma=True)
            ctx.op("g", lambda g, bsl=bsl: g.dma_start(
                out=st["m1sl"][:], in_=st["m1B"].rearrange("(c p) t -> p c t", p=128)[:, :, bsl]),
                dma=True)
        ctx.wait("v", "gd")
        ctx.wait("s", "gd")
        ctx.wait("g", "gd")
        ctx.barrier("v", "s", "t", "g", "y")
        tiles = [(n, dc) for n in range(4) for dc in range(DC)]
        tok_r = {}
        tok_m2 = {}
        tok_ym = {}
        tok_yadd = {}
        tok_z = {}
        tok_scan = {}
        tok_bc = {}

        def emit_bc(n):
            if "dmx_out" in st:
                ctx.wait("t", "y")
                ctx.wait("t", "v")
                for half in range(TB // 512):
                    hsl = slice(half * 512, (half + 1) * 512)
                    ctx.op("t", lambda t, n=n, hsl=hsl: t.matmul(
                        st["pb1"][0:1, :], st["selc"][:, n:n + 1], st["bpc32"][:, hsl],
                        start=True, stop=True))
                    ctx.op("t", lambda t, n=n, hsl=hsl: t.matmul(
                        st["pc1"][0:1, :],
                        st["selc"][:, 4 + n:5 + n], st["bpc32"][:, hsl],
                        start=True, stop=True))
                    nm = ctx.cnt["t"]
                    ctx.wait("v", "t", nm)
                    ctx.op("v", lambda v, hsl=hsl: v.tensor_copy(st["bp1"][:, hsl], st["pb1"][0:1, :]))
                    ctx.op("v", lambda v, hsl=hsl: v.tensor_copy(st["c1"][:, hsl], st["pc1"][0:1, :]))
                    nv = ctx.cnt["v"]
                    ctx.wait("t", "v", nv)
            else:
                ctx.wait("y", "v")
                ctx.op("y", lambda y, n=n, bsl=bsl: y.dma_start(out=st["bp1"][:],
                                                                in_=st["Bp4"][n:n + 1, bsl]), dma=True)
                ctx.op("y", lambda y, n=n, bsl=bsl: y.dma_start(out=st["c1"][:],
                                                                in_=st["C4"][n:n + 1, bsl]), dma=True)
                ctx.wait("t", "y")
            for half in range(TB // 512):
                hsl = slice(half * 512, (half + 1) * 512)
                ctx.wait("t", "v")
                ctx.op("t", lambda t, hsl=hsl: t.matmul(
                    st["pb1"][:], st["ones_f1"][:], st["bp1"][:, hsl],
                    start=True, stop=True))
                ctx.wait("v", "t")
                ctx.op("v", lambda v, hsl=hsl: v.tensor_copy(st["bcB"][:, hsl], st["pb1"][:]))
                ctx.wait("t", "v")
                ctx.op("t", lambda t, hsl=hsl: t.matmul(
                    st["pb1"][:], st["ones_f1"][:], st["c1"][:, hsl],
                    start=True, stop=True))
                ctx.wait("v", "t")
                ctx.op("v", lambda v, hsl=hsl: v.tensor_copy(st["bcC"][:, hsl], st["pb1"][:]))

        def stage_z(i):
            n, dc = tiles[i]
            p = i % 2
            ctx.wait("g", "v", max(tok_bc[n], tok_scan.get(i - 2, 0)))
            ctx.op("g", lambda g, n=n, dc=dc, p=p: g.tensor_tensor(
                st["m2t"][:, p, :], st["m1sl"][:, dc, :], st["bcB"][:], ALU.mult))
            tok_m2[i] = ctx.cnt["g"]

        def stage_lnr(i):
            n, dc = tiles[i]
            p = i % 2
            if i >= 2:
                ctx.wait("s", "v", tok_scan[i - 2])
            ctx.op("s", lambda s, n=n, dc=dc, p=p: s.activation(
                st["rt"][:, p, :], st["dsl"][:, dc, :], AF.Ln,
                scale=st["e2"][:, dc, n:n + 1], bias=1.0))
            ctx.op("s", lambda s, p=p: s.activation(st["rt"][:, p, :], st["rt"][:, p, :],
                                                    AF.Exp, scale=-1.0))
            ctx.op("s", lambda s, p=p: s.activation(st["at"][:, p, :], st["rt"][:, p, :],
                                                    AF.Identity, scale=2.0,
                                                    bias=st["c_neg1"][:]))
            tok_r[i] = ctx.cnt["s"]

        def stage_v(i):
            n, dc = tiles[i]
            ct = n * 8 + dc
            p = i % 2
            ctx.wait("v", "s", tok_r[i])
            ctx.wait("v", "g", tok_m2[i])
            ctx.op("v", lambda v, p=p: v.tensor_mul(st["bt"][:, p, :], st["m2t"][:, p, :],
                                                    st["rt"][:, p, :]))
            ctx.op("v", lambda v, p=p, ct=ct: v.tensor_tensor_scan(
                st["ht"][:, p, :], st["at"][:, p, :], st["bt"][:, p, :],
                st["carry"][:, ct:ct + 1], ALU.mult, ALU.add))
            ctx.op("v", lambda v, p=p, ct=ct: v.tensor_copy(st["carry"][:, ct:ct + 1],
                                                            st["ht"][:, p, TB - 1:TB]))
            tok_scan[i] = ctx.cnt["v"]
            if i >= 2:
                ctx.wait("v", "g", tok_yadd.get(i - 2, 0))
            ctx.op("v", lambda v, p=p: v.tensor_mul(st["mt"][:, p, :], st["ht"][:, p, :],
                                                    st["bcC"][:]))
            tok_ym[i] = ctx.cnt["v"]

        def stage_yadd(i):
            n, dc = tiles[i]
            p = i % 2
            ctx.wait("g", "v", tok_ym[i])
            if n == 0:
                ctx.op("g", lambda g, dc=dc, p=p: g.tensor_copy(st["ya"][:, dc, :],
                                                                st["mt"][:, p, :]))
            else:
                ctx.op("g", lambda g, dc=dc, p=p: g.tensor_tensor(
                    st["ya"][:, dc, :], st["ya"][:, dc, :], st["mt"][:, p, :], ALU.add))
            tok_yadd[i] = ctx.cnt["g"]

        for n in range(4):
            if n > 0:
                ctx.wait("v", "g", tok_m2[n * 8 - 1])
                ctx.wait("t", "g", tok_m2[n * 8 - 1])
                ctx.wait("t", "v", tok_ym[n * 8 - 1])
            emit_bc(n)
            tok_bc[n] = ctx.cnt["v"]
            base = n * 8
            stage_z(base)
            stage_lnr(base)
            for dc in range(DC):
                i = base + dc
                if dc + 1 < DC:
                    stage_z(i + 1)
                    stage_lnr(i + 1)
                stage_v(i)
                stage_yadd(i)
        nv = ctx.cnt["v"]
        ctx.wait("g", "v", nv)
        if "dmx_out" in st:
            ctx.op("g", lambda g, tb=tb: g.dma_start(
                out=st["yp_in"][tb * 1024:(tb + 1) * 1024, :].rearrange(
                    "(c p) t -> p c t", p=128),
                in_=st["ya"][:]), dma=True)
        else:
            ctx.op("g", lambda g, bsl=bsl: g.dma_start(
                out=st["ypart"].rearrange("(c p) t -> p c t", p=128)[:, :, bsl],
                in_=st["ya"][:]), dma=True)
        ctx.wait("v", "gd")


def build_D2():
    return build_program("D2", decl_D2, body_D2)



def decl_D3(nc, st):
    dram_in(nc, st, "x2T", [D, TOK])
    if "D3_srcs" not in st:
        for i in range(4):
            dram_in(nc, st, f"yp{i}", [D, TOK])
    dram_in(nc, st, "yskipT", [D, TOK])
    dram_out(nc, st, "x3T", [D, TOK])

    NT = 512
    st["NT"] = NT
    sb(nc, st, "x2f", [128, DC, NT])
    sb(nc, st, "ypl", [128, 5, DC, NT])
    sb(nc, st, "z", [128, DC, NT])
    sb(nc, st, "ones_bf", [128, 1], BF16)
    sb(nc, st, "ones_f1", [1, 128])
    sb(nc, st, "c_eps", [1, 1])
    sb(nc, st, "ln_zbf", [128, DC, NT], BF16)
    sb(nc, st, "ln_sq", [128, DC, NT], BF16)
    sb(nc, st, "ln_rows", [1, 4 * NT])
    ps(nc, st, "ln_s1", [1, NT])
    ps(nc, st, "ln_s2", [1, NT])
    ps(nc, st, "ln_bca", [128, NT])
    ps(nc, st, "ln_bcb", [128, NT])


def body_D3(ctx, st):
    NT = st["NT"]
    load_consts(ctx, st, {"c_eps": 1e-5})
    ctx.op("g", lambda g: g.memset(st["ones_bf"][:], 1.0))
    ctx.op("g", lambda g: g.memset(st["ones_f1"][:], 1.0))
    for tc in range(TOK // NT):
        tsl = slice(tc * NT, (tc + 1) * NT)
        ctx.barrier("v", "s", "t", "g", "y")
        ctx.op("y", lambda y, tsl=tsl: y.dma_start(
            out=st["x2f"][:], in_=st["x2T"].rearrange("(c p) t -> p c t", p=128)[:, :, tsl]),
            dma=True)
        srcs = st.get("D3_srcs", ["yp0", "yp1", "yp2", "yp3", "yskipT"])
        for i, nm_ in enumerate(srcs):
            ctx.op("g", lambda g, i=i, nm_=nm_, tsl=tsl: g.dma_start(
                out=st["ypl"][:, i, :, :],
                in_=st[nm_].rearrange("(c p) t -> p c t", p=128)[:, :, tsl]), dma=True)
        ctx.wait("v", "y")
        ctx.wait("v", "gd")
        for dc in range(DC):
            ctx.op("v", lambda v, dc=dc: v.tensor_add(st["z"][:, dc, :], st["x2f"][:, dc, :],
                                                      st["ypl"][:, 0, dc, :]))
            for i in range(1, len(srcs)):
                ctx.op("v", lambda v, dc=dc, i=i: v.tensor_add(
                    st["z"][:, dc, :], st["z"][:, dc, :], st["ypl"][:, i, dc, :]))
        emit_ln(ctx, st, st["z"], st["z"], NT, st["c_eps"], "ln")
        nv = ctx.cnt["v"]
        ctx.wait("y", "v", nv)
        ctx.op("y", lambda y, tsl=tsl: y.dma_start(
            out=st["x3T"].rearrange("(c p) t -> p c t", p=128)[:, :, tsl],
            in_=st["z"][:]), dma=True)


def build_D3():
    return build_program("D3", decl_D3, body_D3)


def stage_D1(x2T_cores, inp):
    nwf = np.ascontiguousarray(inp["L1_attn_norm_weight"].reshape(8, 128).T)
    bdf = np.ascontiguousarray(inp["L1_attn_b_delta"].reshape(8, 128).T)
    dskf = np.ascontiguousarray(inp["L1_attn_D_skip"].reshape(8, 128).T)
    WBC = np.ascontiguousarray(np.concatenate([inp["L1_attn_W_B"], inp["L1_attn_W_C"]], axis=1))
    Wd = np.ascontiguousarray(inp["L1_attn_W_delta"])
    return [{"x2T": x2T_cores[i], "nwf": nwf, "Wd": Wd, "bdf": bdf, "WBC": WBC,
             "dskf": dskf} for i in range(NCORE)]


def stage_D2(resD1, inp):
    maps = []
    A = np.ascontiguousarray(inp["L1_attn_A_log"])
    for i in range(NCORE):
        b, nq = i // 4, i % 4
        grp = [4 * b + j for j in range(4)]
        deltaB = np.concatenate([np.asarray(resD1[j]["deltaT"]) for j in grp], axis=1)
        m1B = np.concatenate([np.asarray(resD1[j]["m1T"]) for j in grp], axis=1)
        BpC = np.concatenate([np.asarray(resD1[j]["BpCT"]) for j in grp], axis=1)
        Bp4 = np.ascontiguousarray(BpC[nq * 4:(nq + 1) * 4, :])
        C4 = np.ascontiguousarray(BpC[16 + nq * 4:16 + (nq + 1) * 4, :])
        A4 = np.ascontiguousarray(A[:, nq * 4:(nq + 1) * 4])
        maps.append({"deltaB": np.ascontiguousarray(deltaB),
                     "m1B": np.ascontiguousarray(m1B),
                     "Bp4": Bp4, "C4": C4, "A4": A4})
    return maps


def stage_D3(x2T_cores, resD1, resD2):
    maps = []
    for i in range(NCORE):
        b, qd = i // 4, i % 4
        tsl = slice(qd * TOK, (qd + 1) * TOK)
        m = {"x2T": x2T_cores[i], "yskipT": np.asarray(resD1[i]["yskipT"])}
        for nq in range(4):
            src = np.asarray(resD2[4 * b + nq]["ypart"])[:, tsl]
            m[f"yp{nq}"] = np.ascontiguousarray(src)
        maps.append(m)
    return maps





def decl_E1(nc, st):
    dram_in(nc, st, "x3T", [D, TOK])
    dram_in(nc, st, "gwT", [D, 8])
    dram_out(nc, st, "masko", [16, 128, 8])
    dram_out(nc, st, "combpre", [16, 128, 8])

    sb(nc, st, "x3f32", [128, DC, TOK])
    sb(nc, st, "gwf", [128, DC, 8])
    sb(nc, st, "lg", [128, 8])
    sb(nc, st, "m8", [128, 8])
    sb(nc, st, "negm", [128, 1])
    sb(nc, st, "et", [128, 8])
    sb(nc, st, "srow", [128, 1])
    sb(nc, st, "mask", [128, 8])
    sb(nc, st, "pm", [128, 8])
    sb(nc, st, "cp", [128, 8])
    sb(nc, st, "c_tiny", [128, 1])
    ps(nc, st, "plg", [128, 8])


def body_E1(ctx, st):
    load_consts(ctx, st, {"c_tiny": 1e-20})
    ctx.op("y", lambda y: y.dma_start(
        out=st["x3f32"][:], in_=st["x3T"].rearrange("(c p) t -> p c t", p=128)), dma=True)
    ctx.op("y", lambda y: y.dma_start(
        out=st["gwf"][:], in_=st["gwT"].rearrange("(c p) e -> p c e", p=128)), dma=True)
    ctx.barrier("v", "s", "t", "g", "y")
    for tt in range(16):
        tl = slice(tt * 128, (tt + 1) * 128)
        ctx.wait("t", "v")
        for c in range(DC):
            ctx.op("t", lambda t, c=c, tl=tl: t.matmul(
                st["plg"][:], st["x3f32"][:, c, tl], st["gwf"][:, c, :],
                start=(c == 0), stop=(c == DC - 1)), inc=(c == DC - 1))
        nm = ctx.cnt["t"]
        ctx.wait("v", "t", nm)
        ctx.op("v", lambda v: v.tensor_copy(st["lg"][:], st["plg"][:]))
        ctx.op("v", lambda v: v.max(out=st["m8"][:], in_=st["lg"][:]))
        ctx.op("v", lambda v: v.tensor_scalar(st["negm"][:], st["m8"][:, 0:1], -1.0, None,
                                              ALU.mult))
        ctx.wait("s", "v")
        ctx.op("s", lambda s: s.activation(st["et"][:], st["lg"][:], AF.Exp,
                                           bias=st["negm"][:]))
        ctx.wait("v", "s")
        ctx.op("v", lambda v: v.tensor_reduce(st["srow"][:], st["et"][:], AX.X, ALU.add))
        ctx.op("v", lambda v: v.reciprocal(st["srow"][:], st["srow"][:]))
        ctx.op("v", lambda v: v.tensor_scalar(st["mask"][:], st["lg"][:], st["m8"][:, 1:2],
                                              None, ALU.is_ge))
        ctx.op("v", lambda v: v.tensor_mul(st["pm"][:], st["et"][:], st["mask"][:]))
        ctx.op("v", lambda v: v.tensor_scalar(st["pm"][:], st["pm"][:], st["srow"][:], None,
                                              ALU.mult))
        ctx.op("v", lambda v: v.tensor_reduce(st["srow"][:], st["pm"][:], AX.X, ALU.add))
        ctx.op("v", lambda v: v.tensor_scalar(st["srow"][:], st["srow"][:], st["c_tiny"][:],
                                              None, ALU.add))
        ctx.op("v", lambda v: v.reciprocal(st["srow"][:], st["srow"][:]))
        ctx.op("v", lambda v: v.tensor_scalar(st["cp"][:], st["pm"][:], st["srow"][:], None,
                                              ALU.mult))
        nv = ctx.cnt["v"]
        ctx.wait("y", "v", nv)
        ctx.op("y", lambda y, tt=tt: y.dma_start(out=st["masko"][tt], in_=st["mask"][:]),
               dma=True)
        ctx.op("y", lambda y, tt=tt: y.dma_start(out=st["combpre"][tt], in_=st["cp"][:]),
               dma=True)
        ctx.wait("v", "y")


def build_E1():
    return build_program("E1", decl_E1, body_E1)



def decl_E2(nc, st):
    dram_in(nc, st, "x3T", [D, TOK])
    dram_in(nc, st, "combpre", [16, 128, 8])
    dram_in(nc, st, "maskown", [16, 128, 8])
    if "maskg" not in st:
        dram_in(nc, st, "maskprev", [112, 128, 8])
    else:
        dram_in(nc, st, "rowsel", [128, 1])
    dram_in(nc, st, "wgT", [8, D, 1024])
    dram_in(nc, st, "wuT", [8, D, 1024])
    dram_in(nc, st, "wdT", [8, 1024, D])
    dram_in(nc, st, "wsgT", [D, 2048])
    dram_in(nc, st, "wsuT", [D, 2048])
    dram_in(nc, st, "wsdT", [2048, D])
    dram_out(nc, st, "x4T", [D, TOK])
    st["combd"] = nc.dram_tensor("combd", [8, TOK], F32, kind="Internal").ap()

    NT = 512
    st["NT"] = NT
    if "webf2" not in st:
        st["webf2"] = nc.dram_tensor("webf2", [10, 3, 1024, 1024], BF16, kind="Internal").ap()
        st["E2_precast"] = True
    sb(nc, st, "x3bf", [128, DC, TOK], BF16)
    sb(nc, st, "acte", [128, 8, NT], BF16)
    sb(nc, st, "warena", [128, 2, 3, 8, 1024], BF16)
    sb(nc, st, "ysb", [128, DC, NT])
    sb(nc, st, "x3f", [128, DC, NT])
    if "maskg" in st:
        sb(nc, st, "mpre", [128, 128, 8], BF16)
        sb(nc, st, "rst", [128, 1])
    else:
        sb(nc, st, "mpre", [128, 112, 8], BF16)
    sb(nc, st, "mtile", [128, 8], BF16)
    sb(nc, st, "prow", [128, 8])
    sb(nc, st, "post", [128, 8])
    sb(nc, st, "cpre", [128, 8])
    sb(nc, st, "combm", [128, 16, 8])
    sb(nc, st, "ctile", [8, 128])
    sb(nc, st, "combrow", [1, NT])
    sb(nc, st, "onesrow", [1, NT])
    sb(nc, st, "offs", [1, 8])
    sb(nc, st, "trow", [1, 8])
    sb(nc, st, "tri", [128, 128], BF16)
    sb(nc, st, "trif", [128, 128])
    sb(nc, st, "idf", [128, 128])
    sb(nc, st, "sgt", [128, 2, NT], BF16)
    sb(nc, st, "ones_bf", [128, 1], BF16)
    sb(nc, st, "ones_f1", [1, 128])
    sb(nc, st, "ones_c", [128, 1])
    sb(nc, st, "c_eps", [1, 1])
    sb(nc, st, "ln_zbf", [128, DC, NT], BF16)
    sb(nc, st, "ln_sq", [128, DC, NT], BF16)
    sb(nc, st, "ln_rows", [1, 4 * NT])
    ps(nc, st, "pg", [128, NT])
    ps(nc, st, "pg2", [128, NT])
    ps(nc, st, "pu2", [128, NT])
    ps(nc, st, "pu2b", [128, NT])
    ps(nc, st, "pd", [128, NT])
    ps(nc, st, "pd2", [128, NT])
    ps(nc, st, "pbc2", [128, NT])
    st["ln_s1"] = st["pg"][0:1, :]
    st["ln_s2"] = st["pg2"][0:1, :]
    st["ln_bca"] = st["pu2"][:]
    st["ln_bcb"] = st["pu2b"][:]


def emit_precast_E2(ctx, st):
    w2 = st["webf2"]
    for e in range(8):
        for j, src in ((0, st["wgT"]), (1, st["wuT"]), (2, st["wdT"])):
            ctx.op("g", lambda g, e=e, j=j, src=src: g.dma_start(
                out=w2[e, j], in_=src[e]), dma=True)
    for es in range(2):
        csl = slice(es * 1024, (es + 1) * 1024)
        ctx.op("g", lambda g, csl=csl, es=es: g.dma_start(
            out=w2[8 + es, 0], in_=st["wsgT"][:, csl]), dma=True)
        ctx.op("g", lambda g, csl=csl, es=es: g.dma_start(
            out=w2[8 + es, 1], in_=st["wsuT"][:, csl]), dma=True)
        ctx.op("g", lambda g, csl=csl, es=es: g.dma_start(
            out=w2[8 + es, 2], in_=st["wsdT"][csl, :]), dma=True)


def body_E2(ctx, st):
    NT = st["NT"]
    load_consts(ctx, st, {"c_eps": 1e-5})
    ctx.op("g", lambda g: g.memset(st["ones_bf"][:], 1.0))
    ctx.op("g", lambda g: g.memset(st["ones_f1"][:], 1.0))
    ctx.op("g", lambda g: g.memset(st["ones_c"][:], 1.0))
    ctx.op("g", lambda g: g.dma_start(
        out=st["x3bf"][:], in_=st["x3T"].rearrange("(c p) t -> p c t", p=128)), dma=True)
    if "maskg" in st:
        ctx.op("g", lambda g: g.dma_start(
            out=st["mpre"][:, 0:128, :],
            in_=st["maskg"].rearrange("(a p) e -> a p e", p=128)), dma=True)
        ctx.op("y", lambda y: y.dma_start(out=st["rst"][:], in_=st["rowsel"]), dma=True)
    else:
        ctx.op("g", lambda g: g.dma_start(
            out=st["mpre"][:], in_=st["maskprev"].rearrange("a p e -> p a e")), dma=True)
    ctx.op("g", lambda g: g.memset(st["trif"][:], 1.0))
    ctx.op("g", lambda g: g.affine_select(st["trif"][:], st["trif"][:], pattern=[[1, 128]],
                                          compare_op=ALU.is_gt, fill=0.0, base=0,
                                          channel_multiplier=-1))
    ctx.op("g", lambda g: g.memset(st["idf"][:], 1.0))
    ctx.op("g", lambda g: g.affine_select(st["idf"][:], st["idf"][:], pattern=[[1, 128]],
                                          compare_op=ALU.is_equal, fill=0.0, base=0,
                                          channel_multiplier=-1))
    ctx.op("g", lambda g: g.dma_start(out=st["tri"][:], in_=st["trif"][:]), dma=True)
    ctx.barrier("v", "s", "t", "g", "y")
    ctx.wait("v", "gd")
    ctx.wait("t", "gd")

    if "maskg" in st:
        ctx.wait("v", "y")
        ctx.op("v", lambda v: v.tensor_scalar(
            st["mpre"].ap().rearrange("a p e -> a (p e)"),
            st["mpre"].ap().rearrange("a p e -> a (p e)"),
            st["rst"][:], None, ALU.mult))
        ctx.op("v", lambda v: v.tensor_reduce(
            st["prow"][:], st["mpre"].ap().rearrange("a p e -> a e p"),
            AX.X, ALU.add))
    else:
        ctx.op("v", lambda v: v.tensor_reduce(
            st["prow"][:], st["mpre"].ap().rearrange("p a e -> p e a"), AX.X, ALU.add))
    nv = ctx.cnt["v"]
    ctx.wait("t", "v", nv)
    ctx.op("t", lambda t: t.matmul(st["ln_s1"][:, 0:8], st["ones_c"][:], st["prow"][:],
                                   start=True, stop=True))
    ctx.wait("v", "t")
    ctx.op("v", lambda v: v.tensor_copy(st["offs"][:], st["ln_s1"][:, 0:8]))
    for tt in range(16):
        ctx.wait("y", "v")
        ctx.wait("g", "v")
        ctx.op("g", lambda g, tt=tt: g.dma_start(out=st["mtile"][:], in_=st["maskown"][tt]),
               dma=True)
        ctx.op("y", lambda y, tt=tt: y.dma_start(out=st["cpre"][:], in_=st["combpre"][tt]),
               dma=True)
        ctx.wait("t", "y")
        ctx.wait("t", "gd")
        ctx.wait("t", "v")
        ctx.op("t", lambda t: t.matmul(st["pg"][:, 0:8], st["tri"][:], st["mtile"][:],
                                       start=True, stop=True))
        ctx.op("t", lambda t: t.matmul(st["pbc2"][:, 0:8], st["ones_f1"][:], st["offs"][:],
                                       start=True, stop=True))
        nm = ctx.cnt["t"]
        ctx.wait("v", "t", nm)
        ctx.op("v", lambda v: v.tensor_copy(st["post"][:], st["pg"][:, 0:8]))
        ctx.op("v", lambda v: v.tensor_add(st["post"][:], st["post"][:], st["pbc2"][:, 0:8]))
        ctx.op("v", lambda v: v.tensor_scalar(st["post"][:], st["post"][:], float(CAP), None,
                                              ALU.is_lt))
        ctx.op("v", lambda v, tt=tt: v.tensor_mul(st["combm"][:, tt, :], st["cpre"][:],
                                                  st["post"][:]))
        ctx.op("t", lambda t: t.matmul(st["ln_s2"][:, 0:8], st["ones_bf"][:], st["mtile"][:],
                                       start=True, stop=True, skip_group_check=True))
        ctx.wait("v", "t")
        ctx.op("v", lambda v: v.tensor_copy(st["trow"][:], st["ln_s2"][:, 0:8]))
        ctx.op("v", lambda v: v.tensor_add(st["offs"][:], st["offs"][:], st["trow"][:]))
    for tt in range(16):
        ctx.wait("t", "v")
        ctx.op("t", lambda t, tt=tt: t.transpose(st["pbc2"][0:8, 0:128],
                                                 st["combm"][:, tt, :], st["idf"][:]))
        nm = ctx.cnt["t"]
        ctx.wait("v", "t", nm)
        ctx.op("v", lambda v: v.tensor_copy(st["ctile"][:], st["pbc2"][0:8, 0:128]))
        nv = ctx.cnt["v"]
        ctx.wait("y", "v", nv)
        ctx.op("y", lambda y, tt=tt: y.dma_start(
            out=st["combd"][:, tt * 128:(tt + 1) * 128], in_=st["ctile"][:]), dma=True)
        ctx.wait("v", "y")

    if st.get("E2_precast"):
        emit_precast_E2(ctx, st)
    w2 = st["webf2"]
    wa = st["warena"]

    def load_w(e, buf):
        toks = []
        for j in range(3):
            toks.append(ctx.op("g", lambda g, e=e, j=j, buf=buf: g.dma_start(
                out=wa[:, buf, j, :, :],
                in_=w2[e, j].rearrange("(c p) i -> p c i", p=128)), dma=True))
        return toks

    for tc in range(TOK // NT):
        tsl = slice(tc * NT, (tc + 1) * NT)
        ctx.barrier("v", "s", "t", "g", "y")
        ctx.op("y", lambda y, tsl=tsl: y.dma_start(
            out=st["x3f"][:], in_=st["x3T"].rearrange("(c p) t -> p c t", p=128)[:, :, tsl]),
            dma=True)
        ctx.op("v", lambda v: v.memset(st["onesrow"][:], 1.0))
        wtoks = {0: load_w(0, 0)}
        evac_gu = [None, None]
        evac_pd = [None, None]
        last_acte_v = None
        last_down_t = None
        for e in range(10):
            if e + 1 < 10:
                if e >= 1:
                    ctx.wait("g", "t")
                wtoks[e + 1] = load_w(e + 1, (e + 1) % 2)
            buf = e % 2
            for tok in wtoks[e]:
                ctx.wait_tok("t", tok)
            if e < 8:
                ctx.wait("y", "t")
                cr = ctx.op("y", lambda y, e=e, tsl=tsl: y.dma_start(
                    out=st["combrow"][:], in_=st["combd"][e:e + 1, tsl]), dma=True)
                ctx.wait_tok("t", cr)
                crow = st["combrow"]
            else:
                ctx.wait("t", "v")
                crow = st["onesrow"]
            if last_acte_v is not None:
                ctx.wait("t", "v", last_acte_v)
            ctx.op("t", lambda t, crow=crow: t.matmul(
                st["pbc2"][:], st["ones_f1"][:], crow[:], start=True, stop=True))
            for ic in range(8):
                bk = ic % 2
                pgb = st["pg"] if bk == 0 else st["pg2"]
                pub = st["pu2"] if bk == 0 else st["pu2b"]
                if evac_gu[bk] is not None:
                    ctx.wait("t", "v", evac_gu[bk])
                for c in range(DC):
                    ctx.op("t", lambda t, ic=ic, c=c, tsl=tsl, buf=buf, pgb=pgb: t.matmul(
                        pgb[:], wa[:, buf, 0, c, ic * 128:(ic + 1) * 128],
                        st["x3bf"][:, c, tsl], start=(c == 0), stop=(c == DC - 1)),
                        inc=(c == DC - 1))
                for c in range(DC):
                    ctx.op("t", lambda t, ic=ic, c=c, tsl=tsl, buf=buf, pub=pub: t.matmul(
                        pub[:], wa[:, buf, 1, c, ic * 128:(ic + 1) * 128],
                        st["x3bf"][:, c, tsl], start=(c == 0), stop=(c == DC - 1)),
                        inc=(c == DC - 1))
                nm = ctx.cnt["t"]
                ctx.wait("s", "t", nm)
                if evac_gu[bk] is not None:
                    ctx.wait("s", "v", evac_gu[bk])
                ctx.op("s", lambda s, pgb=pgb, bk=bk: s.activation(st["sgt"][:, bk, :],
                                                                   pgb[:], AF.Silu))
                ctx.wait("v", "s")
                if ic == 0 and last_down_t is not None:
                    ctx.wait("v", "t", last_down_t)
                ctx.op("v", lambda v, ic=ic, pub=pub, bk=bk: v.tensor_mul(
                    st["acte"][:, ic, :], st["sgt"][:, bk, :], pub[:]))
                ctx.op("v", lambda v, ic=ic: v.tensor_mul(
                    st["acte"][:, ic, :], st["acte"][:, ic, :], st["pbc2"][:]))
                evac_gu[bk] = ctx.cnt["v"]
            last_acte_v = ctx.cnt["v"]
            ctx.wait("t", "v", last_acte_v)
            for dc in range(DC):
                bk = dc % 2
                pdb = st["pd"] if bk == 0 else st["pd2"]
                if evac_pd[bk] is not None:
                    ctx.wait("t", "v", evac_pd[bk])
                for ic in range(8):
                    ctx.op("t", lambda t, ic=ic, dc=dc, buf=buf, pdb=pdb: t.matmul(
                        pdb[:], wa[:, buf, 2, ic, dc * 128:(dc + 1) * 128],
                        st["acte"][:, ic, :], start=(ic == 0), stop=(ic == 7)),
                        inc=(ic == 7))
                nm = ctx.cnt["t"]
                ctx.wait("v", "t", nm)
                if e == 0:
                    ctx.op("v", lambda v, dc=dc, pdb=pdb: v.tensor_copy(st["ysb"][:, dc, :],
                                                                        pdb[:]))
                else:
                    ctx.op("v", lambda v, dc=dc, pdb=pdb: v.tensor_add(
                        st["ysb"][:, dc, :], st["ysb"][:, dc, :], pdb[:]))
                evac_pd[bk] = ctx.cnt["v"]
            last_down_t = ctx.cnt["t"]
        ctx.wait("v", "y")
        for dc in range(DC):
            ctx.op("v", lambda v, dc=dc: v.tensor_add(st["ysb"][:, dc, :], st["ysb"][:, dc, :],
                                                      st["x3f"][:, dc, :]))
        ctx.wait("t", "v")
        emit_ln(ctx, st, st["ysb"], st["ysb"], NT, st["c_eps"], "ln")
        nv = ctx.cnt["v"]
        ctx.wait("y", "v", nv)
        ctx.op("y", lambda y, tsl=tsl: y.dma_start(
            out=st["x4T"].rearrange("(c p) t -> p c t", p=128)[:, :, tsl],
            in_=st["ysb"][:]), dma=True)


def build_E2():
    return build_program("E2", decl_E2, body_E2)


def stage_E1(x3T_cores, inp):
    gwT = np.ascontiguousarray(inp["L1_ffn_gate_w"].T)
    return [{"x3T": x3T_cores[i], "gwT": gwT} for i in range(NCORE)]


def stage_E2(x3T_cores, resE1, inp):
    wgT = np.ascontiguousarray(inp["L1_ffn_we_gate"].transpose(0, 2, 1))
    wuT = np.ascontiguousarray(inp["L1_ffn_we_up"].transpose(0, 2, 1))
    wdT = np.ascontiguousarray(inp["L1_ffn_we_down"].transpose(0, 2, 1))
    wsgT = np.ascontiguousarray(inp["L1_ffn_ws_gate"].T)
    wsuT = np.ascontiguousarray(inp["L1_ffn_ws_up"].T)
    wsdT = np.ascontiguousarray(inp["L1_ffn_ws_down"].T)
    maps = []
    for i in range(NCORE):
        mprev = np.zeros((112, 128, 8), np.float32)
        if i > 0:
            prev = np.concatenate([np.asarray(resE1[j]["masko"]) for j in range(i)], axis=0)
            mprev[:prev.shape[0]] = prev
        maps.append({"x3T": x3T_cores[i], "combpre": np.asarray(resE1[i]["combpre"]),
                     "maskown": np.asarray(resE1[i]["masko"]), "maskprev": mprev,
                     "wgT": wgT, "wuT": wuT, "wdT": wdT, "wsgT": wsgT, "wsuT": wsuT,
                     "wsdT": wsdT})
    return maps



def _fm(a):
    return np.ascontiguousarray(a.T)


def run_launch(nc, in_maps, outs=None):
    res = run_bass_kernel_spmd(nc, in_maps, core_ids=list(range(NCORE)))
    return res.results


def stage_C(x1T_cores, W1, b1, W2, b2):
    W1T = np.ascontiguousarray(W1.T)
    W2T = np.ascontiguousarray(W2.T)
    b1f = np.ascontiguousarray(b1.reshape(32, 128).T)
    b2f = np.ascontiguousarray(b2.reshape(8, 128).T)
    return [{"x1T": x1T_cores[i], "W1T": W1T, "b1f": b1f, "W2T": W2T, "b2f": b2f}
            for i in range(NCORE)]


_NC_CACHE = {}


def _get_nc(name, builder):
    if name not in _NC_CACHE:
        _NC_CACHE[name] = builder()
    return _NC_CACHE[name]


def kernel(**inputs):
    return kernel_fused(**inputs)


def kernel_multilaunch(**inputs):
    inp = inputs
    x = np.ascontiguousarray(inp["L0_attn_inputs"], dtype=np.float32)
    xT_cores = []
    for c in range(NCORE):
        b, qd = c // 4, c % 4
        xT_cores.append(_fm(x[b, qd * TOK:(qd + 1) * TOK, :]))

    ncA = _get_nc("A", build_A)
    resA = run_launch(ncA, stage_A(xT_cores, inp["L0_attn_W_uv"], inp["L0_attn_b_uv"],
                                   inp["L0_attn_gamma"], inp["L0_attn_beta"]))
    ncB = _get_nc("B", build_B)
    resB = run_launch(ncB, stage_B(xT_cores, resA, inp["L0_attn_W_uv"], inp["L0_attn_b_uv"],
                                   inp["L0_attn_W_out"], inp["L0_attn_b_out"]))
    x1T_cores = [np.ascontiguousarray(np.asarray(resB[i]["x1T"])) for i in range(NCORE)]

    ncC = _get_nc("C", build_C)
    resC = run_launch(ncC, stage_C(x1T_cores, inp["L0_ffn_W1"], inp["L0_ffn_b1"],
                                   inp["L0_ffn_W2"], inp["L0_ffn_b2"]))
    x2T_cores = [np.ascontiguousarray(np.asarray(resC[i]["x2T"])) for i in range(NCORE)]

    ncD1 = _get_nc("D1", build_D1)
    resD1 = run_launch(ncD1, stage_D1(x2T_cores, inp))
    ncD2 = _get_nc("D2", build_D2)
    resD2 = run_launch(ncD2, stage_D2(resD1, inp))
    ncD3 = _get_nc("D3", build_D3)
    resD3 = run_launch(ncD3, stage_D3(x2T_cores, resD1, resD2))
    x3T_cores = [np.ascontiguousarray(np.asarray(resD3[i]["x3T"])) for i in range(NCORE)]

    ncE1 = _get_nc("E1", build_E1)
    resE1 = run_launch(ncE1, stage_E1(x3T_cores, inp))
    ncE2 = _get_nc("E2", build_E2)
    resE2 = run_launch(ncE2, stage_E2(x3T_cores, resE1, inp))

    out = np.empty((2, S, D), np.float32)
    for c in range(NCORE):
        b, qd = c // 4, c % 4
        out[b, qd * TOK:(qd + 1) * TOK, :] = np.asarray(resE2[c]["x4T"]).T
    return out


def stage_A(xT_cores, W_uv, b_uv, gamma, beta):
    Wvb = np.ascontiguousarray(W_uv[:, 2048:4224])
    bvbf = np.ascontiguousarray(b_uv[4096:4224].reshape(1, 128).T)
    bvrow = np.ascontiguousarray(b_uv[2048:4096].reshape(1, 2048))
    gb = np.ascontiguousarray(np.stack([gamma[0], beta[0], gamma[1], beta[1]], axis=1))
    return [{"xT": xT_cores[i], "Wvb": Wvb, "bvbf": bvbf, "bvrow": bvrow, "gb": gb}
            for i in range(NCORE)]


def stage_B(xT_cores, resA, W_uv, b_uv, W_out, b_out):
    Wu = np.ascontiguousarray(W_uv[:, 0:2048])
    buf = np.ascontiguousarray(b_uv[0:2048].reshape(16, 128).T)
    bof = np.ascontiguousarray(b_out.reshape(8, 128).T)
    maps = []
    for i in range(NCORE):
        b = i // 4
        grp = [4 * b + j for j in range(4)]
        kb = np.concatenate([np.asarray(resA[j]["kf"]) for j in grp], axis=1)
        vblkb = np.concatenate([np.asarray(resA[j]["vblk"]) for j in grp], axis=1)
        maps.append({"xT": xT_cores[i], "Wu": Wu, "buf": buf,
                     "qf": np.asarray(resA[i]["qf"]), "kb": kb,
                     "vblkb": np.ascontiguousarray(vblkb), "Wo": W_out, "bof": bof})
    return maps



def build_F():
    nc = bass.Bass("TRN2", target_bir_lowering=False)
    master = ExitStack()
    with master:
        def idram(name, shape, dt=F32):
            return nc.dram_tensor(name, list(shape), dt, kind="Internal").ap()

        qT = idram("qT", [128, TOK])
        cgki = idram("cgki", [128, TOK], BF16)
        cgko = idram("cgko", [512, TOK], BF16)
        cgvi = idram("cgvi", [16, TOK, 128], BF16)
        cgvo = idram("cgvo", [4 * 16 * TOK, 128], BF16)
        x1T = idram("x1T", [D, TOK])
        x2T = idram("x2T", [D, TOK])
        x3T = idram("x3T", [D, TOK])
        dmx_in = idram("dmx_in", [2080, TOK], BF16)
        dmx_out = idram("dmx_out", [4 * 2080, TOK], BF16)
        yskipT = idram("yskipT", [D, TOK], BF16)
        yp_in = idram("yp_in", [4 * D, TOK], BF16)
        rs_out = idram("rs_out", [D, TOK], BF16)
        masko = idram("masko", [16, 128, 8])
        combpre = idram("combpre", [16, 128, 8])
        maskg = idram("maskg", [16384, 8])

        phases = []

        def mk_phase(pfx, decl_fn, prefill):
            pst = {"_nc": nc, "_pfx": pfx + "_"}
            pst.update(prefill)
            pstack = ExitStack()
            pst["_stack"] = pstack
            with pstack:
                decl_fn(nc, pst)
            return pst

        xT_ext = nc.dram_tensor("xT", [D, TOK], F32, kind="ExternalInput").ap()
        stA = mk_phase("A", decl_A, {"xT": xT_ext, "vblk": cgvi, "qf": qT, "kf": cgki})
        stB = mk_phase("B", decl_B2, {
            "xT": xT_ext, "qf": qT, "x1T": x1T,
            "kb4": [cgko[r * 128:(r + 1) * 128, :] for r in range(4)],
            "vb4": (lambda r, ec: cgvo[r * 16 * TOK + ec * TOK:
                                           r * 16 * TOK + (ec + 1) * TOK, :].rearrange(
                "(kt p) e -> p kt e", p=128)),
        })
        stC = mk_phase("C", decl_C, {"x1T": x1T, "x2T": x2T})
        stD1 = mk_phase("D1", decl_D1x, {
            "x2T": x2T, "deltaT": dmx_in[0:1024, :], "m1T": dmx_in[1024:2048, :],
            "BpCT": dmx_in[2048:2080, :], "yskipT": yskipT})
        stD2 = mk_phase("D2", decl_D2, {"dmx_out": dmx_out, "yp_in": yp_in})
        stD3 = mk_phase("D3", decl_D3, {
            "x2T": x2T, "yskipT": yskipT, "x3T": x3T, "ypr": rs_out,
            "D3_srcs": ["ypr", "yskipT"]})
        stE1 = mk_phase("E1", decl_E1, {"x3T": x3T, "masko": masko, "combpre": combpre})
        stE2 = mk_phase("E2", decl_E2, {"x3T": x3T, "maskg": maskg,
                                        "maskown": masko, "combpre": combpre})

        sems = {}
        semnames = ["v", "s", "t", "g", "cc"]
        for i in range(Ctx.DMA_ROT):
            semnames += [f"y{i}", f"gd{i}"]
        for sname in semnames:
            sems[sname] = master.enter_context(nc.semaphore("sem_" + sname))

        def body_F(ctx):
            body_A(ctx, stA)
            ctx.wait("g", "y")
            ctx.full_barrier()
            ctx.op("g", lambda g: g.collective_compute(
                "AllGather", ALU.bypass, replica_groups=[[0, 1, 2, 3], [4, 5, 6, 7]],
                ins=[cgki], outs=[cgko]), sem="cc")
            ctx.op("g", lambda g: g.collective_compute(
                "AllGather", ALU.bypass, replica_groups=[[0, 1, 2, 3], [4, 5, 6, 7]],
                ins=[cgvi.rearrange("a b c -> (a b) c")],
                outs=[cgvo]), sem="cc")
            ctx.full_barrier()
            body_B2(ctx, stB)
            ctx.full_barrier()
            body_C(ctx, stC)
            ctx.full_barrier()
            body_D1x(ctx, stD1)
            ctx.full_barrier()
            ctx.op("g", lambda g: g.collective_compute(
                "AllGather", ALU.bypass, replica_groups=[[0, 1, 2, 3], [4, 5, 6, 7]],
                ins=[dmx_in], outs=[dmx_out]), sem="cc")
            ctx.full_barrier()
            body_D2(ctx, stD2)
            ctx.full_barrier()
            ctx.op("g", lambda g: g.collective_compute(
                "ReduceScatter", ALU.add, replica_groups=[[0, 1, 2, 3], [4, 5, 6, 7]],
                ins=[yp_in], outs=[rs_out]), sem="cc")
            ctx.full_barrier()
            body_D3(ctx, stD3)
            ctx.full_barrier()
            body_E1(ctx, stE1)
            ctx.full_barrier()
            ctx.op("g", lambda g: g.collective_compute(
                "AllGather", ALU.bypass, replica_groups=[[0, 1, 2, 3, 4, 5, 6, 7]],
                ins=[masko.rearrange("a b c -> (a b) c")], outs=[maskg]), sem="cc")
            ctx.full_barrier()
            body_E2(ctx, stE2)

        with nc.Block() as block:
            def run_pass(en, engine_obj):
                ctx = Ctx()
                ctx.cur = en
                ctx.eng = {en: engine_obj}
                ctx.sems = sems
                body_F(ctx)

            @block.vector
            def _(vector):
                run_pass("v", vector)

            @block.scalar
            def _(scalar):
                run_pass("s", scalar)

            @block.tensor
            def _(tensor):
                run_pass("t", tensor)

            @block.gpsimd
            def _(gpsimd):
                run_pass("g", gpsimd)

            @block.sync
            def _(sync):
                run_pass("y", sync)
                ctx = Ctx()
                ctx.cur = None
                body_F(ctx)
                for i in range(Ctx.DMA_ROT):
                    if ctx.cnt[f"y{i}"]:
                        sync.wait_ge(sems[f"y{i}"], ctx.cnt[f"y{i}"])
                    if ctx.cnt[f"gd{i}"]:
                        sync.wait_ge(sems[f"gd{i}"], ctx.cnt[f"gd{i}"])
    return nc


def stage_F(inputs):
    inp = inputs
    x = np.ascontiguousarray(inp["L0_attn_inputs"], dtype=np.float32)
    W_uv, b_uv = inp["L0_attn_W_uv"], inp["L0_attn_b_uv"]
    Wvb = np.ascontiguousarray(W_uv[:, 2048:4224])
    bvbf = np.ascontiguousarray(b_uv[4096:4224].reshape(1, 128).T)
    bvrow = np.ascontiguousarray(b_uv[2048:4096].reshape(1, 2048))
    gamma, beta = inp["L0_attn_gamma"], inp["L0_attn_beta"]
    gb = np.ascontiguousarray(np.stack([gamma[0], beta[0], gamma[1], beta[1]], axis=1))
    Wu = np.ascontiguousarray(W_uv[:, 0:2048])
    buf = np.ascontiguousarray(b_uv[0:2048].reshape(16, 128).T)
    bof = np.ascontiguousarray(inp["L0_attn_b_out"].reshape(8, 128).T)
    W1T = np.ascontiguousarray(inp["L0_ffn_W1"].T)
    W2T = np.ascontiguousarray(inp["L0_ffn_W2"].T)
    b1f = np.ascontiguousarray(inp["L0_ffn_b1"].reshape(32, 128).T)
    b2f = np.ascontiguousarray(inp["L0_ffn_b2"].reshape(8, 128).T)
    nwf = np.ascontiguousarray(inp["L1_attn_norm_weight"].reshape(8, 128).T)
    bdf = np.ascontiguousarray(inp["L1_attn_b_delta"].reshape(8, 128).T)
    dskf = np.ascontiguousarray(inp["L1_attn_D_skip"].reshape(8, 128).T)
    WBC = np.ascontiguousarray(np.concatenate([inp["L1_attn_W_B"], inp["L1_attn_W_C"]], axis=1))
    Wd = np.ascontiguousarray(inp["L1_attn_W_delta"])
    Alog = np.ascontiguousarray(inp["L1_attn_A_log"])
    gwT = np.ascontiguousarray(inp["L1_ffn_gate_w"].T)
    wgT = np.ascontiguousarray(inp["L1_ffn_we_gate"].transpose(0, 2, 1))
    wuT = np.ascontiguousarray(inp["L1_ffn_we_up"].transpose(0, 2, 1))
    wdT = np.ascontiguousarray(inp["L1_ffn_we_down"].transpose(0, 2, 1))
    wsgT = np.ascontiguousarray(inp["L1_ffn_ws_gate"].T)
    wsuT = np.ascontiguousarray(inp["L1_ffn_ws_up"].T)
    wsdT = np.ascontiguousarray(inp["L1_ffn_ws_down"].T)
    maps = []
    for c in range(NCORE):
        b, qd = c // 4, c % 4
        xT = _fm(x[b, qd * TOK:(qd + 1) * TOK, :])
        A4 = np.ascontiguousarray(Alog[:, qd * 4:(qd + 1) * 4])
        bpcsel = np.zeros((32, 8), np.float32)
        for n in range(4):
            bpcsel[qd * 4 + n, n] = 1.0
            bpcsel[16 + qd * 4 + n, 4 + n] = 1.0
        rowsel = np.zeros((128, 1), np.float32)
        rowsel[0:c * 16, 0] = 1.0
        maps.append({
            "xT": xT, "Wvb": Wvb, "bvbf": bvbf, "bvrow": bvrow, "gb": gb,
            "Wu": Wu, "buf": buf, "Wo": inp["L0_attn_W_out"], "bof": bof,
            "W1T": W1T, "b1f": b1f, "W2T": W2T, "b2f": b2f,
            "nwf": nwf, "Wd": Wd, "bdf": bdf, "WBC": WBC, "dskf": dskf,
            "A4": A4, "bpcsel": bpcsel, "rowsel": rowsel,
            "gwT": gwT,
            "wgT": wgT, "wuT": wuT, "wdT": wdT,
            "wsgT": wsgT, "wsuT": wsuT, "wsdT": wsdT,
        })
    return maps


def kernel_fused(**inputs):
    nc = _get_nc("F", build_F)
    maps = stage_F(inputs)
    res = run_launch(nc, maps)
    out = np.empty((2, S, D), np.float32)
    for c in range(NCORE):
        b, qd = c // 4, c % 4
        out[b, qd * TOK:(qd + 1) * TOK, :] = np.asarray(res[c]["x4T"]).T
    return out


# version 16
# speedup vs baseline: 1.9336x; 1.0539x over previous
import numpy as np
import ml_dtypes
from contextlib import ExitStack

import concourse.bass as bass
import concourse.mybir as mybir
import concourse.bass_isa as bass_isa
from concourse import library_config
from concourse.bass_utils import run_bass_kernel_spmd

F32 = mybir.dt.float32
BF16 = mybir.dt.bfloat16
AF = mybir.ActivationFunctionType
ALU = mybir.AluOpType
AX = mybir.AxisListType

NCORE = 8
D = 1024
DC = 8
TOK = 2048
B = 2
S = 8192
CAP = 5120
ENGS = ("v", "s", "t", "g", "y")


class Ctx:

    DMA_ROT = 4

    def __init__(self):
        self.cnt = {e: 0 for e in ENGS}
        for i in range(self.DMA_ROT):
            self.cnt[f"y{i}"] = 0
            self.cnt[f"gd{i}"] = 0
        self.cnt["cc"] = 0
        self.rr = {"y": 0, "gd": 0}
        self.cur = None
        self.eng = None
        self.sems = None

    def _emit(self, en):
        return self.cur == en and self.eng is not None

    def op(self, en, emit, inc=True, serialize=True, dma=False, sem=None):
        step = 16 if (dma or en in ("y",)) else 1
        if dma and sem is None:
            q = "y" if en == "y" else "gd"
            k = self.rr[q]
            self.rr[q] = (k + 1) % self.DMA_ROT
            sem = f"{q}{k}"
            prev = self.cnt[sem]
            self.cnt[sem] += step
            n = self.cnt[sem]
            if self._emit(en):
                if prev:
                    self.eng[en].wait_ge(self.sems[sem], prev)
                inst = emit(self.eng[en])
                inst.then_inc(self.sems[sem], step)
            return (sem, n)
        if sem is None:
            sem = en
        elif sem == "cc":
            step = 1
        if inc:
            self.cnt[sem] += step
        n = self.cnt[sem]
        if self._emit(en):
            inst = emit(self.eng[en])
            if inc:
                inst.then_inc(self.sems[sem], step)
                if serialize and en != "t":
                    self.eng[en].wait_ge(self.sems[sem], n)
        return n

    def wait(self, en, producer, n=None):
        if producer in ("y", "gd") and n is None:
            for i in range(self.DMA_ROT):
                sname = f"{producer}{i}"
                if self.cnt[sname] and self._emit(en):
                    self.eng[en].wait_ge(self.sems[sname], self.cnt[sname])
            return None
        if n is None:
            n = self.cnt[producer]
        if self._emit(en):
            self.eng[en].wait_ge(self.sems[producer], n)
        return n

    def wait_tok(self, en, tok):
        if tok is None:
            return
        sem, n = tok
        if self._emit(en):
            self.eng[en].wait_ge(self.sems[sem], n)

    def barrier(self, *ens):
        snap = {e: self.cnt.get(e) for e in ens}
        for e in ens:
            for p in ens:
                if p == e:
                    continue
                if p in ("y", "gd"):
                    self.wait(e, p)
                else:
                    self.wait(e, p, snap[p])
            self.wait(e, "y")
            self.wait(e, "gd")

    def full_barrier(self):
        ens = ("v", "s", "t", "g", "y")
        snap = {e: self.cnt.get(e) for e in ens}
        ncc = self.cnt["cc"]
        for e in ens:
            for p in ens:
                if p in (e, "y"):
                    continue
                self.wait(e, p, snap[p])
            self.wait(e, "y")
            self.wait(e, "gd")
            if ncc:
                self.wait(e, "cc", ncc)


def build_program(name, decl_fn, body_fn):
    nc = bass.Bass("TRN2", target_bir_lowering=False)
    st = {}
    stack = ExitStack()
    with stack:
        st["_nc"] = nc
        st["_stack"] = stack
        decl_fn(nc, st)
        sems = {}
        semnames = ["v", "s", "t", "g", "cc"]
        for i in range(Ctx.DMA_ROT):
            semnames += [f"y{i}", f"gd{i}"]
        for sname in semnames:
            sems[sname] = stack.enter_context(nc.semaphore("sem_" + sname))
        with nc.Block() as block:
            def run_pass(en, engine_obj):
                ctx = Ctx()
                ctx.cur = en
                ctx.eng = {en: engine_obj}
                ctx.sems = sems
                body_fn(ctx, st)

            @block.vector
            def _(vector):
                run_pass("v", vector)

            @block.scalar
            def _(scalar):
                run_pass("s", scalar)

            @block.tensor
            def _(tensor):
                run_pass("t", tensor)

            @block.gpsimd
            def _(gpsimd):
                run_pass("g", gpsimd)

            @block.sync
            def _(sync):
                run_pass("y", sync)
                ctx = Ctx()
                ctx.cur = None
                body_fn(ctx, st)
                for i in range(Ctx.DMA_ROT):
                    if ctx.cnt[f"y{i}"]:
                        sync.wait_ge(sems[f"y{i}"], ctx.cnt[f"y{i}"])
                    if ctx.cnt[f"gd{i}"]:
                        sync.wait_ge(sems[f"gd{i}"], ctx.cnt[f"gd{i}"])
    return nc



def sb(nc, st, name, shape, dt=F32):
    tname = st.get("_pfx", "") + name
    st[name] = st["_stack"].enter_context(nc.sbuf_tensor(tname, list(shape), dt))


def ps(nc, st, name, shape, dt=F32):
    tname = st.get("_pfx", "") + name
    st[name] = st["_stack"].enter_context(nc.psum_tensor(tname, list(shape), dt))


def dram_in(nc, st, name, shape, dt=F32):
    if name in st:
        return
    st[name] = nc.dram_tensor(name, list(shape), dt, kind="ExternalInput").ap()


def dram_out(nc, st, name, shape, dt=F32):
    if name in st:
        return
    st[name] = nc.dram_tensor(name, list(shape), dt, kind="ExternalOutput").ap()


def load_consts(ctx, st, vals):
    for cname, v in vals.items():
        ctx.op("g", lambda g, c=cname, vv=v: g.memset(st[c][:], vv))


def emit_ln(ctx, st, z, xout, ncol, eps_tile, pfx, tc=None):
    zbf, sq = st[pfx + "_zbf"], st[pfx + "_sq"]
    rows = st[pfx + "_rows"]
    def _ap(x):
        return x if isinstance(x, bass.AP) else x[:]
    s1, s2 = _ap(st[pfx + "_s1"]), _ap(st[pfx + "_s2"])
    bca, bcb = _ap(st[pfx + "_bca"]), _ap(st[pfx + "_bcb"])
    mean = rows[:, 0 * ncol:1 * ncol]
    var = rows[:, 1 * ncol:2 * ncol]
    rstd = rows[:, 2 * ncol:3 * ncol]
    negmrs = rows[:, 3 * ncol:4 * ncol]

    ctx.wait("s", "v")
    for c in range(DC):
        ctx.op("v", lambda v, c=c: v.tensor_copy(zbf[:, c, :], z[:, c, :]))
        ctx.op("s", lambda s, c=c: s.activation(sq[:, c, :], z[:, c, :], AF.Square))
    nv = ctx.cnt["v"]
    nsq = ctx.cnt["s"]
    ctx.wait("t", "v", nv)
    ctx.wait("t", "s", nsq)
    s2start = not st.get("ln_s2_nostart", False)
    for c in range(DC):
        ctx.op("t", lambda t, c=c: t.matmul(s1, st["ones_bf"][:], zbf[:, c, :],
                                            start=(c == 0), stop=(c == DC - 1),
                                            skip_group_check=True), inc=(c == DC - 1))
    for c in range(DC):
        ctx.op("t", lambda t, c=c: t.matmul(s2, st["ones_bf"][:], sq[:, c, :],
                                            start=(c == 0 and s2start), stop=(c == DC - 1),
                                            skip_group_check=True), inc=(c == DC - 1))
    nmm = ctx.cnt["t"]
    ctx.wait("v", "t", nmm)
    ctx.op("v", lambda v: v.tensor_scalar(mean, s1, 1.0 / D, None, ALU.mult))
    ctx.op("v", lambda v: v.tensor_scalar(var, s2, 1.0 / D, None, ALU.mult))
    ctx.wait("s", "v", ctx.cnt["v"])
    ctx.op("s", lambda s: s.activation(rstd, mean, AF.Square))
    ctx.wait("v", "s", ctx.cnt["s"])
    ctx.op("v", lambda v: v.tensor_sub(var, var, rstd))
    ctx.wait("s", "v", ctx.cnt["v"])
    ctx.op("s", lambda s: s.activation(rstd, var, AF.Sqrt, bias=eps_tile[:]))
    ctx.wait("v", "s", ctx.cnt["s"])
    ctx.op("v", lambda v: v.reciprocal(rstd, rstd))
    ctx.op("v", lambda v: v.tensor_mul(negmrs, mean, rstd))
    ctx.op("v", lambda v: v.tensor_scalar(negmrs, negmrs, -1.0, None, ALU.mult))
    nrows = ctx.cnt["v"]
    ctx.wait("t", "v", nrows)
    bcbstart = not st.get("ln_bcb_nostart", False)
    ctx.op("t", lambda t: t.matmul(bca, st["ones_f1"][:], rstd, start=True, stop=True,
                                   skip_group_check=True))
    ctx.op("t", lambda t: t.matmul(bcb, st["ones_f1"][:], negmrs, start=bcbstart, stop=True,
                                   skip_group_check=True))
    nbc = ctx.cnt["t"]
    ctx.wait("v", "t", nbc)
    for c in range(DC):
        ctx.op("v", lambda v, c=c: v.tensor_mul(xout[:, c, :], z[:, c, :], bca))
        ctx.op("v", lambda v, c=c: v.tensor_add(xout[:, c, :], xout[:, c, :], bcb))
    return ctx.cnt["v"]




NQ2 = 512

def decl_B2(nc, st):
    dram_in(nc, st, "xT", [D, TOK])
    dram_in(nc, st, "Wu", [D, 2048])
    dram_in(nc, st, "buf", [128, 16])
    dram_in(nc, st, "Wo", [2048, D])
    dram_in(nc, st, "bof", [128, 8])
    dram_out(nc, st, "x1T", [D, TOK])

    sb(nc, st, "xbf", [128, DC, TOK], BF16)
    sb(nc, st, "kbf", [128, S], BF16)
    sb(nc, st, "qbf", [128, TOK], BF16)
    sb(nc, st, "uc", [128, 16, NQ2], BF16)
    with ExitStack() as _sub:
        _saved = st["_stack"]
        st["_stack"] = _sub
        sb(nc, st, "sqT", [128, 64, NQ2], BF16)
        st["_stack"] = _saved
    sb(nc, st, "z", [128, DC, NQ2])
    sb(nc, st, "x1f", [128, DC, NQ2])
    sb(nc, st, "ln_zbf", [128, DC, NQ2], BF16)
    sb(nc, st, "ln_sq", [128, DC, NQ2], BF16)
    sb(nc, st, "pad64", [128, 16 * NQ2 // 4])
    sb(nc, st, "warena", [128, 16, 1024], BF16)
    sb(nc, st, "vdb", [128, 2, 64, 128], BF16)
    sb(nc, st, "sctmp", [128, 2, NQ2], BF16)
    sb(nc, st, "but", [128, 16])
    sb(nc, st, "bot", [128, 8])
    sb(nc, st, "ones_bf", [128, 1], BF16)
    sb(nc, st, "ones_f1", [1, 128])
    sb(nc, st, "c_eps", [1, 1])
    sb(nc, st, "ln_rows", [1, 4 * NQ2])
    ps(nc, st, "pa", [128, NQ2])
    ps(nc, st, "pb", [128, NQ2])
    ps(nc, st, "pc", [128, NQ2])
    ps(nc, st, "pdx", [128, NQ2])
    ps(nc, st, "ln_s1", [1, NQ2])
    ps(nc, st, "ln_s2", [1, NQ2])
    ps(nc, st, "ln_bca", [128, NQ2])
    ps(nc, st, "ln_bcb", [128, NQ2])


def body_B2(ctx, st):
    load_consts(ctx, st, {"c_eps": 1e-5})
    ctx.op("g", lambda g: g.memset(st["ones_bf"][:], 1.0))
    ctx.op("g", lambda g: g.memset(st["ones_f1"][:], 1.0))
    ctx.op("y", lambda y: y.dma_start(out=st["but"][:], in_=st["buf"]), dma=True)
    ctx.op("y", lambda y: y.dma_start(out=st["bot"][:], in_=st["bof"]), dma=True)
    ctx.op("g", lambda g: g.dma_start(
        out=st["xbf"][:], in_=st["xT"].rearrange("(c p) t -> p c t", p=128)), dma=True)
    for r in range(4):
        ctx.op("y", lambda y, r=r: y.dma_start(
            out=st["kbf"][:, r * TOK:(r + 1) * TOK], in_=st["kb4"][r]), dma=True)
    ctx.op("g", lambda g: g.dma_start(out=st["qbf"][:], in_=st["qf"]), dma=True)
    ctx.barrier("v", "s", "t", "g", "y")
    ctx.wait("t", "gd")
    ctx.wait("t", "y")
    ctx.wait("v", "gd")
    ctx.wait("s", "gd")

    banksA = ("pa", "pb")
    evacA = [None, None]

    def vload(qc_ec, buf):
        ec = qc_ec
        toks = []
        for r in range(4):
            toks.append(ctx.op("y", lambda y, r=r, ec=ec, buf=buf: y.dma_start(
                out=st["vdb"][:, buf, r * 16:(r + 1) * 16, :],
                in_=st["vb4"](r, ec)), dma=True))
        return toks

    for qc in range(TOK // NQ2):
        qsl = slice(qc * NQ2, (qc + 1) * NQ2)
        ctx.wait("g", "t")
        wut = ctx.op("g", lambda g: g.dma_start(
            out=st["warena"].ap().rearrange("p a b -> p (a b)").rearrange(
                "p (c f) -> p c f", c=DC),
            in_=st["Wu"].rearrange("(c p) f -> p c f", p=128)), dma=True)
        ctx.wait_tok("t", wut)
        wu = st["warena"].ap().rearrange("p a b -> p (a b)").rearrange("p (c f) -> p c f", c=DC)
        for ec in range(16):
            bk = ec % 2
            pgb = st[banksA[bk]]
            if evacA[bk] is not None:
                ctx.wait("t", "s", evacA[bk])
            for c in range(DC):
                ctx.op("t", lambda t, ec=ec, c=c, qsl=qsl, pgb=pgb, wu=wu: t.matmul(
                    pgb[:], wu[:, c, ec * 128:(ec + 1) * 128],
                    st["xbf"][:, c, qsl], start=(c == 0), stop=(c == DC - 1)),
                    inc=(c == DC - 1))
            nm = ctx.cnt["t"]
            ctx.wait("s", "t", nm)
            ctx.op("s", lambda s, ec=ec, pgb=pgb: s.activation(
                st["uc"][:, ec, :], pgb[:], AF.Silu, bias=st["but"][:, ec:ec + 1]))
            evacA[bk] = ctx.cnt["s"]
        ctx.wait("v", "y")
        ctx.wait("s", "v")
        sc_evac = [None, None]
        for kt in range(64):
            bk = kt % 2
            psb = st["pc"] if bk == 0 else st["pdx"]
            if sc_evac[bk] is not None:
                ctx.wait("t", "v", sc_evac[bk])
            ctx.op("t", lambda t, kt=kt, qsl=qsl, psb=psb: t.matmul(
                psb[:], st["kbf"][:, kt * 128:(kt + 1) * 128],
                st["qbf"][:, qsl], start=True, stop=True), inc=True)
            nm = ctx.cnt["t"]
            ctx.wait("v", "t", nm)
            ctx.op("v", lambda v, kt=kt, bk=bk, psb=psb: v.tensor_scalar(
                st["sctmp"][:, bk, :], psb[:], 0.0, 1.0 / S, ALU.max, ALU.mult))
            ctx.op("v", lambda v, kt=kt, bk=bk: v.tensor_mul(
                st["sqT"][:, kt, :], st["sctmp"][:, bk, :], st["sctmp"][:, bk, :]))
            sc_evac[bk] = ctx.cnt["v"]
        ctx.wait("t", "v")
        av_evac = [None, None]
        vtoks = {0: vload(0, 0)}
        for ec in range(16):
            if ec + 1 < 16:
                if ec >= 1:
                    ctx.wait("y", "t")
                vtoks[ec + 1] = vload(ec + 1, (ec + 1) % 2)
            for tok in vtoks[ec]:
                ctx.wait_tok("t", tok)
            bk = ec % 2
            pvb = st["pa"] if bk == 0 else st["pb"]
            if av_evac[bk] is not None:
                ctx.wait("t", "v", av_evac[bk])
            if ec < 2:
                ctx.wait("t", "s", evacA[0] if ec == 0 else evacA[1])
            for kt in range(64):
                ctx.op("t", lambda t, kt=kt, ec=ec, bk=bk, pvb=pvb: t.matmul(
                    pvb[:], st["vdb"][:, ec % 2, kt, :], st["sqT"][:, kt, :],
                    start=(kt == 0), stop=(kt == 63)), inc=(kt == 63))
            nm = ctx.cnt["t"]
            ctx.wait("v", "t", nm)
            ctx.op("v", lambda v, ec=ec, pvb=pvb: v.tensor_mul(
                st["uc"][:, ec, :], st["uc"][:, ec, :], pvb[:]))
            av_evac[bk] = ctx.cnt["v"]
        ctx.wait("g", "t")
        wot = ctx.op("g", lambda g: g.dma_start(
            out=st["warena"][:], in_=st["Wo"].rearrange("(ec p) h -> p ec h", p=128)), dma=True)
        ctx.wait("y", "t")
        ctx.op("y", lambda y, qsl=qsl: y.dma_start(
            out=st["x1f"][:], in_=st["xT"].rearrange("(c p) t -> p c t", p=128)[:, :, qsl]),
            dma=True)
        ctx.wait_tok("t", wot)
        ctx.wait("t", "v")
        po_evac = [None, None]
        for dc in range(DC):
            bk = dc % 2
            pob = st["pc"] if bk == 0 else st["pdx"]
            if po_evac[bk] is not None:
                ctx.wait("t", "s", po_evac[bk])
            for ec in range(16):
                ctx.op("t", lambda t, dc=dc, ec=ec, pob=pob: t.matmul(
                    pob[:], st["warena"][:, ec, dc * 128:(dc + 1) * 128],
                    st["uc"][:, ec, :], start=(ec == 0), stop=(ec == 15)),
                    inc=(ec == 15))
            nm = ctx.cnt["t"]
            ctx.wait("s", "t", nm)
            ctx.op("s", lambda s, dc=dc, pob=pob: s.activation(
                st["z"][:, dc, :], pob[:], AF.Identity, bias=st["bot"][:, dc:dc + 1]))
            po_evac[bk] = ctx.cnt["s"]
        ctx.wait("v", "s")
        ctx.wait("v", "y")
        for dc in range(DC):
            ctx.op("v", lambda v, dc=dc: v.tensor_add(st["z"][:, dc, :], st["z"][:, dc, :],
                                                      st["x1f"][:, dc, :]))
        ctx.wait("t", "v")
        emit_ln(ctx, st, st["z"], st["z"], NQ2, st["c_eps"], "ln")
        nv = ctx.cnt["v"]
        ctx.wait("y", "v", nv)
        ctx.op("y", lambda y, qsl=qsl: y.dma_start(
            out=st["x1T"].rearrange("(c p) t -> p c t", p=128)[:, :, qsl],
            in_=st["z"][:]), dma=True)
        evacA = [None, None]
        for i in range(2):
            sc_evac[i] = None




def decl_Cq(nc, st):
    dram_in(nc, st, "Wqs1", [128, 4096])
    dram_in(nc, st, "Wqs2", [512, D])
    sb(nc, st, "qstg", [128, 1, 512])
    sb(nc, st, "qneg", [128, 512], BF16)
    sb(nc, st, "qpos", [128, 512], BF16)
    sb(nc, st, "qabs", [128, 4])
    sb(nc, st, "qrow", [1, 128])
    sb(nc, st, "qthr", [128, 4])
    sb(nc, st, "qones", [128, 1])
    sb(nc, st, "t28", [2, 8])
    sb(nc, st, "t21", [2, 1])
    sb(nc, st, "trow", [1, 2])


def _cq_slices(st):
    s1 = st["Wqs1"]
    s2 = st["Wqs2"].rearrange("(f p) d -> p f d", p=128)
    w1v = st["wq1in"].rearrange("(c p) f -> p (c f)", p=128)
    w2v = st["wq2in"].rearrange("(f p) d -> p f d", p=128)
    out = []
    for s in range(8):
        out.append((0, s1[:, s * 512:(s + 1) * 512],
                    w1v[:, s * 512:(s + 1) * 512]))
    for s in range(4):
        for hh in range(2):
            out.append((1, s2[:, s, hh * 512:(hh + 1) * 512],
                        w2v[:, s, hh * 512:(hh + 1) * 512]))
    return out


def body_Cq_p1(ctx, st):
    ctx.op("g", lambda g: g.memset(st["qrow"][:], 0.0))
    ctx.op("g", lambda g: g.memset(st["qones"][:], 1.0))
    toks = []
    slices = _cq_slices(st)
    for idx, (m, din, _) in enumerate(slices):
        p = 0
        first = idx == 0 or slices[idx - 1][0] != m
        if idx >= 1:
            ctx.wait("y", "v")
        toks.append(ctx.op("y", lambda y, din=din, p=p: y.dma_start(
            out=st["qstg"][:, p, :], in_=din), dma=True))
        ctx.wait_tok("v", toks[-1])
        ctx.op("v", lambda v, p=p: v.tensor_reduce(
            st["qabs"][:, 2:3], st["qstg"][:, p, :], AX.X, ALU.add,
            apply_absolute_value=True))
        if first:
            ctx.op("v", lambda v, m=m: v.tensor_copy(st["qabs"][:, m:m + 1],
                                                     st["qabs"][:, 2:3]))
        else:
            ctx.op("v", lambda v, m=m: v.tensor_add(
                st["qabs"][:, m:m + 1], st["qabs"][:, m:m + 1], st["qabs"][:, 2:3]))
    nv = ctx.cnt["v"]
    ctx.wait("t", "v", nv)
    ctx.wait("t", "g")
    ctx.op("t", lambda t: t.matmul(st["ps_a"][0:1, 0:2], st["qones"][:],
                                   st["qabs"][:, 0:2], start=True, stop=True,
                                   skip_group_check=True))
    nt = ctx.cnt["t"]
    ctx.wait("v", "t", nt)
    ctx.wait("v", "g")
    ctx.op("v", lambda v: v.tensor_copy(st["qrow"][:, 0:2], st["ps_a"][0:1, 0:2]))
    nv2 = ctx.cnt["v"]
    ctx.wait("g", "v", nv2)
    ctx.op("g", lambda g: g.dma_start(out=st["cqin"], in_=st["qrow"][:]), dma=True)


def body_Cq_p2(ctx, st):
    ctx.op("g", lambda g: g.memset(st["qrow"][:], 1.0))
    ltok = ctx.op("y", lambda y: y.dma_start(
        out=st["t28"][:], in_=st["cqout"][:, 0:2].rearrange("r c -> c r"),
        allow_slow_non_contiguous=True), dma=True)
    ctx.wait_tok("v", ltok)
    ctx.op("v", lambda v: v.tensor_reduce(st["t21"][:], st["t28"][:], AX.X, ALU.add))
    ctx.op("v", lambda v: v.tensor_scalar(st["t21"][:], st["t21"][:],
                                          0.5 / 4194304.0, 0.5e-5, ALU.mult, ALU.add))
    nv = ctx.cnt["v"]
    ctx.wait("y", "v", nv)
    d0 = ctx.op("y", lambda y: y.dma_start(
        out=st["scr2"].rearrange("o c -> c o"), in_=st["t21"][:],
        allow_slow_non_contiguous=True), dma=True)
    ctx.wait_tok("y", d0)
    d1 = ctx.op("y", lambda y: y.dma_start(out=st["trow"][:], in_=st["scr2"]), dma=True)
    ctx.wait_tok("t", d1)
    ctx.wait("t", "g")
    ctx.op("t", lambda t: t.matmul(st["ps_b"][:, 0:2], st["qrow"][:],
                                   st["trow"][:], start=True, stop=True, skip_group_check=True))
    nt = ctx.cnt["t"]
    ctx.wait("v", "t", nt)
    ctx.op("v", lambda v: v.tensor_copy(st["qthr"][:, 0:2], st["ps_b"][:, 0:2]))
    ctx.op("v", lambda v: v.tensor_scalar(st["qthr"][:, 2:4], st["qthr"][:, 0:2],
                                          -1.0, None, ALU.mult))
    stoks = []
    for idx, (m, din, dout) in enumerate(_cq_slices(st)):
        p = 0
        if idx >= 1:
            ctx.wait("y", "v")
        tok = ctx.op("y", lambda y, din=din, p=p: y.dma_start(
            out=st["qstg"][:, p, :], in_=din), dma=True)
        ctx.wait_tok("v", tok)
        ctx.wait("v", "gd")
        ctx.op("v", lambda v, p=p, m=m: v.tensor_scalar(
            st["qpos"][:], st["qstg"][:, p, :],
            st["qthr"][:, m:m + 1], None, ALU.is_gt))
        ctx.op("v", lambda v, p=p, m=m: v.tensor_scalar(
            st["qneg"][:], st["qstg"][:, p, :],
            st["qthr"][:, 2 + m:3 + m], None, ALU.is_lt))
        ctx.op("v", lambda v: v.tensor_sub(st["qpos"][:], st["qpos"][:], st["qneg"][:]))
        nv = ctx.cnt["v"]
        ctx.wait("g", "v", nv)
        stoks.append(ctx.op("g", lambda g, dout=dout: g.dma_start(
            out=dout, in_=st["qpos"][:]), dma=True))
    for tk in stoks:
        ctx.wait_tok("g", tk)
    ctx.op("g", lambda g: g.collective_compute(
        "AllGather", ALU.bypass, replica_groups=[[0, 1, 2, 3, 4, 5, 6, 7]],
        ins=[st["wq1in"].rearrange("(o p) f -> o (p f)", o=1)],
        outs=[st["gwq1"]]), sem="cc")
    ctx.op("g", lambda g: g.collective_compute(
        "AllGather", ALU.bypass, replica_groups=[[0, 1, 2, 3, 4, 5, 6, 7]],
        ins=[st["wq2in"].rearrange("(o f) d -> o (f d)", o=1)],
        outs=[st["gwq2"]]), sem="cc")




def decl_C(nc, st):
    dram_in(nc, st, "x1T", [D, TOK])
    dram_in(nc, st, "b1f", [128, 32])
    dram_in(nc, st, "b2f", [128, 8])
    dram_out(nc, st, "x2T", [D, TOK])

    NT = 256
    st["NT"] = NT
    sb(nc, st, "wq1", [128, DC, 4096], BF16)
    sb(nc, st, "wq2", [128, 32, 1024], BF16)
    sb(nc, st, "x1bf", [128, DC, NT], BF16)
    sb(nc, st, "x1f", [128, DC, NT])
    sb(nc, st, "h", [128, 32, NT], BF16)
    sb(nc, st, "z", [128, DC, NT])
    sb(nc, st, "b1t", [128, 32])
    sb(nc, st, "b2t", [128, 8])
    sb(nc, st, "ones_bf", [128, 1], BF16)
    sb(nc, st, "ones_f1", [1, 128])
    sb(nc, st, "c_eps", [1, 1])
    sb(nc, st, "ln_zbf", [128, DC, NT], BF16)
    sb(nc, st, "ln_sq", [128, DC, NT], BF16)
    sb(nc, st, "ln_rows", [1, 4 * NT])
    ps(nc, st, "pa", [128, NT])
    ps(nc, st, "pb", [128, NT])
    ps(nc, st, "pc", [128, NT])
    ps(nc, st, "pdx", [128, NT])
    ps(nc, st, "ln_s1", [1, NT])
    ps(nc, st, "ln_s2", [1, NT])
    ps(nc, st, "ln_bca", [128, NT])
    ps(nc, st, "ln_bcb", [128, NT])


def body_C(ctx, st):
    NT = st["NT"]
    NTC = TOK // NT
    load_consts(ctx, st, {"c_eps": 1e-5})
    ctx.op("g", lambda g: g.memset(st["ones_bf"][:], 1.0))
    ctx.op("g", lambda g: g.memset(st["ones_f1"][:], 1.0))
    ctx.op("y", lambda y: y.dma_start(out=st["b1t"][:], in_=st["b1f"]), dma=True)
    ctx.op("y", lambda y: y.dma_start(out=st["b2t"][:], in_=st["b2f"]), dma=True)
    ctx.barrier("v", "s", "t", "g", "y")
    w1tok = ctx.op("g", lambda g: g.dma_start(
        out=st["wq1"][:], in_=st["gwq1"].rearrange("c (p f) -> p c f", p=128)), dma=True)
    w2tok = ctx.op("g", lambda g: g.dma_start(
        out=st["wq2"].ap().rearrange("p (r f) d2 -> p r f d2", r=8),
        in_=st["gwq2"].rearrange("r (f p d2) -> p r f d2", p=128, d2=D)), dma=True)

    evac_h = [None, None]
    evac_z = [None, None]
    ztok = None
    for tc in range(NTC):
        tsl = slice(tc * NT, (tc + 1) * NT)
        ctx.wait("y", "v")
        xtok = ctx.op("y", lambda y, tsl=tsl: y.dma_start(
            out=st["x1f"][:], in_=st["x1T"].rearrange("(c p) t -> p c t", p=128)[:, :, tsl]),
            dma=True)
        ctx.wait_tok("v", xtok)
        ctx.wait("v", "t")
        for c in range(DC):
            ctx.op("v", lambda v, c=c: v.tensor_copy(st["x1bf"][:, c, :], st["x1f"][:, c, :]))
        nxbf = ctx.cnt["v"]
        ctx.wait("t", "v", nxbf)
        if tc == 0:
            ctx.wait_tok("t", w1tok)
        for fc in range(32):
            bk = fc % 2
            pgb = st["pa"] if bk == 0 else st["pb"]
            if evac_h[bk] is not None:
                ctx.wait("t", "s", evac_h[bk])
            for c in range(DC):
                ctx.op("t", lambda t, c=c, fc=fc, pgb=pgb: t.matmul(
                    pgb[:], st["wq1"][:, c, fc * 128:(fc + 1) * 128],
                    st["x1bf"][:, c, :], start=(c == 0), stop=(c == DC - 1)),
                    inc=(c == DC - 1))
            nm = ctx.cnt["t"]
            ctx.wait("s", "t", nm)
            ctx.op("s", lambda s, fc=fc, pgb=pgb: s.activation(
                st["h"][:, fc, :], pgb[:], AF.Relu, bias=st["b1t"][:, fc:fc + 1]))
            evac_h[bk] = ctx.cnt["s"]
        for dc in range(DC):
            bk = dc % 2
            pgb = st["pc"] if bk == 0 else st["pdx"]
            if evac_z[bk] is not None:
                ctx.wait("t", "s", evac_z[bk])
            if dc == 0:
                ctx.wait("t", "s", evac_h[0])
                ctx.wait("t", "s", evac_h[1])
                if tc == 0:
                    ctx.wait_tok("t", w2tok)
                if ztok is not None:
                    ctx.wait_tok("s", ztok)
            for fc in range(32):
                ctx.op("t", lambda t, dc=dc, fc=fc, pgb=pgb: t.matmul(
                    pgb[:], st["wq2"][:, fc, dc * 128:(dc + 1) * 128],
                    st["h"][:, fc, :], start=(fc == 0), stop=(fc == 31)),
                    inc=(fc == 31))
            nm = ctx.cnt["t"]
            ctx.wait("s", "t", nm)
            ctx.op("s", lambda s, dc=dc, pgb=pgb: s.activation(
                st["z"][:, dc, :], pgb[:], AF.Identity, bias=st["b2t"][:, dc:dc + 1]))
            evac_z[bk] = ctx.cnt["s"]
        ctx.wait("v", "s")
        for dc in range(DC):
            ctx.op("v", lambda v, dc=dc: v.tensor_add(st["z"][:, dc, :], st["z"][:, dc, :],
                                                      st["x1f"][:, dc, :]))
        ctx.wait("t", "v")
        emit_ln(ctx, st, st["z"], st["z"], NT, st["c_eps"], "ln")
        nv = ctx.cnt["v"]
        ctx.wait("y", "v", nv)
        ztok = ctx.op("y", lambda y, tsl=tsl: y.dma_start(
            out=st["x2T"].rearrange("(c p) t -> p c t", p=128)[:, :, tsl],
            in_=st["z"][:]), dma=True)
        evac_h = [None, None]
        evac_z = [None, None]


def build_C():
    return build_program("C", decl_C, body_C)





def decl_A(nc, st):
    dram_in(nc, st, "xT", [D, TOK])
    dram_in(nc, st, "Wvb", [D, 2176])
    dram_in(nc, st, "bvbf", [128, 1])
    dram_in(nc, st, "bvrow", [1, 2048])
    dram_in(nc, st, "gb", [128, 4])
    dram_out(nc, st, "vblk", [16, TOK, 128])
    dram_out(nc, st, "qf", [128, TOK])
    dram_out(nc, st, "kf", [128, TOK])

    sb(nc, st, "xbf", [128, DC, TOK], BF16)
    sb(nc, st, "wvb", [128, DC, 2176], BF16)
    sb(nc, st, "bcv", [128, 2048])
    sb(nc, st, "bvb", [128, 1])
    sb(nc, st, "bvr", [1, 2048])
    sb(nc, st, "gbt", [128, 4])
    sb(nc, st, "vt", [128, 512], BF16)
    sb(nc, st, "vz", [128, 512])
    sb(nc, st, "baset", [128, TOK])
    sb(nc, st, "qt", [128, TOK])
    sb(nc, st, "kt", [128, TOK])
    sb(nc, st, "ones_f1", [1, 128])
    ps(nc, st, "pv", [128, 512])
    ps(nc, st, "pb", [128, 512])
    ps(nc, st, "pbc", [128, 512])


def body_A(ctx, st):
    ctx.op("g", lambda g: g.memset(st["ones_f1"][:], 1.0))
    ctx.op("y", lambda y: y.dma_start(out=st["bvb"][:], in_=st["bvbf"]), dma=True)
    ctx.op("y", lambda y: y.dma_start(out=st["bvr"][:], in_=st["bvrow"]), dma=True)
    ctx.op("y", lambda y: y.dma_start(out=st["gbt"][:], in_=st["gb"]), dma=True)
    ctx.op("g", lambda g: g.dma_start(
        out=st["xbf"][:], in_=st["xT"].rearrange("(c p) t -> p c t", p=128)), dma=True)
    ctx.op("g", lambda g: g.dma_start(
        out=st["wvb"][:], in_=st["Wvb"].rearrange("(c p) f -> p c f", p=128)), dma=True)
    ctx.barrier("v", "s", "t", "g", "y")
    ctx.wait("v", "gd")
    ctx.wait("t", "gd")

    for ec in range(4):
        ctx.wait("t", "v")
        ctx.op("t", lambda t, ec=ec: t.matmul(
            st["pbc"][:], st["ones_f1"][:], st["bvr"][:, ec * 512:(ec + 1) * 512],
            start=True, stop=True))
        ctx.wait("v", "t")
        ctx.op("v", lambda v, ec=ec: v.tensor_copy(st["bcv"][:, ec * 512:(ec + 1) * 512],
                                                   st["pbc"][:]))
    ctx.wait("t", "v")

    for tk in range(16):
        for ec in range(4):
            for c in range(DC):
                ctx.op("t", lambda t, tk=tk, ec=ec, c=c: t.matmul(
                    st["pv"][:], st["xbf"][:, c, tk * 128:(tk + 1) * 128],
                    st["wvb"][:, c, ec * 512:(ec + 1) * 512],
                    start=(c == 0), stop=(c == DC - 1)), inc=(c == DC - 1))
            nm = ctx.cnt["t"]
            ctx.wait("v", "t", nm)
            ctx.wait("v", "s")
            ctx.op("v", lambda v, ec=ec: v.tensor_add(st["vz"][:], st["pv"][:],
                                                      st["bcv"][:, ec * 512:(ec + 1) * 512]))
            ctx.wait("s", "v")
            ctx.op("s", lambda s: s.activation(st["vt"][:], st["vz"][:], AF.Silu))
            nsi = ctx.cnt["s"]
            ctx.wait("g", "s", nsi)
            ctx.op("g", lambda g, tk=tk, ec=ec: g.dma_start(
                out=st["vblk"][ec * 4:(ec + 1) * 4, tk * 128:(tk + 1) * 128, :].rearrange(
                    "s p e -> p s e"),
                in_=st["vt"].ap().rearrange("p (s e) -> p s e", e=128)), dma=True)
            ctx.wait("t", "v")
            ctx.wait("s", "gd")
    for tc in range(4):
        tsl = slice(tc * 512, (tc + 1) * 512)
        for c in range(DC):
            ctx.op("t", lambda t, tc=tc, c=c: t.matmul(
                st["pb"][:], st["wvb"][:, c, 2048:2176],
                st["xbf"][:, c, tsl], start=(c == 0), stop=(c == DC - 1)),
                inc=(c == DC - 1))
        nm = ctx.cnt["t"]
        ctx.wait("s", "t", nm)
        ctx.op("s", lambda s, tsl=tsl: s.activation(
            st["baset"][:, tsl], st["pb"][:], AF.Silu, bias=st["bvb"][:]))
        ctx.wait("t", "s")
    ctx.wait("v", "s")
    ctx.op("v", lambda v: v.tensor_scalar(st["qt"][:], st["baset"][:], st["gbt"][:, 0:1],
                                          st["gbt"][:, 1:2], ALU.mult, ALU.add))
    ctx.op("v", lambda v: v.tensor_scalar(st["kt"][:], st["baset"][:], st["gbt"][:, 2:3],
                                          st["gbt"][:, 3:4], ALU.mult, ALU.add))
    nv = ctx.cnt["v"]
    ctx.wait("y", "v", nv)
    ctx.wait("g", "v", nv)
    ctx.op("y", lambda y: y.dma_start(out=st["qf"], in_=st["qt"][:]), dma=True)
    ctx.op("g", lambda g: g.dma_start(out=st["kf"], in_=st["kt"][:]), dma=True)


def build_A():
    return build_program("A", decl_A, body_A)



NQ = 256

def decl_B(nc, st):
    dram_in(nc, st, "xT", [D, TOK])
    dram_in(nc, st, "Wu", [D, 2048])
    dram_in(nc, st, "buf", [128, 16])
    dram_in(nc, st, "qf", [128, TOK])
    if "kb4" not in st:
        dram_in(nc, st, "kb", [128, S])
        dram_in(nc, st, "vblkb", [16, S, 128])
    dram_in(nc, st, "Wo", [2048, D])
    dram_in(nc, st, "bof", [128, 8])
    dram_out(nc, st, "x1T", [D, TOK])

    sb(nc, st, "xbf", [128, DC, TOK], BF16)
    sb(nc, st, "kbf", [128, S], BF16)
    sb(nc, st, "qbf", [128, TOK], BF16)
    sb(nc, st, "wstr", [128, 16, 1024], BF16)
    sb(nc, st, "uc", [128, 16, NQ], BF16)
    sb(nc, st, "sqT", [128, 64, NQ], BF16)
    sb(nc, st, "gc", [128, 16, NQ], BF16)
    sb(nc, st, "sqt", [128, NQ], BF16)
    sb(nc, st, "z", [128, DC, NQ])
    sb(nc, st, "x1f", [128, DC, NQ])
    sb(nc, st, "but", [128, 16])
    sb(nc, st, "bot", [128, 8])
    sb(nc, st, "ones_bf", [128, 1], BF16)
    sb(nc, st, "ones_f1", [1, 128])
    sb(nc, st, "c_eps", [1, 1])
    sb(nc, st, "ln_zbf", [128, DC, NQ], BF16)
    sb(nc, st, "ln_sq", [128, DC, NQ], BF16)
    sb(nc, st, "ln_rows", [1, 4 * NQ])
    ps(nc, st, "ln_s1", [1, NQ])
    ps(nc, st, "ln_s2", [1, NQ])
    ps(nc, st, "ln_bca", [128, NQ])
    ps(nc, st, "ln_bcb", [128, NQ])
    ps(nc, st, "pu", [128, NQ])
    ps(nc, st, "psc", [128, NQ])
    ps(nc, st, "pav", [128, NQ])
    ps(nc, st, "po", [128, NQ])


def body_B(ctx, st):
    load_consts(ctx, st, {"c_eps": 1e-5})
    ctx.op("g", lambda g: g.memset(st["ones_bf"][:], 1.0))
    ctx.op("g", lambda g: g.memset(st["ones_f1"][:], 1.0))
    ctx.op("y", lambda y: y.dma_start(out=st["but"][:], in_=st["buf"]), dma=True)
    ctx.op("y", lambda y: y.dma_start(out=st["bot"][:], in_=st["bof"]), dma=True)
    ctx.op("g", lambda g: g.dma_start(
        out=st["xbf"][:], in_=st["xT"].rearrange("(c p) t -> p c t", p=128)), dma=True)
    if "kb4" in st:
        for r in range(4):
            ctx.op("g", lambda g, r=r: g.dma_start(
                out=st["kbf"][:, r * TOK:(r + 1) * TOK], in_=st["kb4"][r]), dma=True)
    else:
        ctx.op("g", lambda g: g.dma_start(out=st["kbf"][:], in_=st["kb"]), dma=True)
    ctx.op("g", lambda g: g.dma_start(out=st["qbf"][:], in_=st["qf"]), dma=True)
    ctx.barrier("v", "s", "t", "g", "y")
    ctx.wait("v", "gd")
    ctx.wait("t", "gd")
    ctx.wait("s", "gd")

    for qc in range(TOK // NQ):
        qsl = slice(qc * NQ, (qc + 1) * NQ)
        ctx.barrier("v", "s", "t", "g", "y")
        ctx.op("g", lambda g: g.dma_start(
            out=st["wstr"].ap().rearrange("p a b -> p (a b)").rearrange("p (c f) -> p c f", c=DC),
            in_=st["Wu"].rearrange("(c p) f -> p c f", p=128)), dma=True)
        ctx.wait("t", "gd")
        wu = st["wstr"].ap().rearrange("p a b -> p (a b)").rearrange("p (c f) -> p c f", c=DC)
        for ec in range(16):
            for c in range(DC):
                ctx.op("t", lambda t, ec=ec, c=c, qsl=qsl, wu=wu: t.matmul(
                    st["pu"][:], wu[:, c, ec * 128:(ec + 1) * 128],
                    st["xbf"][:, c, qsl], start=(c == 0), stop=(c == DC - 1)),
                    inc=(c == DC - 1))
            nm = ctx.cnt["t"]
            ctx.wait("s", "t", nm)
            ctx.op("s", lambda s, ec=ec: s.activation(
                st["uc"][:, ec, :], st["pu"][:], AF.Silu, bias=st["but"][:, ec:ec + 1]))
            ctx.wait("t", "s")
        for kt in range(64):
            ctx.op("t", lambda t, kt=kt, qsl=qsl: t.matmul(
                st["psc"][:], st["kbf"][:, kt * 128:(kt + 1) * 128],
                st["qbf"][:, qsl], start=True, stop=True))
            nm = ctx.cnt["t"]
            ctx.wait("v", "t", nm)
            ctx.op("v", lambda v: v.tensor_scalar(st["sqt"][:], st["psc"][:],
                                                  0.0, 1.0 / S, ALU.max, ALU.mult))
            ctx.op("v", lambda v, kt=kt: v.tensor_mul(st["sqT"][:, kt, :], st["sqt"][:],
                                                      st["sqt"][:]))
            ctx.wait("t", "v")
        ctx.wait("g", "t")
        for ec in range(16):
            if "vb4" in st:
                for r in range(4):
                    ctx.op("g", lambda g, ec=ec, r=r: g.dma_start(
                        out=st["wstr"].ap().rearrange("p a b -> p (a b)")[:, r * 2048:(r + 1) * 2048].rearrange(
                            "p (kt e) -> p kt e", e=128),
                        in_=st["vb4"](r, ec)), dma=True)
            else:
                ctx.op("g", lambda g, ec=ec: g.dma_start(
                    out=st["wstr"].ap().rearrange("p a b -> p (a b)")[:, 0:8192].rearrange(
                        "p (kt e) -> p kt e", e=128),
                    in_=st["vblkb"][ec].rearrange("(kt p) e -> p kt e", p=128)), dma=True)
            ctx.wait("t", "gd")
            vt = st["wstr"].ap().rearrange("p a b -> p (a b)")[:, 0:8192].rearrange(
                "p (kt e) -> p kt e", e=128)
            for kt in range(64):
                ctx.op("t", lambda t, kt=kt, vt=vt: t.matmul(
                    st["pav"][:], vt[:, kt, :], st["sqT"][:, kt, :],
                    start=(kt == 0), stop=(kt == 63)), inc=(kt == 63))
            nm = ctx.cnt["t"]
            ctx.wait("v", "t", nm)
            ctx.op("v", lambda v, ec=ec: v.tensor_mul(st["gc"][:, ec, :], st["pav"][:],
                                                      st["uc"][:, ec, :]))
            nv = ctx.cnt["v"]
            ctx.wait("t", "v", nv)
            ctx.wait("g", "v", nv)
        ctx.op("g", lambda g: g.dma_start(
            out=st["wstr"][:], in_=st["Wo"].rearrange("(ec p) h -> p ec h", p=128)), dma=True)
        ctx.op("y", lambda y, qsl=qsl: y.dma_start(
            out=st["x1f"][:], in_=st["xT"].rearrange("(c p) t -> p c t", p=128)[:, :, qsl]),
            dma=True)
        ctx.wait("t", "gd")
        for dc in range(DC):
            for ec in range(16):
                ctx.op("t", lambda t, dc=dc, ec=ec: t.matmul(
                    st["po"][:], st["wstr"][:, ec, dc * 128:(dc + 1) * 128],
                    st["gc"][:, ec, :], start=(ec == 0), stop=(ec == 15)),
                    inc=(ec == 15))
            nm = ctx.cnt["t"]
            ctx.wait("s", "t", nm)
            ctx.op("s", lambda s, dc=dc: s.activation(
                st["z"][:, dc, :], st["po"][:], AF.Identity, bias=st["bot"][:, dc:dc + 1]))
            ctx.wait("t", "s")
        ctx.wait("v", "s")
        ctx.wait("v", "y")
        for dc in range(DC):
            ctx.op("v", lambda v, dc=dc: v.tensor_add(st["z"][:, dc, :], st["z"][:, dc, :],
                                                      st["x1f"][:, dc, :]))
        emit_ln(ctx, st, st["z"], st["z"], NQ, st["c_eps"], "ln")
        nv = ctx.cnt["v"]
        ctx.wait("y", "v", nv)
        ctx.op("y", lambda y, qsl=qsl: y.dma_start(
            out=st["x1T"].rearrange("(c p) t -> p c t", p=128)[:, :, qsl],
            in_=st["z"][:]), dma=True)


def build_B():
    return build_program("B", decl_B, body_B)





def decl_D1(nc, st):
    dram_in(nc, st, "x2T", [D, TOK])
    dram_in(nc, st, "nwf", [128, 8])
    dram_in(nc, st, "Wd", [D, D])
    dram_in(nc, st, "bdf", [128, 8])
    dram_in(nc, st, "WBC", [D, 32])
    dram_in(nc, st, "dskf", [128, 8])
    dram_out(nc, st, "deltaT", [D, TOK])
    dram_out(nc, st, "m1T", [D, TOK])
    dram_out(nc, st, "BpCT", [32, TOK])
    dram_out(nc, st, "yskipT", [D, TOK])

    NT = 512
    st["NT"] = NT
    sb(nc, st, "wd", [128, DC, D], BF16)
    sb(nc, st, "wbc", [128, DC, 32], BF16)
    sb(nc, st, "x2f", [128, DC, NT])
    sb(nc, st, "xwbf", [128, DC, NT], BF16)
    sb(nc, st, "sqb", [128, DC, NT], BF16)
    sb(nc, st, "brs", [128, NT])
    sb(nc, st, "zt", [128, NT])
    sb(nc, st, "abst", [128, NT])
    sb(nc, st, "relut", [128, NT])
    sb(nc, st, "spt", [128, NT])
    sb(nc, st, "dft", [128, NT])
    sb(nc, st, "sqd", [128, NT])
    sb(nc, st, "xnb", [128, NT], BF16)
    sb(nc, st, "dbo", [128, DC, NT])
    sb(nc, st, "m1o", [128, DC, NT])
    sb(nc, st, "yso", [128, DC, NT])
    sb(nc, st, "bpco", [32, NT])
    sb(nc, st, "rrow", [1, NT])
    sb(nc, st, "nwt", [128, 8])
    sb(nc, st, "bdt", [128, 8])
    sb(nc, st, "dskt", [128, 8])
    sb(nc, st, "ones_bf", [128, 1], BF16)
    sb(nc, st, "ones_f1", [1, 128])
    sb(nc, st, "c_eps6", [1, 1])
    ps(nc, st, "ps2", [1, NT])
    ps(nc, st, "pbc", [128, NT])
    ps(nc, st, "pdd", [128, NT])
    ps(nc, st, "pbp", [32, NT])


SC_C = (10.0 + 1e-4) / 2.0
SC_H = (10.0 - 1e-4) / 2.0


def body_D1(ctx, st):
    NT = st["NT"]
    load_consts(ctx, st, {"c_eps6": 1e-6})
    ctx.op("g", lambda g: g.memset(st["ones_bf"][:], 1.0))
    ctx.op("g", lambda g: g.memset(st["ones_f1"][:], 1.0))
    ctx.op("y", lambda y: y.dma_start(out=st["nwt"][:], in_=st["nwf"]), dma=True)
    ctx.op("y", lambda y: y.dma_start(out=st["bdt"][:], in_=st["bdf"]), dma=True)
    ctx.op("y", lambda y: y.dma_start(out=st["dskt"][:], in_=st["dskf"]), dma=True)
    ctx.op("g", lambda g: g.dma_start(
        out=st["wd"][:], in_=st["Wd"].rearrange("(c p) f -> p c f", p=128)), dma=True)
    ctx.op("g", lambda g: g.dma_start(
        out=st["wbc"][:], in_=st["WBC"].rearrange("(c p) f -> p c f", p=128)), dma=True)
    ctx.barrier("v", "s", "t", "g", "y")
    ctx.wait("t", "gd")

    for tc in range(TOK // NT):
        tsl = slice(tc * NT, (tc + 1) * NT)
        ctx.barrier("v", "s", "t", "g", "y")
        ctx.op("y", lambda y, tsl=tsl: y.dma_start(
            out=st["x2f"][:], in_=st["x2T"].rearrange("(c p) t -> p c t", p=128)[:, :, tsl]),
            dma=True)
        ctx.wait("v", "y")
        ctx.wait("s", "y")
        for c in range(DC):
            ctx.op("v", lambda v, c=c: v.tensor_scalar(
                st["xwbf"][:, c, :], st["x2f"][:, c, :], st["nwt"][:, c:c + 1], None, ALU.mult))
            ctx.op("s", lambda s, c=c: s.activation(st["sqb"][:, c, :], st["x2f"][:, c, :],
                                                    AF.Square))
        nv, ns = ctx.cnt["v"], ctx.cnt["s"]
        ctx.wait("t", "v", nv)
        ctx.wait("t", "s", ns)
        for c in range(DC):
            ctx.op("t", lambda t, c=c: t.matmul(st["ps2"][:], st["ones_bf"][:],
                                                st["sqb"][:, c, :],
                                                start=(c == 0), stop=(c == DC - 1)),
                   inc=(c == DC - 1))
        ctx.wait("s", "t")
        ctx.op("s", lambda s: s.activation(st["rrow"][:], st["ps2"][:], AF.Sqrt,
                                           scale=1.0 / D, bias=st["c_eps6"][:]))
        ctx.wait("v", "s")
        ctx.op("v", lambda v: v.reciprocal(st["rrow"][:], st["rrow"][:]))
        nv = ctx.cnt["v"]
        ctx.wait("t", "v", nv)
        ctx.op("t", lambda t: t.matmul(st["pbc"][:], st["ones_f1"][:], st["rrow"][:],
                                       start=True, stop=True))
        ctx.wait("v", "t")
        ctx.op("v", lambda v: v.tensor_copy(st["brs"][:], st["pbc"][:]))
        for c in range(DC):
            ctx.op("t", lambda t, c=c: t.matmul(st["pbp"][:], st["wbc"][:, c, :],
                                                st["xwbf"][:, c, :],
                                                start=(c == 0), stop=(c == DC - 1)),
                   inc=(c == DC - 1))
        ctx.wait("v", "t")
        ctx.op("v", lambda v: v.tensor_mul(st["bpco"][:], st["pbp"][:],
                                           st["brs"][0:32, :]))
        nv = ctx.cnt["v"]
        ctx.wait("g", "v", nv)
        ctx.op("g", lambda g, tsl=tsl: g.dma_start(
            out=st["BpCT"][:, tsl], in_=st["bpco"][:]), dma=True)
        for dc in range(DC):
            for c in range(DC):
                ctx.op("t", lambda t, dc=dc, c=c: t.matmul(
                    st["pdd"][:], st["wd"][:, c, dc * 128:(dc + 1) * 128],
                    st["xwbf"][:, c, :], start=(c == 0), stop=(c == DC - 1)),
                    inc=(c == DC - 1))
            nm = ctx.cnt["t"]
            ctx.wait("v", "t", nm)
            ctx.op("v", lambda v: v.tensor_mul(st["zt"][:], st["pdd"][:], st["brs"][:]))
            nv = ctx.cnt["v"]
            ctx.wait("t", "v", nv)
            ctx.wait("s", "v", nv)
            ctx.op("s", lambda s, dc=dc: s.activation(st["abst"][:], st["zt"][:], AF.Abs,
                                                      bias=st["bdt"][:, dc:dc + 1]))
            ctx.op("s", lambda s: s.activation(st["abst"][:], st["abst"][:], AF.Exp,
                                               scale=-1.0))
            ctx.op("s", lambda s: s.activation(st["abst"][:], st["abst"][:], AF.Ln,
                                               bias=1.0))
            ctx.op("v", lambda v, dc=dc: v.tensor_scalar(st["relut"][:], st["zt"][:],
                                                         st["bdt"][:, dc:dc + 1], 0.0,
                                                         ALU.add, ALU.max))
            ctx.wait("v", "s")
            ctx.op("v", lambda v: v.tensor_add(st["spt"][:], st["relut"][:], st["abst"][:]))
            ctx.wait("s", "v")
            ctx.op("s", lambda s: s.activation(st["dft"][:], st["spt"][:], AF.Tanh,
                                               scale=1.0 / SC_H, bias=st["c_negch"][:]))
            ctx.wait("v", "s")
            ctx.op("v", lambda v: v.tensor_scalar(st["dft"][:], st["dft"][:], SC_H, SC_C,
                                                  ALU.mult, ALU.add))
            ctx.op("v", lambda v, dc=dc: v.tensor_copy(st["dbo"][:, dc, :], st["dft"][:]))
            ctx.wait("s", "v")
            ctx.op("s", lambda s: s.activation(st["sqd"][:], st["dft"][:], AF.Sqrt))
            ctx.op("v", lambda v, dc=dc: v.tensor_mul(st["xnb"][:], st["xwbf"][:, dc, :],
                                                      st["brs"][:]))
            ctx.wait("v", "s")
            ctx.op("v", lambda v, dc=dc: v.tensor_mul(st["m1o"][:, dc, :], st["sqd"][:],
                                                      st["xnb"][:]))
            ctx.op("v", lambda v, dc=dc: v.tensor_scalar(st["yso"][:, dc, :], st["xnb"][:],
                                                         st["dskt"][:, dc:dc + 1], None,
                                                         ALU.mult))
        nv, ns = ctx.cnt["v"], ctx.cnt["s"]
        ctx.wait("g", "v", nv)
        for nm_, src in (("deltaT", "dbo"), ("m1T", "m1o"), ("yskipT", "yso")):
            ctx.op("g", lambda g, nm_=nm_, src=src, tsl=tsl: g.dma_start(
                out=st[nm_].rearrange("(c p) t -> p c t", p=128)[:, :, tsl],
                in_=st[src][:]), dma=True)


def decl_D1x(nc, st):
    decl_D1(nc, st)
    sb(nc, st, "c_negch", [128, 1])


def body_D1x(ctx, st):
    load_consts(ctx, st, {"c_negch": -SC_C / SC_H})
    body_D1(ctx, st)


def build_D1():
    return build_program("D1", decl_D1x, body_D1x)



def decl_D2(nc, st):
    if "dmx_out" not in st:
        dram_in(nc, st, "deltaB", [D, S])
        dram_in(nc, st, "m1B", [D, S])
        dram_in(nc, st, "Bp4", [4, S])
        dram_in(nc, st, "C4", [4, S])
    else:
        dram_in(nc, st, "bpcsel", [32, 8])
    dram_in(nc, st, "A4", [D, 4])
    if "dmx_out" not in st:
        dram_out(nc, st, "ypart", [D, S])

    TB = 2048
    st["TB"] = TB
    sb(nc, st, "dsl", [128, DC, TB], BF16)
    sb(nc, st, "m1sl", [128, DC, TB], BF16)
    sb(nc, st, "ya", [128, DC, TB], BF16)
    sb(nc, st, "rt", [128, 2, TB])
    sb(nc, st, "at", [128, 2, TB])
    sb(nc, st, "c_neg1", [128, 1])
    sb(nc, st, "bt", [128, 2, TB], BF16)
    sb(nc, st, "ht", [128, 2, TB])
    sb(nc, st, "m2t", [128, 2, TB], BF16)
    sb(nc, st, "mt", [128, 2, TB], BF16)
    sb(nc, st, "bcB", [128, TB], BF16)
    sb(nc, st, "bcC", [128, TB], BF16)
    sb(nc, st, "bp1", [1, TB])
    sb(nc, st, "c1", [1, TB])
    sb(nc, st, "selc", [32, 8])
    sb(nc, st, "bpc32", [32, TB])
    sb(nc, st, "e2", [128, DC, 4])
    sb(nc, st, "carry", [128, 32])
    sb(nc, st, "ones_f1", [1, 128])
    ps(nc, st, "pb1", [128, 512])
    ps(nc, st, "pc1", [1, 512])


def body_D2(ctx, st):
    TB = st["TB"]
    ctx.op("g", lambda g: g.memset(st["ones_f1"][:], 1.0))
    ctx.op("g", lambda g: g.memset(st["carry"][:], 0.0))
    ctx.op("g", lambda g: g.memset(st["c_neg1"][:], -1.0))
    if "dmx_out" in st:
        ctx.op("y", lambda y: y.dma_start(out=st["selc"][:], in_=st["bpcsel"]), dma=True)
    ctx.op("y", lambda y: y.dma_start(
        out=st["e2"][:], in_=st["A4"].rearrange("(c p) n -> p c n", p=128)), dma=True)
    ctx.wait("s", "y")
    for c in range(DC):
        ctx.op("s", lambda s, c=c: s.activation(st["e2"][:, c, :], st["e2"][:, c, :], AF.Exp))
    ctx.wait("v", "s")
    ctx.op("v", lambda v: v.tensor_scalar(
        st["e2"].ap().rearrange("p a b -> p (a b)"),
        st["e2"].ap().rearrange("p a b -> p (a b)"), 0.5, None, ALU.mult))
    ctx.barrier("v", "s", "t", "g", "y")

    for tb in range(S // TB):
        bsl = slice(tb * TB, (tb + 1) * TB)
        ctx.barrier("v", "s", "t", "g", "y")
        if "dmx_out" in st:
            ctx.op("g", lambda g, tb=tb: g.dma_start(
                out=st["dsl"][:],
                in_=st["dmx_out"][tb * 2080:tb * 2080 + 1024, :].rearrange(
                    "(c p) t -> p c t", p=128)), dma=True)
            ctx.op("g", lambda g, tb=tb: g.dma_start(
                out=st["m1sl"][:],
                in_=st["dmx_out"][tb * 2080 + 1024:tb * 2080 + 2048, :].rearrange(
                    "(c p) t -> p c t", p=128)), dma=True)
            ctx.op("g", lambda g, tb=tb: g.dma_start(
                out=st["bpc32"][:], in_=st["dmx_out"][tb * 2080 + 2048:tb * 2080 + 2080, :]),
                dma=True)
        else:
            ctx.op("g", lambda g, bsl=bsl: g.dma_start(
                out=st["dsl"][:], in_=st["deltaB"].rearrange("(c p) t -> p c t", p=128)[:, :, bsl]),
                dma=True)
            ctx.op("g", lambda g, bsl=bsl: g.dma_start(
                out=st["m1sl"][:], in_=st["m1B"].rearrange("(c p) t -> p c t", p=128)[:, :, bsl]),
                dma=True)
        ctx.wait("v", "gd")
        ctx.wait("s", "gd")
        ctx.wait("g", "gd")
        ctx.barrier("v", "s", "t", "g", "y")
        tiles = [(n, dc) for n in range(4) for dc in range(DC)]
        tok_r = {}
        tok_m2 = {}
        tok_ym = {}
        tok_yadd = {}
        tok_z = {}
        tok_scan = {}
        tok_bc = {}

        def emit_bc(n):
            if "dmx_out" in st:
                ctx.wait("t", "y")
                ctx.wait("t", "v")
                for half in range(TB // 512):
                    hsl = slice(half * 512, (half + 1) * 512)
                    ctx.op("t", lambda t, n=n, hsl=hsl: t.matmul(
                        st["pb1"][0:1, :], st["selc"][:, n:n + 1], st["bpc32"][:, hsl],
                        start=True, stop=True))
                    ctx.op("t", lambda t, n=n, hsl=hsl: t.matmul(
                        st["pc1"][0:1, :],
                        st["selc"][:, 4 + n:5 + n], st["bpc32"][:, hsl],
                        start=True, stop=True))
                    nm = ctx.cnt["t"]
                    ctx.wait("v", "t", nm)
                    ctx.op("v", lambda v, hsl=hsl: v.tensor_copy(st["bp1"][:, hsl], st["pb1"][0:1, :]))
                    ctx.op("v", lambda v, hsl=hsl: v.tensor_copy(st["c1"][:, hsl], st["pc1"][0:1, :]))
                    nv = ctx.cnt["v"]
                    ctx.wait("t", "v", nv)
            else:
                ctx.wait("y", "v")
                ctx.op("y", lambda y, n=n, bsl=bsl: y.dma_start(out=st["bp1"][:],
                                                                in_=st["Bp4"][n:n + 1, bsl]), dma=True)
                ctx.op("y", lambda y, n=n, bsl=bsl: y.dma_start(out=st["c1"][:],
                                                                in_=st["C4"][n:n + 1, bsl]), dma=True)
                ctx.wait("t", "y")
            for half in range(TB // 512):
                hsl = slice(half * 512, (half + 1) * 512)
                ctx.wait("t", "v")
                ctx.op("t", lambda t, hsl=hsl: t.matmul(
                    st["pb1"][:], st["ones_f1"][:], st["bp1"][:, hsl],
                    start=True, stop=True))
                ctx.wait("v", "t")
                ctx.op("v", lambda v, hsl=hsl: v.tensor_copy(st["bcB"][:, hsl], st["pb1"][:]))
                ctx.wait("t", "v")
                ctx.op("t", lambda t, hsl=hsl: t.matmul(
                    st["pb1"][:], st["ones_f1"][:], st["c1"][:, hsl],
                    start=True, stop=True))
                ctx.wait("v", "t")
                ctx.op("v", lambda v, hsl=hsl: v.tensor_copy(st["bcC"][:, hsl], st["pb1"][:]))

        def stage_z(i):
            n, dc = tiles[i]
            p = i % 2
            ctx.wait("g", "v", max(tok_bc[n], tok_scan.get(i - 2, 0)))
            ctx.op("g", lambda g, n=n, dc=dc, p=p: g.tensor_tensor(
                st["m2t"][:, p, :], st["m1sl"][:, dc, :], st["bcB"][:], ALU.mult))
            tok_m2[i] = ctx.cnt["g"]

        def stage_lnr(i):
            n, dc = tiles[i]
            p = i % 2
            if i >= 2:
                ctx.wait("s", "v", tok_scan[i - 2])
            ctx.op("s", lambda s, n=n, dc=dc, p=p: s.activation(
                st["rt"][:, p, :], st["dsl"][:, dc, :], AF.Ln,
                scale=st["e2"][:, dc, n:n + 1], bias=1.0))
            ctx.op("s", lambda s, p=p: s.activation(st["rt"][:, p, :], st["rt"][:, p, :],
                                                    AF.Exp, scale=-1.0))
            ctx.op("s", lambda s, p=p: s.activation(st["at"][:, p, :], st["rt"][:, p, :],
                                                    AF.Identity, scale=2.0,
                                                    bias=st["c_neg1"][:]))
            tok_r[i] = ctx.cnt["s"]

        def stage_v(i):
            n, dc = tiles[i]
            ct = n * 8 + dc
            p = i % 2
            ctx.wait("v", "s", tok_r[i])
            ctx.wait("v", "g", tok_m2[i])
            ctx.op("v", lambda v, p=p: v.tensor_mul(st["bt"][:, p, :], st["m2t"][:, p, :],
                                                    st["rt"][:, p, :]))
            ctx.op("v", lambda v, p=p, ct=ct: v.tensor_tensor_scan(
                st["ht"][:, p, :], st["at"][:, p, :], st["bt"][:, p, :],
                st["carry"][:, ct:ct + 1], ALU.mult, ALU.add))
            ctx.op("v", lambda v, p=p, ct=ct: v.tensor_copy(st["carry"][:, ct:ct + 1],
                                                            st["ht"][:, p, TB - 1:TB]))
            tok_scan[i] = ctx.cnt["v"]
            if i >= 2:
                ctx.wait("v", "g", tok_yadd.get(i - 2, 0))
            ctx.op("v", lambda v, p=p: v.tensor_mul(st["mt"][:, p, :], st["ht"][:, p, :],
                                                    st["bcC"][:]))
            tok_ym[i] = ctx.cnt["v"]

        def stage_yadd(i):
            n, dc = tiles[i]
            p = i % 2
            ctx.wait("g", "v", tok_ym[i])
            if n == 0:
                ctx.op("g", lambda g, dc=dc, p=p: g.tensor_copy(st["ya"][:, dc, :],
                                                                st["mt"][:, p, :]))
            else:
                ctx.op("g", lambda g, dc=dc, p=p: g.tensor_tensor(
                    st["ya"][:, dc, :], st["ya"][:, dc, :], st["mt"][:, p, :], ALU.add))
            tok_yadd[i] = ctx.cnt["g"]

        for n in range(4):
            if n > 0:
                ctx.wait("v", "g", tok_m2[n * 8 - 1])
                ctx.wait("t", "g", tok_m2[n * 8 - 1])
                ctx.wait("t", "v", tok_ym[n * 8 - 1])
            emit_bc(n)
            tok_bc[n] = ctx.cnt["v"]
            base = n * 8
            stage_z(base)
            stage_lnr(base)
            for dc in range(DC):
                i = base + dc
                if dc + 1 < DC:
                    stage_z(i + 1)
                    stage_lnr(i + 1)
                stage_v(i)
                stage_yadd(i)
        nv = ctx.cnt["v"]
        ctx.wait("g", "v", nv)
        if "dmx_out" in st:
            ctx.op("g", lambda g, tb=tb: g.dma_start(
                out=st["yp_in"][tb * 1024:(tb + 1) * 1024, :].rearrange(
                    "(c p) t -> p c t", p=128),
                in_=st["ya"][:]), dma=True)
        else:
            ctx.op("g", lambda g, bsl=bsl: g.dma_start(
                out=st["ypart"].rearrange("(c p) t -> p c t", p=128)[:, :, bsl],
                in_=st["ya"][:]), dma=True)
        ctx.wait("v", "gd")


def build_D2():
    return build_program("D2", decl_D2, body_D2)



def decl_D3(nc, st):
    dram_in(nc, st, "x2T", [D, TOK])
    if "D3_srcs" not in st:
        for i in range(4):
            dram_in(nc, st, f"yp{i}", [D, TOK])
    dram_in(nc, st, "yskipT", [D, TOK])
    dram_out(nc, st, "x3T", [D, TOK])

    NT = 512
    st["NT"] = NT
    sb(nc, st, "x2f", [128, DC, NT])
    sb(nc, st, "ypl", [128, 5, DC, NT])
    sb(nc, st, "z", [128, DC, NT])
    sb(nc, st, "ones_bf", [128, 1], BF16)
    sb(nc, st, "ones_f1", [1, 128])
    sb(nc, st, "c_eps", [1, 1])
    sb(nc, st, "ln_zbf", [128, DC, NT], BF16)
    sb(nc, st, "ln_sq", [128, DC, NT], BF16)
    sb(nc, st, "ln_rows", [1, 4 * NT])
    ps(nc, st, "ln_s1", [1, NT])
    ps(nc, st, "ln_s2", [1, NT])
    ps(nc, st, "ln_bca", [128, NT])
    ps(nc, st, "ln_bcb", [128, NT])


def body_D3(ctx, st):
    NT = st["NT"]
    load_consts(ctx, st, {"c_eps": 1e-5})
    ctx.op("g", lambda g: g.memset(st["ones_bf"][:], 1.0))
    ctx.op("g", lambda g: g.memset(st["ones_f1"][:], 1.0))
    for tc in range(TOK // NT):
        tsl = slice(tc * NT, (tc + 1) * NT)
        ctx.barrier("v", "s", "t", "g", "y")
        ctx.op("y", lambda y, tsl=tsl: y.dma_start(
            out=st["x2f"][:], in_=st["x2T"].rearrange("(c p) t -> p c t", p=128)[:, :, tsl]),
            dma=True)
        srcs = st.get("D3_srcs", ["yp0", "yp1", "yp2", "yp3", "yskipT"])
        for i, nm_ in enumerate(srcs):
            ctx.op("g", lambda g, i=i, nm_=nm_, tsl=tsl: g.dma_start(
                out=st["ypl"][:, i, :, :],
                in_=st[nm_].rearrange("(c p) t -> p c t", p=128)[:, :, tsl]), dma=True)
        ctx.wait("v", "y")
        ctx.wait("v", "gd")
        for dc in range(DC):
            ctx.op("v", lambda v, dc=dc: v.tensor_add(st["z"][:, dc, :], st["x2f"][:, dc, :],
                                                      st["ypl"][:, 0, dc, :]))
            for i in range(1, len(srcs)):
                ctx.op("v", lambda v, dc=dc, i=i: v.tensor_add(
                    st["z"][:, dc, :], st["z"][:, dc, :], st["ypl"][:, i, dc, :]))
        emit_ln(ctx, st, st["z"], st["z"], NT, st["c_eps"], "ln")
        nv = ctx.cnt["v"]
        ctx.wait("y", "v", nv)
        ctx.op("y", lambda y, tsl=tsl: y.dma_start(
            out=st["x3T"].rearrange("(c p) t -> p c t", p=128)[:, :, tsl],
            in_=st["z"][:]), dma=True)


def build_D3():
    return build_program("D3", decl_D3, body_D3)


def stage_D1(x2T_cores, inp):
    nwf = np.ascontiguousarray(inp["L1_attn_norm_weight"].reshape(8, 128).T)
    bdf = np.ascontiguousarray(inp["L1_attn_b_delta"].reshape(8, 128).T)
    dskf = np.ascontiguousarray(inp["L1_attn_D_skip"].reshape(8, 128).T)
    WBC = np.ascontiguousarray(np.concatenate([inp["L1_attn_W_B"], inp["L1_attn_W_C"]], axis=1))
    Wd = np.ascontiguousarray(inp["L1_attn_W_delta"])
    return [{"x2T": x2T_cores[i], "nwf": nwf, "Wd": Wd, "bdf": bdf, "WBC": WBC,
             "dskf": dskf} for i in range(NCORE)]


def stage_D2(resD1, inp):
    maps = []
    A = np.ascontiguousarray(inp["L1_attn_A_log"])
    for i in range(NCORE):
        b, nq = i // 4, i % 4
        grp = [4 * b + j for j in range(4)]
        deltaB = np.concatenate([np.asarray(resD1[j]["deltaT"]) for j in grp], axis=1)
        m1B = np.concatenate([np.asarray(resD1[j]["m1T"]) for j in grp], axis=1)
        BpC = np.concatenate([np.asarray(resD1[j]["BpCT"]) for j in grp], axis=1)
        Bp4 = np.ascontiguousarray(BpC[nq * 4:(nq + 1) * 4, :])
        C4 = np.ascontiguousarray(BpC[16 + nq * 4:16 + (nq + 1) * 4, :])
        A4 = np.ascontiguousarray(A[:, nq * 4:(nq + 1) * 4])
        maps.append({"deltaB": np.ascontiguousarray(deltaB),
                     "m1B": np.ascontiguousarray(m1B),
                     "Bp4": Bp4, "C4": C4, "A4": A4})
    return maps


def stage_D3(x2T_cores, resD1, resD2):
    maps = []
    for i in range(NCORE):
        b, qd = i // 4, i % 4
        tsl = slice(qd * TOK, (qd + 1) * TOK)
        m = {"x2T": x2T_cores[i], "yskipT": np.asarray(resD1[i]["yskipT"])}
        for nq in range(4):
            src = np.asarray(resD2[4 * b + nq]["ypart"])[:, tsl]
            m[f"yp{nq}"] = np.ascontiguousarray(src)
        maps.append(m)
    return maps





def decl_E1(nc, st):
    dram_in(nc, st, "x3T", [D, TOK])
    dram_in(nc, st, "gwT", [D, 8])
    dram_out(nc, st, "masko", [16, 128, 8])
    dram_out(nc, st, "combpre", [16, 128, 8])

    sb(nc, st, "x3f32", [128, DC, TOK])
    sb(nc, st, "gwf", [128, DC, 8])
    sb(nc, st, "lg", [128, 8])
    sb(nc, st, "m8", [128, 8])
    sb(nc, st, "negm", [128, 1])
    sb(nc, st, "et", [128, 8])
    sb(nc, st, "srow", [128, 1])
    sb(nc, st, "mask", [128, 8])
    sb(nc, st, "pm", [128, 8])
    sb(nc, st, "cp", [128, 8])
    sb(nc, st, "c_tiny", [128, 1])
    ps(nc, st, "plg", [128, 8])


def body_E1(ctx, st):
    load_consts(ctx, st, {"c_tiny": 1e-20})
    ctx.op("y", lambda y: y.dma_start(
        out=st["x3f32"][:], in_=st["x3T"].rearrange("(c p) t -> p c t", p=128)), dma=True)
    ctx.op("y", lambda y: y.dma_start(
        out=st["gwf"][:], in_=st["gwT"].rearrange("(c p) e -> p c e", p=128)), dma=True)
    ctx.barrier("v", "s", "t", "g", "y")
    for tt in range(16):
        tl = slice(tt * 128, (tt + 1) * 128)
        ctx.wait("t", "v")
        for c in range(DC):
            ctx.op("t", lambda t, c=c, tl=tl: t.matmul(
                st["plg"][:], st["x3f32"][:, c, tl], st["gwf"][:, c, :],
                start=(c == 0), stop=(c == DC - 1)), inc=(c == DC - 1))
        nm = ctx.cnt["t"]
        ctx.wait("v", "t", nm)
        ctx.op("v", lambda v: v.tensor_copy(st["lg"][:], st["plg"][:]))
        ctx.op("v", lambda v: v.max(out=st["m8"][:], in_=st["lg"][:]))
        ctx.op("v", lambda v: v.tensor_scalar(st["negm"][:], st["m8"][:, 0:1], -1.0, None,
                                              ALU.mult))
        ctx.wait("s", "v")
        ctx.op("s", lambda s: s.activation(st["et"][:], st["lg"][:], AF.Exp,
                                           bias=st["negm"][:]))
        ctx.wait("v", "s")
        ctx.op("v", lambda v: v.tensor_reduce(st["srow"][:], st["et"][:], AX.X, ALU.add))
        ctx.op("v", lambda v: v.reciprocal(st["srow"][:], st["srow"][:]))
        ctx.op("v", lambda v: v.tensor_scalar(st["mask"][:], st["lg"][:], st["m8"][:, 1:2],
                                              None, ALU.is_ge))
        ctx.op("v", lambda v: v.tensor_mul(st["pm"][:], st["et"][:], st["mask"][:]))
        ctx.op("v", lambda v: v.tensor_scalar(st["pm"][:], st["pm"][:], st["srow"][:], None,
                                              ALU.mult))
        ctx.op("v", lambda v: v.tensor_reduce(st["srow"][:], st["pm"][:], AX.X, ALU.add))
        ctx.op("v", lambda v: v.tensor_scalar(st["srow"][:], st["srow"][:], st["c_tiny"][:],
                                              None, ALU.add))
        ctx.op("v", lambda v: v.reciprocal(st["srow"][:], st["srow"][:]))
        ctx.op("v", lambda v: v.tensor_scalar(st["cp"][:], st["pm"][:], st["srow"][:], None,
                                              ALU.mult))
        nv = ctx.cnt["v"]
        ctx.wait("y", "v", nv)
        ctx.op("y", lambda y, tt=tt: y.dma_start(out=st["masko"][tt], in_=st["mask"][:]),
               dma=True)
        ctx.op("y", lambda y, tt=tt: y.dma_start(out=st["combpre"][tt], in_=st["cp"][:]),
               dma=True)
        ctx.wait("v", "y")


def build_E1():
    return build_program("E1", decl_E1, body_E1)



def decl_E2(nc, st):
    dram_in(nc, st, "x3T", [D, TOK])
    dram_in(nc, st, "combpre", [16, 128, 8])
    dram_in(nc, st, "maskown", [16, 128, 8])
    if "maskg" not in st:
        dram_in(nc, st, "maskprev", [112, 128, 8])
    else:
        dram_in(nc, st, "rowsel", [128, 1])
    dram_in(nc, st, "wgT", [8, D, 1024])
    dram_in(nc, st, "wuT", [8, D, 1024])
    dram_in(nc, st, "wdT", [8, 1024, D])
    dram_in(nc, st, "wsgT", [D, 2048])
    dram_in(nc, st, "wsuT", [D, 2048])
    dram_in(nc, st, "wsdT", [2048, D])
    dram_out(nc, st, "x4T", [D, TOK])
    st["combd"] = nc.dram_tensor("combd", [8, TOK], F32, kind="Internal").ap()

    NT = 512
    st["NT"] = NT
    if "webf2" not in st:
        st["webf2"] = nc.dram_tensor("webf2", [10, 3, 1024, 1024], BF16, kind="Internal").ap()
        st["E2_precast"] = True
    sb(nc, st, "x3bf", [128, DC, TOK], BF16)
    sb(nc, st, "acte", [128, 8, NT], BF16)
    sb(nc, st, "warena", [128, 2, 3, 8, 1024], BF16)
    sb(nc, st, "ysb", [128, DC, NT])
    sb(nc, st, "x3f", [128, DC, NT])
    if "maskg" in st:
        sb(nc, st, "mpre", [128, 128, 8], BF16)
        sb(nc, st, "rst", [128, 1])
    else:
        sb(nc, st, "mpre", [128, 112, 8], BF16)
    sb(nc, st, "mtile", [128, 8], BF16)
    sb(nc, st, "prow", [128, 8])
    sb(nc, st, "post", [128, 8])
    sb(nc, st, "cpre", [128, 8])
    sb(nc, st, "combm", [128, 16, 8])
    sb(nc, st, "ctile", [8, 128])
    sb(nc, st, "combrow", [1, NT])
    sb(nc, st, "onesrow", [1, NT])
    sb(nc, st, "offs", [1, 8])
    sb(nc, st, "trow", [1, 8])
    sb(nc, st, "tri", [128, 128], BF16)
    sb(nc, st, "trif", [128, 128])
    sb(nc, st, "idf", [128, 128])
    sb(nc, st, "sgt", [128, 2, NT], BF16)
    sb(nc, st, "ones_bf", [128, 1], BF16)
    sb(nc, st, "ones_f1", [1, 128])
    sb(nc, st, "ones_c", [128, 1])
    sb(nc, st, "c_eps", [1, 1])
    sb(nc, st, "ln_zbf", [128, DC, NT], BF16)
    sb(nc, st, "ln_sq", [128, DC, NT], BF16)
    sb(nc, st, "ln_rows", [1, 4 * NT])
    ps(nc, st, "pg", [128, NT])
    ps(nc, st, "pg2", [128, NT])
    ps(nc, st, "pu2", [128, NT])
    ps(nc, st, "pu2b", [128, NT])
    ps(nc, st, "pd", [128, NT])
    ps(nc, st, "pd2", [128, NT])
    ps(nc, st, "pbc2", [128, NT])
    st["ln_s1"] = st["pg"][0:1, :]
    st["ln_s2"] = st["pg2"][0:1, :]
    st["ln_bca"] = st["pu2"][:]
    st["ln_bcb"] = st["pu2b"][:]


def emit_precast_E2(ctx, st):
    w2 = st["webf2"]
    for e in range(8):
        for j, src in ((0, st["wgT"]), (1, st["wuT"]), (2, st["wdT"])):
            ctx.op("g", lambda g, e=e, j=j, src=src: g.dma_start(
                out=w2[e, j], in_=src[e]), dma=True)
    for es in range(2):
        csl = slice(es * 1024, (es + 1) * 1024)
        ctx.op("g", lambda g, csl=csl, es=es: g.dma_start(
            out=w2[8 + es, 0], in_=st["wsgT"][:, csl]), dma=True)
        ctx.op("g", lambda g, csl=csl, es=es: g.dma_start(
            out=w2[8 + es, 1], in_=st["wsuT"][:, csl]), dma=True)
        ctx.op("g", lambda g, csl=csl, es=es: g.dma_start(
            out=w2[8 + es, 2], in_=st["wsdT"][csl, :]), dma=True)


def body_E2(ctx, st):
    NT = st["NT"]
    load_consts(ctx, st, {"c_eps": 1e-5})
    ctx.op("g", lambda g: g.memset(st["ones_bf"][:], 1.0))
    ctx.op("g", lambda g: g.memset(st["ones_f1"][:], 1.0))
    ctx.op("g", lambda g: g.memset(st["ones_c"][:], 1.0))
    ctx.op("g", lambda g: g.dma_start(
        out=st["x3bf"][:], in_=st["x3T"].rearrange("(c p) t -> p c t", p=128)), dma=True)
    if "maskg" in st:
        ctx.op("g", lambda g: g.dma_start(
            out=st["mpre"][:, 0:128, :],
            in_=st["maskg"].rearrange("(a p) e -> a p e", p=128)), dma=True)
        ctx.op("y", lambda y: y.dma_start(out=st["rst"][:], in_=st["rowsel"]), dma=True)
    else:
        ctx.op("g", lambda g: g.dma_start(
            out=st["mpre"][:], in_=st["maskprev"].rearrange("a p e -> p a e")), dma=True)
    ctx.op("g", lambda g: g.memset(st["trif"][:], 1.0))
    ctx.op("g", lambda g: g.affine_select(st["trif"][:], st["trif"][:], pattern=[[1, 128]],
                                          compare_op=ALU.is_gt, fill=0.0, base=0,
                                          channel_multiplier=-1))
    ctx.op("g", lambda g: g.memset(st["idf"][:], 1.0))
    ctx.op("g", lambda g: g.affine_select(st["idf"][:], st["idf"][:], pattern=[[1, 128]],
                                          compare_op=ALU.is_equal, fill=0.0, base=0,
                                          channel_multiplier=-1))
    ctx.op("g", lambda g: g.dma_start(out=st["tri"][:], in_=st["trif"][:]), dma=True)
    ctx.barrier("v", "s", "t", "g", "y")
    ctx.wait("v", "gd")
    ctx.wait("t", "gd")

    if "maskg" in st:
        ctx.wait("v", "y")
        ctx.op("v", lambda v: v.tensor_scalar(
            st["mpre"].ap().rearrange("a p e -> a (p e)"),
            st["mpre"].ap().rearrange("a p e -> a (p e)"),
            st["rst"][:], None, ALU.mult))
        ctx.op("v", lambda v: v.tensor_reduce(
            st["prow"][:], st["mpre"].ap().rearrange("a p e -> a e p"),
            AX.X, ALU.add))
    else:
        ctx.op("v", lambda v: v.tensor_reduce(
            st["prow"][:], st["mpre"].ap().rearrange("p a e -> p e a"), AX.X, ALU.add))
    nv = ctx.cnt["v"]
    ctx.wait("t", "v", nv)
    ctx.op("t", lambda t: t.matmul(st["ln_s1"][:, 0:8], st["ones_c"][:], st["prow"][:],
                                   start=True, stop=True))
    ctx.wait("v", "t")
    ctx.op("v", lambda v: v.tensor_copy(st["offs"][:], st["ln_s1"][:, 0:8]))
    for tt in range(16):
        ctx.wait("y", "v")
        ctx.wait("g", "v")
        ctx.op("g", lambda g, tt=tt: g.dma_start(out=st["mtile"][:], in_=st["maskown"][tt]),
               dma=True)
        ctx.op("y", lambda y, tt=tt: y.dma_start(out=st["cpre"][:], in_=st["combpre"][tt]),
               dma=True)
        ctx.wait("t", "y")
        ctx.wait("t", "gd")
        ctx.wait("t", "v")
        ctx.op("t", lambda t: t.matmul(st["pg"][:, 0:8], st["tri"][:], st["mtile"][:],
                                       start=True, stop=True))
        ctx.op("t", lambda t: t.matmul(st["pbc2"][:, 0:8], st["ones_f1"][:], st["offs"][:],
                                       start=True, stop=True))
        nm = ctx.cnt["t"]
        ctx.wait("v", "t", nm)
        ctx.op("v", lambda v: v.tensor_copy(st["post"][:], st["pg"][:, 0:8]))
        ctx.op("v", lambda v: v.tensor_add(st["post"][:], st["post"][:], st["pbc2"][:, 0:8]))
        ctx.op("v", lambda v: v.tensor_scalar(st["post"][:], st["post"][:], float(CAP), None,
                                              ALU.is_lt))
        ctx.op("v", lambda v, tt=tt: v.tensor_mul(st["combm"][:, tt, :], st["cpre"][:],
                                                  st["post"][:]))
        ctx.op("t", lambda t: t.matmul(st["ln_s2"][:, 0:8], st["ones_bf"][:], st["mtile"][:],
                                       start=True, stop=True, skip_group_check=True))
        ctx.wait("v", "t")
        ctx.op("v", lambda v: v.tensor_copy(st["trow"][:], st["ln_s2"][:, 0:8]))
        ctx.op("v", lambda v: v.tensor_add(st["offs"][:], st["offs"][:], st["trow"][:]))
    for tt in range(16):
        ctx.wait("t", "v")
        ctx.op("t", lambda t, tt=tt: t.transpose(st["pbc2"][0:8, 0:128],
                                                 st["combm"][:, tt, :], st["idf"][:]))
        nm = ctx.cnt["t"]
        ctx.wait("v", "t", nm)
        ctx.op("v", lambda v: v.tensor_copy(st["ctile"][:], st["pbc2"][0:8, 0:128]))
        nv = ctx.cnt["v"]
        ctx.wait("y", "v", nv)
        ctx.op("y", lambda y, tt=tt: y.dma_start(
            out=st["combd"][:, tt * 128:(tt + 1) * 128], in_=st["ctile"][:]), dma=True)
        ctx.wait("v", "y")

    if st.get("E2_precast"):
        emit_precast_E2(ctx, st)
    w2 = st["webf2"]
    wa = st["warena"]

    def load_w(e, buf):
        toks = []
        for j in range(3):
            toks.append(ctx.op("g", lambda g, e=e, j=j, buf=buf: g.dma_start(
                out=wa[:, buf, j, :, :],
                in_=w2[e, j].rearrange("(c p) i -> p c i", p=128)), dma=True))
        return toks

    for tc in range(TOK // NT):
        tsl = slice(tc * NT, (tc + 1) * NT)
        ctx.barrier("v", "s", "t", "g", "y")
        ctx.op("y", lambda y, tsl=tsl: y.dma_start(
            out=st["x3f"][:], in_=st["x3T"].rearrange("(c p) t -> p c t", p=128)[:, :, tsl]),
            dma=True)
        ctx.op("v", lambda v: v.memset(st["onesrow"][:], 1.0))
        wtoks = {0: load_w(0, 0)}
        evac_gu = [None, None]
        evac_pd = [None, None]
        last_acte_v = None
        last_down_t = None
        for e in range(10):
            if e + 1 < 10:
                if e >= 1:
                    ctx.wait("g", "t")
                wtoks[e + 1] = load_w(e + 1, (e + 1) % 2)
            buf = e % 2
            for tok in wtoks[e]:
                ctx.wait_tok("t", tok)
            if e < 8:
                ctx.wait("y", "t")
                cr = ctx.op("y", lambda y, e=e, tsl=tsl: y.dma_start(
                    out=st["combrow"][:], in_=st["combd"][e:e + 1, tsl]), dma=True)
                ctx.wait_tok("t", cr)
                crow = st["combrow"]
            else:
                ctx.wait("t", "v")
                crow = st["onesrow"]
            if last_acte_v is not None:
                ctx.wait("t", "v", last_acte_v)
            ctx.op("t", lambda t, crow=crow: t.matmul(
                st["pbc2"][:], st["ones_f1"][:], crow[:], start=True, stop=True))
            for ic in range(8):
                bk = ic % 2
                pgb = st["pg"] if bk == 0 else st["pg2"]
                pub = st["pu2"] if bk == 0 else st["pu2b"]
                if evac_gu[bk] is not None:
                    ctx.wait("t", "v", evac_gu[bk])
                for c in range(DC):
                    ctx.op("t", lambda t, ic=ic, c=c, tsl=tsl, buf=buf, pgb=pgb: t.matmul(
                        pgb[:], wa[:, buf, 0, c, ic * 128:(ic + 1) * 128],
                        st["x3bf"][:, c, tsl], start=(c == 0), stop=(c == DC - 1)),
                        inc=(c == DC - 1))
                for c in range(DC):
                    ctx.op("t", lambda t, ic=ic, c=c, tsl=tsl, buf=buf, pub=pub: t.matmul(
                        pub[:], wa[:, buf, 1, c, ic * 128:(ic + 1) * 128],
                        st["x3bf"][:, c, tsl], start=(c == 0), stop=(c == DC - 1)),
                        inc=(c == DC - 1))
                nm = ctx.cnt["t"]
                ctx.wait("s", "t", nm)
                if evac_gu[bk] is not None:
                    ctx.wait("s", "v", evac_gu[bk])
                ctx.op("s", lambda s, pgb=pgb, bk=bk: s.activation(st["sgt"][:, bk, :],
                                                                   pgb[:], AF.Silu))
                ctx.wait("v", "s")
                if ic == 0 and last_down_t is not None:
                    ctx.wait("v", "t", last_down_t)
                ctx.op("v", lambda v, ic=ic, pub=pub, bk=bk: v.tensor_mul(
                    st["acte"][:, ic, :], st["sgt"][:, bk, :], pub[:]))
                ctx.op("v", lambda v, ic=ic: v.tensor_mul(
                    st["acte"][:, ic, :], st["acte"][:, ic, :], st["pbc2"][:]))
                evac_gu[bk] = ctx.cnt["v"]
            last_acte_v = ctx.cnt["v"]
            ctx.wait("t", "v", last_acte_v)
            for dc in range(DC):
                bk = dc % 2
                pdb = st["pd"] if bk == 0 else st["pd2"]
                if evac_pd[bk] is not None:
                    ctx.wait("t", "v", evac_pd[bk])
                for ic in range(8):
                    ctx.op("t", lambda t, ic=ic, dc=dc, buf=buf, pdb=pdb: t.matmul(
                        pdb[:], wa[:, buf, 2, ic, dc * 128:(dc + 1) * 128],
                        st["acte"][:, ic, :], start=(ic == 0), stop=(ic == 7)),
                        inc=(ic == 7))
                nm = ctx.cnt["t"]
                ctx.wait("v", "t", nm)
                if e == 0:
                    ctx.op("v", lambda v, dc=dc, pdb=pdb: v.tensor_copy(st["ysb"][:, dc, :],
                                                                        pdb[:]))
                else:
                    ctx.op("v", lambda v, dc=dc, pdb=pdb: v.tensor_add(
                        st["ysb"][:, dc, :], st["ysb"][:, dc, :], pdb[:]))
                evac_pd[bk] = ctx.cnt["v"]
            last_down_t = ctx.cnt["t"]
        ctx.wait("v", "y")
        for dc in range(DC):
            ctx.op("v", lambda v, dc=dc: v.tensor_add(st["ysb"][:, dc, :], st["ysb"][:, dc, :],
                                                      st["x3f"][:, dc, :]))
        ctx.wait("t", "v")
        emit_ln(ctx, st, st["ysb"], st["ysb"], NT, st["c_eps"], "ln")
        nv = ctx.cnt["v"]
        ctx.wait("y", "v", nv)
        ctx.op("y", lambda y, tsl=tsl: y.dma_start(
            out=st["x4T"].rearrange("(c p) t -> p c t", p=128)[:, :, tsl],
            in_=st["ysb"][:]), dma=True)


def build_E2():
    return build_program("E2", decl_E2, body_E2)


def stage_E1(x3T_cores, inp):
    gwT = np.ascontiguousarray(inp["L1_ffn_gate_w"].T)
    return [{"x3T": x3T_cores[i], "gwT": gwT} for i in range(NCORE)]


def stage_E2(x3T_cores, resE1, inp):
    wgT = np.ascontiguousarray(inp["L1_ffn_we_gate"].transpose(0, 2, 1))
    wuT = np.ascontiguousarray(inp["L1_ffn_we_up"].transpose(0, 2, 1))
    wdT = np.ascontiguousarray(inp["L1_ffn_we_down"].transpose(0, 2, 1))
    wsgT = np.ascontiguousarray(inp["L1_ffn_ws_gate"].T)
    wsuT = np.ascontiguousarray(inp["L1_ffn_ws_up"].T)
    wsdT = np.ascontiguousarray(inp["L1_ffn_ws_down"].T)
    maps = []
    for i in range(NCORE):
        mprev = np.zeros((112, 128, 8), np.float32)
        if i > 0:
            prev = np.concatenate([np.asarray(resE1[j]["masko"]) for j in range(i)], axis=0)
            mprev[:prev.shape[0]] = prev
        maps.append({"x3T": x3T_cores[i], "combpre": np.asarray(resE1[i]["combpre"]),
                     "maskown": np.asarray(resE1[i]["masko"]), "maskprev": mprev,
                     "wgT": wgT, "wuT": wuT, "wdT": wdT, "wsgT": wsgT, "wsuT": wsuT,
                     "wsdT": wsdT})
    return maps



def _fm(a):
    return np.ascontiguousarray(a.T)


def run_launch(nc, in_maps, outs=None):
    res = run_bass_kernel_spmd(nc, in_maps, core_ids=list(range(NCORE)))
    return res.results


def stage_C(x1T_cores, W1, b1, W2, b2):
    W1T = np.ascontiguousarray(W1.T)
    W2T = np.ascontiguousarray(W2.T)
    b1f = np.ascontiguousarray(b1.reshape(32, 128).T)
    b2f = np.ascontiguousarray(b2.reshape(8, 128).T)
    return [{"x1T": x1T_cores[i], "W1T": W1T, "b1f": b1f, "W2T": W2T, "b2f": b2f}
            for i in range(NCORE)]


_NC_CACHE = {}


def _get_nc(name, builder):
    if name not in _NC_CACHE:
        _NC_CACHE[name] = builder()
    return _NC_CACHE[name]


def kernel(**inputs):
    return kernel_fused(**inputs)


def kernel_multilaunch(**inputs):
    inp = inputs
    x = np.ascontiguousarray(inp["L0_attn_inputs"], dtype=np.float32)
    xT_cores = []
    for c in range(NCORE):
        b, qd = c // 4, c % 4
        xT_cores.append(_fm(x[b, qd * TOK:(qd + 1) * TOK, :]))

    ncA = _get_nc("A", build_A)
    resA = run_launch(ncA, stage_A(xT_cores, inp["L0_attn_W_uv"], inp["L0_attn_b_uv"],
                                   inp["L0_attn_gamma"], inp["L0_attn_beta"]))
    ncB = _get_nc("B", build_B)
    resB = run_launch(ncB, stage_B(xT_cores, resA, inp["L0_attn_W_uv"], inp["L0_attn_b_uv"],
                                   inp["L0_attn_W_out"], inp["L0_attn_b_out"]))
    x1T_cores = [np.ascontiguousarray(np.asarray(resB[i]["x1T"])) for i in range(NCORE)]

    ncC = _get_nc("C", build_C)
    resC = run_launch(ncC, stage_C(x1T_cores, inp["L0_ffn_W1"], inp["L0_ffn_b1"],
                                   inp["L0_ffn_W2"], inp["L0_ffn_b2"]))
    x2T_cores = [np.ascontiguousarray(np.asarray(resC[i]["x2T"])) for i in range(NCORE)]

    ncD1 = _get_nc("D1", build_D1)
    resD1 = run_launch(ncD1, stage_D1(x2T_cores, inp))
    ncD2 = _get_nc("D2", build_D2)
    resD2 = run_launch(ncD2, stage_D2(resD1, inp))
    ncD3 = _get_nc("D3", build_D3)
    resD3 = run_launch(ncD3, stage_D3(x2T_cores, resD1, resD2))
    x3T_cores = [np.ascontiguousarray(np.asarray(resD3[i]["x3T"])) for i in range(NCORE)]

    ncE1 = _get_nc("E1", build_E1)
    resE1 = run_launch(ncE1, stage_E1(x3T_cores, inp))
    ncE2 = _get_nc("E2", build_E2)
    resE2 = run_launch(ncE2, stage_E2(x3T_cores, resE1, inp))

    out = np.empty((2, S, D), np.float32)
    for c in range(NCORE):
        b, qd = c // 4, c % 4
        out[b, qd * TOK:(qd + 1) * TOK, :] = np.asarray(resE2[c]["x4T"]).T
    return out


def stage_A(xT_cores, W_uv, b_uv, gamma, beta):
    Wvb = np.ascontiguousarray(W_uv[:, 2048:4224])
    bvbf = np.ascontiguousarray(b_uv[4096:4224].reshape(1, 128).T)
    bvrow = np.ascontiguousarray(b_uv[2048:4096].reshape(1, 2048))
    gb = np.ascontiguousarray(np.stack([gamma[0], beta[0], gamma[1], beta[1]], axis=1))
    return [{"xT": xT_cores[i], "Wvb": Wvb, "bvbf": bvbf, "bvrow": bvrow, "gb": gb}
            for i in range(NCORE)]


def stage_B(xT_cores, resA, W_uv, b_uv, W_out, b_out):
    Wu = np.ascontiguousarray(W_uv[:, 0:2048])
    buf = np.ascontiguousarray(b_uv[0:2048].reshape(16, 128).T)
    bof = np.ascontiguousarray(b_out.reshape(8, 128).T)
    maps = []
    for i in range(NCORE):
        b = i // 4
        grp = [4 * b + j for j in range(4)]
        kb = np.concatenate([np.asarray(resA[j]["kf"]) for j in grp], axis=1)
        vblkb = np.concatenate([np.asarray(resA[j]["vblk"]) for j in grp], axis=1)
        maps.append({"xT": xT_cores[i], "Wu": Wu, "buf": buf,
                     "qf": np.asarray(resA[i]["qf"]), "kb": kb,
                     "vblkb": np.ascontiguousarray(vblkb), "Wo": W_out, "bof": bof})
    return maps



def build_F():
    nc = bass.Bass("TRN2", target_bir_lowering=False)
    master = ExitStack()
    with master:
        def idram(name, shape, dt=F32):
            return nc.dram_tensor(name, list(shape), dt, kind="Internal").ap()

        qT = idram("qT", [128, TOK])
        cgki = idram("cgki", [128, TOK], BF16)
        cgko = idram("cgko", [512, TOK], BF16)
        cgvi = idram("cgvi", [16, TOK, 128], BF16)
        cgvo = idram("cgvo", [4 * 16 * TOK, 128], BF16)
        x1T = idram("x1T", [D, TOK])
        x2T = idram("x2T", [D, TOK])
        x3T = idram("x3T", [D, TOK])
        cqin = idram("cqin", [1, 128])
        cqout = idram("cqout", [8, 128])
        scr2 = idram("scr2", [1, 2])
        wq1in = idram("wq1in", [128, 4096], BF16)
        wq2in = idram("wq2in", [512, D], BF16)
        gwq1 = idram("gwq1", [8, 524288], BF16)
        gwq2 = idram("gwq2", [8, 524288], BF16)
        dmx_in = idram("dmx_in", [2080, TOK], BF16)
        dmx_out = idram("dmx_out", [4 * 2080, TOK], BF16)
        yskipT = idram("yskipT", [D, TOK], BF16)
        yp_in = idram("yp_in", [4 * D, TOK], BF16)
        rs_out = idram("rs_out", [D, TOK], BF16)
        masko = idram("masko", [16, 128, 8])
        combpre = idram("combpre", [16, 128, 8])
        maskg = idram("maskg", [16384, 8])

        phases = []

        def mk_phase(pfx, decl_fn, prefill):
            pst = {"_nc": nc, "_pfx": pfx + "_"}
            pst.update(prefill)
            pstack = ExitStack()
            pst["_stack"] = pstack
            with pstack:
                decl_fn(nc, pst)
            return pst

        xT_ext = nc.dram_tensor("xT", [D, TOK], F32, kind="ExternalInput").ap()
        stCq = {"_nc": nc, "_pfx": "Cq_", "_stack": master,
                "cqin": cqin, "cqout": cqout, "scr2": scr2,
                "wq1in": wq1in, "wq2in": wq2in,
                "gwq1": gwq1, "gwq2": gwq2}
        decl_Cq(nc, stCq)
        stA = mk_phase("A", decl_A, {"xT": xT_ext, "vblk": cgvi, "qf": qT, "kf": cgki})
        stB = mk_phase("B", decl_B2, {
            "xT": xT_ext, "qf": qT, "x1T": x1T,
            "kb4": [cgko[r * 128:(r + 1) * 128, :] for r in range(4)],
            "vb4": (lambda r, ec: cgvo[r * 16 * TOK + ec * TOK:
                                           r * 16 * TOK + (ec + 1) * TOK, :].rearrange(
                "(kt p) e -> p kt e", p=128)),
        })
        stCq["ps_a"] = stA["pbc"]
        stCq["ps_b"] = stB["pc"]
        stC = mk_phase("C", decl_C, {"x1T": x1T, "x2T": x2T, "gwq1": gwq1, "gwq2": gwq2})
        stD1 = mk_phase("D1", decl_D1x, {
            "x2T": x2T, "deltaT": dmx_in[0:1024, :], "m1T": dmx_in[1024:2048, :],
            "BpCT": dmx_in[2048:2080, :], "yskipT": yskipT})
        stD2 = mk_phase("D2", decl_D2, {"dmx_out": dmx_out, "yp_in": yp_in})
        stD3 = mk_phase("D3", decl_D3, {
            "x2T": x2T, "yskipT": yskipT, "x3T": x3T, "ypr": rs_out,
            "D3_srcs": ["ypr", "yskipT"]})
        stE1 = mk_phase("E1", decl_E1, {"x3T": x3T, "masko": masko, "combpre": combpre})
        stE2 = mk_phase("E2", decl_E2, {"x3T": x3T, "maskg": maskg,
                                        "maskown": masko, "combpre": combpre})

        sems = {}
        semnames = ["v", "s", "t", "g", "cc"]
        for i in range(Ctx.DMA_ROT):
            semnames += [f"y{i}", f"gd{i}"]
        for sname in semnames:
            sems[sname] = master.enter_context(nc.semaphore("sem_" + sname))

        def body_F(ctx):
            body_Cq_p1(ctx, stCq)
            body_A(ctx, stA)
            ctx.wait("g", "y")
            ctx.full_barrier()
            ctx.op("g", lambda g: g.collective_compute(
                "AllGather", ALU.bypass, replica_groups=[[0, 1, 2, 3], [4, 5, 6, 7]],
                ins=[cgki], outs=[cgko]), sem="cc")
            ctx.op("g", lambda g: g.collective_compute(
                "AllGather", ALU.bypass, replica_groups=[[0, 1, 2, 3], [4, 5, 6, 7]],
                ins=[cgvi.rearrange("a b c -> (a b) c")],
                outs=[cgvo]), sem="cc")
            ctx.op("g", lambda g: g.collective_compute(
                "AllGather", ALU.bypass, replica_groups=[[0, 1, 2, 3, 4, 5, 6, 7]],
                ins=[cqin], outs=[cqout]), sem="cc")
            ctx.full_barrier()
            body_Cq_p2(ctx, stCq)
            ctx.wait("t", "v")
            body_B2(ctx, stB)
            ctx.full_barrier()
            body_C(ctx, stC)
            ctx.full_barrier()
            body_D1x(ctx, stD1)
            ctx.full_barrier()
            ctx.op("g", lambda g: g.collective_compute(
                "AllGather", ALU.bypass, replica_groups=[[0, 1, 2, 3], [4, 5, 6, 7]],
                ins=[dmx_in], outs=[dmx_out]), sem="cc")
            ctx.full_barrier()
            body_D2(ctx, stD2)
            ctx.full_barrier()
            ctx.op("g", lambda g: g.collective_compute(
                "ReduceScatter", ALU.add, replica_groups=[[0, 1, 2, 3], [4, 5, 6, 7]],
                ins=[yp_in], outs=[rs_out]), sem="cc")
            ctx.full_barrier()
            body_D3(ctx, stD3)
            ctx.full_barrier()
            body_E1(ctx, stE1)
            ctx.full_barrier()
            ctx.op("g", lambda g: g.collective_compute(
                "AllGather", ALU.bypass, replica_groups=[[0, 1, 2, 3, 4, 5, 6, 7]],
                ins=[masko.rearrange("a b c -> (a b) c")], outs=[maskg]), sem="cc")
            ctx.full_barrier()
            body_E2(ctx, stE2)

        with nc.Block() as block:
            def run_pass(en, engine_obj):
                ctx = Ctx()
                ctx.cur = en
                ctx.eng = {en: engine_obj}
                ctx.sems = sems
                body_F(ctx)

            @block.vector
            def _(vector):
                run_pass("v", vector)

            @block.scalar
            def _(scalar):
                run_pass("s", scalar)

            @block.tensor
            def _(tensor):
                run_pass("t", tensor)

            @block.gpsimd
            def _(gpsimd):
                run_pass("g", gpsimd)

            @block.sync
            def _(sync):
                run_pass("y", sync)
                ctx = Ctx()
                ctx.cur = None
                body_F(ctx)
                for i in range(Ctx.DMA_ROT):
                    if ctx.cnt[f"y{i}"]:
                        sync.wait_ge(sems[f"y{i}"], ctx.cnt[f"y{i}"])
                    if ctx.cnt[f"gd{i}"]:
                        sync.wait_ge(sems[f"gd{i}"], ctx.cnt[f"gd{i}"])
    return nc


def stage_F(inputs):
    inp = inputs
    x = np.ascontiguousarray(inp["L0_attn_inputs"], dtype=np.float32)
    W_uv, b_uv = inp["L0_attn_W_uv"], inp["L0_attn_b_uv"]
    Wvb = np.ascontiguousarray(W_uv[:, 2048:4224])
    bvbf = np.ascontiguousarray(b_uv[4096:4224].reshape(1, 128).T)
    bvrow = np.ascontiguousarray(b_uv[2048:4096].reshape(1, 2048))
    gamma, beta = inp["L0_attn_gamma"], inp["L0_attn_beta"]
    gb = np.ascontiguousarray(np.stack([gamma[0], beta[0], gamma[1], beta[1]], axis=1))
    Wu = np.ascontiguousarray(W_uv[:, 0:2048])
    buf = np.ascontiguousarray(b_uv[0:2048].reshape(16, 128).T)
    bof = np.ascontiguousarray(inp["L0_attn_b_out"].reshape(8, 128).T)
    W1T = np.ascontiguousarray(inp["L0_ffn_W1"], dtype=np.float32).T
    W2T = np.ascontiguousarray(inp["L0_ffn_W2"], dtype=np.float32).T
    b1f = np.ascontiguousarray(inp["L0_ffn_b1"].reshape(32, 128).T)
    b2f = np.ascontiguousarray(inp["L0_ffn_b2"].reshape(8, 128).T)
    nwf = np.ascontiguousarray(inp["L1_attn_norm_weight"].reshape(8, 128).T)
    bdf = np.ascontiguousarray(inp["L1_attn_b_delta"].reshape(8, 128).T)
    dskf = np.ascontiguousarray(inp["L1_attn_D_skip"].reshape(8, 128).T)
    WBC = np.ascontiguousarray(np.concatenate([inp["L1_attn_W_B"], inp["L1_attn_W_C"]], axis=1))
    Wd = np.ascontiguousarray(inp["L1_attn_W_delta"])
    Alog = np.ascontiguousarray(inp["L1_attn_A_log"])
    gwT = np.ascontiguousarray(inp["L1_ffn_gate_w"].T)
    wgT = np.ascontiguousarray(inp["L1_ffn_we_gate"].transpose(0, 2, 1))
    wuT = np.ascontiguousarray(inp["L1_ffn_we_up"].transpose(0, 2, 1))
    wdT = np.ascontiguousarray(inp["L1_ffn_we_down"].transpose(0, 2, 1))
    wsgT = np.ascontiguousarray(inp["L1_ffn_ws_gate"].T)
    wsuT = np.ascontiguousarray(inp["L1_ffn_ws_up"].T)
    wsdT = np.ascontiguousarray(inp["L1_ffn_ws_down"].T)
    maps = []
    for c in range(NCORE):
        b, qd = c // 4, c % 4
        xT = _fm(x[b, qd * TOK:(qd + 1) * TOK, :])
        A4 = np.ascontiguousarray(Alog[:, qd * 4:(qd + 1) * 4])
        bpcsel = np.zeros((32, 8), np.float32)
        for n in range(4):
            bpcsel[qd * 4 + n, n] = 1.0
            bpcsel[16 + qd * 4 + n, 4 + n] = 1.0
        rowsel = np.zeros((128, 1), np.float32)
        rowsel[0:c * 16, 0] = 1.0
        maps.append({
            "xT": xT, "Wvb": Wvb, "bvbf": bvbf, "bvrow": bvrow, "gb": gb,
            "Wu": Wu, "buf": buf, "Wo": inp["L0_attn_W_out"], "bof": bof,
            "b1f": b1f, "b2f": b2f,
            "Wqs1": np.ascontiguousarray(W1T[c * 128:(c + 1) * 128]),
            "Wqs2": np.ascontiguousarray(W2T[c * 512:(c + 1) * 512]),
            "nwf": nwf, "Wd": Wd, "bdf": bdf, "WBC": WBC, "dskf": dskf,
            "A4": A4, "bpcsel": bpcsel, "rowsel": rowsel,
            "gwT": gwT,
            "wgT": wgT, "wuT": wuT, "wdT": wdT,
            "wsgT": wsgT, "wsuT": wsuT, "wsdT": wsdT,
        })
    return maps


def kernel_fused(**inputs):
    nc = _get_nc("F", build_F)
    maps = stage_F(inputs)
    res = run_launch(nc, maps)
    out = np.empty((2, S, D), np.float32)
    for c in range(NCORE):
        b, qd = c // 4, c % 4
        out[b, qd * TOK:(qd + 1) * TOK, :] = np.asarray(res[c]["x4T"]).T
    return out


# version 17
# speedup vs baseline: 1.9592x; 1.0132x over previous
import numpy as np
import ml_dtypes
from contextlib import ExitStack

import concourse.bass as bass
import concourse.mybir as mybir
import concourse.bass_isa as bass_isa
from concourse import library_config
from concourse.bass_utils import run_bass_kernel_spmd

F32 = mybir.dt.float32
BF16 = mybir.dt.bfloat16
AF = mybir.ActivationFunctionType
ALU = mybir.AluOpType
AX = mybir.AxisListType

NCORE = 8
D = 1024
DC = 8
TOK = 2048
B = 2
S = 8192
CAP = 5120
ENGS = ("v", "s", "t", "g", "y")


class Ctx:

    DMA_ROT = 4

    def __init__(self):
        self.cnt = {e: 0 for e in ENGS}
        for i in range(self.DMA_ROT):
            self.cnt[f"y{i}"] = 0
            self.cnt[f"gd{i}"] = 0
        self.cnt["cc"] = 0
        self.rr = {"y": 0, "gd": 0}
        self.cur = None
        self.eng = None
        self.sems = None

    def _emit(self, en):
        return self.cur == en and self.eng is not None

    def op(self, en, emit, inc=True, serialize=True, dma=False, sem=None):
        step = 16 if (dma or en in ("y",)) else 1
        if dma and sem is None:
            q = "y" if en == "y" else "gd"
            k = self.rr[q]
            self.rr[q] = (k + 1) % self.DMA_ROT
            sem = f"{q}{k}"
            prev = self.cnt[sem]
            self.cnt[sem] += step
            n = self.cnt[sem]
            if self._emit(en):
                if prev:
                    self.eng[en].wait_ge(self.sems[sem], prev)
                inst = emit(self.eng[en])
                inst.then_inc(self.sems[sem], step)
            return (sem, n)
        if sem is None:
            sem = en
        elif sem == "cc":
            step = 1
        if inc:
            self.cnt[sem] += step
        n = self.cnt[sem]
        if self._emit(en):
            inst = emit(self.eng[en])
            if inc:
                inst.then_inc(self.sems[sem], step)
                if serialize and en != "t":
                    self.eng[en].wait_ge(self.sems[sem], n)
        return n

    def wait(self, en, producer, n=None):
        if producer in ("y", "gd") and n is None:
            for i in range(self.DMA_ROT):
                sname = f"{producer}{i}"
                if self.cnt[sname] and self._emit(en):
                    self.eng[en].wait_ge(self.sems[sname], self.cnt[sname])
            return None
        if n is None:
            n = self.cnt[producer]
        if self._emit(en):
            self.eng[en].wait_ge(self.sems[producer], n)
        return n

    def wait_tok(self, en, tok):
        if tok is None:
            return
        sem, n = tok
        if self._emit(en):
            self.eng[en].wait_ge(self.sems[sem], n)

    def barrier(self, *ens):
        snap = {e: self.cnt.get(e) for e in ens}
        for e in ens:
            for p in ens:
                if p == e:
                    continue
                if p in ("y", "gd"):
                    self.wait(e, p)
                else:
                    self.wait(e, p, snap[p])
            self.wait(e, "y")
            self.wait(e, "gd")

    def full_barrier(self):
        ens = ("v", "s", "t", "g", "y")
        snap = {e: self.cnt.get(e) for e in ens}
        ncc = self.cnt["cc"]
        for e in ens:
            for p in ens:
                if p in (e, "y"):
                    continue
                self.wait(e, p, snap[p])
            self.wait(e, "y")
            self.wait(e, "gd")
            if ncc:
                self.wait(e, "cc", ncc)


def build_program(name, decl_fn, body_fn):
    nc = bass.Bass("TRN2", target_bir_lowering=False)
    st = {}
    stack = ExitStack()
    with stack:
        st["_nc"] = nc
        st["_stack"] = stack
        decl_fn(nc, st)
        sems = {}
        semnames = ["v", "s", "t", "g", "cc"]
        for i in range(Ctx.DMA_ROT):
            semnames += [f"y{i}", f"gd{i}"]
        for sname in semnames:
            sems[sname] = stack.enter_context(nc.semaphore("sem_" + sname))
        with nc.Block() as block:
            def run_pass(en, engine_obj):
                ctx = Ctx()
                ctx.cur = en
                ctx.eng = {en: engine_obj}
                ctx.sems = sems
                body_fn(ctx, st)

            @block.vector
            def _(vector):
                run_pass("v", vector)

            @block.scalar
            def _(scalar):
                run_pass("s", scalar)

            @block.tensor
            def _(tensor):
                run_pass("t", tensor)

            @block.gpsimd
            def _(gpsimd):
                run_pass("g", gpsimd)

            @block.sync
            def _(sync):
                run_pass("y", sync)
                ctx = Ctx()
                ctx.cur = None
                body_fn(ctx, st)
                for i in range(Ctx.DMA_ROT):
                    if ctx.cnt[f"y{i}"]:
                        sync.wait_ge(sems[f"y{i}"], ctx.cnt[f"y{i}"])
                    if ctx.cnt[f"gd{i}"]:
                        sync.wait_ge(sems[f"gd{i}"], ctx.cnt[f"gd{i}"])
    return nc



def sb(nc, st, name, shape, dt=F32):
    tname = st.get("_pfx", "") + name
    st[name] = st["_stack"].enter_context(nc.sbuf_tensor(tname, list(shape), dt))


def ps(nc, st, name, shape, dt=F32):
    tname = st.get("_pfx", "") + name
    st[name] = st["_stack"].enter_context(nc.psum_tensor(tname, list(shape), dt))


def dram_in(nc, st, name, shape, dt=F32):
    if name in st:
        return
    st[name] = nc.dram_tensor(name, list(shape), dt, kind="ExternalInput").ap()


def dram_out(nc, st, name, shape, dt=F32):
    if name in st:
        return
    st[name] = nc.dram_tensor(name, list(shape), dt, kind="ExternalOutput").ap()


def load_consts(ctx, st, vals):
    for cname, v in vals.items():
        ctx.op("g", lambda g, c=cname, vv=v: g.memset(st[c][:], vv))


def emit_ln(ctx, st, z, xout, ncol, eps_tile, pfx, tc=None):
    zbf, sq = st[pfx + "_zbf"], st[pfx + "_sq"]
    rows = st[pfx + "_rows"]
    def _ap(x):
        return x if isinstance(x, bass.AP) else x[:]
    s1, s2 = _ap(st[pfx + "_s1"]), _ap(st[pfx + "_s2"])
    bca, bcb = _ap(st[pfx + "_bca"]), _ap(st[pfx + "_bcb"])
    mean = rows[:, 0 * ncol:1 * ncol]
    var = rows[:, 1 * ncol:2 * ncol]
    rstd = rows[:, 2 * ncol:3 * ncol]
    negmrs = rows[:, 3 * ncol:4 * ncol]

    ctx.wait("s", "v")
    for c in range(DC):
        ctx.op("v", lambda v, c=c: v.tensor_copy(zbf[:, c, :], z[:, c, :]))
        ctx.op("s", lambda s, c=c: s.activation(sq[:, c, :], z[:, c, :], AF.Square))
    nv = ctx.cnt["v"]
    nsq = ctx.cnt["s"]
    ctx.wait("t", "v", nv)
    ctx.wait("t", "s", nsq)
    s2start = not st.get("ln_s2_nostart", False)
    for c in range(DC):
        ctx.op("t", lambda t, c=c: t.matmul(s1, st["ones_bf"][:], zbf[:, c, :],
                                            start=(c == 0), stop=(c == DC - 1),
                                            skip_group_check=True), inc=(c == DC - 1))
    for c in range(DC):
        ctx.op("t", lambda t, c=c: t.matmul(s2, st["ones_bf"][:], sq[:, c, :],
                                            start=(c == 0 and s2start), stop=(c == DC - 1),
                                            skip_group_check=True), inc=(c == DC - 1))
    nmm = ctx.cnt["t"]
    ctx.wait("v", "t", nmm)
    ctx.op("v", lambda v: v.tensor_scalar(mean, s1, 1.0 / D, None, ALU.mult))
    ctx.op("v", lambda v: v.tensor_scalar(var, s2, 1.0 / D, None, ALU.mult))
    ctx.wait("s", "v", ctx.cnt["v"])
    ctx.op("s", lambda s: s.activation(rstd, mean, AF.Square))
    ctx.wait("v", "s", ctx.cnt["s"])
    ctx.op("v", lambda v: v.tensor_sub(var, var, rstd))
    ctx.wait("s", "v", ctx.cnt["v"])
    ctx.op("s", lambda s: s.activation(rstd, var, AF.Sqrt, bias=eps_tile[:]))
    ctx.wait("v", "s", ctx.cnt["s"])
    ctx.op("v", lambda v: v.reciprocal(rstd, rstd))
    ctx.op("v", lambda v: v.tensor_mul(negmrs, mean, rstd))
    ctx.op("v", lambda v: v.tensor_scalar(negmrs, negmrs, -1.0, None, ALU.mult))
    nrows = ctx.cnt["v"]
    ctx.wait("t", "v", nrows)
    bcbstart = not st.get("ln_bcb_nostart", False)
    ctx.op("t", lambda t: t.matmul(bca, st["ones_f1"][:], rstd, start=True, stop=True,
                                   skip_group_check=True))
    ctx.op("t", lambda t: t.matmul(bcb, st["ones_f1"][:], negmrs, start=bcbstart, stop=True,
                                   skip_group_check=True))
    nbc = ctx.cnt["t"]
    ctx.wait("v", "t", nbc)
    for c in range(DC):
        ctx.op("v", lambda v, c=c: v.tensor_mul(xout[:, c, :], z[:, c, :], bca))
        ctx.op("v", lambda v, c=c: v.tensor_add(xout[:, c, :], xout[:, c, :], bcb))
    return ctx.cnt["v"]




NQ2 = 512

def decl_B2(nc, st):
    dram_in(nc, st, "xT", [D, TOK])
    dram_in(nc, st, "Wu", [D, 2048])
    dram_in(nc, st, "buf", [128, 16])
    dram_in(nc, st, "Wo", [2048, D])
    dram_in(nc, st, "bof", [128, 8])
    dram_out(nc, st, "x1T", [D, TOK])

    sb(nc, st, "xbf", [128, DC, TOK], BF16)
    sb(nc, st, "kbf", [128, S], BF16)
    sb(nc, st, "qbf", [128, TOK], BF16)
    sb(nc, st, "uc", [128, 16, NQ2], BF16)
    with ExitStack() as _sub:
        _saved = st["_stack"]
        st["_stack"] = _sub
        sb(nc, st, "sqT", [128, 64, NQ2], BF16)
        st["_stack"] = _saved
    sb(nc, st, "z", [128, DC, NQ2])
    sb(nc, st, "x1f", [128, DC, NQ2])
    sb(nc, st, "ln_zbf", [128, DC, NQ2], BF16)
    sb(nc, st, "ln_sq", [128, DC, NQ2], BF16)
    sb(nc, st, "pad64", [128, 16 * NQ2 // 4])
    sb(nc, st, "warena", [128, 16, 1024], BF16)
    sb(nc, st, "vdb", [128, 2, 64, 128], BF16)
    sb(nc, st, "sctmp", [128, 2, NQ2], BF16)
    sb(nc, st, "but", [128, 16])
    sb(nc, st, "bot", [128, 8])
    sb(nc, st, "ones_bf", [128, 1], BF16)
    sb(nc, st, "ones_f1", [1, 128])
    sb(nc, st, "c_eps", [1, 1])
    sb(nc, st, "ln_rows", [1, 4 * NQ2])
    ps(nc, st, "pa", [128, NQ2])
    ps(nc, st, "pb", [128, NQ2])
    ps(nc, st, "pc", [128, NQ2])
    ps(nc, st, "pdx", [128, NQ2])
    ps(nc, st, "ln_s1", [1, NQ2])
    ps(nc, st, "ln_s2", [1, NQ2])
    ps(nc, st, "ln_bca", [128, NQ2])
    ps(nc, st, "ln_bcb", [128, NQ2])


def body_B2(ctx, st):
    load_consts(ctx, st, {"c_eps": 1e-5})
    ctx.op("g", lambda g: g.memset(st["ones_bf"][:], 1.0))
    ctx.op("g", lambda g: g.memset(st["ones_f1"][:], 1.0))
    ctx.op("y", lambda y: y.dma_start(out=st["but"][:], in_=st["buf"]), dma=True)
    ctx.op("y", lambda y: y.dma_start(out=st["bot"][:], in_=st["bof"]), dma=True)
    ctx.op("g", lambda g: g.dma_start(
        out=st["xbf"][:], in_=st["xT"].rearrange("(c p) t -> p c t", p=128)), dma=True)
    for r in range(4):
        ctx.op("y", lambda y, r=r: y.dma_start(
            out=st["kbf"][:, r * TOK:(r + 1) * TOK], in_=st["kb4"][r]), dma=True)
    ctx.op("g", lambda g: g.dma_start(out=st["qbf"][:], in_=st["qf"]), dma=True)
    ctx.barrier("v", "s", "t", "g", "y")
    ctx.wait("t", "gd")
    ctx.wait("t", "y")
    ctx.wait("v", "gd")
    ctx.wait("s", "gd")

    banksA = ("pa", "pb")
    evacA = [None, None]

    def vload(qc_ec, buf):
        ec = qc_ec
        toks = []
        for r in range(4):
            toks.append(ctx.op("y", lambda y, r=r, ec=ec, buf=buf: y.dma_start(
                out=st["vdb"][:, buf, r * 16:(r + 1) * 16, :],
                in_=st["vb4"](r, ec)), dma=True))
        return toks

    for qc in range(TOK // NQ2):
        qsl = slice(qc * NQ2, (qc + 1) * NQ2)
        ctx.wait("g", "t")
        wut = ctx.op("g", lambda g: g.dma_start(
            out=st["warena"].ap().rearrange("p a b -> p (a b)").rearrange(
                "p (c f) -> p c f", c=DC),
            in_=st["Wu"].rearrange("(c p) f -> p c f", p=128)), dma=True)
        ctx.wait_tok("t", wut)
        wu = st["warena"].ap().rearrange("p a b -> p (a b)").rearrange("p (c f) -> p c f", c=DC)
        for ec in range(16):
            bk = ec % 2
            pgb = st[banksA[bk]]
            if evacA[bk] is not None:
                ctx.wait("t", "s", evacA[bk])
            for c in range(DC):
                ctx.op("t", lambda t, ec=ec, c=c, qsl=qsl, pgb=pgb, wu=wu: t.matmul(
                    pgb[:], wu[:, c, ec * 128:(ec + 1) * 128],
                    st["xbf"][:, c, qsl], start=(c == 0), stop=(c == DC - 1)),
                    inc=(c == DC - 1))
            nm = ctx.cnt["t"]
            ctx.wait("s", "t", nm)
            ctx.op("s", lambda s, ec=ec, pgb=pgb: s.activation(
                st["uc"][:, ec, :], pgb[:], AF.Silu, bias=st["but"][:, ec:ec + 1]))
            evacA[bk] = ctx.cnt["s"]
        ctx.wait("v", "y")
        ctx.wait("s", "v")
        sc_evac = [None, None]
        for kt in range(64):
            bk = kt % 2
            psb = st["pc"] if bk == 0 else st["pdx"]
            if sc_evac[bk] is not None:
                ctx.wait("t", "v", sc_evac[bk])
            ctx.op("t", lambda t, kt=kt, qsl=qsl, psb=psb: t.matmul(
                psb[:], st["kbf"][:, kt * 128:(kt + 1) * 128],
                st["qbf"][:, qsl], start=True, stop=True), inc=True)
            nm = ctx.cnt["t"]
            ctx.wait("v", "t", nm)
            ctx.op("v", lambda v, kt=kt, bk=bk, psb=psb: v.tensor_scalar(
                st["sctmp"][:, bk, :], psb[:], 0.0, 1.0 / S, ALU.max, ALU.mult))
            ctx.op("v", lambda v, kt=kt, bk=bk: v.tensor_mul(
                st["sqT"][:, kt, :], st["sctmp"][:, bk, :], st["sctmp"][:, bk, :]))
            sc_evac[bk] = ctx.cnt["v"]
        ctx.wait("t", "v")
        av_evac = [None, None]
        vtoks = {0: vload(0, 0)}
        for ec in range(16):
            if ec + 1 < 16:
                if ec >= 1:
                    ctx.wait("y", "t")
                vtoks[ec + 1] = vload(ec + 1, (ec + 1) % 2)
            for tok in vtoks[ec]:
                ctx.wait_tok("t", tok)
            bk = ec % 2
            pvb = st["pa"] if bk == 0 else st["pb"]
            if av_evac[bk] is not None:
                ctx.wait("t", "v", av_evac[bk])
            if ec < 2:
                ctx.wait("t", "s", evacA[0] if ec == 0 else evacA[1])
            for kt in range(64):
                ctx.op("t", lambda t, kt=kt, ec=ec, bk=bk, pvb=pvb: t.matmul(
                    pvb[:], st["vdb"][:, ec % 2, kt, :], st["sqT"][:, kt, :],
                    start=(kt == 0), stop=(kt == 63)), inc=(kt == 63))
            nm = ctx.cnt["t"]
            ctx.wait("v", "t", nm)
            ctx.op("v", lambda v, ec=ec, pvb=pvb: v.tensor_mul(
                st["uc"][:, ec, :], st["uc"][:, ec, :], pvb[:]))
            av_evac[bk] = ctx.cnt["v"]
        ctx.wait("g", "t")
        wot = ctx.op("g", lambda g: g.dma_start(
            out=st["warena"][:], in_=st["Wo"].rearrange("(ec p) h -> p ec h", p=128)), dma=True)
        ctx.wait("y", "t")
        ctx.op("y", lambda y, qsl=qsl: y.dma_start(
            out=st["x1f"][:], in_=st["xT"].rearrange("(c p) t -> p c t", p=128)[:, :, qsl]),
            dma=True)
        ctx.wait_tok("t", wot)
        ctx.wait("t", "v")
        po_evac = [None, None]
        for dc in range(DC):
            bk = dc % 2
            pob = st["pc"] if bk == 0 else st["pdx"]
            if po_evac[bk] is not None:
                ctx.wait("t", "s", po_evac[bk])
            for ec in range(16):
                ctx.op("t", lambda t, dc=dc, ec=ec, pob=pob: t.matmul(
                    pob[:], st["warena"][:, ec, dc * 128:(dc + 1) * 128],
                    st["uc"][:, ec, :], start=(ec == 0), stop=(ec == 15)),
                    inc=(ec == 15))
            nm = ctx.cnt["t"]
            ctx.wait("s", "t", nm)
            ctx.op("s", lambda s, dc=dc, pob=pob: s.activation(
                st["z"][:, dc, :], pob[:], AF.Identity, bias=st["bot"][:, dc:dc + 1]))
            po_evac[bk] = ctx.cnt["s"]
        ctx.wait("v", "s")
        ctx.wait("v", "y")
        for dc in range(DC):
            ctx.op("v", lambda v, dc=dc: v.tensor_add(st["z"][:, dc, :], st["z"][:, dc, :],
                                                      st["x1f"][:, dc, :]))
        ctx.wait("t", "v")
        emit_ln(ctx, st, st["z"], st["z"], NQ2, st["c_eps"], "ln")
        nv = ctx.cnt["v"]
        ctx.wait("y", "v", nv)
        ctx.op("y", lambda y, qsl=qsl: y.dma_start(
            out=st["x1T"].rearrange("(c p) t -> p c t", p=128)[:, :, qsl],
            in_=st["z"][:]), dma=True)
        evacA = [None, None]
        for i in range(2):
            sc_evac[i] = None




def decl_Cq(nc, st):
    dram_in(nc, st, "Wqs1", [128, 4096])
    dram_in(nc, st, "Wqs2", [512, D])
    sb(nc, st, "qstg", [128, 1, 512])
    sb(nc, st, "qneg", [128, 512], BF16)
    sb(nc, st, "qpos", [128, 512], BF16)
    sb(nc, st, "qabs", [128, 4])
    sb(nc, st, "qrow", [1, 128])
    sb(nc, st, "qthr", [128, 4])
    sb(nc, st, "qones", [128, 1])
    sb(nc, st, "t28", [2, 8])
    sb(nc, st, "t21", [2, 1])
    sb(nc, st, "trow", [1, 2])


def _cq_slices(st):
    s1 = st["Wqs1"]
    s2 = st["Wqs2"].rearrange("(f p) d -> p f d", p=128)
    w1v = st["wq1in"].rearrange("(c p) f -> p (c f)", p=128)
    w2v = st["wq2in"].rearrange("(f p) d -> p f d", p=128)
    out = []
    for s in range(8):
        out.append((0, s1[:, s * 512:(s + 1) * 512],
                    w1v[:, s * 512:(s + 1) * 512]))
    for s in range(4):
        for hh in range(2):
            out.append((1, s2[:, s, hh * 512:(hh + 1) * 512],
                        w2v[:, s, hh * 512:(hh + 1) * 512]))
    return out


def body_Cq_p1(ctx, st):
    ctx.op("g", lambda g: g.memset(st["qrow"][:], 0.0))
    ctx.op("g", lambda g: g.memset(st["qones"][:], 1.0))
    toks = []
    slices = _cq_slices(st)
    for idx, (m, din, _) in enumerate(slices):
        p = 0
        first = idx == 0 or slices[idx - 1][0] != m
        if idx >= 1:
            ctx.wait("y", "v")
        toks.append(ctx.op("y", lambda y, din=din, p=p: y.dma_start(
            out=st["qstg"][:, p, :], in_=din), dma=True))
        ctx.wait_tok("v", toks[-1])
        ctx.op("v", lambda v, p=p: v.tensor_reduce(
            st["qabs"][:, 2:3], st["qstg"][:, p, :], AX.X, ALU.add,
            apply_absolute_value=True))
        if first:
            ctx.op("v", lambda v, m=m: v.tensor_copy(st["qabs"][:, m:m + 1],
                                                     st["qabs"][:, 2:3]))
        else:
            ctx.op("v", lambda v, m=m: v.tensor_add(
                st["qabs"][:, m:m + 1], st["qabs"][:, m:m + 1], st["qabs"][:, 2:3]))
    nv = ctx.cnt["v"]
    ctx.wait("t", "v", nv)
    ctx.wait("t", "g")
    ctx.op("t", lambda t: t.matmul(st["ps_a"][0:1, 0:2], st["qones"][:],
                                   st["qabs"][:, 0:2], start=True, stop=True,
                                   skip_group_check=True))
    nt = ctx.cnt["t"]
    ctx.wait("v", "t", nt)
    ctx.wait("v", "g")
    ctx.op("v", lambda v: v.tensor_copy(st["qrow"][:, 0:2], st["ps_a"][0:1, 0:2]))
    nv2 = ctx.cnt["v"]
    ctx.wait("g", "v", nv2)
    ctx.op("g", lambda g: g.dma_start(out=st["cqin"], in_=st["qrow"][:]), dma=True)


def body_Cq_p2(ctx, st):
    ctx.op("g", lambda g: g.memset(st["qrow"][:], 1.0))
    ltok = ctx.op("y", lambda y: y.dma_start(
        out=st["t28"][:], in_=st["cqout"][:, 0:2].rearrange("r c -> c r"),
        allow_slow_non_contiguous=True), dma=True)
    ctx.wait_tok("v", ltok)
    ctx.op("v", lambda v: v.tensor_reduce(st["t21"][:], st["t28"][:], AX.X, ALU.add))
    ctx.op("v", lambda v: v.tensor_scalar(st["t21"][:], st["t21"][:],
                                          0.5 / 4194304.0, 0.5e-5, ALU.mult, ALU.add))
    nv = ctx.cnt["v"]
    ctx.wait("y", "v", nv)
    d0 = ctx.op("y", lambda y: y.dma_start(
        out=st["scr2"].rearrange("o c -> c o"), in_=st["t21"][:],
        allow_slow_non_contiguous=True), dma=True)
    ctx.wait_tok("y", d0)
    d1 = ctx.op("y", lambda y: y.dma_start(out=st["trow"][:], in_=st["scr2"]), dma=True)
    ctx.wait_tok("t", d1)
    ctx.wait("t", "g")
    ctx.op("t", lambda t: t.matmul(st["ps_b"][:, 0:2], st["qrow"][:],
                                   st["trow"][:], start=True, stop=True, skip_group_check=True))
    nt = ctx.cnt["t"]
    ctx.wait("v", "t", nt)
    ctx.op("v", lambda v: v.tensor_copy(st["qthr"][:, 0:2], st["ps_b"][:, 0:2]))
    ctx.op("v", lambda v: v.tensor_scalar(st["qthr"][:, 2:4], st["qthr"][:, 0:2],
                                          -1.0, None, ALU.mult))
    stoks = []
    for idx, (m, din, dout) in enumerate(_cq_slices(st)):
        p = 0
        if idx >= 1:
            ctx.wait("y", "v")
        tok = ctx.op("y", lambda y, din=din, p=p: y.dma_start(
            out=st["qstg"][:, p, :], in_=din), dma=True)
        ctx.wait_tok("v", tok)
        ctx.wait("v", "gd")
        ctx.op("v", lambda v, p=p, m=m: v.tensor_scalar(
            st["qpos"][:], st["qstg"][:, p, :],
            st["qthr"][:, m:m + 1], None, ALU.is_gt))
        ctx.op("v", lambda v, p=p, m=m: v.tensor_scalar(
            st["qneg"][:], st["qstg"][:, p, :],
            st["qthr"][:, 2 + m:3 + m], None, ALU.is_lt))
        ctx.op("v", lambda v: v.tensor_sub(st["qpos"][:], st["qpos"][:], st["qneg"][:]))
        nv = ctx.cnt["v"]
        ctx.wait("g", "v", nv)
        stoks.append(ctx.op("g", lambda g, dout=dout: g.dma_start(
            out=dout, in_=st["qpos"][:]), dma=True))
    for tk in stoks:
        ctx.wait_tok("g", tk)
    ctx.op("g", lambda g: g.collective_compute(
        "AllGather", ALU.bypass, replica_groups=[[0, 1, 2, 3, 4, 5, 6, 7]],
        ins=[st["wq1in"].rearrange("(o p) f -> o (p f)", o=1)],
        outs=[st["gwq1"]]), sem="cc")
    ctx.op("g", lambda g: g.collective_compute(
        "AllGather", ALU.bypass, replica_groups=[[0, 1, 2, 3, 4, 5, 6, 7]],
        ins=[st["wq2in"].rearrange("(o f) d -> o (f d)", o=1)],
        outs=[st["gwq2"]]), sem="cc")




def decl_C(nc, st):
    dram_in(nc, st, "x1T", [D, TOK])
    dram_in(nc, st, "b1f", [128, 32])
    dram_in(nc, st, "b2f", [128, 8])
    dram_out(nc, st, "x2T", [D, TOK])

    NT = 256
    st["NT"] = NT
    sb(nc, st, "wq1", [128, DC, 4096], BF16)
    sb(nc, st, "wq2", [128, 32, 1024], BF16)
    sb(nc, st, "x1bf", [128, DC, NT], BF16)
    sb(nc, st, "x1f", [128, DC, NT])
    sb(nc, st, "h", [128, 32, NT], BF16)
    sb(nc, st, "z", [128, DC, NT])
    sb(nc, st, "b1t", [128, 32])
    sb(nc, st, "b2t", [128, 8])
    sb(nc, st, "ones_bf", [128, 1], BF16)
    sb(nc, st, "ones_f1", [1, 128])
    sb(nc, st, "c_eps", [1, 1])
    sb(nc, st, "ln_zbf", [128, DC, NT], BF16)
    sb(nc, st, "ln_sq", [128, DC, NT], BF16)
    sb(nc, st, "ln_rows", [1, 4 * NT])
    ps(nc, st, "pa", [128, NT])
    ps(nc, st, "pb", [128, NT])
    ps(nc, st, "pc", [128, NT])
    ps(nc, st, "pdx", [128, NT])
    ps(nc, st, "ln_s1", [1, NT])
    ps(nc, st, "ln_s2", [1, NT])
    ps(nc, st, "ln_bca", [128, NT])
    ps(nc, st, "ln_bcb", [128, NT])


def body_C(ctx, st):
    NT = st["NT"]
    NTC = TOK // NT
    load_consts(ctx, st, {"c_eps": 1e-5})
    ctx.op("g", lambda g: g.memset(st["ones_bf"][:], 1.0))
    ctx.op("g", lambda g: g.memset(st["ones_f1"][:], 1.0))
    ctx.op("y", lambda y: y.dma_start(out=st["b1t"][:], in_=st["b1f"]), dma=True)
    ctx.op("y", lambda y: y.dma_start(out=st["b2t"][:], in_=st["b2f"]), dma=True)
    ctx.barrier("v", "s", "t", "g", "y")
    w1tok = ctx.op("g", lambda g: g.dma_start(
        out=st["wq1"][:], in_=st["gwq1"].rearrange("c (p f) -> p c f", p=128)), dma=True)
    w2tok = ctx.op("g", lambda g: g.dma_start(
        out=st["wq2"].ap().rearrange("p (r f) d2 -> p r f d2", r=8),
        in_=st["gwq2"].rearrange("r (f p d2) -> p r f d2", p=128, d2=D)), dma=True)

    evac_h = [None, None]
    evac_z = [None, None]
    ztok = None
    for tc in range(NTC):
        tsl = slice(tc * NT, (tc + 1) * NT)
        ctx.wait("y", "v")
        xtok = ctx.op("y", lambda y, tsl=tsl: y.dma_start(
            out=st["x1f"][:], in_=st["x1T"].rearrange("(c p) t -> p c t", p=128)[:, :, tsl]),
            dma=True)
        ctx.wait_tok("v", xtok)
        ctx.wait("v", "t")
        for c in range(DC):
            ctx.op("v", lambda v, c=c: v.tensor_copy(st["x1bf"][:, c, :], st["x1f"][:, c, :]))
        nxbf = ctx.cnt["v"]
        ctx.wait("t", "v", nxbf)
        if tc == 0:
            ctx.wait_tok("t", w1tok)
        for fc in range(32):
            bk = fc % 2
            pgb = st["pa"] if bk == 0 else st["pb"]
            if evac_h[bk] is not None:
                ctx.wait("t", "s", evac_h[bk])
            for c in range(DC):
                ctx.op("t", lambda t, c=c, fc=fc, pgb=pgb: t.matmul(
                    pgb[:], st["wq1"][:, c, fc * 128:(fc + 1) * 128],
                    st["x1bf"][:, c, :], start=(c == 0), stop=(c == DC - 1)),
                    inc=(c == DC - 1))
            nm = ctx.cnt["t"]
            ctx.wait("s", "t", nm)
            ctx.op("s", lambda s, fc=fc, pgb=pgb: s.activation(
                st["h"][:, fc, :], pgb[:], AF.Relu, bias=st["b1t"][:, fc:fc + 1]))
            evac_h[bk] = ctx.cnt["s"]
        for dc in range(DC):
            bk = dc % 2
            pgb = st["pc"] if bk == 0 else st["pdx"]
            if evac_z[bk] is not None:
                ctx.wait("t", "s", evac_z[bk])
            if dc == 0:
                ctx.wait("t", "s", evac_h[0])
                ctx.wait("t", "s", evac_h[1])
                if tc == 0:
                    ctx.wait_tok("t", w2tok)
                if ztok is not None:
                    ctx.wait_tok("s", ztok)
            for fc in range(32):
                ctx.op("t", lambda t, dc=dc, fc=fc, pgb=pgb: t.matmul(
                    pgb[:], st["wq2"][:, fc, dc * 128:(dc + 1) * 128],
                    st["h"][:, fc, :], start=(fc == 0), stop=(fc == 31)),
                    inc=(fc == 31))
            nm = ctx.cnt["t"]
            ctx.wait("s", "t", nm)
            ctx.op("s", lambda s, dc=dc, pgb=pgb: s.activation(
                st["z"][:, dc, :], pgb[:], AF.Identity, bias=st["b2t"][:, dc:dc + 1]))
            evac_z[bk] = ctx.cnt["s"]
        ctx.wait("v", "s")
        for dc in range(DC):
            ctx.op("v", lambda v, dc=dc: v.tensor_add(st["z"][:, dc, :], st["z"][:, dc, :],
                                                      st["x1f"][:, dc, :]))
        ctx.wait("t", "v")
        emit_ln(ctx, st, st["z"], st["z"], NT, st["c_eps"], "ln")
        nv = ctx.cnt["v"]
        ctx.wait("y", "v", nv)
        ztok = ctx.op("y", lambda y, tsl=tsl: y.dma_start(
            out=st["x2T"].rearrange("(c p) t -> p c t", p=128)[:, :, tsl],
            in_=st["z"][:]), dma=True)
        evac_h = [None, None]
        evac_z = [None, None]


def build_C():
    return build_program("C", decl_C, body_C)





def decl_A(nc, st):
    dram_in(nc, st, "xT", [D, TOK])
    dram_in(nc, st, "Wvb", [D, 2176])
    dram_in(nc, st, "bvbf", [128, 1])
    dram_in(nc, st, "bvrow", [1, 2048])
    dram_in(nc, st, "gb", [128, 4])
    dram_out(nc, st, "vblk", [16, TOK, 128])
    dram_out(nc, st, "qf", [128, TOK])
    dram_out(nc, st, "kf", [128, TOK])

    sb(nc, st, "xbf", [128, DC, TOK], BF16)
    sb(nc, st, "wvb", [128, DC, 2176], BF16)
    sb(nc, st, "bcv", [128, 2048])
    sb(nc, st, "bvb", [128, 1])
    sb(nc, st, "bvr", [1, 2048])
    sb(nc, st, "gbt", [128, 4])
    sb(nc, st, "vt", [128, 512], BF16)
    sb(nc, st, "vz", [128, 512])
    sb(nc, st, "baset", [128, TOK])
    sb(nc, st, "qt", [128, TOK])
    sb(nc, st, "kt", [128, TOK])
    sb(nc, st, "ones_f1", [1, 128])
    ps(nc, st, "pv", [128, 512])
    ps(nc, st, "pb", [128, 512])
    ps(nc, st, "pbc", [128, 512])


def body_A(ctx, st):
    ctx.op("g", lambda g: g.memset(st["ones_f1"][:], 1.0))
    ctx.op("y", lambda y: y.dma_start(out=st["bvb"][:], in_=st["bvbf"]), dma=True)
    ctx.op("y", lambda y: y.dma_start(out=st["bvr"][:], in_=st["bvrow"]), dma=True)
    ctx.op("y", lambda y: y.dma_start(out=st["gbt"][:], in_=st["gb"]), dma=True)
    ctx.op("g", lambda g: g.dma_start(
        out=st["xbf"][:], in_=st["xT"].rearrange("(c p) t -> p c t", p=128)), dma=True)
    ctx.op("g", lambda g: g.dma_start(
        out=st["wvb"][:], in_=st["Wvb"].rearrange("(c p) f -> p c f", p=128)), dma=True)
    ctx.barrier("v", "s", "t", "g", "y")
    ctx.wait("v", "gd")
    ctx.wait("t", "gd")

    for ec in range(4):
        ctx.wait("t", "v")
        ctx.op("t", lambda t, ec=ec: t.matmul(
            st["pbc"][:], st["ones_f1"][:], st["bvr"][:, ec * 512:(ec + 1) * 512],
            start=True, stop=True))
        ctx.wait("v", "t")
        ctx.op("v", lambda v, ec=ec: v.tensor_copy(st["bcv"][:, ec * 512:(ec + 1) * 512],
                                                   st["pbc"][:]))
    ctx.wait("t", "v")

    for tk in range(16):
        for ec in range(4):
            for c in range(DC):
                ctx.op("t", lambda t, tk=tk, ec=ec, c=c: t.matmul(
                    st["pv"][:], st["xbf"][:, c, tk * 128:(tk + 1) * 128],
                    st["wvb"][:, c, ec * 512:(ec + 1) * 512],
                    start=(c == 0), stop=(c == DC - 1)), inc=(c == DC - 1))
            nm = ctx.cnt["t"]
            ctx.wait("v", "t", nm)
            ctx.wait("v", "s")
            ctx.op("v", lambda v, ec=ec: v.tensor_add(st["vz"][:], st["pv"][:],
                                                      st["bcv"][:, ec * 512:(ec + 1) * 512]))
            ctx.wait("s", "v")
            ctx.op("s", lambda s: s.activation(st["vt"][:], st["vz"][:], AF.Silu))
            nsi = ctx.cnt["s"]
            ctx.wait("g", "s", nsi)
            ctx.op("g", lambda g, tk=tk, ec=ec: g.dma_start(
                out=st["vblk"][ec * 4:(ec + 1) * 4, tk * 128:(tk + 1) * 128, :].rearrange(
                    "s p e -> p s e"),
                in_=st["vt"].ap().rearrange("p (s e) -> p s e", e=128)), dma=True)
            ctx.wait("t", "v")
            ctx.wait("s", "gd")
    for tc in range(4):
        tsl = slice(tc * 512, (tc + 1) * 512)
        for c in range(DC):
            ctx.op("t", lambda t, tc=tc, c=c: t.matmul(
                st["pb"][:], st["wvb"][:, c, 2048:2176],
                st["xbf"][:, c, tsl], start=(c == 0), stop=(c == DC - 1)),
                inc=(c == DC - 1))
        nm = ctx.cnt["t"]
        ctx.wait("s", "t", nm)
        ctx.op("s", lambda s, tsl=tsl: s.activation(
            st["baset"][:, tsl], st["pb"][:], AF.Silu, bias=st["bvb"][:]))
        ctx.wait("t", "s")
    ctx.wait("v", "s")
    ctx.op("v", lambda v: v.tensor_scalar(st["qt"][:], st["baset"][:], st["gbt"][:, 0:1],
                                          st["gbt"][:, 1:2], ALU.mult, ALU.add))
    ctx.op("v", lambda v: v.tensor_scalar(st["kt"][:], st["baset"][:], st["gbt"][:, 2:3],
                                          st["gbt"][:, 3:4], ALU.mult, ALU.add))
    nv = ctx.cnt["v"]
    ctx.wait("y", "v", nv)
    ctx.wait("g", "v", nv)
    ctx.op("y", lambda y: y.dma_start(out=st["qf"], in_=st["qt"][:]), dma=True)
    ctx.op("g", lambda g: g.dma_start(out=st["kf"], in_=st["kt"][:]), dma=True)


def build_A():
    return build_program("A", decl_A, body_A)



NQ = 256

def decl_B(nc, st):
    dram_in(nc, st, "xT", [D, TOK])
    dram_in(nc, st, "Wu", [D, 2048])
    dram_in(nc, st, "buf", [128, 16])
    dram_in(nc, st, "qf", [128, TOK])
    if "kb4" not in st:
        dram_in(nc, st, "kb", [128, S])
        dram_in(nc, st, "vblkb", [16, S, 128])
    dram_in(nc, st, "Wo", [2048, D])
    dram_in(nc, st, "bof", [128, 8])
    dram_out(nc, st, "x1T", [D, TOK])

    sb(nc, st, "xbf", [128, DC, TOK], BF16)
    sb(nc, st, "kbf", [128, S], BF16)
    sb(nc, st, "qbf", [128, TOK], BF16)
    sb(nc, st, "wstr", [128, 16, 1024], BF16)
    sb(nc, st, "uc", [128, 16, NQ], BF16)
    sb(nc, st, "sqT", [128, 64, NQ], BF16)
    sb(nc, st, "gc", [128, 16, NQ], BF16)
    sb(nc, st, "sqt", [128, NQ], BF16)
    sb(nc, st, "z", [128, DC, NQ])
    sb(nc, st, "x1f", [128, DC, NQ])
    sb(nc, st, "but", [128, 16])
    sb(nc, st, "bot", [128, 8])
    sb(nc, st, "ones_bf", [128, 1], BF16)
    sb(nc, st, "ones_f1", [1, 128])
    sb(nc, st, "c_eps", [1, 1])
    sb(nc, st, "ln_zbf", [128, DC, NQ], BF16)
    sb(nc, st, "ln_sq", [128, DC, NQ], BF16)
    sb(nc, st, "ln_rows", [1, 4 * NQ])
    ps(nc, st, "ln_s1", [1, NQ])
    ps(nc, st, "ln_s2", [1, NQ])
    ps(nc, st, "ln_bca", [128, NQ])
    ps(nc, st, "ln_bcb", [128, NQ])
    ps(nc, st, "pu", [128, NQ])
    ps(nc, st, "psc", [128, NQ])
    ps(nc, st, "pav", [128, NQ])
    ps(nc, st, "po", [128, NQ])


def body_B(ctx, st):
    load_consts(ctx, st, {"c_eps": 1e-5})
    ctx.op("g", lambda g: g.memset(st["ones_bf"][:], 1.0))
    ctx.op("g", lambda g: g.memset(st["ones_f1"][:], 1.0))
    ctx.op("y", lambda y: y.dma_start(out=st["but"][:], in_=st["buf"]), dma=True)
    ctx.op("y", lambda y: y.dma_start(out=st["bot"][:], in_=st["bof"]), dma=True)
    ctx.op("g", lambda g: g.dma_start(
        out=st["xbf"][:], in_=st["xT"].rearrange("(c p) t -> p c t", p=128)), dma=True)
    if "kb4" in st:
        for r in range(4):
            ctx.op("g", lambda g, r=r: g.dma_start(
                out=st["kbf"][:, r * TOK:(r + 1) * TOK], in_=st["kb4"][r]), dma=True)
    else:
        ctx.op("g", lambda g: g.dma_start(out=st["kbf"][:], in_=st["kb"]), dma=True)
    ctx.op("g", lambda g: g.dma_start(out=st["qbf"][:], in_=st["qf"]), dma=True)
    ctx.barrier("v", "s", "t", "g", "y")
    ctx.wait("v", "gd")
    ctx.wait("t", "gd")
    ctx.wait("s", "gd")

    for qc in range(TOK // NQ):
        qsl = slice(qc * NQ, (qc + 1) * NQ)
        ctx.barrier("v", "s", "t", "g", "y")
        ctx.op("g", lambda g: g.dma_start(
            out=st["wstr"].ap().rearrange("p a b -> p (a b)").rearrange("p (c f) -> p c f", c=DC),
            in_=st["Wu"].rearrange("(c p) f -> p c f", p=128)), dma=True)
        ctx.wait("t", "gd")
        wu = st["wstr"].ap().rearrange("p a b -> p (a b)").rearrange("p (c f) -> p c f", c=DC)
        for ec in range(16):
            for c in range(DC):
                ctx.op("t", lambda t, ec=ec, c=c, qsl=qsl, wu=wu: t.matmul(
                    st["pu"][:], wu[:, c, ec * 128:(ec + 1) * 128],
                    st["xbf"][:, c, qsl], start=(c == 0), stop=(c == DC - 1)),
                    inc=(c == DC - 1))
            nm = ctx.cnt["t"]
            ctx.wait("s", "t", nm)
            ctx.op("s", lambda s, ec=ec: s.activation(
                st["uc"][:, ec, :], st["pu"][:], AF.Silu, bias=st["but"][:, ec:ec + 1]))
            ctx.wait("t", "s")
        for kt in range(64):
            ctx.op("t", lambda t, kt=kt, qsl=qsl: t.matmul(
                st["psc"][:], st["kbf"][:, kt * 128:(kt + 1) * 128],
                st["qbf"][:, qsl], start=True, stop=True))
            nm = ctx.cnt["t"]
            ctx.wait("v", "t", nm)
            ctx.op("v", lambda v: v.tensor_scalar(st["sqt"][:], st["psc"][:],
                                                  0.0, 1.0 / S, ALU.max, ALU.mult))
            ctx.op("v", lambda v, kt=kt: v.tensor_mul(st["sqT"][:, kt, :], st["sqt"][:],
                                                      st["sqt"][:]))
            ctx.wait("t", "v")
        ctx.wait("g", "t")
        for ec in range(16):
            if "vb4" in st:
                for r in range(4):
                    ctx.op("g", lambda g, ec=ec, r=r: g.dma_start(
                        out=st["wstr"].ap().rearrange("p a b -> p (a b)")[:, r * 2048:(r + 1) * 2048].rearrange(
                            "p (kt e) -> p kt e", e=128),
                        in_=st["vb4"](r, ec)), dma=True)
            else:
                ctx.op("g", lambda g, ec=ec: g.dma_start(
                    out=st["wstr"].ap().rearrange("p a b -> p (a b)")[:, 0:8192].rearrange(
                        "p (kt e) -> p kt e", e=128),
                    in_=st["vblkb"][ec].rearrange("(kt p) e -> p kt e", p=128)), dma=True)
            ctx.wait("t", "gd")
            vt = st["wstr"].ap().rearrange("p a b -> p (a b)")[:, 0:8192].rearrange(
                "p (kt e) -> p kt e", e=128)
            for kt in range(64):
                ctx.op("t", lambda t, kt=kt, vt=vt: t.matmul(
                    st["pav"][:], vt[:, kt, :], st["sqT"][:, kt, :],
                    start=(kt == 0), stop=(kt == 63)), inc=(kt == 63))
            nm = ctx.cnt["t"]
            ctx.wait("v", "t", nm)
            ctx.op("v", lambda v, ec=ec: v.tensor_mul(st["gc"][:, ec, :], st["pav"][:],
                                                      st["uc"][:, ec, :]))
            nv = ctx.cnt["v"]
            ctx.wait("t", "v", nv)
            ctx.wait("g", "v", nv)
        ctx.op("g", lambda g: g.dma_start(
            out=st["wstr"][:], in_=st["Wo"].rearrange("(ec p) h -> p ec h", p=128)), dma=True)
        ctx.op("y", lambda y, qsl=qsl: y.dma_start(
            out=st["x1f"][:], in_=st["xT"].rearrange("(c p) t -> p c t", p=128)[:, :, qsl]),
            dma=True)
        ctx.wait("t", "gd")
        for dc in range(DC):
            for ec in range(16):
                ctx.op("t", lambda t, dc=dc, ec=ec: t.matmul(
                    st["po"][:], st["wstr"][:, ec, dc * 128:(dc + 1) * 128],
                    st["gc"][:, ec, :], start=(ec == 0), stop=(ec == 15)),
                    inc=(ec == 15))
            nm = ctx.cnt["t"]
            ctx.wait("s", "t", nm)
            ctx.op("s", lambda s, dc=dc: s.activation(
                st["z"][:, dc, :], st["po"][:], AF.Identity, bias=st["bot"][:, dc:dc + 1]))
            ctx.wait("t", "s")
        ctx.wait("v", "s")
        ctx.wait("v", "y")
        for dc in range(DC):
            ctx.op("v", lambda v, dc=dc: v.tensor_add(st["z"][:, dc, :], st["z"][:, dc, :],
                                                      st["x1f"][:, dc, :]))
        emit_ln(ctx, st, st["z"], st["z"], NQ, st["c_eps"], "ln")
        nv = ctx.cnt["v"]
        ctx.wait("y", "v", nv)
        ctx.op("y", lambda y, qsl=qsl: y.dma_start(
            out=st["x1T"].rearrange("(c p) t -> p c t", p=128)[:, :, qsl],
            in_=st["z"][:]), dma=True)


def build_B():
    return build_program("B", decl_B, body_B)





def decl_D1(nc, st):
    dram_in(nc, st, "x2T", [D, TOK])
    dram_in(nc, st, "nwf", [128, 8])
    dram_in(nc, st, "Wd", [D, D])
    dram_in(nc, st, "bdf", [128, 8])
    dram_in(nc, st, "WBC", [D, 32])
    dram_in(nc, st, "dskf", [128, 8])
    dram_out(nc, st, "deltaT", [D, TOK])
    dram_out(nc, st, "m1T", [D, TOK])
    dram_out(nc, st, "BpCT", [32, TOK])
    dram_out(nc, st, "yskipT", [D, TOK])

    NT = 512
    st["NT"] = NT
    sb(nc, st, "wd", [128, DC, D], BF16)
    sb(nc, st, "wbc", [128, DC, 32], BF16)
    sb(nc, st, "x2f", [128, DC, NT])
    sb(nc, st, "xwbf", [128, DC, NT], BF16)
    sb(nc, st, "sqb", [128, DC, NT], BF16)
    sb(nc, st, "brs", [128, NT])
    sb(nc, st, "zt", [128, NT])
    sb(nc, st, "abst", [128, NT])
    sb(nc, st, "relut", [128, NT])
    sb(nc, st, "spt", [128, NT])
    sb(nc, st, "dft", [128, NT])
    sb(nc, st, "sqd", [128, NT])
    sb(nc, st, "xnb", [128, NT], BF16)
    sb(nc, st, "dbo", [128, DC, NT])
    sb(nc, st, "m1o", [128, DC, NT])
    sb(nc, st, "yso", [128, DC, NT])
    sb(nc, st, "bpco", [32, NT])
    sb(nc, st, "rrow", [1, NT])
    sb(nc, st, "nwt", [128, 8])
    sb(nc, st, "bdt", [128, 8])
    sb(nc, st, "dskt", [128, 8])
    sb(nc, st, "ones_bf", [128, 1], BF16)
    sb(nc, st, "ones_f1", [1, 128])
    sb(nc, st, "c_eps6", [1, 1])
    ps(nc, st, "ps2", [1, NT])
    ps(nc, st, "pbc", [128, NT])
    ps(nc, st, "pdd", [128, NT])
    ps(nc, st, "pbp", [32, NT])


SC_C = (10.0 + 1e-4) / 2.0
SC_H = (10.0 - 1e-4) / 2.0


def body_D1(ctx, st):
    NT = st["NT"]
    load_consts(ctx, st, {"c_eps6": 1e-6})
    ctx.op("g", lambda g: g.memset(st["ones_bf"][:], 1.0))
    ctx.op("g", lambda g: g.memset(st["ones_f1"][:], 1.0))
    ctx.op("y", lambda y: y.dma_start(out=st["nwt"][:], in_=st["nwf"]), dma=True)
    ctx.op("y", lambda y: y.dma_start(out=st["bdt"][:], in_=st["bdf"]), dma=True)
    ctx.op("y", lambda y: y.dma_start(out=st["dskt"][:], in_=st["dskf"]), dma=True)
    ctx.op("g", lambda g: g.dma_start(
        out=st["wd"][:], in_=st["Wd"].rearrange("(c p) f -> p c f", p=128)), dma=True)
    ctx.op("g", lambda g: g.dma_start(
        out=st["wbc"][:], in_=st["WBC"].rearrange("(c p) f -> p c f", p=128)), dma=True)
    ctx.barrier("v", "s", "t", "g", "y")
    ctx.wait("t", "gd")

    for tc in range(TOK // NT):
        tsl = slice(tc * NT, (tc + 1) * NT)
        ctx.barrier("v", "s", "t", "g", "y")
        ctx.op("y", lambda y, tsl=tsl: y.dma_start(
            out=st["x2f"][:], in_=st["x2T"].rearrange("(c p) t -> p c t", p=128)[:, :, tsl]),
            dma=True)
        ctx.wait("v", "y")
        ctx.wait("s", "y")
        for c in range(DC):
            ctx.op("v", lambda v, c=c: v.tensor_scalar(
                st["xwbf"][:, c, :], st["x2f"][:, c, :], st["nwt"][:, c:c + 1], None, ALU.mult))
            ctx.op("s", lambda s, c=c: s.activation(st["sqb"][:, c, :], st["x2f"][:, c, :],
                                                    AF.Square))
        nv, ns = ctx.cnt["v"], ctx.cnt["s"]
        ctx.wait("t", "v", nv)
        ctx.wait("t", "s", ns)
        for c in range(DC):
            ctx.op("t", lambda t, c=c: t.matmul(st["ps2"][:], st["ones_bf"][:],
                                                st["sqb"][:, c, :],
                                                start=(c == 0), stop=(c == DC - 1)),
                   inc=(c == DC - 1))
        ctx.wait("s", "t")
        ctx.op("s", lambda s: s.activation(st["rrow"][:], st["ps2"][:], AF.Sqrt,
                                           scale=1.0 / D, bias=st["c_eps6"][:]))
        ctx.wait("v", "s")
        ctx.op("v", lambda v: v.reciprocal(st["rrow"][:], st["rrow"][:]))
        nv = ctx.cnt["v"]
        ctx.wait("t", "v", nv)
        ctx.op("t", lambda t: t.matmul(st["pbc"][:], st["ones_f1"][:], st["rrow"][:],
                                       start=True, stop=True))
        ctx.wait("v", "t")
        ctx.op("v", lambda v: v.tensor_copy(st["brs"][:], st["pbc"][:]))
        for c in range(DC):
            ctx.op("t", lambda t, c=c: t.matmul(st["pbp"][:], st["wbc"][:, c, :],
                                                st["xwbf"][:, c, :],
                                                start=(c == 0), stop=(c == DC - 1)),
                   inc=(c == DC - 1))
        ctx.wait("v", "t")
        ctx.op("v", lambda v: v.tensor_mul(st["bpco"][:], st["pbp"][:],
                                           st["brs"][0:32, :]))
        nv = ctx.cnt["v"]
        ctx.wait("g", "v", nv)
        if isinstance(st["BpCT"], list):
            tokbpc = ctx.op("g", lambda g, tc=tc: g.dma_start(
                out=st["BpCT"][tc], in_=st["bpco"][:]), dma=True)
        else:
            tokbpc = ctx.op("g", lambda g, tsl=tsl: g.dma_start(
                out=st["BpCT"][:, tsl], in_=st["bpco"][:]), dma=True)
        for dc in range(DC):
            for c in range(DC):
                ctx.op("t", lambda t, dc=dc, c=c: t.matmul(
                    st["pdd"][:], st["wd"][:, c, dc * 128:(dc + 1) * 128],
                    st["xwbf"][:, c, :], start=(c == 0), stop=(c == DC - 1)),
                    inc=(c == DC - 1))
            nm = ctx.cnt["t"]
            ctx.wait("v", "t", nm)
            ctx.op("v", lambda v: v.tensor_mul(st["zt"][:], st["pdd"][:], st["brs"][:]))
            nv = ctx.cnt["v"]
            ctx.wait("t", "v", nv)
            ctx.wait("s", "v", nv)
            ctx.op("s", lambda s, dc=dc: s.activation(st["abst"][:], st["zt"][:], AF.Abs,
                                                      bias=st["bdt"][:, dc:dc + 1]))
            ctx.op("s", lambda s: s.activation(st["abst"][:], st["abst"][:], AF.Exp,
                                               scale=-1.0))
            ctx.op("s", lambda s: s.activation(st["abst"][:], st["abst"][:], AF.Ln,
                                               bias=1.0))
            ctx.op("v", lambda v, dc=dc: v.tensor_scalar(st["relut"][:], st["zt"][:],
                                                         st["bdt"][:, dc:dc + 1], 0.0,
                                                         ALU.add, ALU.max))
            ctx.wait("v", "s")
            ctx.op("v", lambda v: v.tensor_add(st["spt"][:], st["relut"][:], st["abst"][:]))
            ctx.wait("s", "v")
            ctx.op("s", lambda s: s.activation(st["dft"][:], st["spt"][:], AF.Tanh,
                                               scale=1.0 / SC_H, bias=st["c_negch"][:]))
            ctx.wait("v", "s")
            ctx.op("v", lambda v: v.tensor_scalar(st["dft"][:], st["dft"][:], SC_H, SC_C,
                                                  ALU.mult, ALU.add))
            ctx.op("v", lambda v, dc=dc: v.tensor_copy(st["dbo"][:, dc, :], st["dft"][:]))
            ctx.wait("s", "v")
            ctx.op("s", lambda s: s.activation(st["sqd"][:], st["dft"][:], AF.Sqrt))
            ctx.op("v", lambda v, dc=dc: v.tensor_mul(st["xnb"][:], st["xwbf"][:, dc, :],
                                                      st["brs"][:]))
            ctx.wait("v", "s")
            ctx.op("v", lambda v, dc=dc: v.tensor_mul(st["m1o"][:, dc, :], st["sqd"][:],
                                                      st["xnb"][:]))
            ctx.op("v", lambda v, dc=dc: v.tensor_scalar(st["yso"][:, dc, :], st["xnb"][:],
                                                         st["dskt"][:, dc:dc + 1], None,
                                                         ALU.mult))
        nv, ns = ctx.cnt["v"], ctx.cnt["s"]
        ctx.wait("g", "v", nv)
        stoks = [tokbpc]
        for nm_, src in (("deltaT", "dbo"), ("m1T", "m1o"), ("yskipT", "yso")):
            if isinstance(st[nm_], list):
                stoks.append(ctx.op("g", lambda g, nm_=nm_, src=src, tc=tc: g.dma_start(
                    out=st[nm_][tc].rearrange("(c p) t -> p c t", p=128),
                    in_=st[src][:]), dma=True))
            else:
                stoks.append(ctx.op("g", lambda g, nm_=nm_, src=src, tsl=tsl: g.dma_start(
                    out=st[nm_].rearrange("(c p) t -> p c t", p=128)[:, :, tsl],
                    in_=st[src][:]), dma=True))
        if "cc_tc" in st:
            st["cc_tc"](ctx, tc, stoks)


def decl_D1x(nc, st):
    decl_D1(nc, st)
    sb(nc, st, "c_negch", [128, 1])


def body_D1x(ctx, st):
    load_consts(ctx, st, {"c_negch": -SC_C / SC_H})
    body_D1(ctx, st)


def build_D1():
    return build_program("D1", decl_D1x, body_D1x)



def decl_D2(nc, st):
    if "dmx_out" not in st:
        dram_in(nc, st, "deltaB", [D, S])
        dram_in(nc, st, "m1B", [D, S])
        dram_in(nc, st, "Bp4", [4, S])
        dram_in(nc, st, "C4", [4, S])
    else:
        dram_in(nc, st, "bpcsel", [32, 8])
    dram_in(nc, st, "A4", [D, 4])
    if "dmx_out" not in st:
        dram_out(nc, st, "ypart", [D, S])

    TB = 2048
    st["TB"] = TB
    sb(nc, st, "dsl", [128, DC, TB], BF16)
    sb(nc, st, "m1sl", [128, DC, TB], BF16)
    sb(nc, st, "ya", [128, DC, TB], BF16)
    sb(nc, st, "rt", [128, 2, TB])
    sb(nc, st, "at", [128, 2, TB])
    sb(nc, st, "c_neg1", [128, 1])
    sb(nc, st, "bt", [128, 2, TB], BF16)
    sb(nc, st, "ht", [128, 2, TB])
    sb(nc, st, "m2t", [128, 2, TB], BF16)
    sb(nc, st, "mt", [128, 2, TB], BF16)
    sb(nc, st, "bcB", [128, TB], BF16)
    sb(nc, st, "bcC", [128, TB], BF16)
    sb(nc, st, "bp1", [1, TB])
    sb(nc, st, "c1", [1, TB])
    sb(nc, st, "selc", [32, 8])
    sb(nc, st, "bpc32", [32, TB])
    sb(nc, st, "e2", [128, DC, 4])
    sb(nc, st, "carry", [128, 32])
    sb(nc, st, "ones_f1", [1, 128])
    ps(nc, st, "pb1", [128, 512])
    ps(nc, st, "pc1", [1, 512])


def body_D2(ctx, st):
    TB = st["TB"]
    ctx.op("g", lambda g: g.memset(st["ones_f1"][:], 1.0))
    ctx.op("g", lambda g: g.memset(st["carry"][:], 0.0))
    ctx.op("g", lambda g: g.memset(st["c_neg1"][:], -1.0))
    if "dmx_out" in st:
        ctx.op("y", lambda y: y.dma_start(out=st["selc"][:], in_=st["bpcsel"]), dma=True)
    ctx.op("y", lambda y: y.dma_start(
        out=st["e2"][:], in_=st["A4"].rearrange("(c p) n -> p c n", p=128)), dma=True)
    ctx.wait("s", "y")
    for c in range(DC):
        ctx.op("s", lambda s, c=c: s.activation(st["e2"][:, c, :], st["e2"][:, c, :], AF.Exp))
    ctx.wait("v", "s")
    ctx.op("v", lambda v: v.tensor_scalar(
        st["e2"].ap().rearrange("p a b -> p (a b)"),
        st["e2"].ap().rearrange("p a b -> p (a b)"), 0.5, None, ALU.mult))
    ctx.barrier("v", "s", "t", "g", "y")

    for tb in range(S // TB):
        bsl = slice(tb * TB, (tb + 1) * TB)
        ctx.barrier("v", "s", "t", "g", "y")
        if "dmx_out" in st:
            for i in range(4):
                isl = slice(i * 512, (i + 1) * 512)
                dmo = st["dmx_out"][i]
                ctx.op("g", lambda g, tb=tb, isl=isl, dmo=dmo: g.dma_start(
                    out=st["dsl"][:, :, isl],
                    in_=dmo[tb * 2080:tb * 2080 + 1024, :].rearrange(
                        "(c p) t -> p c t", p=128)), dma=True)
                ctx.op("g", lambda g, tb=tb, isl=isl, dmo=dmo: g.dma_start(
                    out=st["m1sl"][:, :, isl],
                    in_=dmo[tb * 2080 + 1024:tb * 2080 + 2048, :].rearrange(
                        "(c p) t -> p c t", p=128)), dma=True)
                ctx.op("g", lambda g, tb=tb, isl=isl, dmo=dmo: g.dma_start(
                    out=st["bpc32"][:, isl],
                    in_=dmo[tb * 2080 + 2048:tb * 2080 + 2080, :]), dma=True)
        else:
            ctx.op("g", lambda g, bsl=bsl: g.dma_start(
                out=st["dsl"][:], in_=st["deltaB"].rearrange("(c p) t -> p c t", p=128)[:, :, bsl]),
                dma=True)
            ctx.op("g", lambda g, bsl=bsl: g.dma_start(
                out=st["m1sl"][:], in_=st["m1B"].rearrange("(c p) t -> p c t", p=128)[:, :, bsl]),
                dma=True)
        ctx.wait("v", "gd")
        ctx.wait("s", "gd")
        ctx.wait("g", "gd")
        ctx.barrier("v", "s", "t", "g", "y")
        tiles = [(n, dc) for n in range(4) for dc in range(DC)]
        tok_r = {}
        tok_m2 = {}
        tok_ym = {}
        tok_yadd = {}
        tok_z = {}
        tok_scan = {}
        tok_bc = {}

        def emit_bc(n):
            if "dmx_out" in st:
                ctx.wait("t", "y")
                ctx.wait("t", "v")
                for half in range(TB // 512):
                    hsl = slice(half * 512, (half + 1) * 512)
                    ctx.op("t", lambda t, n=n, hsl=hsl: t.matmul(
                        st["pb1"][0:1, :], st["selc"][:, n:n + 1], st["bpc32"][:, hsl],
                        start=True, stop=True))
                    ctx.op("t", lambda t, n=n, hsl=hsl: t.matmul(
                        st["pc1"][0:1, :],
                        st["selc"][:, 4 + n:5 + n], st["bpc32"][:, hsl],
                        start=True, stop=True))
                    nm = ctx.cnt["t"]
                    ctx.wait("v", "t", nm)
                    ctx.op("v", lambda v, hsl=hsl: v.tensor_copy(st["bp1"][:, hsl], st["pb1"][0:1, :]))
                    ctx.op("v", lambda v, hsl=hsl: v.tensor_copy(st["c1"][:, hsl], st["pc1"][0:1, :]))
                    nv = ctx.cnt["v"]
                    ctx.wait("t", "v", nv)
            else:
                ctx.wait("y", "v")
                ctx.op("y", lambda y, n=n, bsl=bsl: y.dma_start(out=st["bp1"][:],
                                                                in_=st["Bp4"][n:n + 1, bsl]), dma=True)
                ctx.op("y", lambda y, n=n, bsl=bsl: y.dma_start(out=st["c1"][:],
                                                                in_=st["C4"][n:n + 1, bsl]), dma=True)
                ctx.wait("t", "y")
            for half in range(TB // 512):
                hsl = slice(half * 512, (half + 1) * 512)
                ctx.wait("t", "v")
                ctx.op("t", lambda t, hsl=hsl: t.matmul(
                    st["pb1"][:], st["ones_f1"][:], st["bp1"][:, hsl],
                    start=True, stop=True))
                ctx.wait("v", "t")
                ctx.op("v", lambda v, hsl=hsl: v.tensor_copy(st["bcB"][:, hsl], st["pb1"][:]))
                ctx.wait("t", "v")
                ctx.op("t", lambda t, hsl=hsl: t.matmul(
                    st["pb1"][:], st["ones_f1"][:], st["c1"][:, hsl],
                    start=True, stop=True))
                ctx.wait("v", "t")
                ctx.op("v", lambda v, hsl=hsl: v.tensor_copy(st["bcC"][:, hsl], st["pb1"][:]))

        def stage_z(i):
            n, dc = tiles[i]
            p = i % 2
            ctx.wait("g", "v", max(tok_bc[n], tok_scan.get(i - 2, 0)))
            ctx.op("g", lambda g, n=n, dc=dc, p=p: g.tensor_tensor(
                st["m2t"][:, p, :], st["m1sl"][:, dc, :], st["bcB"][:], ALU.mult))
            tok_m2[i] = ctx.cnt["g"]

        def stage_lnr(i):
            n, dc = tiles[i]
            p = i % 2
            if i >= 2:
                ctx.wait("s", "v", tok_scan[i - 2])
            ctx.op("s", lambda s, n=n, dc=dc, p=p: s.activation(
                st["rt"][:, p, :], st["dsl"][:, dc, :], AF.Ln,
                scale=st["e2"][:, dc, n:n + 1], bias=1.0))
            ctx.op("s", lambda s, p=p: s.activation(st["rt"][:, p, :], st["rt"][:, p, :],
                                                    AF.Exp, scale=-1.0))
            ctx.op("s", lambda s, p=p: s.activation(st["at"][:, p, :], st["rt"][:, p, :],
                                                    AF.Identity, scale=2.0,
                                                    bias=st["c_neg1"][:]))
            tok_r[i] = ctx.cnt["s"]

        def stage_v(i):
            n, dc = tiles[i]
            ct = n * 8 + dc
            p = i % 2
            ctx.wait("v", "s", tok_r[i])
            ctx.wait("v", "g", tok_m2[i])
            ctx.op("v", lambda v, p=p: v.tensor_mul(st["bt"][:, p, :], st["m2t"][:, p, :],
                                                    st["rt"][:, p, :]))
            ctx.op("v", lambda v, p=p, ct=ct: v.tensor_tensor_scan(
                st["ht"][:, p, :], st["at"][:, p, :], st["bt"][:, p, :],
                st["carry"][:, ct:ct + 1], ALU.mult, ALU.add))
            ctx.op("v", lambda v, p=p, ct=ct: v.tensor_copy(st["carry"][:, ct:ct + 1],
                                                            st["ht"][:, p, TB - 1:TB]))
            tok_scan[i] = ctx.cnt["v"]
            if i >= 2:
                ctx.wait("v", "g", tok_yadd.get(i - 2, 0))
            ctx.op("v", lambda v, p=p: v.tensor_mul(st["mt"][:, p, :], st["ht"][:, p, :],
                                                    st["bcC"][:]))
            tok_ym[i] = ctx.cnt["v"]

        def stage_yadd(i):
            n, dc = tiles[i]
            p = i % 2
            ctx.wait("g", "v", tok_ym[i])
            if n == 0:
                ctx.op("g", lambda g, dc=dc, p=p: g.tensor_copy(st["ya"][:, dc, :],
                                                                st["mt"][:, p, :]))
            else:
                ctx.op("g", lambda g, dc=dc, p=p: g.tensor_tensor(
                    st["ya"][:, dc, :], st["ya"][:, dc, :], st["mt"][:, p, :], ALU.add))
            tok_yadd[i] = ctx.cnt["g"]

        for n in range(4):
            if n > 0:
                ctx.wait("v", "g", tok_m2[n * 8 - 1])
                ctx.wait("t", "g", tok_m2[n * 8 - 1])
                ctx.wait("t", "v", tok_ym[n * 8 - 1])
            emit_bc(n)
            tok_bc[n] = ctx.cnt["v"]
            base = n * 8
            stage_z(base)
            stage_lnr(base)
            for dc in range(DC):
                i = base + dc
                if dc + 1 < DC:
                    stage_z(i + 1)
                    stage_lnr(i + 1)
                stage_v(i)
                stage_yadd(i)
        nv = ctx.cnt["v"]
        ctx.wait("g", "v", nv)
        if "dmx_out" in st:
            ctx.op("g", lambda g, tb=tb: g.dma_start(
                out=st["yp_in"][tb * 1024:(tb + 1) * 1024, :].rearrange(
                    "(c p) t -> p c t", p=128),
                in_=st["ya"][:]), dma=True)
        else:
            ctx.op("g", lambda g, bsl=bsl: g.dma_start(
                out=st["ypart"].rearrange("(c p) t -> p c t", p=128)[:, :, bsl],
                in_=st["ya"][:]), dma=True)
        ctx.wait("v", "gd")


def build_D2():
    return build_program("D2", decl_D2, body_D2)



def decl_D3(nc, st):
    dram_in(nc, st, "x2T", [D, TOK])
    if "D3_srcs" not in st:
        for i in range(4):
            dram_in(nc, st, f"yp{i}", [D, TOK])
    dram_in(nc, st, "yskipT", [D, TOK])
    dram_out(nc, st, "x3T", [D, TOK])

    NT = 512
    st["NT"] = NT
    sb(nc, st, "x2f", [128, DC, NT])
    sb(nc, st, "ypl", [128, 5, DC, NT])
    sb(nc, st, "z", [128, DC, NT])
    sb(nc, st, "ones_bf", [128, 1], BF16)
    sb(nc, st, "ones_f1", [1, 128])
    sb(nc, st, "c_eps", [1, 1])
    sb(nc, st, "ln_zbf", [128, DC, NT], BF16)
    sb(nc, st, "ln_sq", [128, DC, NT], BF16)
    sb(nc, st, "ln_rows", [1, 4 * NT])
    ps(nc, st, "ln_s1", [1, NT])
    ps(nc, st, "ln_s2", [1, NT])
    ps(nc, st, "ln_bca", [128, NT])
    ps(nc, st, "ln_bcb", [128, NT])


def body_D3(ctx, st):
    NT = st["NT"]
    load_consts(ctx, st, {"c_eps": 1e-5})
    ctx.op("g", lambda g: g.memset(st["ones_bf"][:], 1.0))
    ctx.op("g", lambda g: g.memset(st["ones_f1"][:], 1.0))
    for tc in range(TOK // NT):
        tsl = slice(tc * NT, (tc + 1) * NT)
        ctx.barrier("v", "s", "t", "g", "y")
        ctx.op("y", lambda y, tsl=tsl: y.dma_start(
            out=st["x2f"][:], in_=st["x2T"].rearrange("(c p) t -> p c t", p=128)[:, :, tsl]),
            dma=True)
        srcs = st.get("D3_srcs", ["yp0", "yp1", "yp2", "yp3", "yskipT"])
        for i, nm_ in enumerate(srcs):
            ctx.op("g", lambda g, i=i, nm_=nm_, tsl=tsl: g.dma_start(
                out=st["ypl"][:, i, :, :],
                in_=st[nm_].rearrange("(c p) t -> p c t", p=128)[:, :, tsl]), dma=True)
        ctx.wait("v", "y")
        ctx.wait("v", "gd")
        for dc in range(DC):
            ctx.op("v", lambda v, dc=dc: v.tensor_add(st["z"][:, dc, :], st["x2f"][:, dc, :],
                                                      st["ypl"][:, 0, dc, :]))
            for i in range(1, len(srcs)):
                ctx.op("v", lambda v, dc=dc, i=i: v.tensor_add(
                    st["z"][:, dc, :], st["z"][:, dc, :], st["ypl"][:, i, dc, :]))
        emit_ln(ctx, st, st["z"], st["z"], NT, st["c_eps"], "ln")
        nv = ctx.cnt["v"]
        ctx.wait("y", "v", nv)
        ctx.op("y", lambda y, tsl=tsl: y.dma_start(
            out=st["x3T"].rearrange("(c p) t -> p c t", p=128)[:, :, tsl],
            in_=st["z"][:]), dma=True)


def build_D3():
    return build_program("D3", decl_D3, body_D3)


def stage_D1(x2T_cores, inp):
    nwf = np.ascontiguousarray(inp["L1_attn_norm_weight"].reshape(8, 128).T)
    bdf = np.ascontiguousarray(inp["L1_attn_b_delta"].reshape(8, 128).T)
    dskf = np.ascontiguousarray(inp["L1_attn_D_skip"].reshape(8, 128).T)
    WBC = np.ascontiguousarray(np.concatenate([inp["L1_attn_W_B"], inp["L1_attn_W_C"]], axis=1))
    Wd = np.ascontiguousarray(inp["L1_attn_W_delta"])
    return [{"x2T": x2T_cores[i], "nwf": nwf, "Wd": Wd, "bdf": bdf, "WBC": WBC,
             "dskf": dskf} for i in range(NCORE)]


def stage_D2(resD1, inp):
    maps = []
    A = np.ascontiguousarray(inp["L1_attn_A_log"])
    for i in range(NCORE):
        b, nq = i // 4, i % 4
        grp = [4 * b + j for j in range(4)]
        deltaB = np.concatenate([np.asarray(resD1[j]["deltaT"]) for j in grp], axis=1)
        m1B = np.concatenate([np.asarray(resD1[j]["m1T"]) for j in grp], axis=1)
        BpC = np.concatenate([np.asarray(resD1[j]["BpCT"]) for j in grp], axis=1)
        Bp4 = np.ascontiguousarray(BpC[nq * 4:(nq + 1) * 4, :])
        C4 = np.ascontiguousarray(BpC[16 + nq * 4:16 + (nq + 1) * 4, :])
        A4 = np.ascontiguousarray(A[:, nq * 4:(nq + 1) * 4])
        maps.append({"deltaB": np.ascontiguousarray(deltaB),
                     "m1B": np.ascontiguousarray(m1B),
                     "Bp4": Bp4, "C4": C4, "A4": A4})
    return maps


def stage_D3(x2T_cores, resD1, resD2):
    maps = []
    for i in range(NCORE):
        b, qd = i // 4, i % 4
        tsl = slice(qd * TOK, (qd + 1) * TOK)
        m = {"x2T": x2T_cores[i], "yskipT": np.asarray(resD1[i]["yskipT"])}
        for nq in range(4):
            src = np.asarray(resD2[4 * b + nq]["ypart"])[:, tsl]
            m[f"yp{nq}"] = np.ascontiguousarray(src)
        maps.append(m)
    return maps





def decl_E1(nc, st):
    dram_in(nc, st, "x3T", [D, TOK])
    dram_in(nc, st, "gwT", [D, 8])
    dram_out(nc, st, "masko", [16, 128, 8])
    dram_out(nc, st, "combpre", [16, 128, 8])

    sb(nc, st, "x3f32", [128, DC, TOK])
    sb(nc, st, "gwf", [128, DC, 8])
    sb(nc, st, "lg", [128, 8])
    sb(nc, st, "m8", [128, 8])
    sb(nc, st, "negm", [128, 1])
    sb(nc, st, "et", [128, 8])
    sb(nc, st, "srow", [128, 1])
    sb(nc, st, "mask", [128, 8])
    sb(nc, st, "pm", [128, 8])
    sb(nc, st, "cp", [128, 8])
    sb(nc, st, "c_tiny", [128, 1])
    ps(nc, st, "plg", [128, 8])


def body_E1(ctx, st):
    load_consts(ctx, st, {"c_tiny": 1e-20})
    ctx.op("y", lambda y: y.dma_start(
        out=st["x3f32"][:], in_=st["x3T"].rearrange("(c p) t -> p c t", p=128)), dma=True)
    ctx.op("y", lambda y: y.dma_start(
        out=st["gwf"][:], in_=st["gwT"].rearrange("(c p) e -> p c e", p=128)), dma=True)
    ctx.barrier("v", "s", "t", "g", "y")
    for tt in range(16):
        tl = slice(tt * 128, (tt + 1) * 128)
        ctx.wait("t", "v")
        for c in range(DC):
            ctx.op("t", lambda t, c=c, tl=tl: t.matmul(
                st["plg"][:], st["x3f32"][:, c, tl], st["gwf"][:, c, :],
                start=(c == 0), stop=(c == DC - 1)), inc=(c == DC - 1))
        nm = ctx.cnt["t"]
        ctx.wait("v", "t", nm)
        ctx.op("v", lambda v: v.tensor_copy(st["lg"][:], st["plg"][:]))
        ctx.op("v", lambda v: v.max(out=st["m8"][:], in_=st["lg"][:]))
        ctx.op("v", lambda v: v.tensor_scalar(st["negm"][:], st["m8"][:, 0:1], -1.0, None,
                                              ALU.mult))
        ctx.wait("s", "v")
        ctx.op("s", lambda s: s.activation(st["et"][:], st["lg"][:], AF.Exp,
                                           bias=st["negm"][:]))
        ctx.wait("v", "s")
        ctx.op("v", lambda v: v.tensor_reduce(st["srow"][:], st["et"][:], AX.X, ALU.add))
        ctx.op("v", lambda v: v.reciprocal(st["srow"][:], st["srow"][:]))
        ctx.op("v", lambda v: v.tensor_scalar(st["mask"][:], st["lg"][:], st["m8"][:, 1:2],
                                              None, ALU.is_ge))
        ctx.op("v", lambda v: v.tensor_mul(st["pm"][:], st["et"][:], st["mask"][:]))
        ctx.op("v", lambda v: v.tensor_scalar(st["pm"][:], st["pm"][:], st["srow"][:], None,
                                              ALU.mult))
        ctx.op("v", lambda v: v.tensor_reduce(st["srow"][:], st["pm"][:], AX.X, ALU.add))
        ctx.op("v", lambda v: v.tensor_scalar(st["srow"][:], st["srow"][:], st["c_tiny"][:],
                                              None, ALU.add))
        ctx.op("v", lambda v: v.reciprocal(st["srow"][:], st["srow"][:]))
        ctx.op("v", lambda v: v.tensor_scalar(st["cp"][:], st["pm"][:], st["srow"][:], None,
                                              ALU.mult))
        nv = ctx.cnt["v"]
        ctx.wait("y", "v", nv)
        ctx.op("y", lambda y, tt=tt: y.dma_start(out=st["masko"][tt], in_=st["mask"][:]),
               dma=True)
        ctx.op("y", lambda y, tt=tt: y.dma_start(out=st["combpre"][tt], in_=st["cp"][:]),
               dma=True)
        ctx.wait("v", "y")


def build_E1():
    return build_program("E1", decl_E1, body_E1)



def decl_E2(nc, st):
    dram_in(nc, st, "x3T", [D, TOK])
    dram_in(nc, st, "combpre", [16, 128, 8])
    dram_in(nc, st, "maskown", [16, 128, 8])
    if "maskg" not in st:
        dram_in(nc, st, "maskprev", [112, 128, 8])
    else:
        dram_in(nc, st, "rowsel", [128, 1])
    dram_in(nc, st, "wgT", [8, D, 1024])
    dram_in(nc, st, "wuT", [8, D, 1024])
    dram_in(nc, st, "wdT", [8, 1024, D])
    dram_in(nc, st, "wsgT", [D, 2048])
    dram_in(nc, st, "wsuT", [D, 2048])
    dram_in(nc, st, "wsdT", [2048, D])
    dram_out(nc, st, "x4T", [D, TOK])
    st["combd"] = nc.dram_tensor("combd", [8, TOK], F32, kind="Internal").ap()

    NT = 512
    st["NT"] = NT
    if "webf2" not in st:
        st["webf2"] = nc.dram_tensor("webf2", [10, 3, 1024, 1024], BF16, kind="Internal").ap()
        st["E2_precast"] = True
    sb(nc, st, "x3bf", [128, DC, TOK], BF16)
    sb(nc, st, "acte", [128, 8, NT], BF16)
    sb(nc, st, "warena", [128, 2, 3, 8, 1024], BF16)
    sb(nc, st, "ysb", [128, DC, NT])
    sb(nc, st, "x3f", [128, DC, NT])
    if "maskg" in st:
        sb(nc, st, "mpre", [128, 128, 8], BF16)
        sb(nc, st, "rst", [128, 1])
    else:
        sb(nc, st, "mpre", [128, 112, 8], BF16)
    sb(nc, st, "mtile", [128, 8], BF16)
    sb(nc, st, "prow", [128, 8])
    sb(nc, st, "post", [128, 8])
    sb(nc, st, "cpre", [128, 8])
    sb(nc, st, "combm", [128, 16, 8])
    sb(nc, st, "ctile", [8, 128])
    sb(nc, st, "combrow", [1, NT])
    sb(nc, st, "onesrow", [1, NT])
    sb(nc, st, "offs", [1, 8])
    sb(nc, st, "trow", [1, 8])
    sb(nc, st, "tri", [128, 128], BF16)
    sb(nc, st, "trif", [128, 128])
    sb(nc, st, "idf", [128, 128])
    sb(nc, st, "sgt", [128, 2, NT], BF16)
    sb(nc, st, "ones_bf", [128, 1], BF16)
    sb(nc, st, "ones_f1", [1, 128])
    sb(nc, st, "ones_c", [128, 1])
    sb(nc, st, "c_eps", [1, 1])
    sb(nc, st, "ln_zbf", [128, DC, NT], BF16)
    sb(nc, st, "ln_sq", [128, DC, NT], BF16)
    sb(nc, st, "ln_rows", [1, 4 * NT])
    ps(nc, st, "pg", [128, NT])
    ps(nc, st, "pg2", [128, NT])
    ps(nc, st, "pu2", [128, NT])
    ps(nc, st, "pu2b", [128, NT])
    ps(nc, st, "pd", [128, NT])
    ps(nc, st, "pd2", [128, NT])
    ps(nc, st, "pbc2", [128, NT])
    st["ln_s1"] = st["pg"][0:1, :]
    st["ln_s2"] = st["pg2"][0:1, :]
    st["ln_bca"] = st["pu2"][:]
    st["ln_bcb"] = st["pu2b"][:]


def emit_precast_E2(ctx, st):
    w2 = st["webf2"]
    for e in range(8):
        for j, src in ((0, st["wgT"]), (1, st["wuT"]), (2, st["wdT"])):
            ctx.op("g", lambda g, e=e, j=j, src=src: g.dma_start(
                out=w2[e, j], in_=src[e]), dma=True)
    for es in range(2):
        csl = slice(es * 1024, (es + 1) * 1024)
        ctx.op("g", lambda g, csl=csl, es=es: g.dma_start(
            out=w2[8 + es, 0], in_=st["wsgT"][:, csl]), dma=True)
        ctx.op("g", lambda g, csl=csl, es=es: g.dma_start(
            out=w2[8 + es, 1], in_=st["wsuT"][:, csl]), dma=True)
        ctx.op("g", lambda g, csl=csl, es=es: g.dma_start(
            out=w2[8 + es, 2], in_=st["wsdT"][csl, :]), dma=True)


def body_E2(ctx, st):
    NT = st["NT"]
    load_consts(ctx, st, {"c_eps": 1e-5})
    ctx.op("g", lambda g: g.memset(st["ones_bf"][:], 1.0))
    ctx.op("g", lambda g: g.memset(st["ones_f1"][:], 1.0))
    ctx.op("g", lambda g: g.memset(st["ones_c"][:], 1.0))
    ctx.op("g", lambda g: g.dma_start(
        out=st["x3bf"][:], in_=st["x3T"].rearrange("(c p) t -> p c t", p=128)), dma=True)
    if "maskg" in st:
        ctx.op("g", lambda g: g.dma_start(
            out=st["mpre"][:, 0:128, :],
            in_=st["maskg"].rearrange("(a p) e -> a p e", p=128)), dma=True)
        ctx.op("y", lambda y: y.dma_start(out=st["rst"][:], in_=st["rowsel"]), dma=True)
    else:
        ctx.op("g", lambda g: g.dma_start(
            out=st["mpre"][:], in_=st["maskprev"].rearrange("a p e -> p a e")), dma=True)
    ctx.op("g", lambda g: g.memset(st["trif"][:], 1.0))
    ctx.op("g", lambda g: g.affine_select(st["trif"][:], st["trif"][:], pattern=[[1, 128]],
                                          compare_op=ALU.is_gt, fill=0.0, base=0,
                                          channel_multiplier=-1))
    ctx.op("g", lambda g: g.memset(st["idf"][:], 1.0))
    ctx.op("g", lambda g: g.affine_select(st["idf"][:], st["idf"][:], pattern=[[1, 128]],
                                          compare_op=ALU.is_equal, fill=0.0, base=0,
                                          channel_multiplier=-1))
    ctx.op("g", lambda g: g.dma_start(out=st["tri"][:], in_=st["trif"][:]), dma=True)
    ctx.barrier("v", "s", "t", "g", "y")
    ctx.wait("v", "gd")
    ctx.wait("t", "gd")

    if "maskg" in st:
        ctx.wait("v", "y")
        ctx.op("v", lambda v: v.tensor_scalar(
            st["mpre"].ap().rearrange("a p e -> a (p e)"),
            st["mpre"].ap().rearrange("a p e -> a (p e)"),
            st["rst"][:], None, ALU.mult))
        ctx.op("v", lambda v: v.tensor_reduce(
            st["prow"][:], st["mpre"].ap().rearrange("a p e -> a e p"),
            AX.X, ALU.add))
    else:
        ctx.op("v", lambda v: v.tensor_reduce(
            st["prow"][:], st["mpre"].ap().rearrange("p a e -> p e a"), AX.X, ALU.add))
    nv = ctx.cnt["v"]
    ctx.wait("t", "v", nv)
    ctx.op("t", lambda t: t.matmul(st["ln_s1"][:, 0:8], st["ones_c"][:], st["prow"][:],
                                   start=True, stop=True))
    ctx.wait("v", "t")
    ctx.op("v", lambda v: v.tensor_copy(st["offs"][:], st["ln_s1"][:, 0:8]))
    for tt in range(16):
        ctx.wait("y", "v")
        ctx.wait("g", "v")
        ctx.op("g", lambda g, tt=tt: g.dma_start(out=st["mtile"][:], in_=st["maskown"][tt]),
               dma=True)
        ctx.op("y", lambda y, tt=tt: y.dma_start(out=st["cpre"][:], in_=st["combpre"][tt]),
               dma=True)
        ctx.wait("t", "y")
        ctx.wait("t", "gd")
        ctx.wait("t", "v")
        ctx.op("t", lambda t: t.matmul(st["pg"][:, 0:8], st["tri"][:], st["mtile"][:],
                                       start=True, stop=True))
        ctx.op("t", lambda t: t.matmul(st["pbc2"][:, 0:8], st["ones_f1"][:], st["offs"][:],
                                       start=True, stop=True))
        nm = ctx.cnt["t"]
        ctx.wait("v", "t", nm)
        ctx.op("v", lambda v: v.tensor_copy(st["post"][:], st["pg"][:, 0:8]))
        ctx.op("v", lambda v: v.tensor_add(st["post"][:], st["post"][:], st["pbc2"][:, 0:8]))
        ctx.op("v", lambda v: v.tensor_scalar(st["post"][:], st["post"][:], float(CAP), None,
                                              ALU.is_lt))
        ctx.op("v", lambda v, tt=tt: v.tensor_mul(st["combm"][:, tt, :], st["cpre"][:],
                                                  st["post"][:]))
        ctx.op("t", lambda t: t.matmul(st["ln_s2"][:, 0:8], st["ones_bf"][:], st["mtile"][:],
                                       start=True, stop=True, skip_group_check=True))
        ctx.wait("v", "t")
        ctx.op("v", lambda v: v.tensor_copy(st["trow"][:], st["ln_s2"][:, 0:8]))
        ctx.op("v", lambda v: v.tensor_add(st["offs"][:], st["offs"][:], st["trow"][:]))
    for tt in range(16):
        ctx.wait("t", "v")
        ctx.op("t", lambda t, tt=tt: t.transpose(st["pbc2"][0:8, 0:128],
                                                 st["combm"][:, tt, :], st["idf"][:]))
        nm = ctx.cnt["t"]
        ctx.wait("v", "t", nm)
        ctx.op("v", lambda v: v.tensor_copy(st["ctile"][:], st["pbc2"][0:8, 0:128]))
        nv = ctx.cnt["v"]
        ctx.wait("y", "v", nv)
        ctx.op("y", lambda y, tt=tt: y.dma_start(
            out=st["combd"][:, tt * 128:(tt + 1) * 128], in_=st["ctile"][:]), dma=True)
        ctx.wait("v", "y")

    if st.get("E2_precast"):
        emit_precast_E2(ctx, st)
    w2 = st["webf2"]
    wa = st["warena"]

    def load_w(e, buf):
        toks = []
        for j in range(3):
            toks.append(ctx.op("g", lambda g, e=e, j=j, buf=buf: g.dma_start(
                out=wa[:, buf, j, :, :],
                in_=w2[e, j].rearrange("(c p) i -> p c i", p=128)), dma=True))
        return toks

    for tc in range(TOK // NT):
        tsl = slice(tc * NT, (tc + 1) * NT)
        ctx.barrier("v", "s", "t", "g", "y")
        ctx.op("y", lambda y, tsl=tsl: y.dma_start(
            out=st["x3f"][:], in_=st["x3T"].rearrange("(c p) t -> p c t", p=128)[:, :, tsl]),
            dma=True)
        ctx.op("v", lambda v: v.memset(st["onesrow"][:], 1.0))
        wtoks = {0: load_w(0, 0)}
        evac_gu = [None, None]
        evac_pd = [None, None]
        last_acte_v = None
        last_down_t = None
        for e in range(10):
            if e + 1 < 10:
                if e >= 1:
                    ctx.wait("g", "t")
                wtoks[e + 1] = load_w(e + 1, (e + 1) % 2)
            buf = e % 2
            for tok in wtoks[e]:
                ctx.wait_tok("t", tok)
            if e < 8:
                ctx.wait("y", "t")
                cr = ctx.op("y", lambda y, e=e, tsl=tsl: y.dma_start(
                    out=st["combrow"][:], in_=st["combd"][e:e + 1, tsl]), dma=True)
                ctx.wait_tok("t", cr)
                crow = st["combrow"]
            else:
                ctx.wait("t", "v")
                crow = st["onesrow"]
            if last_acte_v is not None:
                ctx.wait("t", "v", last_acte_v)
            ctx.op("t", lambda t, crow=crow: t.matmul(
                st["pbc2"][:], st["ones_f1"][:], crow[:], start=True, stop=True))
            for ic in range(8):
                bk = ic % 2
                pgb = st["pg"] if bk == 0 else st["pg2"]
                pub = st["pu2"] if bk == 0 else st["pu2b"]
                if evac_gu[bk] is not None:
                    ctx.wait("t", "v", evac_gu[bk])
                for c in range(DC):
                    ctx.op("t", lambda t, ic=ic, c=c, tsl=tsl, buf=buf, pgb=pgb: t.matmul(
                        pgb[:], wa[:, buf, 0, c, ic * 128:(ic + 1) * 128],
                        st["x3bf"][:, c, tsl], start=(c == 0), stop=(c == DC - 1)),
                        inc=(c == DC - 1))
                for c in range(DC):
                    ctx.op("t", lambda t, ic=ic, c=c, tsl=tsl, buf=buf, pub=pub: t.matmul(
                        pub[:], wa[:, buf, 1, c, ic * 128:(ic + 1) * 128],
                        st["x3bf"][:, c, tsl], start=(c == 0), stop=(c == DC - 1)),
                        inc=(c == DC - 1))
                nm = ctx.cnt["t"]
                ctx.wait("s", "t", nm)
                if evac_gu[bk] is not None:
                    ctx.wait("s", "v", evac_gu[bk])
                ctx.op("s", lambda s, pgb=pgb, bk=bk: s.activation(st["sgt"][:, bk, :],
                                                                   pgb[:], AF.Silu))
                ctx.wait("v", "s")
                if ic == 0 and last_down_t is not None:
                    ctx.wait("v", "t", last_down_t)
                ctx.op("v", lambda v, ic=ic, pub=pub, bk=bk: v.tensor_mul(
                    st["acte"][:, ic, :], st["sgt"][:, bk, :], pub[:]))
                ctx.op("v", lambda v, ic=ic: v.tensor_mul(
                    st["acte"][:, ic, :], st["acte"][:, ic, :], st["pbc2"][:]))
                evac_gu[bk] = ctx.cnt["v"]
            last_acte_v = ctx.cnt["v"]
            ctx.wait("t", "v", last_acte_v)
            for dc in range(DC):
                bk = dc % 2
                pdb = st["pd"] if bk == 0 else st["pd2"]
                if evac_pd[bk] is not None:
                    ctx.wait("t", "v", evac_pd[bk])
                for ic in range(8):
                    ctx.op("t", lambda t, ic=ic, dc=dc, buf=buf, pdb=pdb: t.matmul(
                        pdb[:], wa[:, buf, 2, ic, dc * 128:(dc + 1) * 128],
                        st["acte"][:, ic, :], start=(ic == 0), stop=(ic == 7)),
                        inc=(ic == 7))
                nm = ctx.cnt["t"]
                ctx.wait("v", "t", nm)
                if e == 0:
                    ctx.op("v", lambda v, dc=dc, pdb=pdb: v.tensor_copy(st["ysb"][:, dc, :],
                                                                        pdb[:]))
                else:
                    ctx.op("v", lambda v, dc=dc, pdb=pdb: v.tensor_add(
                        st["ysb"][:, dc, :], st["ysb"][:, dc, :], pdb[:]))
                evac_pd[bk] = ctx.cnt["v"]
            last_down_t = ctx.cnt["t"]
        ctx.wait("v", "y")
        for dc in range(DC):
            ctx.op("v", lambda v, dc=dc: v.tensor_add(st["ysb"][:, dc, :], st["ysb"][:, dc, :],
                                                      st["x3f"][:, dc, :]))
        ctx.wait("t", "v")
        emit_ln(ctx, st, st["ysb"], st["ysb"], NT, st["c_eps"], "ln")
        nv = ctx.cnt["v"]
        ctx.wait("y", "v", nv)
        ctx.op("y", lambda y, tsl=tsl: y.dma_start(
            out=st["x4T"].rearrange("(c p) t -> p c t", p=128)[:, :, tsl],
            in_=st["ysb"][:]), dma=True)


def build_E2():
    return build_program("E2", decl_E2, body_E2)


def stage_E1(x3T_cores, inp):
    gwT = np.ascontiguousarray(inp["L1_ffn_gate_w"].T)
    return [{"x3T": x3T_cores[i], "gwT": gwT} for i in range(NCORE)]


def stage_E2(x3T_cores, resE1, inp):
    wgT = np.ascontiguousarray(inp["L1_ffn_we_gate"].transpose(0, 2, 1))
    wuT = np.ascontiguousarray(inp["L1_ffn_we_up"].transpose(0, 2, 1))
    wdT = np.ascontiguousarray(inp["L1_ffn_we_down"].transpose(0, 2, 1))
    wsgT = np.ascontiguousarray(inp["L1_ffn_ws_gate"].T)
    wsuT = np.ascontiguousarray(inp["L1_ffn_ws_up"].T)
    wsdT = np.ascontiguousarray(inp["L1_ffn_ws_down"].T)
    maps = []
    for i in range(NCORE):
        mprev = np.zeros((112, 128, 8), np.float32)
        if i > 0:
            prev = np.concatenate([np.asarray(resE1[j]["masko"]) for j in range(i)], axis=0)
            mprev[:prev.shape[0]] = prev
        maps.append({"x3T": x3T_cores[i], "combpre": np.asarray(resE1[i]["combpre"]),
                     "maskown": np.asarray(resE1[i]["masko"]), "maskprev": mprev,
                     "wgT": wgT, "wuT": wuT, "wdT": wdT, "wsgT": wsgT, "wsuT": wsuT,
                     "wsdT": wsdT})
    return maps



def _fm(a):
    return np.ascontiguousarray(a.T)


def run_launch(nc, in_maps, outs=None):
    res = run_bass_kernel_spmd(nc, in_maps, core_ids=list(range(NCORE)))
    return res.results


def stage_C(x1T_cores, W1, b1, W2, b2):
    W1T = np.ascontiguousarray(W1.T)
    W2T = np.ascontiguousarray(W2.T)
    b1f = np.ascontiguousarray(b1.reshape(32, 128).T)
    b2f = np.ascontiguousarray(b2.reshape(8, 128).T)
    return [{"x1T": x1T_cores[i], "W1T": W1T, "b1f": b1f, "W2T": W2T, "b2f": b2f}
            for i in range(NCORE)]


_NC_CACHE = {}


def _get_nc(name, builder):
    if name not in _NC_CACHE:
        _NC_CACHE[name] = builder()
    return _NC_CACHE[name]


def kernel(**inputs):
    return kernel_fused(**inputs)


def kernel_multilaunch(**inputs):
    inp = inputs
    x = np.ascontiguousarray(inp["L0_attn_inputs"], dtype=np.float32)
    xT_cores = []
    for c in range(NCORE):
        b, qd = c // 4, c % 4
        xT_cores.append(_fm(x[b, qd * TOK:(qd + 1) * TOK, :]))

    ncA = _get_nc("A", build_A)
    resA = run_launch(ncA, stage_A(xT_cores, inp["L0_attn_W_uv"], inp["L0_attn_b_uv"],
                                   inp["L0_attn_gamma"], inp["L0_attn_beta"]))
    ncB = _get_nc("B", build_B)
    resB = run_launch(ncB, stage_B(xT_cores, resA, inp["L0_attn_W_uv"], inp["L0_attn_b_uv"],
                                   inp["L0_attn_W_out"], inp["L0_attn_b_out"]))
    x1T_cores = [np.ascontiguousarray(np.asarray(resB[i]["x1T"])) for i in range(NCORE)]

    ncC = _get_nc("C", build_C)
    resC = run_launch(ncC, stage_C(x1T_cores, inp["L0_ffn_W1"], inp["L0_ffn_b1"],
                                   inp["L0_ffn_W2"], inp["L0_ffn_b2"]))
    x2T_cores = [np.ascontiguousarray(np.asarray(resC[i]["x2T"])) for i in range(NCORE)]

    ncD1 = _get_nc("D1", build_D1)
    resD1 = run_launch(ncD1, stage_D1(x2T_cores, inp))
    ncD2 = _get_nc("D2", build_D2)
    resD2 = run_launch(ncD2, stage_D2(resD1, inp))
    ncD3 = _get_nc("D3", build_D3)
    resD3 = run_launch(ncD3, stage_D3(x2T_cores, resD1, resD2))
    x3T_cores = [np.ascontiguousarray(np.asarray(resD3[i]["x3T"])) for i in range(NCORE)]

    ncE1 = _get_nc("E1", build_E1)
    resE1 = run_launch(ncE1, stage_E1(x3T_cores, inp))
    ncE2 = _get_nc("E2", build_E2)
    resE2 = run_launch(ncE2, stage_E2(x3T_cores, resE1, inp))

    out = np.empty((2, S, D), np.float32)
    for c in range(NCORE):
        b, qd = c // 4, c % 4
        out[b, qd * TOK:(qd + 1) * TOK, :] = np.asarray(resE2[c]["x4T"]).T
    return out


def stage_A(xT_cores, W_uv, b_uv, gamma, beta):
    Wvb = np.ascontiguousarray(W_uv[:, 2048:4224])
    bvbf = np.ascontiguousarray(b_uv[4096:4224].reshape(1, 128).T)
    bvrow = np.ascontiguousarray(b_uv[2048:4096].reshape(1, 2048))
    gb = np.ascontiguousarray(np.stack([gamma[0], beta[0], gamma[1], beta[1]], axis=1))
    return [{"xT": xT_cores[i], "Wvb": Wvb, "bvbf": bvbf, "bvrow": bvrow, "gb": gb}
            for i in range(NCORE)]


def stage_B(xT_cores, resA, W_uv, b_uv, W_out, b_out):
    Wu = np.ascontiguousarray(W_uv[:, 0:2048])
    buf = np.ascontiguousarray(b_uv[0:2048].reshape(16, 128).T)
    bof = np.ascontiguousarray(b_out.reshape(8, 128).T)
    maps = []
    for i in range(NCORE):
        b = i // 4
        grp = [4 * b + j for j in range(4)]
        kb = np.concatenate([np.asarray(resA[j]["kf"]) for j in grp], axis=1)
        vblkb = np.concatenate([np.asarray(resA[j]["vblk"]) for j in grp], axis=1)
        maps.append({"xT": xT_cores[i], "Wu": Wu, "buf": buf,
                     "qf": np.asarray(resA[i]["qf"]), "kb": kb,
                     "vblkb": np.ascontiguousarray(vblkb), "Wo": W_out, "bof": bof})
    return maps



def build_F():
    nc = bass.Bass("TRN2", target_bir_lowering=False)
    master = ExitStack()
    with master:
        def idram(name, shape, dt=F32):
            return nc.dram_tensor(name, list(shape), dt, kind="Internal").ap()

        qT = idram("qT", [128, TOK])
        cgki = idram("cgki", [128, TOK], BF16)
        cgko = idram("cgko", [512, TOK], BF16)
        cgvi = idram("cgvi", [16, TOK, 128], BF16)
        cgvo = idram("cgvo", [4 * 16 * TOK, 128], BF16)
        x1T = idram("x1T", [D, TOK])
        x2T = idram("x2T", [D, TOK])
        x3T = idram("x3T", [D, TOK])
        cqin = idram("cqin", [1, 128])
        cqout = idram("cqout", [8, 128])
        scr2 = idram("scr2", [1, 2])
        wq1in = idram("wq1in", [128, 4096], BF16)
        wq2in = idram("wq2in", [512, D], BF16)
        gwq1 = idram("gwq1", [8, 524288], BF16)
        gwq2 = idram("gwq2", [8, 524288], BF16)
        dmx_in = idram("dmx_in", [4, 2080, 512], BF16)
        dmxo = [idram(f"dmxo{i}", [4 * 2080, 512], BF16) for i in range(4)]
        yskipT = idram("yskipT", [D, TOK], BF16)
        yp_in = idram("yp_in", [4 * D, TOK], BF16)
        rs_out = idram("rs_out", [D, TOK], BF16)
        masko = idram("masko", [16, 128, 8])
        combpre = idram("combpre", [16, 128, 8])
        maskg = idram("maskg", [16384, 8])

        phases = []

        def mk_phase(pfx, decl_fn, prefill):
            pst = {"_nc": nc, "_pfx": pfx + "_"}
            pst.update(prefill)
            pstack = ExitStack()
            pst["_stack"] = pstack
            with pstack:
                decl_fn(nc, pst)
            return pst

        xT_ext = nc.dram_tensor("xT", [D, TOK], F32, kind="ExternalInput").ap()
        stCq = {"_nc": nc, "_pfx": "Cq_", "_stack": master,
                "cqin": cqin, "cqout": cqout, "scr2": scr2,
                "wq1in": wq1in, "wq2in": wq2in,
                "gwq1": gwq1, "gwq2": gwq2}
        decl_Cq(nc, stCq)
        stA = mk_phase("A", decl_A, {"xT": xT_ext, "vblk": cgvi, "qf": qT, "kf": cgki})
        stB = mk_phase("B", decl_B2, {
            "xT": xT_ext, "qf": qT, "x1T": x1T,
            "kb4": [cgko[r * 128:(r + 1) * 128, :] for r in range(4)],
            "vb4": (lambda r, ec: cgvo[r * 16 * TOK + ec * TOK:
                                           r * 16 * TOK + (ec + 1) * TOK, :].rearrange(
                "(kt p) e -> p kt e", p=128)),
        })
        stCq["ps_a"] = stA["pbc"]
        stCq["ps_b"] = stB["pc"]
        stC = mk_phase("C", decl_C, {"x1T": x1T, "x2T": x2T, "gwq1": gwq1, "gwq2": gwq2})
        stD1 = mk_phase("D1", decl_D1x, {
            "x2T": x2T,
            "deltaT": [dmx_in[i, 0:1024, :] for i in range(4)],
            "m1T": [dmx_in[i, 1024:2048, :] for i in range(4)],
            "BpCT": [dmx_in[i, 2048:2080, :] for i in range(4)],
            "yskipT": yskipT})

        def dmx_cc_tc(ctx, tc, toks):
            for tk in toks:
                ctx.wait_tok("g", tk)
            ctx.op("g", lambda g, tc=tc: g.collective_compute(
                "AllGather", ALU.bypass, replica_groups=[[0, 1, 2, 3], [4, 5, 6, 7]],
                ins=[dmx_in[tc, :, :]], outs=[dmxo[tc]]), sem="cc")
        stD1["cc_tc"] = dmx_cc_tc
        stD2 = mk_phase("D2", decl_D2, {"dmx_out": dmxo, "yp_in": yp_in})
        stD3 = mk_phase("D3", decl_D3, {
            "x2T": x2T, "yskipT": yskipT, "x3T": x3T, "ypr": rs_out,
            "D3_srcs": ["ypr", "yskipT"]})
        stE1 = mk_phase("E1", decl_E1, {"x3T": x3T, "masko": masko, "combpre": combpre})
        stE2 = mk_phase("E2", decl_E2, {"x3T": x3T, "maskg": maskg,
                                        "maskown": masko, "combpre": combpre})

        sems = {}
        semnames = ["v", "s", "t", "g", "cc"]
        for i in range(Ctx.DMA_ROT):
            semnames += [f"y{i}", f"gd{i}"]
        for sname in semnames:
            sems[sname] = master.enter_context(nc.semaphore("sem_" + sname))

        def body_F(ctx):
            body_Cq_p1(ctx, stCq)
            body_A(ctx, stA)
            ctx.wait("g", "y")
            ctx.full_barrier()
            ctx.op("g", lambda g: g.collective_compute(
                "AllGather", ALU.bypass, replica_groups=[[0, 1, 2, 3], [4, 5, 6, 7]],
                ins=[cgki], outs=[cgko]), sem="cc")
            ctx.op("g", lambda g: g.collective_compute(
                "AllGather", ALU.bypass, replica_groups=[[0, 1, 2, 3], [4, 5, 6, 7]],
                ins=[cgvi.rearrange("a b c -> (a b) c")],
                outs=[cgvo]), sem="cc")
            ctx.op("g", lambda g: g.collective_compute(
                "AllGather", ALU.bypass, replica_groups=[[0, 1, 2, 3, 4, 5, 6, 7]],
                ins=[cqin], outs=[cqout]), sem="cc")
            ctx.full_barrier()
            body_Cq_p2(ctx, stCq)
            ctx.wait("t", "v")
            body_B2(ctx, stB)
            ctx.full_barrier()
            body_C(ctx, stC)
            ctx.full_barrier()
            body_D1x(ctx, stD1)
            ctx.full_barrier()
            body_D2(ctx, stD2)
            ctx.full_barrier()
            ctx.op("g", lambda g: g.collective_compute(
                "ReduceScatter", ALU.add, replica_groups=[[0, 1, 2, 3], [4, 5, 6, 7]],
                ins=[yp_in], outs=[rs_out]), sem="cc")
            ctx.full_barrier()
            body_D3(ctx, stD3)
            ctx.full_barrier()
            body_E1(ctx, stE1)
            ctx.full_barrier()
            ctx.op("g", lambda g: g.collective_compute(
                "AllGather", ALU.bypass, replica_groups=[[0, 1, 2, 3, 4, 5, 6, 7]],
                ins=[masko.rearrange("a b c -> (a b) c")], outs=[maskg]), sem="cc")
            ctx.full_barrier()
            body_E2(ctx, stE2)

        with nc.Block() as block:
            def run_pass(en, engine_obj):
                ctx = Ctx()
                ctx.cur = en
                ctx.eng = {en: engine_obj}
                ctx.sems = sems
                body_F(ctx)

            @block.vector
            def _(vector):
                run_pass("v", vector)

            @block.scalar
            def _(scalar):
                run_pass("s", scalar)

            @block.tensor
            def _(tensor):
                run_pass("t", tensor)

            @block.gpsimd
            def _(gpsimd):
                run_pass("g", gpsimd)

            @block.sync
            def _(sync):
                run_pass("y", sync)
                ctx = Ctx()
                ctx.cur = None
                body_F(ctx)
                for i in range(Ctx.DMA_ROT):
                    if ctx.cnt[f"y{i}"]:
                        sync.wait_ge(sems[f"y{i}"], ctx.cnt[f"y{i}"])
                    if ctx.cnt[f"gd{i}"]:
                        sync.wait_ge(sems[f"gd{i}"], ctx.cnt[f"gd{i}"])
    return nc


def stage_F(inputs):
    inp = inputs
    x = np.ascontiguousarray(inp["L0_attn_inputs"], dtype=np.float32)
    W_uv, b_uv = inp["L0_attn_W_uv"], inp["L0_attn_b_uv"]
    Wvb = np.ascontiguousarray(W_uv[:, 2048:4224])
    bvbf = np.ascontiguousarray(b_uv[4096:4224].reshape(1, 128).T)
    bvrow = np.ascontiguousarray(b_uv[2048:4096].reshape(1, 2048))
    gamma, beta = inp["L0_attn_gamma"], inp["L0_attn_beta"]
    gb = np.ascontiguousarray(np.stack([gamma[0], beta[0], gamma[1], beta[1]], axis=1))
    Wu = np.ascontiguousarray(W_uv[:, 0:2048])
    buf = np.ascontiguousarray(b_uv[0:2048].reshape(16, 128).T)
    bof = np.ascontiguousarray(inp["L0_attn_b_out"].reshape(8, 128).T)
    W1T = np.ascontiguousarray(inp["L0_ffn_W1"], dtype=np.float32).T
    W2T = np.ascontiguousarray(inp["L0_ffn_W2"], dtype=np.float32).T
    b1f = np.ascontiguousarray(inp["L0_ffn_b1"].reshape(32, 128).T)
    b2f = np.ascontiguousarray(inp["L0_ffn_b2"].reshape(8, 128).T)
    nwf = np.ascontiguousarray(inp["L1_attn_norm_weight"].reshape(8, 128).T)
    bdf = np.ascontiguousarray(inp["L1_attn_b_delta"].reshape(8, 128).T)
    dskf = np.ascontiguousarray(inp["L1_attn_D_skip"].reshape(8, 128).T)
    WBC = np.ascontiguousarray(np.concatenate([inp["L1_attn_W_B"], inp["L1_attn_W_C"]], axis=1))
    Wd = np.ascontiguousarray(inp["L1_attn_W_delta"])
    Alog = np.ascontiguousarray(inp["L1_attn_A_log"])
    gwT = np.ascontiguousarray(inp["L1_ffn_gate_w"].T)
    wgT = np.ascontiguousarray(inp["L1_ffn_we_gate"].transpose(0, 2, 1))
    wuT = np.ascontiguousarray(inp["L1_ffn_we_up"].transpose(0, 2, 1))
    wdT = np.ascontiguousarray(inp["L1_ffn_we_down"].transpose(0, 2, 1))
    wsgT = np.ascontiguousarray(inp["L1_ffn_ws_gate"].T)
    wsuT = np.ascontiguousarray(inp["L1_ffn_ws_up"].T)
    wsdT = np.ascontiguousarray(inp["L1_ffn_ws_down"].T)
    maps = []
    for c in range(NCORE):
        b, qd = c // 4, c % 4
        xT = _fm(x[b, qd * TOK:(qd + 1) * TOK, :])
        A4 = np.ascontiguousarray(Alog[:, qd * 4:(qd + 1) * 4])
        bpcsel = np.zeros((32, 8), np.float32)
        for n in range(4):
            bpcsel[qd * 4 + n, n] = 1.0
            bpcsel[16 + qd * 4 + n, 4 + n] = 1.0
        rowsel = np.zeros((128, 1), np.float32)
        rowsel[0:c * 16, 0] = 1.0
        maps.append({
            "xT": xT, "Wvb": Wvb, "bvbf": bvbf, "bvrow": bvrow, "gb": gb,
            "Wu": Wu, "buf": buf, "Wo": inp["L0_attn_W_out"], "bof": bof,
            "b1f": b1f, "b2f": b2f,
            "Wqs1": np.ascontiguousarray(W1T[c * 128:(c + 1) * 128]),
            "Wqs2": np.ascontiguousarray(W2T[c * 512:(c + 1) * 512]),
            "nwf": nwf, "Wd": Wd, "bdf": bdf, "WBC": WBC, "dskf": dskf,
            "A4": A4, "bpcsel": bpcsel, "rowsel": rowsel,
            "gwT": gwT,
            "wgT": wgT, "wuT": wuT, "wdT": wdT,
            "wsgT": wsgT, "wsuT": wsuT, "wsdT": wsdT,
        })
    return maps


def kernel_fused(**inputs):
    nc = _get_nc("F", build_F)
    maps = stage_F(inputs)
    res = run_launch(nc, maps)
    out = np.empty((2, S, D), np.float32)
    for c in range(NCORE):
        b, qd = c // 4, c % 4
        out[b, qd * TOK:(qd + 1) * TOK, :] = np.asarray(res[c]["x4T"]).T
    return out
